# version 04
# baseline (speedup 1.0000x reference)
import numpy as np
from contextlib import ExitStack
import ml_dtypes

import concourse.bass as bass
import concourse.mybir as mybir
from concourse.bass import IndirectOffsetOnAxis
import bass_rust
ROP = bass_rust.ReduceOp
from concourse.bass_utils import run_bass_kernel_spmd

F32 = mybir.dt.float32
BF16 = mybir.dt.bfloat16
I32 = mybir.dt.int32
AF = mybir.ActivationFunctionType
OP = mybir.AluOpType
AX = mybir.AxisListType

D = 1024
E = 2048
S = 16
RK = 64
H = 4096
LB = 512
KD = D // 128
KE = E // 128
KH = H // 128
MAGIC = 12582912.0
EPS = 1e-5

ENGS = ("sync", "vector", "scalar", "tensor", "gpsimd")
SEMS = ("sq", "sp", "sg", "sv", "ss", "st", "sgp", "scc")
_QSEM = {"sync": "sq", "scalar": "sp", "gpsimd": "sg"}
_ESEM = {"vector": "sv", "scalar": "ss", "tensor": "st", "gpsimd": "sgp"}


class Prog:
    def __init__(self, nc):
        self.nc = nc
        self.q = {e: [] for e in ENGS}
        self.cnt = {s: 0 for s in SEMS}
        self.sem = {}

    def _push(self, eng, fn, inc, waits):
        sems = self.sem

        def run(h):
            for sname, val in waits:
                if val > 0:
                    h.wait_ge(sems[sname], val)
            ins = fn(h)
            if ins is not None and inc is not None:
                ins.then_inc(sems[inc[0]], inc[1])
        self.q[eng].append(run)

    def op(self, eng, fn, waits=()):
        s = _ESEM[eng]
        selfw = ((s, self.cnt[s]),)
        self.cnt[s] += 1
        self._push(eng, fn, (s, 1), tuple(waits) + selfw)
        return self.cnt[s]

    def dma(self, eng, fn, waits=()):
        s = _QSEM[eng]
        self.cnt[s] += 16
        self._push(eng, fn, (s, 16), tuple(waits))
        return self.cnt[s]

    def cc(self, fn, waits=()):
        self.cnt["scc"] += 1
        self._push("gpsimd", fn, ("scc", 1), tuple(waits))
        return self.cnt["scc"]

    def wait(self, eng, waits):
        self._push(eng, lambda h: None, None, tuple(waits))

    def barrier(self):
        snap = [(s, self.cnt[s]) for s in SEMS if self.cnt[s] > 0]
        for eng in ENGS:
            self.wait(eng, snap)

    def finalize(self, block):
        secs = {"sync": block.sync, "vector": block.vector, "scalar": block.scalar,
                "tensor": block.tensor, "gpsimd": block.gpsimd}
        for eng in ENGS:
            fns = self.q[eng]
            if not fns:
                continue
            def body(h, fns=tuple(fns)):
                for f in fns:
                    f(h)
            secs[eng](body)


IDXC = {}
_col = 0
for _j in range(4):
    for _q in range(4):
        IDXC[("dt", _j, _q)] = _col; _col += 1
        IDXC[("dx", _j, _q)] = _col; _col += 1
for _j in range(4):
    IDXC[("bcs", _j)] = _col; _col += 1
for _ch in range(16):
    IDXC[("y", _ch)] = _col; _col += 1
for _ob in range(8):
    IDXC[("xh", _ob)] = _col; _col += 1
NIDX = 96


def build_nc(NB, debug=False):
    NR = NB // 4
    OB = NB // 4
    nc = bass.Bass("TRN2", target_bir_lowering=False, detect_race_conditions=False)

    dr = lambda n, shp, dt_: nc.dram_tensor(n, shp, dt_, kind="ExternalInput").ap()
    x_own = dr("x_own", [OB, LB + 3, D], F32)
    in_projT = dr("in_projT", [D, 2 * E], F32)
    out_projT = dr("out_projT", [E, D], F32)
    w1T = dr("w1T", [D, H], F32)
    w2T = dr("w2T", [H, D], F32)
    x_projTd = dr("x_projTd", [E, 96], F32)
    dt_projTd = dr("dt_projTd", [RK, E], F32)
    ln_g8 = dr("ln_g8", [128, KD], F32)
    ln_b8 = dr("ln_b8", [128, KD], F32)
    A_log_my = dr("A_log_my", [128, 4 * S], F32)
    conv_w64 = dr("conv_w64", [128, KE * 4], F32)
    conv_b16 = dr("conv_b16", [128, KE], F32)
    dtb16 = dr("dtb16", [128, KE], F32)
    D16 = dr("D16", [128, KE], F32)
    fcw96 = dr("fcw96", [128, KH * 3], F32)
    fcb32 = dr("fcb32", [128, KH], F32)
    rs_col = dr("rs_col", [128, 1], F32)
    idxs = dr("idxs", [128, NIDX], I32)
    maskcv = dr("maskcv", [128, 3 * 8], F32)
    maskfn = dr("maskfn", [128, 2 * 8], F32)
    identc = dr("identc", [128, 128], BF16)
    identf = dr("identf", [128, 128], F32)
    sel32 = dr("sel32", [32, 32 * 128], BF16)
    onesk2 = dr("onesk2", [2, 128], BF16)

    y_out = nc.dram_tensor("y_out", [OB, LB, D], F32, kind="ExternalOutput").ap()
    dbgt = nc.dram_tensor("dbgt", [8, 128, 4160], F32, kind="ExternalOutput").ap() if debug else None

    di = lambda n, shp, dt_: nc.dram_tensor(n, shp, dt_, kind="Internal").ap()
    wb_ip = di("wb_ip", [D, 2 * E], BF16)
    wb_op = di("wb_op", [E, D], BF16)
    wb_w1 = di("wb_w1", [D, H], BF16)
    wb_w2 = di("wb_w2", [H, D], BF16)
    ccF_in = di("ccF_in", [NR, 4128, LB], F32)
    ccF_out = [di(f"ccF_out{r}", [4 * 4128, LB], F32) for r in range(NR)]
    ccR_in = di("ccR_in", [NR, 4 * 512, LB], F32)
    ccR_out = [di(f"ccR_out{r}", [4 * 4 * 512, LB], F32) for r in range(NR)]
    ccX_in = di("ccX_in", [NR, 2, D], F32)
    ccX_out = di("ccX_out", [NR * 8, D], F32)
    spillX = di("spillX", [OB, 128, KE * LB], BF16)
    spillZ = di("spillZ", [OB, 128, KE * LB], BF16)
    spillH = di("spillH", [OB, 128, KH * LB], BF16)
    spillH2 = di("spillH2", [OB, 128, KH * LB], BF16)
    spillM = di("spillM", [OB, 128, 4 * D], F32)
    bias0d = di("bias0d", [2 * E], F32)

    es = ExitStack()
    sb = lambda n, shp, dt_: es.enter_context(nc.sbuf_tensor(n, shp, dt_))
    pt = lambda n, shp: es.enter_context(nc.psum_tensor(n, shp, F32))
    p = Prog(nc)
    for sname in SEMS:
        p.sem[sname] = es.enter_context(nc.semaphore(sname))

    ident = sb("ident", [128, 128], BF16)
    idf = sb("idf", [128, 128], F32)
    selt = sb("selt", [32, 32 * 128], BF16)
    ok2 = sb("ok2", [2, 128], BF16)
    k2st = sb("k2st", [2, 512], F32)
    k2hi = sb("k2hi", [2, 512], BF16)
    k2lo = sb("k2lo", [2, 512], BF16)
    lnG = sb("lnG", [128, KD], F32)
    lnB = sb("lnB", [128, KD], F32)
    lnBb = sb("lnBb", [128, KD], BF16)
    A64 = sb("A64", [128, 4 * S], F32)
    cw64 = sb("cw64", [128, KE * 4], F32)
    cb16 = sb("cb16", [128, KE], F32)
    dtb = sb("dtb", [128, KE], F32)
    Dc16 = sb("Dc16", [128, KE], F32)
    fcw = sb("fcw", [128, KH * 3], F32)
    fcb = sb("fcb", [128, KH], F32)
    rsc = sb("rsc", [128, 1], F32)
    magic = sb("magic", [128, 1], F32)
    idxt = sb("idxt", [128, NIDX], I32)
    mcv = sb("mcv", [128, 3 * 8], F32)
    mfn = sb("mfn", [128, 2 * 8], F32)
    bias0 = sb("bias0", [128, 2 * KE], F32)
    xpT = sb("xpT", [128, KE * 96], BF16)
    dtpT = sb("dtpT", [128, E], BF16)
    statc = sb("statc", [128, 8 * OB], F32)
    colw = sb("colw", [128, 16], F32)

    ar1 = sb("ar1", [128, KE * LB + 2 * D], F32)
    ar2 = sb("ar2", [128, KE * LB], F32)
    ar3 = sb("ar3", [128, 9520], F32)
    ar3b = ar3[:].bitcast(BF16)
    xslb = ar3b[:, 0:KE * LB]
    xnf = ar3b[:, KE * LB:KE * LB + KD * 520]
    xcvs = ar3[:, 6176:6176 + 2 * 520]
    tcv = ar3[:, 7216:7216 + 2 * LB]
    zbs = ar3b[:, 2 * 8240:2 * 8240 + 2 * LB]
    xdb = ar3[:, 8752:8752 + LB]
    xdbb = ar3b[:, 2 * 9264:2 * 9264 + LB]
    stat = sb("stat", [128, 40], F32)

    ppS1 = pt("ppS1", [128, 1024])
    ppS2 = pt("ppS2", [128, 1024])
    ppP1 = pt("ppP1", [128, 1024])
    ppP2 = pt("ppP2", [128, 1024])
    ppA = ppS1[:, 0:512]
    ppB = ppS1[:, 512:1024]
    ppF = ppS2[:, 0:512]
    ppG = ppS2[:, 512:1024]
    ppD = ppP1
    ppE = ppP2
    psms8 = [ppA, ppB, ppF, ppG, ppD[:, 0:512], ppD[:, 512:1024],
             ppE[:, 0:512], ppE[:, 512:1024]]
    pbank4 = [ppP1[:, 0:512], ppP1[:, 512:1024],
              ppP2[:, 0:512], ppP2[:, 512:1024]]

    xres = ar1[:, 0:4 * D]
    xhal = ar1[:, 4 * D:5 * D]
    ar1b = ar1[:].bitcast(BF16)
    xnbB = ar1[:, 5 * D:9 * D]
    xnhB = ar1[:, 9 * D:10 * D]
    xsl = ar1[:, 0:KE * LB]
    ar2b = ar2[:].bitcast(BF16)
    wstr = [ar2b[:, k * 2048:(k + 1) * 2048] for k in range(KD)]
    dtt = ar2[:, 0:KE * LB]
    wfull = ar1[:, 0:2 * E]
    wcastB = ar1b[:, 2 * (2 * E):2 * (2 * E) + 2 * E]
    stat0row = ar2[:, 0:2 * E]

    def bcast_row(psdst, w, src_row_off, waitv):
        c1 = p.op("vector", lambda h: h.tensor_copy(
            k2hi[0:1, src_row_off:src_row_off + w],
            k2st[0:1, src_row_off:src_row_off + w]), waits=[("sv", waitv)])
        c2 = p.op("vector", lambda h: h.tensor_tensor(
            k2st[1:2, src_row_off:src_row_off + w] if False else
            k2lo[0:1, src_row_off:src_row_off + w],
            k2st[0:1, src_row_off:src_row_off + w],
            k2hi[0:1, src_row_off:src_row_off + w], OP.subtract))
        t1 = p.op("tensor", lambda h: h.matmul(
            psdst, ok2[:, 0:128], k2hi[0:2, src_row_off:src_row_off + w],
            start=True, stop=False), waits=[("sv", c2)])
        t2 = p.op("tensor", lambda h: h.matmul(
            psdst, ok2[:, 0:128], k2lo[0:2, src_row_off:src_row_off + w],
            start=False, stop=True))
        return t2



    for dst, src in ((ident[:], identc), (idf[:], identf), (lnG[:], ln_g8),
                     (lnB[:], ln_b8), (A64[:], A_log_my), (cw64[:], conv_w64),
                     (cb16[:], conv_b16), (dtb[:], dtb16), (Dc16[:], D16),
                     (fcw[:], fcw96), (fcb[:], fcb32), (rsc[:], rs_col),
                     (idxt[:], idxs), (mcv[:], maskcv), (mfn[:], maskfn),
                     (selt[:], sel32), (ok2[:], onesk2)):
        p.dma("sync", lambda h, d=dst, s=src: h.dma_start(out=d, in_=s))
    ld0 = p.cnt["sq"]
    p.dma("gpsimd", lambda h: h.dma_start(
        out=xpT[:].rearrange("p (k n) -> p k n", n=96),
        in_=x_projTd.rearrange("(k p) n -> p k n", p=128)))
    p.dma("gpsimd", lambda h: h.dma_start(out=dtpT[0:RK, :], in_=dt_projTd))
    lg0 = p.cnt["sg"]
    sA = p.op("scalar", lambda h: h.activation(A64[:], A64[:], AF.Exp),
              waits=[("sq", ld0)])
    p.op("vector", lambda h: h.tensor_scalar_mul(A64[:], A64[:], -1.0),
         waits=[("ss", sA)])
    p.op("vector", lambda h: h.tensor_copy(lnBb[:], lnB[:]), waits=[("sq", ld0)])
    p.op("vector", lambda h: h.memset(magic[:], MAGIC))
    p.op("vector", lambda h: h.memset(k2st[:], 0.0))
    p.op("vector", lambda h: h.memset(k2hi[:], 0.0))
    p.op("vector", lambda h: h.memset(k2lo[:], 0.0))

    for k in range(KD):
        ld = p.dma("sync", lambda h, k=k: h.dma_start(
            out=wfull, in_=in_projT[k * 128:(k + 1) * 128, :]),
            waits=[("sv", p.cnt["sv"])])
        c1 = p.op("vector", lambda h: h.tensor_copy(wcastB, wfull),
                  waits=[("sq", p.cnt["sq"])])
        for nt in range(8):
            p.op("tensor", lambda h, k=k, nt=nt: h.matmul(
                psms8[nt][0:1, :], lnBb[:, k:k + 1],
                wcastB[:, nt * 512:(nt + 1) * 512],
                start=(k == 0), stop=(k == KD - 1)),
                waits=[("sv", c1)] if nt == 0 else ())
        c2 = p.op("vector", lambda h, k=k: h.tensor_scalar(
            wcastB, wfull, lnG[:, k:k + 1], None, OP.mult),
            waits=[("st", p.cnt["st"])])
        p.dma("sync", lambda h, k=k: h.dma_start(
            out=wb_ip[k * 128:(k + 1) * 128, :], in_=wcastB), waits=[("sv", c2)])
    cb0 = 0
    for nt in range(8):
        cb0 = p.op("vector", lambda h, nt=nt: h.tensor_copy(
            stat0row[0:1, nt * 512:(nt + 1) * 512], psms8[nt][0:1, :]),
            waits=[("st", p.cnt["st"])] if nt == 0 else ())
    db0 = p.dma("sync", lambda h: h.dma_start(out=bias0d, in_=stat0row[0:1, :]),
                waits=[("sv", cb0)])
    p.dma("sync", lambda h: h.dma_start(
        out=bias0[:], in_=bias0d.rearrange("(c p) -> p c", p=128),
        allow_slow_non_contiguous=True), waits=[("sq", db0)])

    for k in range(KE):
        ld = p.dma("sync", lambda h, k=k: h.dma_start(
            out=wfull[:, 0:D], in_=out_projT[k * 128:(k + 1) * 128, :]),
            waits=[("sv", p.cnt["sv"])])
        c = p.op("vector", lambda h: h.tensor_copy(wcastB[:, 0:D], wfull[:, 0:D]),
                 waits=[("sq", p.cnt["sq"])])
        p.dma("sync", lambda h, k=k: h.dma_start(
            out=wb_op[k * 128:(k + 1) * 128, :], in_=wcastB[:, 0:D]),
            waits=[("sv", c)])

    def quant_weight(wsrc, wdst, rows, cols, sdst):
        nch = rows // 128
        acc = colw[:, 0:1]
        last = p.op("vector", lambda h: h.memset(acc, 0.0))
        for k in range(nch):
            ld = p.dma("sync", lambda h, k=k: h.dma_start(
                out=wfull[:, 0:cols], in_=wsrc[k * 128:(k + 1) * 128, :]),
                waits=[("sv", p.cnt["sv"])])
            p.op("vector", lambda h: h.tensor_reduce(
                colw[:, 1:2], wfull[:, 0:cols], AX.X, OP.add,
                apply_absolute_value=True), waits=[("sq", ld)])
            last = p.op("vector", lambda h: h.tensor_add(acc, acc, colw[:, 1:2]))
        tt = p.op("tensor", lambda h: h.matmul(
            ppB[0:1, 0:128], acc, idf[:], is_transpose=True, start=True, stop=True),
            waits=[("sv", last)])
        rr = p.op("vector", lambda h: h.tensor_reduce(
            k2st[0:1, 0:1], ppB[0:1, 0:128], AX.X, OP.add), waits=[("st", tt)])
        p.op("vector", lambda h: h.tensor_scalar_mul(
            k2st[0:1, 0:1], k2st[0:1, 0:1], 1.0 / (rows * cols)))
        p.op("vector", lambda h: h.tensor_scalar_max(k2st[0:1, 0:1], k2st[0:1, 0:1], 1e-5))
        rr = p.op("vector", lambda h: h.reciprocal(k2st[0:1, 1:2], k2st[0:1, 0:1]))
        t2 = bcast_row(ppB[:, 0:1], 1, 1, rr)
        cs = p.op("vector", lambda h, sdst=sdst: h.tensor_copy(sdst, ppB[:, 0:1]),
                  waits=[("st", t2)])
        for k in range(nch):
            ld = p.dma("sync", lambda h, k=k: h.dma_start(
                out=wfull[:, 0:cols], in_=wsrc[k * 128:(k + 1) * 128, :]),
                waits=[("sv", p.cnt["sv"]), ("sq", p.cnt["sq"])])
            p.op("vector", lambda h, sdst=sdst: h.tensor_scalar(
                wfull[:, 0:cols], wfull[:, 0:cols], sdst, magic[:, 0:1],
                OP.mult, OP.add), waits=[("sq", ld)])
            p.op("vector", lambda h: h.tensor_scalar(
                wfull[:, 0:cols], wfull[:, 0:cols], magic[:, 0:1], 1.0,
                OP.subtract, OP.min))
            c1 = p.op("vector", lambda h: h.tensor_scalar(
                wcastB[:, 0:cols], wfull[:, 0:cols], -1.0, None, OP.max))
            p.dma("sync", lambda h, k=k: h.dma_start(
                out=wdst[k * 128:(k + 1) * 128, :], in_=wcastB[:, 0:cols]),
                waits=[("sv", c1)])

    sw1 = colw[:, 8:9]
    sw2 = colw[:, 9:10]
    quant_weight(w1T, wb_w1, D, H, sw1)
    quant_weight(w2T, wb_w2, H, D, sw2)
    rw1 = colw[:, 10:11]
    rw2 = colw[:, 11:12]
    p.op("vector", lambda h: h.reciprocal(rw1, sw1))
    p.op("vector", lambda h: h.reciprocal(rw2, sw2))
    p.op("vector", lambda h: h.tensor_scalar(fcw[:], fcw[:], rw1, None, OP.mult))

    zv = p.op("vector", lambda h: h.memset(wfull[:, 0:D], 0.0))
    nzr = (NR * 8 + 127) // 128
    for zi in range(nzr):
        r0, r1 = zi * 128, min((zi + 1) * 128, NR * 8)
        p.dma("sync", lambda h, r0=r0, r1=r1: h.dma_start(
            out=ccX_out[r0:r1, :], in_=wfull[0:r1 - r0, 0:D]), waits=[("sv", zv)])

    p.barrier()

    def rsqrt_newton(w, dst, var_col, s0, s1):
        cv = p.op("vector", lambda h: h.tensor_scalar_add(s0[0:w, :], var_col[0:w, :], EPS))
        c = p.op("scalar", lambda h: h.activation(s1[0:w, :], s0[0:w, :], AF.Sqrt),
                 waits=[("sv", cv)])
        p.op("vector", lambda h: h.reciprocal(dst[0:w, :], s1[0:w, :]), waits=[("ss", c)])
        p.op("vector", lambda h: h.tensor_mul(s1[0:w, :], dst[0:w, :], dst[0:w, :]))
        p.op("vector", lambda h: h.tensor_mul(s1[0:w, :], s1[0:w, :], s0[0:w, :]))
        p.op("vector", lambda h: h.tensor_scalar(s1[0:w, :], s1[0:w, :], -0.5, 1.5,
                                                 OP.mult, OP.add))
        return p.op("vector", lambda h: h.tensor_mul(dst[0:w, :], dst[0:w, :], s1[0:w, :]))

    def ln_stats(src, w, mu_dst, r1_dst, rtot_dst=None):
        bno = stat[:, 20:32]
        p.op("vector", lambda h: h.bn_stats(bno[0:w, 0:6], src[0:w, 0:D // 2]))
        p.op("vector", lambda h: h.bn_stats(bno[0:w, 6:12], src[0:w, D // 2:D]))
        p.op("vector", lambda h: h.bn_aggr(stat[0:w, 32:34], bno[0:w, 0:12]))
        p.op("vector", lambda h: h.tensor_copy(mu_dst[0:w, :], stat[0:w, 32:33]))
        rsqrt_newton(w, r1_dst, stat[:, 33:34], stat[:, 35:36], stat[:, 36:37])
        if rtot_dst is not None:
            p.op("vector", lambda h: h.tensor_mul(
                stat[0:w, 37:38], r1_dst[0:w, :], r1_dst[0:w, :]))
            p.op("vector", lambda h: h.tensor_mul(
                stat[0:w, 37:38], stat[0:w, 37:38], stat[0:w, 33:34]))
            rsqrt_newton(w, stat[:, 38:39], stat[:, 37:38], stat[:, 35:36],
                         stat[:, 36:37])
            return p.op("vector", lambda h: h.tensor_mul(
                rtot_dst[0:w, :], r1_dst[0:w, :], stat[0:w, 38:39]))
        return p.cnt["sv"]


    def bcast_row(psdst, w, src_row_off, waitv):
        c1 = p.op("vector", lambda h: h.tensor_copy(
            k2hi[0:1, src_row_off:src_row_off + w],
            k2st[0:1, src_row_off:src_row_off + w]), waits=[("sv", waitv)])
        c2 = p.op("vector", lambda h: h.tensor_tensor(
            k2st[1:2, src_row_off:src_row_off + w] if False else
            k2lo[0:1, src_row_off:src_row_off + w],
            k2st[0:1, src_row_off:src_row_off + w],
            k2hi[0:1, src_row_off:src_row_off + w], OP.subtract))
        t1 = p.op("tensor", lambda h: h.matmul(
            psdst, ok2[:, 0:128], k2hi[0:2, src_row_off:src_row_off + w],
            start=True, stop=False), waits=[("sv", c2)])
        t2 = p.op("tensor", lambda h: h.matmul(
            psdst, ok2[:, 0:128], k2lo[0:2, src_row_off:src_row_off + w],
            start=False, stop=True))
        return t2

    ccF_done = {}
    ccR_done = {}
    ccX_done = {}
    sends_end = {}
    ysend_tok = {0: 0}
    last_scan_par = {0: 0, 1: 0}

    prev_sends = {"d1": 0, "d2": 0, "d3": 0, "spx": 0}
    for ob in range(OB):
        ldx = p.dma("sync", lambda h, ob=ob: h.dma_start(
            out=xres.rearrange("p (t n) -> p t n", n=D),
            in_=x_own[ob, 3:3 + LB, :].rearrange("(t p) n -> p t n", p=128)),
            waits=[("sq", prev_sends["d3"]), ("sv", p.cnt["sv"]),
                   ("st", p.cnt["st"])])
        ldh = p.dma("sync", lambda h, ob=ob: h.dma_start(
            out=xhal[0:3, :], in_=x_own[ob, 0:3, :]))

        p.wait("vector", [("sq", ldh), ("st", p.cnt["st"])])
        for lt in range(4):
            src = xres[:, lt * D:(lt + 1) * D]
            mu = stat[:, lt:lt + 1]
            rt = stat[:, 4 + lt:5 + lt]
            r1 = stat[:, 34:35]
            ln_stats(src, 128, mu, r1, rt)
            p.op("vector", lambda h, ob=ob, lt=lt, r1=r1: h.tensor_scalar_add(
                statc[:, ob * 8 + lt:ob * 8 + lt + 1], r1, 1.0))
            p.op("vector", lambda h, mu=mu, r1=r1: h.tensor_mul(
                stat[:, 12:13], mu, r1))
            p.op("vector", lambda h, ob=ob, lt=lt: h.tensor_scalar_mul(
                statc[:, ob * 8 + 4 + lt:ob * 8 + 5 + lt], stat[:, 12:13], -1.0))
            p.op("vector", lambda h, src=src, lt=lt, mu=mu, rt=rt: h.tensor_scalar(
                xnbB[:, lt * D:(lt + 1) * D], src, mu, rt, OP.subtract, OP.mult))
        p.wait("vector", [("sq", ldh)])
        muh = stat[:, 8:9]
        rth = stat[:, 9:10]
        ln_stats(xhal, 3, muh, stat[:, 34:35], rth)
        svh = p.op("vector", lambda h: h.tensor_scalar(
            xnhB[0:3, :], xhal[0:3, :], muh[0:3, :], rth[0:3, :],
            OP.subtract, OP.mult))

        sc = p.cnt["ss"]
        for dch in range(KD):
            t0 = p.op("tensor", lambda h, dch=dch: h.matmul(
                ppD[:, 0:3], xnhB[0:3, dch * 128:(dch + 1) * 128],
                idf[0:3, 0:3], is_transpose=True, start=True, stop=True),
                waits=[("sv", svh), ("ss", sc)])
            sc = p.op("scalar", lambda h, dch=dch: h.copy(
                xnf[:, dch * 520:dch * 520 + 3], ppD[:, 0:3]), waits=[("st", t0)])
            for lt in range(4):
                t1 = p.op("tensor", lambda h, dch=dch, lt=lt: h.matmul(
                    ppD[:, 512:640], xnbB[:, lt * D + dch * 128:lt * D + (dch + 1) * 128],
                    idf[:], is_transpose=True, start=True, stop=True),
                    waits=[("ss", sc)])
                sc = p.op("scalar", lambda h, dch=dch, lt=lt: h.copy(
                    xnf[:, dch * 520 + 3 + lt * 128:dch * 520 + 3 + (lt + 1) * 128],
                    ppD[:, 512:640]), waits=[("st", t1)])
        ss_xnf = p.cnt["ss"]
        xnf3 = xnf[:].rearrange("p (c n) -> p c n", n=520)

        zlds = []
        for k in range(KD):
            if k == 0:
                zw = [("st", p.cnt["st"]), ("sq", prev_sends["d3"]),
                      ("sv", p.cnt["sv"])]
            elif k % 2 == 0:
                zw = [("sq", zlds[k - 1])]
            else:
                zw = []
            zlds.append(p.dma("sync", lambda h, k=k: h.dma_start(
                out=wstr[k], in_=wb_ip[k * 128:(k + 1) * 128, E:2 * E]),
                waits=zw))
        p.wait("tensor", [("ss", ss_xnf)])
        z_evac = {0: 0, 1: 0}
        for mz in range(KE):
            psm = ppA if mz % 2 == 0 else ppB
            for k in range(KD):
                w_ = []
                if mz == 0:
                    w_.append(("sq", zlds[k | 1]))
                if k == 0:
                    w_.append(("ss", z_evac[mz % 2]))
                p.op("tensor", lambda h, mz=mz, k=k, psm=psm: h.matmul(
                    psm[:], wstr[k][:, mz * 128:(mz + 1) * 128],
                    xnf3[:, k, 3:515], start=(k == 0), stop=(k == KD - 1)),
                    waits=w_)
            stz = p.cnt["st"]
            zslot = zbs[:, (mz % 2) * LB:(mz % 2 + 1) * LB]
            scz = p.op("scalar", lambda h, mz=mz, psm=psm, zslot=zslot: h.activation(
                zslot, psm[:], AF.Silu, bias=bias0[:, KE + mz:KE + mz + 1]),
                waits=[("st", stz), ("sp", p.cnt["sp"])])
            z_evac[mz % 2] = scz
            prev_sends[("z", mz % 2)] = p.dma(
                "scalar", lambda h, ob=ob, mz=mz, zslot=zslot: h.dma_start(
                    out=spillZ[ob, :, mz * LB:(mz + 1) * LB], in_=zslot),
                waits=[("ss", scz)])

        xlds = []
        for k in range(KD):
            if k == 0:
                xw = [("st", p.cnt["st"])]
            elif k % 2 == 0:
                xw = [("sq", xlds[k - 1])]
            else:
                xw = []
            xlds.append(p.dma("sync", lambda h, k=k: h.dma_start(
                out=wstr[k], in_=wb_ip[k * 128:(k + 1) * 128, 0:E]),
                waits=xw))
        x_evac = 0
        for mx in range(KE):
            for k in range(KD):
                w_ = []
                if mx == 0:
                    w_.append(("sq", xlds[k | 1]))
                if k == 0:
                    w_.append(("sv", p.cnt["sv"]))
                p.op("tensor", lambda h, mx=mx, k=k: h.matmul(
                    ppD[:, 0:257], wstr[k][:, mx * 128:(mx + 1) * 128],
                    xnf3[:, k, 0:257], start=(k == 0), stop=(k == KD - 1)),
                    waits=w_)
            for k in range(KD):
                p.op("tensor", lambda h, mx=mx, k=k: h.matmul(
                    ppD[:, 512:770], wstr[k][:, mx * 128:(mx + 1) * 128],
                    xnf3[:, k, 257:515], start=(k == 0), stop=(k == KD - 1)))
            stx = p.cnt["st"]
            xc = xcvs[:, (mx % 2) * 520:(mx % 2) * 520 + 515]
            p.op("vector", lambda h, mx=mx, xc=xc: h.tensor_scalar(
                xc[:, 0:257], ppD[:, 0:257], bias0[:, mx:mx + 1], None, OP.add),
                waits=[("st", stx)])
            sc2 = p.op("vector", lambda h, mx=mx, xc=xc: h.tensor_scalar(
                xc[:, 257:515], ppD[:, 512:770], bias0[:, mx:mx + 1], None, OP.add))
            x_evac = sc2
            p.op("vector", lambda h, ob=ob, xc=xc: h.tensor_scalar(
                xc[:, 0:3], xc[:, 0:3], mcv[:, 3 * ob:3 * ob + 1], None, OP.mult))
            t = tcv[:, (mx % 2) * LB:(mx % 2 + 1) * LB]
            p.op("vector", lambda h, mx=mx, xc=xc, t=t: h.tensor_scalar(
                t, xc[:, 0:LB], cw64[:, mx * 4:mx * 4 + 1],
                cb16[:, mx:mx + 1], OP.mult, OP.add))
            for kk in range(1, 4):
                p.op("vector", lambda h, mx=mx, xc=xc, t=t, kk=kk: h.scalar_tensor_tensor(
                    t, xc[:, kk:kk + LB], cw64[:, mx * 4 + kk:mx * 4 + kk + 1], t,
                    OP.mult, OP.add))
            svt = p.cnt["sv"]
            scs = p.op("scalar", lambda h, mx=mx, t=t: h.activation(
                xsl[:, mx * LB:(mx + 1) * LB], t, AF.Silu), waits=[("sv", svt)])
            p.op("vector", lambda h, mx=mx: h.tensor_copy(
                xslb[:, mx * LB:(mx + 1) * LB], xsl[:, mx * LB:(mx + 1) * LB]),
                waits=[("ss", scs)])
        sv_x = p.cnt["sv"]
        prev_sends["spx"] = p.dma("scalar", lambda h, ob=ob: h.dma_start(
            out=spillX[ob], in_=xslb[:]), waits=[("sv", sv_x)])

        for k in range(KE):
            p.op("tensor", lambda h, k=k: h.matmul(
                ppF[0:96, :], xpT[:, k * 96:(k + 1) * 96],
                xslb[:, k * LB:(k + 1) * LB], start=(k == 0), stop=(k == KE - 1)),
                waits=[("sv", sv_x)] if k == 0 else ())
        st_xd = p.cnt["st"]
        sv_xd = p.op("vector", lambda h: h.tensor_copy(xdb[0:96, :], ppF[0:96, :]),
                     waits=[("st", st_xd), ("sq", prev_sends["d3"])])
        sv_xdb = p.op("vector", lambda h: h.tensor_copy(xdbb[0:RK, :], xdb[0:RK, :]))

        dt_evac = {0: 0, 1: 0}
        for md in range(KE):
            psm = ppF if md % 2 == 0 else ppG
            w_ = [("ss", dt_evac[md % 2])]
            if md == 0:
                w_.append(("sv", sv_xdb))
            tm = p.op("tensor", lambda h, md=md, psm=psm: h.matmul(
                psm[:], dtpT[0:RK, md * 128:(md + 1) * 128], xdbb[0:RK, :],
                start=True, stop=True), waits=w_)
            p.op("scalar", lambda h, md=md, psm=psm: h.activation(
                dtt[:, md * LB:(md + 1) * LB], psm[:], AF.Exp,
                bias=dtb[:, md:md + 1]), waits=[("st", tm)])
            dt_evac[md % 2] = p.op("scalar", lambda h, md=md: h.activation(
                dtt[:, md * LB:(md + 1) * LB], dtt[:, md * LB:(md + 1) * LB],
                AF.Ln, bias=1.0))
        ss_dt = p.cnt["ss"]

        p.wait("vector", [("ss", ss_dt), ("sp", prev_sends["spx"])])
        for mx in range(KE):
            p.op("vector", lambda h, mx=mx: h.tensor_mul(
                xsl[:, mx * LB:(mx + 1) * LB], xsl[:, mx * LB:(mx + 1) * LB],
                dtt[:, mx * LB:(mx + 1) * LB]))
        sv_dtx = p.cnt["sv"]

        prev_sends["d1"] = p.dma("sync", lambda h, ob=ob: h.dma_start(
            out=ccF_in[ob, 0:E, :].rearrange("(q pp) n -> pp q n", pp=128),
            in_=dtt[:].rearrange("p (q n) -> p q n", n=LB)),
            waits=[("ss", ss_dt)])
        prev_sends["d2"] = p.dma("sync", lambda h, ob=ob: h.dma_start(
            out=ccF_in[ob, E:2 * E, :].rearrange("(q pp) n -> pp q n", pp=128),
            in_=xsl[:].rearrange("p (q n) -> p q n", n=LB)),
            waits=[("sv", sv_dtx)])
        prev_sends["d3"] = p.dma("sync", lambda h, ob=ob: h.dma_start(
            out=ccF_in[ob, 2 * E:2 * E + 32, :], in_=xdb[RK:96, :]),
            waits=[("sv", sv_xdb)])
        ccF_done[ob] = p.cc(lambda h, ob=ob: h.collective_compute(
            "AllGather", OP.bypass,
            replica_groups=[[0, 1, 2, 3], [4, 5, 6, 7]],
            ins=[ccF_in[ob]], outs=[ccF_out[ob]]),
            waits=[("sq", prev_sends["d3"])])

        if debug and ob == 0:
            w0 = p.cnt["sv"]
            s0 = p.cnt["ss"]
            p.dma("sync", lambda h: h.dma_start(out=dbgt[0, :, 0:32], in_=bias0[:]),
                  waits=[("sv", w0), ("ss", s0)])
            p.dma("sync", lambda h: h.dma_start(out=dbgt[0, :, 32:96], in_=A64[:]))
            p.dma("sync", lambda h: h.dma_start(out=dbgt[0, :, 96:112], in_=colw[:]))
            p.dma("sync", lambda h: h.dma_start(out=dbgt[0, :, 112:120],
                                                in_=statc[:, 0:8]))
            p.dma("gpsimd", lambda h: h.dma_start(out=dbgt[1], in_=xnf[:]),
                  waits=[("sv", w0), ("ss", s0)])
            p.dma("sync", lambda h: h.dma_start(out=dbgt[2, :, 0:4096],
                                                in_=xsl[:, 0:4096]))
            p.dma("sync", lambda h: h.dma_start(out=dbgt[3, :, 0:4096],
                                                in_=dtt[:, 0:4096]))
            p.dma("sync", lambda h: h.dma_start(out=dbgt[4, :, 0:512], in_=xdb[:]))
            p.dma("gpsimd", lambda h: h.dma_start(out=dbgt[5, :, 0:4096],
                                                  in_=xslb[:, 0:4096]))
            p.dma("gpsimd", lambda h: h.dma_start(out=dbgt[6, :, 0:KE * 96],
                                                  in_=xpT[:]),
                  waits=[("sv", w0)])
            p.dma("gpsimd", lambda h: h.dma_start(out=dbgt[7, :, 0:2048],
                                                  in_=dtpT[:]))

    p.barrier()

    bcf = ar1[:, 0:LB]
    bchi = ar1b[:, 2 * LB:3 * LB]
    bclo = ar1b[:, 3 * LB:4 * LB]
    rdt = ar1[:, 2 * LB:6 * LB]
    rdx = ar1[:, 6 * LB:10 * LB]
    hbuf = ar1[:, 10 * LB:14 * LB]
    yq = ar1[:, 14 * LB:18 * LB]
    carry = ar1[:, 18 * LB:18 * LB + 64]
    dAs = sb("dAs", [128, 2 * LB], F32)
    usl = sb("usl", [128, 2 * LB], F32)
    yin = sb("yin", [128, 4 * LB], BF16)
    ysl2 = sb("ysl2", [128, 2 * LB], F32)
    xsz = sb("xsz", [128, 4 * LB], BF16)
    xmidt = ar3[:, 0:4 * D]
    ohi = ar3[:, 4 * D:4 * D + KD * LB]
    xoslot = ar3[:, 4 * D + KD * LB:4 * D + KD * LB + D]
    xqf = sb("xqf", [128, KD * 516], BF16)

    hq = sb("hq", [128, 4 * LB], BF16)
    mqt = sb("mqt", [128, LB], F32)
    shb = sb("shb", [128, LB], F32)
    hmini = sb("hmini", [128, D], F32)
    xqtk = hmini
    hslot = sb("hslot", [128, 2 * LB], F32)
    xqf3 = xqf[:].rearrange("p (c n) -> p c n", n=516)

    p.op("vector", lambda h: h.memset(carry[:], 0.0))

    def scan_round(r):
        for blkslot in range(4):
            gw = [("scc", ccF_done[r]), ("ss", p.cnt["ss"]), ("sv", p.cnt["sv"])]
            gd = []
            for qq in range(4):
                gd.append(p.dma("gpsimd", lambda h, r=r, blkslot=blkslot, qq=qq:
                    h.indirect_dma_start(
                        out=rdt[:, qq * LB:(qq + 1) * LB], out_offset=None,
                        in_=ccF_out[r],
                        in_offset=IndirectOffsetOnAxis(
                            ap=idxt[:, IDXC[("dt", blkslot, qq)]:IDXC[("dt", blkslot, qq)] + 1],
                            axis=0)), waits=gw if qq == 0 else ()))
                gd.append(p.dma("gpsimd", lambda h, r=r, blkslot=blkslot, qq=qq:
                    h.indirect_dma_start(
                        out=rdx[:, qq * LB:(qq + 1) * LB], out_offset=None,
                        in_=ccF_out[r],
                        in_offset=IndirectOffsetOnAxis(
                            ap=idxt[:, IDXC[("dx", blkslot, qq)]:IDXC[("dx", blkslot, qq)] + 1],
                            axis=0))))
            p.dma("gpsimd", lambda h, r=r, blkslot=blkslot: h.indirect_dma_start(
                out=bcf[0:32, :], out_offset=None, in_=ccF_out[r],
                in_offset=IndirectOffsetOnAxis(
                    ap=idxt[0:32, IDXC[("bcs", blkslot)]:IDXC[("bcs", blkslot)] + 1],
                    axis=0)))
            gall = p.cnt["sg"]
            ch = p.op("vector", lambda h: h.tensor_copy(bchi[0:32, :], bcf[0:32, :]),
                      waits=[("sg", gall), ("st", p.cnt["st"])])
            cl = p.op("vector", lambda h: h.tensor_tensor(
                bclo[0:32, :], bcf[0:32, :], bchi[0:32, :], OP.subtract))

            for qt in range(4):
                for jp in range(2):
                    selw = [("sv", p.cnt["sv"])]
                    for j2 in range(2):
                        s = 4 * qt + 2 * jp + j2
                        dB = ppS1[:, j2 * 512:(j2 + 1) * 512]
                        dC = ppS2[:, j2 * 512:(j2 + 1) * 512]
                        p.op("tensor", lambda h, s=s, dB=dB: h.matmul(
                            dB, selt[0:32, s * 128:(s + 1) * 128],
                            bchi[0:32, :], start=True, stop=False),
                            waits=selw if j2 == 0 else ())
                        p.op("tensor", lambda h, s=s, dB=dB: h.matmul(
                            dB, selt[0:32, s * 128:(s + 1) * 128],
                            bclo[0:32, :], start=False, stop=True))
                        p.op("tensor", lambda h, s=s, dC=dC: h.matmul(
                            dC, selt[0:32, (16 + s) * 128:(17 + s) * 128],
                            bchi[0:32, :], start=True, stop=False))
                        p.op("tensor", lambda h, s=s, dC=dC: h.matmul(
                            dC, selt[0:32, (16 + s) * 128:(17 + s) * 128],
                            bclo[0:32, :], start=False, stop=True))
                    stsel = p.cnt["st"]
                    for qq in range(4):
                        for j2 in range(2):
                            s = 4 * qt + 2 * jp + j2
                            par = j2
                            da = dAs[:, par * LB:(par + 1) * LB]
                            uu = usl[:, par * LB:(par + 1) * LB]
                            sw_ = ([("sg", gall)]
                                   if (qt == 0 and jp == 0 and qq == 0 and j2 == 0)
                                   else [])
                            sw_.append(("sv", last_scan_par[par]))
                            se = p.op("scalar", lambda h, qq=qq, s=s, da=da: h.activation(
                                da, rdt[:, qq * LB:(qq + 1) * LB], AF.Exp,
                                scale=A64[:, qq * S + s:qq * S + s + 1]), waits=sw_)
                            uw = [("ss", se), ("st", stsel)]
                            if qt == 0 and jp == 0 and qq == 0 and j2 == 0:
                                uw.append(("sq", p.cnt["sq"]))
                            p.op("vector", lambda h, qq=qq, j2=j2, uu=uu: h.tensor_mul(
                                uu, rdx[:, qq * LB:(qq + 1) * LB],
                                ppS1[:, j2 * 512:(j2 + 1) * 512]), waits=uw)
                            last_scan_par[par] = p.op("vector",
                                lambda h, qq=qq, s=s, j2=j2, da=da, uu=uu: h.tensor_tensor_scan(
                                    hbuf[:, j2 * LB:(j2 + 1) * LB], da, uu,
                                    carry[:, qq * S + s:qq * S + s + 1],
                                    OP.mult, OP.add))
                        p.op("vector", lambda h, qq=qq, qt=qt, jp=jp: h.tensor_copy(
                            carry[:, qq * S + qt * 4 + 2 * jp:qq * S + qt * 4 + 2 * jp + 2],
                            hbuf[:, 0:2 * LB].rearrange("p (s n) -> p s n", n=LB)[:, :, LB - 1:LB]))
                        p.op("vector", lambda h: h.tensor_mul(
                            hbuf[:, 0:2 * LB], hbuf[:, 0:2 * LB], ppS2[:, 0:1024]))
                        h3 = hbuf[:, 0:2 * LB].rearrange("p (s n) -> p n s", n=LB)
                        if qt == 0 and jp == 0:
                            p.op("vector", lambda h, qq=qq, h3=h3: h.tensor_reduce(
                                yq[:, qq * LB:(qq + 1) * LB], h3, AX.X, OP.add))
                        else:
                            p.op("vector", lambda h, h3=h3: h.tensor_reduce(
                                hbuf[:, 2 * LB:3 * LB], h3, AX.X, OP.add))
                            p.op("vector", lambda h, qq=qq: h.tensor_add(
                                yq[:, qq * LB:(qq + 1) * LB],
                                yq[:, qq * LB:(qq + 1) * LB], hbuf[:, 2 * LB:3 * LB]))
            svy = p.cnt["sv"]
            ysend_tok[0] = p.dma("sync", lambda h, r=r, blkslot=blkslot: h.dma_start(
                out=ccR_in[r, blkslot * 512:(blkslot + 1) * 512, :]
                    .rearrange("(q pp) n -> pp q n", pp=128),
                in_=yq[:].rearrange("p (q n) -> p q n", n=LB)),
                waits=[("sv", svy)])
        sy = p.cnt["sq"]
        sends_end[r] = sy
        ccR_done[r] = p.cc(lambda h, r=r: h.collective_compute(
            "AllGather", OP.bypass,
            replica_groups=[[0, 1, 2, 3], [4, 5, 6, 7]],
            ins=[ccR_in[r]], outs=[ccR_out[r]]), waits=[("sq", sy)])

    def post_a(r):
        ob = r
        oplds = []
        for k in range(KE):
            if k == 0:
                ow = [("st", p.cnt["st"]), ("sq", sends_end[r])]
            elif k % 4 == 0:
                ow = [("sq", oplds[k - 1])]
            else:
                ow = []
            oplds.append(p.dma("sync", lambda h, k=k: h.dma_start(
                out=ar2b[:, k * D:(k + 1) * D], in_=wb_op[k * 128:(k + 1) * 128, :]),
                waits=ow))
        p.wait("tensor", [("sv", p.cnt["sv"]), ("ss", p.cnt["ss"])])

        def build_yin(k, slot_war):
            par = k % 2
            spar = k % 4
            ys = ysl2[:, par * LB:(par + 1) * LB]
            gy = p.dma("gpsimd", lambda h, r=r, k=k, ys=ys: h.indirect_dma_start(
                out=ys, out_offset=None, in_=ccR_out[r],
                in_offset=IndirectOffsetOnAxis(
                    ap=idxt[:, IDXC[("y", k)]:IDXC[("y", k)] + 1], axis=0)),
                waits=[("scc", ccR_done[r]), ("sv", p.cnt["sv"])])
            xs = xsz[:, par * LB:(par + 1) * LB]
            zs = xsz[:, (2 + par) * LB:(3 + par) * LB]
            lx = p.dma("scalar", lambda h, ob=ob, k=k, xs=xs: h.dma_start(
                out=xs, in_=spillX[ob, :, k * LB:(k + 1) * LB]),
                waits=[("sv", p.cnt["sv"])])
            lz = p.dma("scalar", lambda h, ob=ob, k=k, zs=zs: h.dma_start(
                out=zs, in_=spillZ[ob, :, k * LB:(k + 1) * LB]))
            p.op("vector", lambda h, k=k, ys=ys, xs=xs: h.scalar_tensor_tensor(
                ys, xs, Dc16[:, k:k + 1], ys, OP.mult, OP.add),
                waits=[("sg", gy), ("sp", lz)])
            w_ = [("sp", lz)]
            if slot_war:
                w_.append(("st", slot_war))
            yslot = yin[:, spar * LB:(spar + 1) * LB]
            sv_y = p.op("vector", lambda h, ys=ys, zs=zs, yslot=yslot: h.tensor_mul(
                yslot, ys, zs), waits=w_)
            return yslot, sv_y

        mmk = {}
        for k in range(KE):
            yslot, sv_y = build_yin(k, mmk.get(k - 4))
            for m in range(4):
                p.op("tensor", lambda h, k=k, m=m, yslot=yslot: h.matmul(
                    pbank4[m], ar2b[:, k * D + m * 128:k * D + (m + 1) * 128],
                    yslot, start=(k == 0), stop=(k == KE - 1)),
                    waits=([("sq", oplds[(k // 4) * 4 + 3]), ("sv", sv_y)]
                           if m == 0 else ()))
            mmk[k] = p.cnt["st"]
        stmm = p.cnt["st"]
        ev1 = 0
        for m in range(4):
            ev1 = p.op("vector", lambda h, m=m: h.tensor_copy(
                ohi[:, m * LB:(m + 1) * LB], pbank4[m]),
                waits=[("st", stmm)] if m == 0 else ())
        mmk2nd = {}
        for k in range(KE):
            yslot, sv_y = build_yin(k, mmk2nd.get(k - 4, mmk.get(12 + k)))
            for m in range(4):
                w_ = []
                if m == 0:
                    w_ = [("sv", sv_y)] + ([("sv", ev1)] if k == 0 else [])
                p.op("tensor", lambda h, k=k, m=m, yslot=yslot: h.matmul(
                    pbank4[m], ar2b[:, k * D + (4 + m) * 128:k * D + (5 + m) * 128],
                    yslot, start=(k == 0), stop=(k == KE - 1)), waits=w_)
            mmk2nd[k] = p.cnt["st"]
        stmm = p.cnt["st"]
        ev = 0
        for m in range(4):
            ev = p.op("vector", lambda h, m=m: h.tensor_copy(
                ohi[:, (4 + m) * LB:(5 + m) * LB], pbank4[m]),
                waits=[("st", stmm)] if m == 0 else ())
        svev = ev

        ldxr = p.dma("sync", lambda h, ob=ob: h.dma_start(
            out=xmidt.rearrange("p (t n) -> p t n", n=D),
            in_=x_own[ob, 3:3 + LB, :].rearrange("(t p) n -> p t n", p=128)),
            waits=[("sv", p.cnt["sv"]), ("sp", p.cnt["sp"])])
        xm_done = 0
        for lt in range(4):
            pdst = ppD if lt % 2 == 0 else ppE
            for m in range(KD):
                w_ = []
                if m == 0:
                    w_.append(("sv", svev if lt < 2 else xm_done))
                p.op("tensor", lambda h, lt=lt, m=m, pdst=pdst: h.matmul(
                    pdst[:, m * 128:(m + 1) * 128],
                    ohi[:, m * LB + lt * 128:m * LB + (lt + 1) * 128],
                    idf[:], is_transpose=True, start=True, stop=True), waits=w_)
            sttr = p.cnt["st"]
            p.op("vector", lambda h, ob=ob, lt=lt: h.tensor_scalar(
                xoslot[:, 0:D], xmidt[:, lt * D:(lt + 1) * D],
                statc[:, ob * 8 + lt:ob * 8 + lt + 1],
                statc[:, ob * 8 + 4 + lt:ob * 8 + 5 + lt], OP.mult, OP.add),
                waits=[("sq", ldxr)] if lt == 0 else ())
            xm_done = p.op("vector", lambda h, lt=lt, pdst=pdst: h.scalar_tensor_tensor(
                xmidt[:, lt * D:(lt + 1) * D], pdst[:, 0:D], rsc[:, 0:1],
                xoslot[:, 0:D], OP.mult, OP.add), waits=[("st", sttr)])
        svxm = xm_done
        spm = p.dma("scalar", lambda h, ob=ob: h.dma_start(
            out=spillM[ob], in_=xmidt[:]), waits=[("sv", svxm)])
        d1 = p.dma("sync", lambda h, r=r: h.dma_start(
            out=ccX_in[r, 0:1, :], in_=xmidt[0:1, 0:D]), waits=[("sv", svxm)])
        d2 = p.dma("sync", lambda h, r=r: h.dma_start(
            out=ccX_in[r, 1:2, :], in_=xmidt[127:128, 3 * D:4 * D]))
        ccX_done[r] = p.cc(lambda h, r=r: h.collective_compute(
            "AllGather", OP.bypass,
            replica_groups=[[0, 1, 2, 3], [4, 5, 6, 7]],
            ins=[ccX_in[r]], outs=[ccX_out[r * 8:(r + 1) * 8, :]]),
            waits=[("sq", d2)])

    def post_b(r):
        ob = r
        need_right = ccX_done[min(r + 1, NR - 1)] if (4 * r + 3) + 1 <= NB - 1 else ccX_done[r]
        gh = p.dma("gpsimd", lambda h, ob=ob: h.indirect_dma_start(
            out=hmini[0:2, :], out_offset=None, in_=ccX_out,
            in_offset=IndirectOffsetOnAxis(
                ap=idxt[0:2, IDXC[("xh", ob)]:IDXC[("xh", ob)] + 1], axis=0)),
            waits=[("scc", need_right), ("sv", p.cnt["sv"])])
        p.wait("vector", [("sg", gh)])
        muh = stat[:, 8:9]
        r3h = stat[:, 9:10]
        ln_stats(hmini, 2, muh, r3h)
        p.op("vector", lambda h: h.tensor_scalar(
            hmini[0:2, :], hmini[0:2, :], muh[0:2, :], None, OP.subtract))
        p.op("vector", lambda h: h.tensor_reduce(
            stat[0:2, 10:11], hmini[0:2, :], AX.X, OP.max, apply_absolute_value=True))
        p.op("vector", lambda h: h.tensor_mul(stat[0:2, 10:11], stat[0:2, 10:11], r3h[0:2, :]))
        p.op("vector", lambda h: h.tensor_scalar_max(stat[0:2, 10:11], stat[0:2, 10:11], 1e-5))
        p.op("vector", lambda h: h.reciprocal(stat[0:2, 11:12], stat[0:2, 10:11]))
        p.op("vector", lambda h: h.tensor_scalar_mul(stat[0:2, 11:12], stat[0:2, 11:12], 127.0))
        p.op("vector", lambda h: h.reciprocal(stat[0:2, 12:13], stat[0:2, 11:12]))
        p.op("vector", lambda h: h.tensor_mul(stat[0:2, 14:15], r3h[0:2, :], stat[0:2, 11:12]))
        p.op("vector", lambda h: h.tensor_scalar(
            hmini[0:2, :], hmini[0:2, :], stat[0:2, 14:15], magic[0:2, 0:1],
            OP.mult, OP.add))
        cq = p.op("vector", lambda h: h.tensor_scalar(
            hmini[0:2, :], hmini[0:2, :], magic[0:2, 0:1], stat[0:2, 12:13],
            OP.subtract, OP.mult))
        hvm = cq
        for dch in range(KD):
            t0 = p.op("tensor", lambda h, dch=dch: h.matmul(
                ppD[:, 0:2], hmini[0:2, dch * 128:(dch + 1) * 128],
                idf[0:2, 0:2], is_transpose=True, start=True, stop=True),
                waits=[("sv", hvm)])
            p.op("vector", lambda h, dch=dch, ob=ob: h.tensor_scalar(
                xqf[:, dch * 516:dch * 516 + 1], ppD[:, 0:1],
                mfn[:, 2 * ob:2 * ob + 1], None, OP.mult), waits=[("st", t0)])
            hvm = p.op("vector", lambda h, dch=dch, ob=ob: h.tensor_scalar(
                xqf[:, dch * 516 + 513:dch * 516 + 514], ppD[:, 1:2],
                mfn[:, 2 * ob + 1:2 * ob + 2], None, OP.mult))

        ldm = p.dma("scalar", lambda h, ob=ob: h.dma_start(
            out=xmidt[:], in_=spillM[ob]),
            waits=[("sv", p.cnt["sv"]), ("sp", p.cnt["sp"])])
        p.wait("vector", [("sp", ldm)])
        for lt in range(4):
            src = xmidt[:, lt * D:(lt + 1) * D]
            mu3 = stat[:, 0:1]
            r3 = stat[:, 1:2]
            ln_stats(src, 128, mu3, r3)
            p.op("vector", lambda h, src=src, mu3=mu3: h.tensor_scalar(
                xoslot[:, 0:D], src, mu3, None, OP.subtract))
            p.op("vector", lambda h: h.tensor_reduce(
                stat[:, 10:11], xoslot[:, 0:D], AX.X, OP.max,
                apply_absolute_value=True))
            p.op("vector", lambda h, r3=r3: h.tensor_mul(stat[:, 10:11], stat[:, 10:11], r3))
            p.op("vector", lambda h: h.tensor_scalar_max(stat[:, 10:11], stat[:, 10:11], 1e-5))
            p.op("vector", lambda h: h.reciprocal(stat[:, 11:12], stat[:, 10:11]))
            p.op("vector", lambda h: h.tensor_scalar_mul(stat[:, 11:12], stat[:, 11:12], 127.0))
            p.op("vector", lambda h: h.reciprocal(stat[:, 12:13], stat[:, 11:12]))
            p.op("vector", lambda h, r3=r3: h.tensor_mul(stat[:, 14:15], r3, stat[:, 11:12]))
            p.op("vector", lambda h: h.tensor_scalar(
                xoslot[:, 0:D], xoslot[:, 0:D], stat[:, 14:15], magic[:, 0:1],
                OP.mult, OP.add))
            cq2 = p.op("vector", lambda h: h.tensor_scalar(
                xqtk[:], xoslot[:, 0:D], magic[:, 0:1], stat[:, 12:13],
                OP.subtract, OP.mult))
            sc = p.cnt["ss"]
            for dch in range(KD):
                t1 = p.op("tensor", lambda h, dch=dch, lt=lt: h.matmul(
                    ppD[:, 512:640], xqtk[:, dch * 128:(dch + 1) * 128],
                    idf[:], is_transpose=True, start=True, stop=True),
                    waits=[("sv", cq2), ("ss", sc)])
                sc = p.op("scalar", lambda h, dch=dch, lt=lt: h.copy(
                    xqf[:, dch * 516 + 1 + lt * 128:dch * 516 + 1 + (lt + 1) * 128],
                    ppD[:, 512:640]), waits=[("st", t1)])
        ss_xq = p.cnt["ss"]

        mzero = p.op("vector", lambda h: h.memset(mqt[:], 0.0))
        p.wait("tensor", [("ss", ss_xq)])
        convdone = 0
        for mh in range(2):
            w1lds = []
            for k in range(KD):
                if k == 0:
                    ww = [("st", p.cnt["st"])]
                elif k % 2 == 0:
                    ww = [("sq", w1lds[k - 1])]
                else:
                    ww = []
                w1lds.append(p.dma("sync", lambda h, k=k, mh=mh: h.dma_start(
                    out=wstr[k], in_=wb_w1[k * 128:(k + 1) * 128,
                                          mh * 2048:(mh + 1) * 2048]),
                    waits=ww))
            for mi in range(KE):
                m = mh * KE + mi
                for k in range(KD):
                    w_ = []
                    if mi == 0:
                        w_.append(("sq", w1lds[k | 1]))
                    if k == 0:
                        w_.append(("sv", convdone))
                    p.op("tensor", lambda h, mi=mi, k=k: h.matmul(
                        ppD[:, 0:257], wstr[k][:, mi * 128:(mi + 1) * 128],
                        xqf3[:, k, 0:257], start=(k == 0), stop=(k == KD - 1)),
                        waits=w_)
                for k in range(KD):
                    p.op("tensor", lambda h, mi=mi, k=k: h.matmul(
                        ppD[:, 512:769], wstr[k][:, mi * 128:(mi + 1) * 128],
                        xqf3[:, k, 257:514], start=(k == 0), stop=(k == KD - 1)))
                stw1 = p.cnt["st"]
                t = hslot[:, (m % 2) * LB:(m % 2 + 1) * LB]
                p.op("vector", lambda h, m=m, t=t: h.tensor_scalar(
                    t[:, 0:257], ppD[:, 0:257], fcw[:, m * 3:m * 3 + 1],
                    fcb[:, m:m + 1], OP.mult, OP.add), waits=[("st", stw1)])
                p.op("vector", lambda h, m=m, t=t: h.tensor_scalar(
                    t[:, 257:512], ppD[:, 512:767], fcw[:, m * 3:m * 3 + 1],
                    fcb[:, m:m + 1], OP.mult, OP.add))
                p.op("vector", lambda h, m=m, t=t: h.scalar_tensor_tensor(
                    t[:, 0:256], ppD[:, 1:257], fcw[:, m * 3 + 1:m * 3 + 2],
                    t[:, 0:256], OP.mult, OP.add))
                p.op("vector", lambda h, m=m, t=t: h.scalar_tensor_tensor(
                    t[:, 256:512], ppD[:, 512:768], fcw[:, m * 3 + 1:m * 3 + 2],
                    t[:, 256:512], OP.mult, OP.add))
                p.op("vector", lambda h, m=m, t=t: h.scalar_tensor_tensor(
                    t[:, 0:255], ppD[:, 2:257], fcw[:, m * 3 + 2:m * 3 + 3],
                    t[:, 0:255], OP.mult, OP.add))
                convdone = p.op("vector", lambda h, m=m, t=t: h.scalar_tensor_tensor(
                    t[:, 255:512], ppD[:, 512:769], fcw[:, m * 3 + 2:m * 3 + 3],
                    t[:, 255:512], OP.mult, OP.add))
                hqs = hq[:, (m % 4) * LB:((m % 4) + 1) * LB]
                scs = p.op("scalar", lambda h, m=m, t=t, hqs=hqs: h.activation(
                    hqs, t, AF.Silu),
                    waits=[("sv", convdone), ("sp", p.cnt["sp"])])
                p.dma("scalar", lambda h, ob=ob, m=m, hqs=hqs: h.dma_start(
                    out=spillH[ob, :, m * LB:(m + 1) * LB],
                    in_=hqs), waits=[("ss", scs)])
                sab = p.op("scalar", lambda h, hqs=hqs: h.activation(
                    hslot[:, LB:2 * LB], hqs, AF.Abs))
                p.op("vector", lambda h, m=m: h.tensor_max(
                    mqt[:], mqt[:], hslot[:, LB:2 * LB]),
                    waits=[("ss", sab), ("sv", mzero)] if m == 0 else [("ss", sab)])
        svmx = p.cnt["sv"]

        mcols = stat[:, 24:28]
        prevv = svmx
        for lt in range(4):
            pdst = ppD if lt % 2 == 0 else ppE
            tt = p.op("tensor", lambda h, lt=lt, pdst=pdst: h.matmul(
                pdst[:, 0:128], mqt[:, lt * 128:(lt + 1) * 128], idf[:],
                is_transpose=True, start=True, stop=True),
                waits=[("sv", prevv), ("st", p.cnt["st"])] if lt == 0
                else [("sv", prevv)])
            p.op("vector", lambda h, pdst=pdst: h.tensor_reduce(
                stat[:, 12:13], pdst[:, 0:128], AX.X, OP.max), waits=[("st", tt)])
            p.op("vector", lambda h, lt=lt: h.tensor_scalar_max(
                mcols[:, lt:lt + 1], stat[:, 12:13], 1e-5))
            p.op("vector", lambda h, lt=lt: h.reciprocal(
                stat[:, 13:14], mcols[:, lt:lt + 1]))
            cshc = p.op("vector", lambda h: h.tensor_scalar_mul(
                stat[:, 13:14], stat[:, 13:14], 127.0))
            prevv = cshc
            p.op("tensor", lambda h, lt=lt: h.matmul(
                ppP1[0:1, 512 + lt * 128:512 + (lt + 1) * 128], stat[:, 13:14],
                idf[:], is_transpose=True, start=True, stop=True),
                waits=[("sv", cshc)])
        crow = p.op("vector", lambda h: h.tensor_copy(
            k2st[0:1, 0:LB], ppP1[0:1, 512:512 + LB]), waits=[("st", p.cnt["st"])])
        tb = bcast_row(ppP2[:, 512:1024], LB, 0, crow)
        csh = p.op("vector", lambda h: h.tensor_copy(shb[:], ppP2[:, 512:1024]),
                   waits=[("st", tb)])

        w2lds = {}
        mmk2 = {}
        for k in range(KH):
            if k % 4 == 0:
                for kk in range(k, k + 4):
                    w_ = []
                    if kk % 4 == 0:
                        w_.append(("st", p.cnt["st"]) if kk == 0
                                  else ("sq", w2lds[kk - 1]))
                        if kk >= 16:
                            w_.append(("st", mmk2[kk - 13]))
                    w2lds[kk] = p.dma("sync", lambda h, kk=kk: h.dma_start(
                        out=ar2b[:, (kk % 16) * D:((kk % 16) + 1) * D],
                        in_=wb_w2[kk * 128:(kk + 1) * 128, :]), waits=w_)
            spar = k % 4
            hqs = hq[:, spar * LB:(spar + 1) * LB]
            lh = p.dma("scalar", lambda h, ob=ob, k=k, hqs=hqs: h.dma_start(
                out=hqs, in_=spillH[ob, :, k * LB:(k + 1) * LB]),
                waits=[("st", mmk2.get(k - 4, 0)), ("sp", p.cnt["sp"])] if k >= 4
                else [("sp", p.cnt["sp"])])
            p.op("vector", lambda h, hqs=hqs: h.tensor_mul(
                hslot[:, 0:LB], hqs, shb[:]),
                waits=[("sp", lh)] + ([("sv", csh)] if k == 0 else []))
            svr = p.op("vector", lambda h, hqs=hqs: h.tensor_scalar(
                hqs, hslot[:, 0:LB], magic[:, 0:1], magic[:, 0:1],
                OP.add, OP.subtract))
            p.dma("scalar", lambda h, ob=ob, k=k, hqs=hqs: h.dma_start(
                out=spillH2[ob, :, k * LB:(k + 1) * LB], in_=hqs),
                waits=[("sv", svr)])
            for m in range(4):
                w_ = []
                if m == 0:
                    w_.append(("sq", w2lds[(k // 4) * 4 + 3]))
                    w_.append(("sv", svr))
                    if k == 0:
                        w_.append(("ss", p.cnt["ss"]))
                p.op("tensor", lambda h, k=k, m=m, hqs=hqs: h.matmul(
                    pbank4[m], ar2b[:, (k % 16) * D + m * 128:(k % 16) * D + (m + 1) * 128],
                    hqs, start=(k == 0), stop=(k == KH - 1)), waits=w_)
            mmk2[k] = p.cnt["st"]
        stmm = p.cnt["st"]
        evw1 = 0
        for m in range(4):
            evw1 = p.op("vector", lambda h, m=m: h.tensor_copy(
                ohi[:, m * LB:(m + 1) * LB], pbank4[m]),
                waits=[("st", stmm)] if m == 0 else ())
        w2lds2 = {}
        mmk2b = {}
        for k in range(KH):
            if k % 4 == 0:
                for kk in range(k, k + 4):
                    w_ = []
                    if kk % 4 == 0:
                        w_.append(("sq", w2lds[KH - 1] if kk == 0
                                   else w2lds2[kk - 1]))
                        w_.append(("st", mmk2[19 + kk] if kk < 16
                                   else mmk2b[kk - 13]))
                    w2lds2[kk] = p.dma("sync", lambda h, kk=kk: h.dma_start(
                        out=ar2b[:, (kk % 16) * D:((kk % 16) + 1) * D],
                        in_=wb_w2[kk * 128:(kk + 1) * 128, :]), waits=w_)
            spar = k % 4
            hqs = hq[:, spar * LB:(spar + 1) * LB]
            lh = p.dma("scalar", lambda h, ob=ob, k=k, hqs=hqs: h.dma_start(
                out=hqs, in_=spillH2[ob, :, k * LB:(k + 1) * LB]),
                waits=[("st", mmk2b[k - 4] if k >= 4 else mmk2[28 + k]),
                       ("sp", p.cnt["sp"])])
            for m in range(4):
                w_ = []
                if m == 0:
                    w_.append(("sq", w2lds2[(k // 4) * 4 + 3]))
                    w_.append(("sp", lh))
                    if k == 0:
                        w_.append(("sv", evw1))
                p.op("tensor", lambda h, k=k, m=m, hqs=hqs: h.matmul(
                    pbank4[m],
                    ar2b[:, (k % 16) * D + (4 + m) * 128:(k % 16) * D + (5 + m) * 128],
                    hqs, start=(k == 0), stop=(k == KH - 1)), waits=w_)
            mmk2b[k] = p.cnt["st"]
        stmm = p.cnt["st"]
        ev2 = 0
        for m in range(4):
            ev2 = p.op("vector", lambda h, m=m: h.tensor_copy(
                ohi[:, (4 + m) * LB:(5 + m) * LB], pbank4[m]),
                waits=[("st", stmm)] if m == 0 else ())
        svev2 = ev2

        ldm2 = p.dma("scalar", lambda h, ob=ob: h.dma_start(
            out=xmidt[:], in_=spillM[ob]),
            waits=[("sv", p.cnt["sv"]), ("sp", p.cnt["sp"])])
        fin = 0
        cf = 0
        lastout = 0
        for lt in range(4):
            pdst = ppD if lt % 2 == 0 else ppE
            p.op("vector", lambda h, lt=lt: h.tensor_scalar_mul(
                stat[:, 17:18], mcols[:, lt:lt + 1], 1.0 / 127.0))
            cf = p.op("vector", lambda h: h.tensor_mul(stat[:, 17:18], stat[:, 17:18], rw2))
            for m in range(KD):
                w_ = [("sv", svev2 if lt < 2 else fin)] if m == 0 else []
                p.op("tensor", lambda h, lt=lt, m=m, pdst=pdst: h.matmul(
                    pdst[:, m * 128:(m + 1) * 128],
                    ohi[:, m * LB + lt * 128:m * LB + (lt + 1) * 128],
                    idf[:], is_transpose=True, start=True, stop=True), waits=w_)
            sttr = p.cnt["st"]
            fin = p.op("vector", lambda h, lt=lt, pdst=pdst: h.scalar_tensor_tensor(
                xoslot[:, 0:D], pdst[:, 0:D], stat[:, 17:18],
                xmidt[:, lt * D:(lt + 1) * D], OP.mult, OP.add),
                waits=[("st", sttr), ("sp", ldm2), ("sv", cf), ("sp", lastout)])
            lastout = p.dma("scalar", lambda h, ob=ob, lt=lt: h.dma_start(
                out=y_out[ob, lt * 128:(lt + 1) * 128, :], in_=xoslot[:, 0:D]),
                waits=[("sv", fin)])

    for r in range(NR):
        scan_round(r)
        post_a(r)
        if r >= 1:
            post_b(r - 1)
    post_b(NR - 1)

    p.wait("vector", [("sp", p.cnt["sp"]), ("sq", p.cnt["sq"]),
                      ("sg", p.cnt["sg"])])

    with nc.Block() as block:
        p.finalize(block)
    es.close()
    return nc


def _mk_sel32():
    sel = np.zeros((32, 32, 128), np.float32)
    for s in range(32):
        sel[s, s, :] = 1.0
    return np.ascontiguousarray(sel.transpose(1, 0, 2).reshape(32, 32 * 128)
                                ).astype(ml_dtypes.bfloat16)


def host_prep(inputs, NB):
    NR = NB // 4
    OB = NB // 4
    L = NB * LB
    x = np.ascontiguousarray(inputs["L0_attn_seq"], dtype=np.float32)
    f32 = lambda v: np.ascontiguousarray(np.asarray(v), dtype=np.float32)

    shared = {
        "in_projT": f32(np.asarray(inputs["L0_attn_in_proj_w"]).T),
        "out_projT": f32(np.asarray(inputs["L0_attn_out_proj_w"]).T),
        "w1T": f32(np.asarray(inputs["L0_ffn_w1"]).T),
        "w2T": f32(np.asarray(inputs["L0_ffn_w2"]).T),
        "x_projTd": f32(np.asarray(inputs["L0_attn_x_proj_w"]).T),
        "dt_projTd": f32(np.asarray(inputs["L0_attn_dt_proj_w"]).T),
        "ln_g8": f32(np.asarray(inputs["L0_attn_ln_g"]).reshape(KD, 128).T),
        "ln_b8": f32(np.asarray(inputs["L0_attn_ln_b"]).reshape(KD, 128).T),
        "conv_w64": f32(np.asarray(inputs["L0_attn_conv_w"]).reshape(KE, 128, 4)
                        .transpose(1, 0, 2).reshape(128, KE * 4)),
        "conv_b16": f32(np.asarray(inputs["L0_attn_conv_b"]).reshape(KE, 128).T),
        "dtb16": f32(np.asarray(inputs["L0_attn_dt_proj_b"]).reshape(KE, 128).T),
        "D16": f32(np.asarray(inputs["L0_attn_D"]).reshape(KE, 128).T),
        "fcw96": f32(np.asarray(inputs["L0_ffn_conv_w"])[:, 0, :].reshape(KH, 128, 3)
                     .transpose(1, 0, 2).reshape(128, KH * 3)),
        "fcb32": f32(np.asarray(inputs["L0_ffn_conv_b"]).reshape(KH, 128).T),
        "rs_col": f32(np.broadcast_to(np.asarray(inputs["L0_attn_res_scale"],
                                                 np.float32).reshape(1, 1), (128, 1))),
        "identc": np.eye(128, dtype=ml_dtypes.bfloat16),
        "identf": np.eye(128, dtype=np.float32),
        "sel32": _mk_sel32(),
        "onesk2": np.concatenate([np.ones((1, 128)), np.zeros((1, 128))]
                                 ).astype(ml_dtypes.bfloat16),
    }

    in_maps = []
    for c in range(8):
        b, m = c // 4, c % 4
        xb = x[b, :L].reshape(NB, LB, D)
        own = []
        for ob in range(OB):
            g = 4 * ob + m
            blk = np.zeros((LB + 3, D), np.float32)
            blk[3:] = xb[g]
            if g > 0:
                blk[0:3] = xb[g - 1][LB - 3:]
            own.append(blk)
        x_own = np.ascontiguousarray(np.stack(own))

        Al = np.asarray(inputs["L0_attn_A_log"])[512 * m:512 * (m + 1)]
        A_my = f32(Al.reshape(4, 128, S).transpose(1, 0, 2).reshape(128, 4 * S))

        idx = np.zeros((128, NIDX), np.int32)
        pcol = np.arange(128, dtype=np.int32)
        for j in range(4):
            for q in range(4):
                idx[:, IDXC[("dt", j, q)]] = j * 4128 + (m * 4 + q) * 128 + pcol
                idx[:, IDXC[("dx", j, q)]] = j * 4128 + E + (m * 4 + q) * 128 + pcol
            idx[0:32, IDXC[("bcs", j)]] = j * 4128 + 2 * E + np.arange(32)
        for ch in range(16):
            j, qd = ch // 4, ch % 4
            idx[:, IDXC[("y", ch)]] = j * 2048 + m * 512 + qd * 128 + pcol
        for ob in range(min(OB, 8)):
            g = 4 * ob + m
            lrow = 0 if g == 0 else ((g - 1) // 4) * 8 + ((g - 1) % 4) * 2 + 1
            rrow = 0 if g == NB - 1 else ((g + 1) // 4) * 8 + ((g + 1) % 4) * 2 + 0
            idx[0, IDXC[("xh", ob)]] = lrow
            idx[1, IDXC[("xh", ob)]] = rrow

        mcv_ = np.ones((128, 24), np.float32)
        mfn_ = np.ones((128, 16), np.float32)
        for ob in range(min(OB, 8)):
            g = 4 * ob + m
            if g == 0:
                mcv_[:, 3 * ob:3 * ob + 3] = 0.0
                mfn_[:, 2 * ob] = 0.0
            if g == NB - 1:
                mfn_[:, 2 * ob + 1] = 0.0

        im = dict(shared)
        im.update(x_own=x_own, A_log_my=A_my, idxs=idx, maskcv=mcv_, maskfn=mfn_)
        in_maps.append(im)
    return in_maps


def kernel(**inputs):
    NB = np.asarray(inputs["L0_attn_seq"]).shape[1] // LB
    in_maps = host_prep(inputs, NB)
    nc = build_nc(NB)
    res = run_bass_kernel_spmd(nc, in_maps, core_ids=list(range(8)))
    L = NB * LB
    OB = NB // 4
    out = np.empty((2, L, D), np.float32)
    for c in range(8):
        b, m = c // 4, c % 4
        yo = np.asarray(res.results[c]["y_out"])
        for ob in range(OB):
            g = 4 * ob + m
            out[b, g * LB:(g + 1) * LB] = yo[ob]
    return out


# version 05
# speedup vs baseline: 1.0272x; 1.0272x over previous
import numpy as np
from contextlib import ExitStack
import ml_dtypes

import concourse.bass as bass
import concourse.mybir as mybir
from concourse.bass import IndirectOffsetOnAxis
import bass_rust
ROP = bass_rust.ReduceOp
from concourse.bass_utils import run_bass_kernel_spmd

F32 = mybir.dt.float32
BF16 = mybir.dt.bfloat16
I32 = mybir.dt.int32
AF = mybir.ActivationFunctionType
OP = mybir.AluOpType
AX = mybir.AxisListType

D = 1024
E = 2048
S = 16
RK = 64
H = 4096
LB = 512
KD = D // 128
KE = E // 128
KH = H // 128
MAGIC = 12582912.0
EPS = 1e-5

ENGS = ("sync", "vector", "scalar", "tensor", "gpsimd")
SEMS = ("sq", "sp", "sg", "sv", "ss", "st", "sgp", "scc", "syq", "sxq")
_QSEM = {"sync": "sq", "scalar": "sp", "gpsimd": "sg"}
_ESEM = {"vector": "sv", "scalar": "ss", "tensor": "st", "gpsimd": "sgp"}


class Prog:
    def __init__(self, nc):
        self.nc = nc
        self.q = {e: [] for e in ENGS}
        self.cnt = {s: 0 for s in SEMS}
        self.sem = {}

    def _push(self, eng, fn, inc, waits):
        sems = self.sem

        def run(h):
            for sname, val in waits:
                if val > 0:
                    h.wait_ge(sems[sname], val)
            ins = fn(h)
            if ins is not None and inc is not None:
                ins.then_inc(sems[inc[0]], inc[1])
        self.q[eng].append(run)

    def op(self, eng, fn, waits=()):
        s = _ESEM[eng]
        selfw = ((s, self.cnt[s]),)
        self.cnt[s] += 1
        self._push(eng, fn, (s, 1), tuple(waits) + selfw)
        return self.cnt[s]

    def dma(self, eng, fn, waits=(), sem=None):
        s = sem or _QSEM[eng]
        self.cnt[s] += 16
        self._push(eng, fn, (s, 16), tuple(waits))
        return self.cnt[s]

    def cc(self, fn, waits=()):
        self.cnt["scc"] += 1
        self._push("gpsimd", fn, ("scc", 1), tuple(waits))
        return self.cnt["scc"]

    def wait(self, eng, waits):
        self._push(eng, lambda h: None, None, tuple(waits))

    def barrier(self):
        snap = [(s, self.cnt[s]) for s in SEMS if self.cnt[s] > 0]
        for eng in ENGS:
            self.wait(eng, snap)

    def finalize(self, block):
        secs = {"sync": block.sync, "vector": block.vector, "scalar": block.scalar,
                "tensor": block.tensor, "gpsimd": block.gpsimd}
        for eng in ENGS:
            fns = self.q[eng]
            if not fns:
                continue
            def body(h, fns=tuple(fns)):
                for f in fns:
                    f(h)
            secs[eng](body)


IDXC = {}
_col = 0
for _j in range(4):
    for _q in range(4):
        IDXC[("dt", _j, _q)] = _col; _col += 1
        IDXC[("dx", _j, _q)] = _col; _col += 1
for _j in range(4):
    IDXC[("bcs", _j)] = _col; _col += 1
for _ch in range(16):
    IDXC[("y", _ch)] = _col; _col += 1
for _ob in range(8):
    IDXC[("xh", _ob)] = _col; _col += 1
NIDX = 96


def build_nc(NB, debug=False):
    NR = NB // 4
    OB = NB // 4
    nc = bass.Bass("TRN2", target_bir_lowering=False, detect_race_conditions=False)

    dr = lambda n, shp, dt_: nc.dram_tensor(n, shp, dt_, kind="ExternalInput").ap()
    x_own = dr("x_own", [OB, LB + 3, D], F32)
    in_projT = dr("in_projT", [D, 2 * E], F32)
    out_projT = dr("out_projT", [E, D], F32)
    w1T = dr("w1T", [D, H], F32)
    w2T = dr("w2T", [H, D], F32)
    x_projTd = dr("x_projTd", [E, 96], F32)
    dt_projTd = dr("dt_projTd", [RK, E], F32)
    ln_g8 = dr("ln_g8", [128, KD], F32)
    ln_b8 = dr("ln_b8", [128, KD], F32)
    A_log_my = dr("A_log_my", [128, 4 * S], F32)
    conv_w64 = dr("conv_w64", [128, KE * 4], F32)
    conv_b16 = dr("conv_b16", [128, KE], F32)
    dtb16 = dr("dtb16", [128, KE], F32)
    D16 = dr("D16", [128, KE], F32)
    fcw96 = dr("fcw96", [128, KH * 3], F32)
    fcb32 = dr("fcb32", [128, KH], F32)
    rs_col = dr("rs_col", [128, 1], F32)
    idxs = dr("idxs", [128, NIDX], I32)
    maskcv = dr("maskcv", [128, 3 * 8], F32)
    maskfn = dr("maskfn", [128, 2 * 8], F32)
    identc = dr("identc", [128, 128], BF16)
    identf = dr("identf", [128, 128], F32)
    sel32 = dr("sel32", [32, 32 * 128], BF16)
    onesk2 = dr("onesk2", [2, 128], BF16)

    y_out = nc.dram_tensor("y_out", [OB, LB, D], F32, kind="ExternalOutput").ap()
    dbgt = nc.dram_tensor("dbgt", [8, 128, 4160], F32, kind="ExternalOutput").ap() if debug else None

    di = lambda n, shp, dt_: nc.dram_tensor(n, shp, dt_, kind="Internal").ap()
    wb_ip = di("wb_ip", [D, 2 * E], BF16)
    wb_op = di("wb_op", [E, D], BF16)
    wb_w1 = di("wb_w1", [D, H], BF16)
    wb_w2 = di("wb_w2", [H, D], BF16)
    ccF_in = di("ccF_in", [NR, 4128, LB], F32)
    ccF_out = [di(f"ccF_out{r}", [4 * 4128, LB], F32) for r in range(NR)]
    ccR_in = di("ccR_in", [NR, 4 * 512, LB], F32)
    ccR_out = [di(f"ccR_out{r}", [4 * 4 * 512, LB], F32) for r in range(NR)]
    ccX_in = di("ccX_in", [NR, 2, D], F32)
    ccX_out = di("ccX_out", [NR * 8, D], F32)
    spillX = di("spillX", [OB, 128, KE * LB], BF16)
    spillZ = di("spillZ", [OB, 128, KE * LB], BF16)
    spillH = di("spillH", [OB, 128, KH * LB], BF16)
    spillH2 = di("spillH2", [OB, 128, KH * LB], BF16)
    spillM = di("spillM", [OB, 128, 4 * D], F32)
    bias0d = di("bias0d", [2 * E], F32)

    es = ExitStack()
    sb = lambda n, shp, dt_: es.enter_context(nc.sbuf_tensor(n, shp, dt_))
    pt = lambda n, shp: es.enter_context(nc.psum_tensor(n, shp, F32))
    p = Prog(nc)
    for sname in SEMS:
        p.sem[sname] = es.enter_context(nc.semaphore(sname))

    ident = sb("ident", [128, 128], BF16)
    idf = sb("idf", [128, 128], F32)
    selt = sb("selt", [32, 32 * 128], BF16)
    ok2 = sb("ok2", [2, 128], BF16)
    k2st = sb("k2st", [2, 512], F32)
    k2hi = sb("k2hi", [2, 512], BF16)
    k2lo = sb("k2lo", [2, 512], BF16)
    lnG = sb("lnG", [128, KD], F32)
    lnB = sb("lnB", [128, KD], F32)
    lnBb = sb("lnBb", [128, KD], BF16)
    A64 = sb("A64", [128, 4 * S], F32)
    cw64 = sb("cw64", [128, KE * 4], F32)
    cb16 = sb("cb16", [128, KE], F32)
    dtb = sb("dtb", [128, KE], F32)
    Dc16 = sb("Dc16", [128, KE], F32)
    fcw = sb("fcw", [128, KH * 3], F32)
    fcb = sb("fcb", [128, KH], F32)
    rsc = sb("rsc", [128, 1], F32)
    magic = sb("magic", [128, 1], F32)
    idxt = sb("idxt", [128, NIDX], I32)
    mcv = sb("mcv", [128, 3 * 8], F32)
    mfn = sb("mfn", [128, 2 * 8], F32)
    bias0 = sb("bias0", [128, 2 * KE], F32)
    xpT = sb("xpT", [128, KE * 96], BF16)
    dtpT = sb("dtpT", [128, E], BF16)
    statc = sb("statc", [128, 8 * OB], F32)
    colw = sb("colw", [128, 16], F32)

    ar1 = sb("ar1", [128, KE * LB + 2 * D], F32)
    ar2 = sb("ar2", [128, KE * LB], F32)
    ar3 = sb("ar3", [128, 9520], F32)
    ar3b = ar3[:].bitcast(BF16)
    xslb = ar3b[:, 0:KE * LB]
    xnf = ar3b[:, KE * LB:KE * LB + KD * 520]
    xcvs = ar3[:, 6176:6176 + 2 * 520]
    tcv = ar3[:, 7216:7216 + 2 * LB]
    zbs = ar3b[:, 2 * 8240:2 * 8240 + 2 * LB]
    xdb = ar3[:, 8752:8752 + LB]
    xdbb = ar3b[:, 2 * 9264:2 * 9264 + LB]
    stat = sb("stat", [128, 40], F32)

    ppS1 = pt("ppS1", [128, 1024])
    ppS2 = pt("ppS2", [128, 1024])
    ppP1 = pt("ppP1", [128, 1024])
    ppP2 = pt("ppP2", [128, 1024])
    ppA = ppS1[:, 0:512]
    ppB = ppS1[:, 512:1024]
    ppF = ppS2[:, 0:512]
    ppG = ppS2[:, 512:1024]
    ppD = ppP1
    ppE = ppP2
    psms8 = [ppA, ppB, ppF, ppG, ppD[:, 0:512], ppD[:, 512:1024],
             ppE[:, 0:512], ppE[:, 512:1024]]
    pbank4 = [ppP1[:, 0:512], ppP1[:, 512:1024],
              ppP2[:, 0:512], ppP2[:, 512:1024]]

    xres = ar1[:, 0:4 * D]
    xhal = ar1[:, 4 * D:5 * D]
    ar1b = ar1[:].bitcast(BF16)
    xnbB = ar1[:, 5 * D:9 * D]
    xnhB = ar1[:, 9 * D:10 * D]
    xsl = ar1[:, 0:KE * LB]
    ar2b = ar2[:].bitcast(BF16)
    wstr = [ar2b[:, k * 2048:(k + 1) * 2048] for k in range(KD)]
    dtt = ar2[:, 0:KE * LB]
    wfull = ar1[:, 0:2 * E]
    wcastB = ar1b[:, 2 * (2 * E):2 * (2 * E) + 2 * E]
    stat0row = ar2[:, 0:2 * E]

    def bcast_row(psdst, w, src_row_off, waitv):
        c1 = p.op("vector", lambda h: h.tensor_copy(
            k2hi[0:1, src_row_off:src_row_off + w],
            k2st[0:1, src_row_off:src_row_off + w]), waits=[("sv", waitv)])
        c2 = p.op("vector", lambda h: h.tensor_tensor(
            k2st[1:2, src_row_off:src_row_off + w] if False else
            k2lo[0:1, src_row_off:src_row_off + w],
            k2st[0:1, src_row_off:src_row_off + w],
            k2hi[0:1, src_row_off:src_row_off + w], OP.subtract))
        t1 = p.op("tensor", lambda h: h.matmul(
            psdst, ok2[:, 0:128], k2hi[0:2, src_row_off:src_row_off + w],
            start=True, stop=False), waits=[("sv", c2)])
        t2 = p.op("tensor", lambda h: h.matmul(
            psdst, ok2[:, 0:128], k2lo[0:2, src_row_off:src_row_off + w],
            start=False, stop=True))
        return t2



    for dst, src in ((ident[:], identc), (idf[:], identf), (lnG[:], ln_g8),
                     (lnB[:], ln_b8), (A64[:], A_log_my), (cw64[:], conv_w64),
                     (cb16[:], conv_b16), (dtb[:], dtb16), (Dc16[:], D16),
                     (fcw[:], fcw96), (fcb[:], fcb32), (rsc[:], rs_col),
                     (idxt[:], idxs), (mcv[:], maskcv), (mfn[:], maskfn),
                     (selt[:], sel32), (ok2[:], onesk2)):
        p.dma("sync", lambda h, d=dst, s=src: h.dma_start(out=d, in_=s))
    ld0 = p.cnt["sq"]
    p.dma("gpsimd", lambda h: h.dma_start(
        out=xpT[:].rearrange("p (k n) -> p k n", n=96),
        in_=x_projTd.rearrange("(k p) n -> p k n", p=128)))
    p.dma("gpsimd", lambda h: h.dma_start(out=dtpT[0:RK, :], in_=dt_projTd))
    lg0 = p.cnt["sg"]
    sA = p.op("scalar", lambda h: h.activation(A64[:], A64[:], AF.Exp),
              waits=[("sq", ld0)])
    p.op("vector", lambda h: h.tensor_scalar_mul(A64[:], A64[:], -1.0),
         waits=[("ss", sA)])
    p.op("vector", lambda h: h.tensor_copy(lnBb[:], lnB[:]), waits=[("sq", ld0)])
    p.op("vector", lambda h: h.memset(magic[:], MAGIC))
    p.op("vector", lambda h: h.memset(k2st[:], 0.0))
    p.op("vector", lambda h: h.memset(k2hi[:], 0.0))
    p.op("vector", lambda h: h.memset(k2lo[:], 0.0))

    for k in range(KD):
        ld = p.dma("sync", lambda h, k=k: h.dma_start(
            out=wfull, in_=in_projT[k * 128:(k + 1) * 128, :]),
            waits=[("sv", p.cnt["sv"])])
        c1 = p.op("vector", lambda h: h.tensor_copy(wcastB, wfull),
                  waits=[("sq", p.cnt["sq"])])
        for nt in range(8):
            p.op("tensor", lambda h, k=k, nt=nt: h.matmul(
                psms8[nt][0:1, :], lnBb[:, k:k + 1],
                wcastB[:, nt * 512:(nt + 1) * 512],
                start=(k == 0), stop=(k == KD - 1)),
                waits=[("sv", c1)] if nt == 0 else ())
        c2 = p.op("vector", lambda h, k=k: h.tensor_scalar(
            wcastB, wfull, lnG[:, k:k + 1], None, OP.mult),
            waits=[("st", p.cnt["st"])])
        p.dma("sync", lambda h, k=k: h.dma_start(
            out=wb_ip[k * 128:(k + 1) * 128, :], in_=wcastB), waits=[("sv", c2)])
    cb0 = 0
    for nt in range(8):
        cb0 = p.op("vector", lambda h, nt=nt: h.tensor_copy(
            stat0row[0:1, nt * 512:(nt + 1) * 512], psms8[nt][0:1, :]),
            waits=[("st", p.cnt["st"])] if nt == 0 else ())
    db0 = p.dma("sync", lambda h: h.dma_start(out=bias0d, in_=stat0row[0:1, :]),
                waits=[("sv", cb0)])
    p.dma("sync", lambda h: h.dma_start(
        out=bias0[:], in_=bias0d.rearrange("(c p) -> p c", p=128),
        allow_slow_non_contiguous=True), waits=[("sq", db0)])

    for k in range(KE):
        ld = p.dma("sync", lambda h, k=k: h.dma_start(
            out=wfull[:, 0:D], in_=out_projT[k * 128:(k + 1) * 128, :]),
            waits=[("sv", p.cnt["sv"])])
        c = p.op("vector", lambda h: h.tensor_copy(wcastB[:, 0:D], wfull[:, 0:D]),
                 waits=[("sq", p.cnt["sq"])])
        p.dma("sync", lambda h, k=k: h.dma_start(
            out=wb_op[k * 128:(k + 1) * 128, :], in_=wcastB[:, 0:D]),
            waits=[("sv", c)])

    def quant_weight(wsrc, wdst, rows, cols, sdst):
        nch = rows // 128
        acc = colw[:, 0:1]
        last = p.op("vector", lambda h: h.memset(acc, 0.0))
        for k in range(nch):
            ld = p.dma("sync", lambda h, k=k: h.dma_start(
                out=wfull[:, 0:cols], in_=wsrc[k * 128:(k + 1) * 128, :]),
                waits=[("sv", p.cnt["sv"])])
            p.op("vector", lambda h: h.tensor_reduce(
                colw[:, 1:2], wfull[:, 0:cols], AX.X, OP.add,
                apply_absolute_value=True), waits=[("sq", ld)])
            last = p.op("vector", lambda h: h.tensor_add(acc, acc, colw[:, 1:2]))
        tt = p.op("tensor", lambda h: h.matmul(
            ppB[0:1, 0:128], acc, idf[:], is_transpose=True, start=True, stop=True),
            waits=[("sv", last)])
        rr = p.op("vector", lambda h: h.tensor_reduce(
            k2st[0:1, 0:1], ppB[0:1, 0:128], AX.X, OP.add), waits=[("st", tt)])
        p.op("vector", lambda h: h.tensor_scalar_mul(
            k2st[0:1, 0:1], k2st[0:1, 0:1], 1.0 / (rows * cols)))
        p.op("vector", lambda h: h.tensor_scalar_max(k2st[0:1, 0:1], k2st[0:1, 0:1], 1e-5))
        rr = p.op("vector", lambda h: h.reciprocal(k2st[0:1, 1:2], k2st[0:1, 0:1]))
        t2 = bcast_row(ppB[:, 0:1], 1, 1, rr)
        cs = p.op("vector", lambda h, sdst=sdst: h.tensor_copy(sdst, ppB[:, 0:1]),
                  waits=[("st", t2)])
        for k in range(nch):
            ld = p.dma("sync", lambda h, k=k: h.dma_start(
                out=wfull[:, 0:cols], in_=wsrc[k * 128:(k + 1) * 128, :]),
                waits=[("sv", p.cnt["sv"]), ("sq", p.cnt["sq"])])
            p.op("vector", lambda h, sdst=sdst: h.tensor_scalar(
                wfull[:, 0:cols], wfull[:, 0:cols], sdst, magic[:, 0:1],
                OP.mult, OP.add), waits=[("sq", ld)])
            p.op("vector", lambda h: h.tensor_scalar(
                wfull[:, 0:cols], wfull[:, 0:cols], magic[:, 0:1], 1.0,
                OP.subtract, OP.min))
            c1 = p.op("vector", lambda h: h.tensor_scalar(
                wcastB[:, 0:cols], wfull[:, 0:cols], -1.0, None, OP.max))
            p.dma("sync", lambda h, k=k: h.dma_start(
                out=wdst[k * 128:(k + 1) * 128, :], in_=wcastB[:, 0:cols]),
                waits=[("sv", c1)])

    sw1 = colw[:, 8:9]
    sw2 = colw[:, 9:10]
    quant_weight(w1T, wb_w1, D, H, sw1)
    quant_weight(w2T, wb_w2, H, D, sw2)
    rw1 = colw[:, 10:11]
    rw2 = colw[:, 11:12]
    p.op("vector", lambda h: h.reciprocal(rw1, sw1))
    p.op("vector", lambda h: h.reciprocal(rw2, sw2))
    p.op("vector", lambda h: h.tensor_scalar(fcw[:], fcw[:], rw1, None, OP.mult))

    zv = p.op("vector", lambda h: h.memset(wfull[:, 0:D], 0.0))
    nzr = (NR * 8 + 127) // 128
    for zi in range(nzr):
        r0, r1 = zi * 128, min((zi + 1) * 128, NR * 8)
        p.dma("sync", lambda h, r0=r0, r1=r1: h.dma_start(
            out=ccX_out[r0:r1, :], in_=wfull[0:r1 - r0, 0:D]), waits=[("sv", zv)])

    p.barrier()

    def rsqrt_newton(w, dst, var_col, s0, s1):
        cv = p.op("vector", lambda h: h.tensor_scalar_add(s0[0:w, :], var_col[0:w, :], EPS))
        c = p.op("scalar", lambda h: h.activation(s1[0:w, :], s0[0:w, :], AF.Sqrt),
                 waits=[("sv", cv)])
        p.op("vector", lambda h: h.reciprocal(dst[0:w, :], s1[0:w, :]), waits=[("ss", c)])
        p.op("vector", lambda h: h.tensor_mul(s1[0:w, :], dst[0:w, :], dst[0:w, :]))
        p.op("vector", lambda h: h.tensor_mul(s1[0:w, :], s1[0:w, :], s0[0:w, :]))
        p.op("vector", lambda h: h.tensor_scalar(s1[0:w, :], s1[0:w, :], -0.5, 1.5,
                                                 OP.mult, OP.add))
        return p.op("vector", lambda h: h.tensor_mul(dst[0:w, :], dst[0:w, :], s1[0:w, :]))

    def ln_stats(src, w, mu_dst, r1_dst, rtot_dst=None):
        bno = stat[:, 20:32]
        p.op("vector", lambda h: h.bn_stats(bno[0:w, 0:6], src[0:w, 0:D // 2]))
        p.op("vector", lambda h: h.bn_stats(bno[0:w, 6:12], src[0:w, D // 2:D]))
        p.op("vector", lambda h: h.bn_aggr(stat[0:w, 32:34], bno[0:w, 0:12]))
        p.op("vector", lambda h: h.tensor_copy(mu_dst[0:w, :], stat[0:w, 32:33]))
        rsqrt_newton(w, r1_dst, stat[:, 33:34], stat[:, 35:36], stat[:, 36:37])
        if rtot_dst is not None:
            p.op("vector", lambda h: h.tensor_mul(
                stat[0:w, 37:38], r1_dst[0:w, :], r1_dst[0:w, :]))
            p.op("vector", lambda h: h.tensor_mul(
                stat[0:w, 37:38], stat[0:w, 37:38], stat[0:w, 33:34]))
            rsqrt_newton(w, stat[:, 38:39], stat[:, 37:38], stat[:, 35:36],
                         stat[:, 36:37])
            return p.op("vector", lambda h: h.tensor_mul(
                rtot_dst[0:w, :], r1_dst[0:w, :], stat[0:w, 38:39]))
        return p.cnt["sv"]


    def bcast_row(psdst, w, src_row_off, waitv):
        c1 = p.op("vector", lambda h: h.tensor_copy(
            k2hi[0:1, src_row_off:src_row_off + w],
            k2st[0:1, src_row_off:src_row_off + w]), waits=[("sv", waitv)])
        c2 = p.op("vector", lambda h: h.tensor_tensor(
            k2st[1:2, src_row_off:src_row_off + w] if False else
            k2lo[0:1, src_row_off:src_row_off + w],
            k2st[0:1, src_row_off:src_row_off + w],
            k2hi[0:1, src_row_off:src_row_off + w], OP.subtract))
        t1 = p.op("tensor", lambda h: h.matmul(
            psdst, ok2[:, 0:128], k2hi[0:2, src_row_off:src_row_off + w],
            start=True, stop=False), waits=[("sv", c2)])
        t2 = p.op("tensor", lambda h: h.matmul(
            psdst, ok2[:, 0:128], k2lo[0:2, src_row_off:src_row_off + w],
            start=False, stop=True))
        return t2

    ccF_done = {}
    ccR_done = {}
    ccX_done = {}
    sends_end = {}
    ysend_tok = {0: 0}
    last_scan_par = {0: 0, 1: 0}

    prev_sends = {"d1": 0, "d2": 0, "d3": 0, "spx": 0}
    for ob in range(OB):
        ldx = p.dma("sync", lambda h, ob=ob: h.dma_start(
            out=xres.rearrange("p (t n) -> p t n", n=D),
            in_=x_own[ob, 3:3 + LB, :].rearrange("(t p) n -> p t n", p=128)),
            waits=[("sxq", prev_sends["d3"]), ("sv", p.cnt["sv"]),
                   ("st", p.cnt["st"])], sem="sxq")
        ldh = p.dma("sync", lambda h, ob=ob: h.dma_start(
            out=xhal[0:3, :], in_=x_own[ob, 0:3, :]), sem="sxq")

        p.wait("vector", [("sxq", ldh), ("st", p.cnt["st"])])
        for lt in range(4):
            src = xres[:, lt * D:(lt + 1) * D]
            mu = stat[:, lt:lt + 1]
            rt = stat[:, 4 + lt:5 + lt]
            r1 = stat[:, 34:35]
            ln_stats(src, 128, mu, r1, rt)
            p.op("vector", lambda h, ob=ob, lt=lt, r1=r1: h.tensor_scalar_add(
                statc[:, ob * 8 + lt:ob * 8 + lt + 1], r1, 1.0))
            p.op("vector", lambda h, mu=mu, r1=r1: h.tensor_mul(
                stat[:, 12:13], mu, r1))
            p.op("vector", lambda h, ob=ob, lt=lt: h.tensor_scalar_mul(
                statc[:, ob * 8 + 4 + lt:ob * 8 + 5 + lt], stat[:, 12:13], -1.0))
            p.op("vector", lambda h, src=src, lt=lt, mu=mu, rt=rt: h.tensor_scalar(
                xnbB[:, lt * D:(lt + 1) * D], src, mu, rt, OP.subtract, OP.mult))
        p.wait("vector", [("sxq", ldh)])
        muh = stat[:, 8:9]
        rth = stat[:, 9:10]
        ln_stats(xhal, 3, muh, stat[:, 34:35], rth)
        svh = p.op("vector", lambda h: h.tensor_scalar(
            xnhB[0:3, :], xhal[0:3, :], muh[0:3, :], rth[0:3, :],
            OP.subtract, OP.mult))

        sc = p.cnt["ss"]
        for dch in range(KD):
            t0 = p.op("tensor", lambda h, dch=dch: h.matmul(
                ppD[:, 0:3], xnhB[0:3, dch * 128:(dch + 1) * 128],
                idf[0:3, 0:3], is_transpose=True, start=True, stop=True),
                waits=[("sv", svh), ("ss", sc)])
            sc = p.op("scalar", lambda h, dch=dch: h.copy(
                xnf[:, dch * 520:dch * 520 + 3], ppD[:, 0:3]), waits=[("st", t0)])
            for lt in range(4):
                t1 = p.op("tensor", lambda h, dch=dch, lt=lt: h.matmul(
                    ppD[:, 512:640], xnbB[:, lt * D + dch * 128:lt * D + (dch + 1) * 128],
                    idf[:], is_transpose=True, start=True, stop=True),
                    waits=[("ss", sc)])
                sc = p.op("scalar", lambda h, dch=dch, lt=lt: h.copy(
                    xnf[:, dch * 520 + 3 + lt * 128:dch * 520 + 3 + (lt + 1) * 128],
                    ppD[:, 512:640]), waits=[("st", t1)])
        ss_xnf = p.cnt["ss"]
        xnf3 = xnf[:].rearrange("p (c n) -> p c n", n=520)

        zlds = []
        for k in range(KD):
            if k == 0:
                zw = [("st", p.cnt["st"]), ("sxq", prev_sends["d3"]),
                      ("sv", p.cnt["sv"])]
            elif k % 2 == 0:
                zw = [("sq", zlds[k - 1])]
            else:
                zw = []
            zlds.append(p.dma("sync", lambda h, k=k: h.dma_start(
                out=wstr[k], in_=wb_ip[k * 128:(k + 1) * 128, E:2 * E]),
                waits=zw))
        p.wait("tensor", [("ss", ss_xnf)])
        z_evac = {0: 0, 1: 0}
        for mz in range(KE):
            psm = ppA if mz % 2 == 0 else ppB
            for k in range(KD):
                w_ = []
                if mz == 0:
                    w_.append(("sq", zlds[k | 1]))
                if k == 0:
                    w_.append(("ss", z_evac[mz % 2]))
                p.op("tensor", lambda h, mz=mz, k=k, psm=psm: h.matmul(
                    psm[:], wstr[k][:, mz * 128:(mz + 1) * 128],
                    xnf3[:, k, 3:515], start=(k == 0), stop=(k == KD - 1)),
                    waits=w_)
            stz = p.cnt["st"]
            zslot = zbs[:, (mz % 2) * LB:(mz % 2 + 1) * LB]
            scz = p.op("scalar", lambda h, mz=mz, psm=psm, zslot=zslot: h.activation(
                zslot, psm[:], AF.Silu, bias=bias0[:, KE + mz:KE + mz + 1]),
                waits=[("st", stz), ("sp", p.cnt["sp"])])
            z_evac[mz % 2] = scz
            prev_sends[("z", mz % 2)] = p.dma(
                "scalar", lambda h, ob=ob, mz=mz, zslot=zslot: h.dma_start(
                    out=spillZ[ob, :, mz * LB:(mz + 1) * LB], in_=zslot),
                waits=[("ss", scz)])

        xlds = []
        for k in range(KD):
            if k == 0:
                xw = [("st", p.cnt["st"])]
            elif k % 2 == 0:
                xw = [("sq", xlds[k - 1])]
            else:
                xw = []
            xlds.append(p.dma("sync", lambda h, k=k: h.dma_start(
                out=wstr[k], in_=wb_ip[k * 128:(k + 1) * 128, 0:E]),
                waits=xw))
        x_evac = 0
        for mx in range(KE):
            for k in range(KD):
                w_ = []
                if mx == 0:
                    w_.append(("sq", xlds[k | 1]))
                if k == 0:
                    w_.append(("sv", p.cnt["sv"]))
                p.op("tensor", lambda h, mx=mx, k=k: h.matmul(
                    ppD[:, 0:257], wstr[k][:, mx * 128:(mx + 1) * 128],
                    xnf3[:, k, 0:257], start=(k == 0), stop=(k == KD - 1)),
                    waits=w_)
            for k in range(KD):
                p.op("tensor", lambda h, mx=mx, k=k: h.matmul(
                    ppD[:, 512:770], wstr[k][:, mx * 128:(mx + 1) * 128],
                    xnf3[:, k, 257:515], start=(k == 0), stop=(k == KD - 1)))
            stx = p.cnt["st"]
            xc = xcvs[:, (mx % 2) * 520:(mx % 2) * 520 + 515]
            p.op("vector", lambda h, mx=mx, xc=xc: h.tensor_scalar(
                xc[:, 0:257], ppD[:, 0:257], bias0[:, mx:mx + 1], None, OP.add),
                waits=[("st", stx)])
            sc2 = p.op("vector", lambda h, mx=mx, xc=xc: h.tensor_scalar(
                xc[:, 257:515], ppD[:, 512:770], bias0[:, mx:mx + 1], None, OP.add))
            x_evac = sc2
            p.op("vector", lambda h, ob=ob, xc=xc: h.tensor_scalar(
                xc[:, 0:3], xc[:, 0:3], mcv[:, 3 * ob:3 * ob + 1], None, OP.mult))
            t = tcv[:, (mx % 2) * LB:(mx % 2 + 1) * LB]
            p.op("vector", lambda h, mx=mx, xc=xc, t=t: h.tensor_scalar(
                t, xc[:, 0:LB], cw64[:, mx * 4:mx * 4 + 1],
                cb16[:, mx:mx + 1], OP.mult, OP.add))
            for kk in range(1, 4):
                p.op("vector", lambda h, mx=mx, xc=xc, t=t, kk=kk: h.scalar_tensor_tensor(
                    t, xc[:, kk:kk + LB], cw64[:, mx * 4 + kk:mx * 4 + kk + 1], t,
                    OP.mult, OP.add))
            svt = p.cnt["sv"]
            scs = p.op("scalar", lambda h, mx=mx, t=t: h.activation(
                xsl[:, mx * LB:(mx + 1) * LB], t, AF.Silu), waits=[("sv", svt)])
            p.op("vector", lambda h, mx=mx: h.tensor_copy(
                xslb[:, mx * LB:(mx + 1) * LB], xsl[:, mx * LB:(mx + 1) * LB]),
                waits=[("ss", scs)])
        sv_x = p.cnt["sv"]
        prev_sends["spx"] = p.dma("scalar", lambda h, ob=ob: h.dma_start(
            out=spillX[ob], in_=xslb[:]), waits=[("sv", sv_x)])

        for k in range(KE):
            p.op("tensor", lambda h, k=k: h.matmul(
                ppF[0:96, :], xpT[:, k * 96:(k + 1) * 96],
                xslb[:, k * LB:(k + 1) * LB], start=(k == 0), stop=(k == KE - 1)),
                waits=[("sv", sv_x)] if k == 0 else ())
        st_xd = p.cnt["st"]
        sv_xd = p.op("vector", lambda h: h.tensor_copy(xdb[0:96, :], ppF[0:96, :]),
                     waits=[("st", st_xd), ("sxq", prev_sends["d3"])])
        sv_xdb = p.op("vector", lambda h: h.tensor_copy(xdbb[0:RK, :], xdb[0:RK, :]))

        dt_evac = {0: 0, 1: 0}
        for md in range(KE):
            psm = ppF if md % 2 == 0 else ppG
            w_ = [("ss", dt_evac[md % 2])]
            if md == 0:
                w_.append(("sv", sv_xdb))
            tm = p.op("tensor", lambda h, md=md, psm=psm: h.matmul(
                psm[:], dtpT[0:RK, md * 128:(md + 1) * 128], xdbb[0:RK, :],
                start=True, stop=True), waits=w_)
            p.op("scalar", lambda h, md=md, psm=psm: h.activation(
                dtt[:, md * LB:(md + 1) * LB], psm[:], AF.Exp,
                bias=dtb[:, md:md + 1]), waits=[("st", tm)])
            dt_evac[md % 2] = p.op("scalar", lambda h, md=md: h.activation(
                dtt[:, md * LB:(md + 1) * LB], dtt[:, md * LB:(md + 1) * LB],
                AF.Ln, bias=1.0))
        ss_dt = p.cnt["ss"]

        p.wait("vector", [("ss", ss_dt), ("sp", prev_sends["spx"])])
        for mx in range(KE):
            p.op("vector", lambda h, mx=mx: h.tensor_mul(
                xsl[:, mx * LB:(mx + 1) * LB], xsl[:, mx * LB:(mx + 1) * LB],
                dtt[:, mx * LB:(mx + 1) * LB]))
        sv_dtx = p.cnt["sv"]

        prev_sends["d1"] = p.dma("sync", lambda h, ob=ob: h.dma_start(
            out=ccF_in[ob, 0:E, :].rearrange("(q pp) n -> pp q n", pp=128),
            in_=dtt[:].rearrange("p (q n) -> p q n", n=LB)),
            waits=[("ss", ss_dt)], sem="sxq")
        prev_sends["d2"] = p.dma("sync", lambda h, ob=ob: h.dma_start(
            out=ccF_in[ob, E:2 * E, :].rearrange("(q pp) n -> pp q n", pp=128),
            in_=xsl[:].rearrange("p (q n) -> p q n", n=LB)),
            waits=[("sv", sv_dtx)], sem="sxq")
        prev_sends["d3"] = p.dma("sync", lambda h, ob=ob: h.dma_start(
            out=ccF_in[ob, 2 * E:2 * E + 32, :], in_=xdb[RK:96, :]),
            waits=[("sv", sv_xdb)], sem="sxq")
        ccF_done[ob] = p.cc(lambda h, ob=ob: h.collective_compute(
            "AllGather", OP.bypass,
            replica_groups=[[0, 1, 2, 3], [4, 5, 6, 7]],
            ins=[ccF_in[ob]], outs=[ccF_out[ob]]),
            waits=[("sxq", prev_sends["d3"])])

        if debug and ob == 0:
            w0 = p.cnt["sv"]
            s0 = p.cnt["ss"]
            p.dma("sync", lambda h: h.dma_start(out=dbgt[0, :, 0:32], in_=bias0[:]),
                  waits=[("sv", w0), ("ss", s0)])
            p.dma("sync", lambda h: h.dma_start(out=dbgt[0, :, 32:96], in_=A64[:]))
            p.dma("sync", lambda h: h.dma_start(out=dbgt[0, :, 96:112], in_=colw[:]))
            p.dma("sync", lambda h: h.dma_start(out=dbgt[0, :, 112:120],
                                                in_=statc[:, 0:8]))
            p.dma("gpsimd", lambda h: h.dma_start(out=dbgt[1], in_=xnf[:]),
                  waits=[("sv", w0), ("ss", s0)])
            p.dma("sync", lambda h: h.dma_start(out=dbgt[2, :, 0:4096],
                                                in_=xsl[:, 0:4096]))
            p.dma("sync", lambda h: h.dma_start(out=dbgt[3, :, 0:4096],
                                                in_=dtt[:, 0:4096]))
            p.dma("sync", lambda h: h.dma_start(out=dbgt[4, :, 0:512], in_=xdb[:]))
            p.dma("gpsimd", lambda h: h.dma_start(out=dbgt[5, :, 0:4096],
                                                  in_=xslb[:, 0:4096]))
            p.dma("gpsimd", lambda h: h.dma_start(out=dbgt[6, :, 0:KE * 96],
                                                  in_=xpT[:]),
                  waits=[("sv", w0)])
            p.dma("gpsimd", lambda h: h.dma_start(out=dbgt[7, :, 0:2048],
                                                  in_=dtpT[:]))

    pbE = {"pre": {"sv": 0, "ss": 0}}
    p.barrier()

    bcf = ar1[:, 0:LB]
    bchi = ar1b[:, 2 * LB:3 * LB]
    bclo = ar1b[:, 3 * LB:4 * LB]
    rdt = ar1[:, 2 * LB:6 * LB]
    rdx = ar1[:, 6 * LB:10 * LB]
    hbuf = ar1[:, 10 * LB:14 * LB]
    yq = ar1[:, 14 * LB:18 * LB]
    carry = ar1[:, 18 * LB:18 * LB + 64]
    dAs = sb("dAs", [128, 2 * LB], F32)
    usl = sb("usl", [128, 2 * LB], F32)
    yin = sb("yin", [128, 4 * LB], BF16)
    ysl2 = sb("ysl2", [128, 2 * LB], F32)
    xsz = sb("xsz", [128, 4 * LB], BF16)
    xmidt = ar3[:, 0:4 * D]
    ohi = ar3[:, 4 * D:4 * D + KD * LB]
    xoslot = ar3[:, 4 * D + KD * LB:4 * D + KD * LB + D]
    xqf = sb("xqf", [128, KD * 516], BF16)

    hq = sb("hq", [128, 4 * LB], BF16)
    mqt = sb("mqt", [128, LB], F32)
    shb = sb("shb", [128, LB], F32)
    hmini = sb("hmini", [128, D], F32)
    xqtk = hmini
    hslot = sb("hslot", [128, 2 * LB], F32)
    xqf3 = xqf[:].rearrange("p (c n) -> p c n", n=516)

    p.op("vector", lambda h: h.memset(carry[:], 0.0))

    def scan_gen(r):
        for blkslot in range(4):
            gw = [("scc", ccF_done[r]), ("ss", p.cnt["ss"]), ("sv", p.cnt["sv"])]
            gd = []
            for qq in range(4):
                gd.append(p.dma("gpsimd", lambda h, r=r, blkslot=blkslot, qq=qq:
                    h.indirect_dma_start(
                        out=rdt[:, qq * LB:(qq + 1) * LB], out_offset=None,
                        in_=ccF_out[r],
                        in_offset=IndirectOffsetOnAxis(
                            ap=idxt[:, IDXC[("dt", blkslot, qq)]:IDXC[("dt", blkslot, qq)] + 1],
                            axis=0)), waits=gw if qq == 0 else ()))
                gd.append(p.dma("gpsimd", lambda h, r=r, blkslot=blkslot, qq=qq:
                    h.indirect_dma_start(
                        out=rdx[:, qq * LB:(qq + 1) * LB], out_offset=None,
                        in_=ccF_out[r],
                        in_offset=IndirectOffsetOnAxis(
                            ap=idxt[:, IDXC[("dx", blkslot, qq)]:IDXC[("dx", blkslot, qq)] + 1],
                            axis=0))))
            p.dma("gpsimd", lambda h, r=r, blkslot=blkslot: h.indirect_dma_start(
                out=bcf[0:32, :], out_offset=None, in_=ccF_out[r],
                in_offset=IndirectOffsetOnAxis(
                    ap=idxt[0:32, IDXC[("bcs", blkslot)]:IDXC[("bcs", blkslot)] + 1],
                    axis=0)))
            gall = p.cnt["sg"]
            ch = p.op("vector", lambda h: h.tensor_copy(bchi[0:32, :], bcf[0:32, :]),
                      waits=[("sg", gall), ("st", p.cnt["st"])])
            cl = p.op("vector", lambda h: h.tensor_tensor(
                bclo[0:32, :], bcf[0:32, :], bchi[0:32, :], OP.subtract))

            for qt in range(4):
                for jp in range(2):
                    selw = [("sv", p.cnt["sv"])]
                    for j2 in range(2):
                        s = 4 * qt + 2 * jp + j2
                        dB = ppS1[:, j2 * 512:(j2 + 1) * 512]
                        dC = ppS2[:, j2 * 512:(j2 + 1) * 512]
                        p.op("tensor", lambda h, s=s, dB=dB: h.matmul(
                            dB, selt[0:32, s * 128:(s + 1) * 128],
                            bchi[0:32, :], start=True, stop=False),
                            waits=selw if j2 == 0 else ())
                        p.op("tensor", lambda h, s=s, dB=dB: h.matmul(
                            dB, selt[0:32, s * 128:(s + 1) * 128],
                            bclo[0:32, :], start=False, stop=True))
                        p.op("tensor", lambda h, s=s, dC=dC: h.matmul(
                            dC, selt[0:32, (16 + s) * 128:(17 + s) * 128],
                            bchi[0:32, :], start=True, stop=False))
                        p.op("tensor", lambda h, s=s, dC=dC: h.matmul(
                            dC, selt[0:32, (16 + s) * 128:(17 + s) * 128],
                            bclo[0:32, :], start=False, stop=True))
                    stsel = p.cnt["st"]
                    for qq in range(4):
                        for j2 in range(2):
                            s = 4 * qt + 2 * jp + j2
                            par = j2
                            da = dAs[:, par * LB:(par + 1) * LB]
                            uu = usl[:, par * LB:(par + 1) * LB]
                            sw_ = ([("sg", gall)]
                                   if (qt == 0 and jp == 0 and qq == 0 and j2 == 0)
                                   else [])
                            sw_.append(("sv", last_scan_par[par]))
                            se = p.op("scalar", lambda h, qq=qq, s=s, da=da: h.activation(
                                da, rdt[:, qq * LB:(qq + 1) * LB], AF.Exp,
                                scale=A64[:, qq * S + s:qq * S + s + 1]), waits=sw_)
                            uw = [("ss", se), ("st", stsel)]
                            if qt == 0 and jp == 0 and qq == 0 and j2 == 0:
                                uw.append(("syq", p.cnt["syq"]))
                            p.op("vector", lambda h, qq=qq, j2=j2, uu=uu: h.tensor_mul(
                                uu, rdx[:, qq * LB:(qq + 1) * LB],
                                ppS1[:, j2 * 512:(j2 + 1) * 512]), waits=uw)
                            last_scan_par[par] = p.op("vector",
                                lambda h, qq=qq, s=s, j2=j2, da=da, uu=uu: h.tensor_tensor_scan(
                                    hbuf[:, j2 * LB:(j2 + 1) * LB], da, uu,
                                    carry[:, qq * S + s:qq * S + s + 1],
                                    OP.mult, OP.add))
                        p.op("vector", lambda h, qq=qq, qt=qt, jp=jp: h.tensor_copy(
                            carry[:, qq * S + qt * 4 + 2 * jp:qq * S + qt * 4 + 2 * jp + 2],
                            hbuf[:, 0:2 * LB].rearrange("p (s n) -> p s n", n=LB)[:, :, LB - 1:LB]))
                        p.op("vector", lambda h: h.tensor_mul(
                            hbuf[:, 0:2 * LB], hbuf[:, 0:2 * LB], ppS2[:, 0:1024]))
                        h3 = hbuf[:, 0:2 * LB].rearrange("p (s n) -> p n s", n=LB)
                        if qt == 0 and jp == 0:
                            p.op("vector", lambda h, qq=qq, h3=h3: h.tensor_reduce(
                                yq[:, qq * LB:(qq + 1) * LB], h3, AX.X, OP.add))
                        else:
                            p.op("vector", lambda h, h3=h3: h.tensor_reduce(
                                hbuf[:, 2 * LB:3 * LB], h3, AX.X, OP.add))
                            p.op("vector", lambda h, qq=qq: h.tensor_add(
                                yq[:, qq * LB:(qq + 1) * LB],
                                yq[:, qq * LB:(qq + 1) * LB], hbuf[:, 2 * LB:3 * LB]))
            svy = p.cnt["sv"]
            ysend_tok[0] = p.dma("sync", lambda h, r=r, blkslot=blkslot: h.dma_start(
                out=ccR_in[r, blkslot * 512:(blkslot + 1) * 512, :]
                    .rearrange("(q pp) n -> pp q n", pp=128),
                in_=yq[:].rearrange("p (q n) -> p q n", n=LB)),
                waits=[("sv", svy)], sem="syq")
            if blkslot < 3:
                yield
        sy = p.cnt["syq"]
        sends_end[r] = sy
        ccR_done[r] = p.cc(lambda h, r=r: h.collective_compute(
            "AllGather", OP.bypass,
            replica_groups=[[0, 1, 2, 3], [4, 5, 6, 7]],
            ins=[ccR_in[r]], outs=[ccR_out[r]]), waits=[("syq", sy)])

    def post_a_gen(r):
        ob = r
        oplds = []
        for k in range(KE):
            if k == 0:
                ow = [("st", p.cnt["st"])]
            elif k % 4 == 0:
                ow = [("sq", oplds[k - 1])]
            else:
                ow = []
            oplds.append(p.dma("sync", lambda h, k=k: h.dma_start(
                out=ar2b[:, k * D:(k + 1) * D], in_=wb_op[k * 128:(k + 1) * 128, :]),
                waits=ow))
        pbe = pbE.get(r - 2, pbE.get("pre", {"sv": 0, "ss": 0}))
        p.wait("tensor", [("sv", pbe["sv"]), ("ss", pbe["ss"])])

        def build_yin(k, slot_war):
            par = k % 2
            spar = k % 4
            ys = ysl2[:, par * LB:(par + 1) * LB]
            gy = p.dma("gpsimd", lambda h, r=r, k=k, ys=ys: h.indirect_dma_start(
                out=ys, out_offset=None, in_=ccR_out[r],
                in_offset=IndirectOffsetOnAxis(
                    ap=idxt[:, IDXC[("y", k)]:IDXC[("y", k)] + 1], axis=0)),
                waits=[("scc", ccR_done[r]), ("sv", p.cnt["sv"])])
            xs = xsz[:, par * LB:(par + 1) * LB]
            zs = xsz[:, (2 + par) * LB:(3 + par) * LB]
            lx = p.dma("scalar", lambda h, ob=ob, k=k, xs=xs: h.dma_start(
                out=xs, in_=spillX[ob, :, k * LB:(k + 1) * LB]),
                waits=[("sv", p.cnt["sv"])])
            lz = p.dma("scalar", lambda h, ob=ob, k=k, zs=zs: h.dma_start(
                out=zs, in_=spillZ[ob, :, k * LB:(k + 1) * LB]))
            p.op("vector", lambda h, k=k, ys=ys, xs=xs: h.scalar_tensor_tensor(
                ys, xs, Dc16[:, k:k + 1], ys, OP.mult, OP.add),
                waits=[("sg", gy), ("sp", lz)])
            w_ = [("sp", lz)]
            if slot_war:
                w_.append(("st", slot_war))
            yslot = yin[:, spar * LB:(spar + 1) * LB]
            sv_y = p.op("vector", lambda h, ys=ys, zs=zs, yslot=yslot: h.tensor_mul(
                yslot, ys, zs), waits=w_)
            return yslot, sv_y

        mmk = {}
        for k in range(KE):
            yslot, sv_y = build_yin(k, mmk.get(k - 4))
            for m in range(4):
                p.op("tensor", lambda h, k=k, m=m, yslot=yslot: h.matmul(
                    pbank4[m], ar2b[:, k * D + m * 128:k * D + (m + 1) * 128],
                    yslot, start=(k == 0), stop=(k == KE - 1)),
                    waits=([("sq", oplds[(k // 4) * 4 + 3]), ("sv", sv_y)]
                           if m == 0 else ()))
            mmk[k] = p.cnt["st"]
        stmm = p.cnt["st"]
        ev1 = 0
        for m in range(4):
            ev1 = p.op("vector", lambda h, m=m: h.tensor_copy(
                ohi[:, m * LB:(m + 1) * LB], pbank4[m]),
                waits=[("st", stmm)] if m == 0 else ())
        yield
        mmk2nd = {}
        for k in range(KE):
            yslot, sv_y = build_yin(k, mmk2nd.get(k - 4, mmk.get(12 + k)))
            for m in range(4):
                w_ = []
                if m == 0:
                    w_ = [("sv", sv_y)] + ([("sv", ev1)] if k == 0 else [])
                p.op("tensor", lambda h, k=k, m=m, yslot=yslot: h.matmul(
                    pbank4[m], ar2b[:, k * D + (4 + m) * 128:k * D + (5 + m) * 128],
                    yslot, start=(k == 0), stop=(k == KE - 1)), waits=w_)
            mmk2nd[k] = p.cnt["st"]
        stmm = p.cnt["st"]
        ev = 0
        for m in range(4):
            ev = p.op("vector", lambda h, m=m: h.tensor_copy(
                ohi[:, (4 + m) * LB:(5 + m) * LB], pbank4[m]),
                waits=[("st", stmm)] if m == 0 else ())
        svev = ev

        ldxr = p.dma("sync", lambda h, ob=ob: h.dma_start(
            out=xmidt.rearrange("p (t n) -> p t n", n=D),
            in_=x_own[ob, 3:3 + LB, :].rearrange("(t p) n -> p t n", p=128)),
            waits=[("sv", p.cnt["sv"]), ("sp", p.cnt["sp"]),
                   ("sxq", p.cnt["sxq"])], sem="sxq")
        xm_done = 0
        for lt in range(4):
            pdst = ppD if lt % 2 == 0 else ppE
            for m in range(KD):
                w_ = []
                if m == 0:
                    w_.append(("sv", svev if lt < 2 else xm_done))
                p.op("tensor", lambda h, lt=lt, m=m, pdst=pdst: h.matmul(
                    pdst[:, m * 128:(m + 1) * 128],
                    ohi[:, m * LB + lt * 128:m * LB + (lt + 1) * 128],
                    idf[:], is_transpose=True, start=True, stop=True), waits=w_)
            sttr = p.cnt["st"]
            p.op("vector", lambda h, ob=ob, lt=lt: h.tensor_scalar(
                xoslot[:, 0:D], xmidt[:, lt * D:(lt + 1) * D],
                statc[:, ob * 8 + lt:ob * 8 + lt + 1],
                statc[:, ob * 8 + 4 + lt:ob * 8 + 5 + lt], OP.mult, OP.add),
                waits=[("sxq", ldxr)] if lt == 0 else ())
            xm_done = p.op("vector", lambda h, lt=lt, pdst=pdst: h.scalar_tensor_tensor(
                xmidt[:, lt * D:(lt + 1) * D], pdst[:, 0:D], rsc[:, 0:1],
                xoslot[:, 0:D], OP.mult, OP.add), waits=[("st", sttr)])
        svxm = xm_done
        spm = p.dma("scalar", lambda h, ob=ob: h.dma_start(
            out=spillM[ob], in_=xmidt[:]), waits=[("sv", svxm)])
        d1 = p.dma("sync", lambda h, r=r: h.dma_start(
            out=ccX_in[r, 0:1, :], in_=xmidt[0:1, 0:D]), waits=[("sv", svxm)],
            sem="sxq")
        d2 = p.dma("sync", lambda h, r=r: h.dma_start(
            out=ccX_in[r, 1:2, :], in_=xmidt[127:128, 3 * D:4 * D]), sem="sxq")
        ccX_done[r] = p.cc(lambda h, r=r: h.collective_compute(
            "AllGather", OP.bypass,
            replica_groups=[[0, 1, 2, 3], [4, 5, 6, 7]],
            ins=[ccX_in[r]], outs=[ccX_out[r * 8:(r + 1) * 8, :]]),
            waits=[("sxq", d2)])

    def post_b_gen(r):
        ob = r
        need_right = ccX_done[min(r + 1, NR - 1)] if (4 * r + 3) + 1 <= NB - 1 else ccX_done[r]
        gh = p.dma("gpsimd", lambda h, ob=ob: h.indirect_dma_start(
            out=hmini[0:2, :], out_offset=None, in_=ccX_out,
            in_offset=IndirectOffsetOnAxis(
                ap=idxt[0:2, IDXC[("xh", ob)]:IDXC[("xh", ob)] + 1], axis=0)),
            waits=[("scc", need_right), ("sv", p.cnt["sv"])])
        p.wait("vector", [("sg", gh)])
        muh = stat[:, 8:9]
        r3h = stat[:, 9:10]
        ln_stats(hmini, 2, muh, r3h)
        p.op("vector", lambda h: h.tensor_scalar(
            hmini[0:2, :], hmini[0:2, :], muh[0:2, :], None, OP.subtract))
        p.op("vector", lambda h: h.tensor_reduce(
            stat[0:2, 10:11], hmini[0:2, :], AX.X, OP.max, apply_absolute_value=True))
        p.op("vector", lambda h: h.tensor_mul(stat[0:2, 10:11], stat[0:2, 10:11], r3h[0:2, :]))
        p.op("vector", lambda h: h.tensor_scalar_max(stat[0:2, 10:11], stat[0:2, 10:11], 1e-5))
        p.op("vector", lambda h: h.reciprocal(stat[0:2, 11:12], stat[0:2, 10:11]))
        p.op("vector", lambda h: h.tensor_scalar_mul(stat[0:2, 11:12], stat[0:2, 11:12], 127.0))
        p.op("vector", lambda h: h.reciprocal(stat[0:2, 12:13], stat[0:2, 11:12]))
        p.op("vector", lambda h: h.tensor_mul(stat[0:2, 14:15], r3h[0:2, :], stat[0:2, 11:12]))
        p.op("vector", lambda h: h.tensor_scalar(
            hmini[0:2, :], hmini[0:2, :], stat[0:2, 14:15], magic[0:2, 0:1],
            OP.mult, OP.add))
        cq = p.op("vector", lambda h: h.tensor_scalar(
            hmini[0:2, :], hmini[0:2, :], magic[0:2, 0:1], stat[0:2, 12:13],
            OP.subtract, OP.mult))
        hvm = cq
        for dch in range(KD):
            t0 = p.op("tensor", lambda h, dch=dch: h.matmul(
                ppD[:, 0:2], hmini[0:2, dch * 128:(dch + 1) * 128],
                idf[0:2, 0:2], is_transpose=True, start=True, stop=True),
                waits=[("sv", hvm)])
            p.op("vector", lambda h, dch=dch, ob=ob: h.tensor_scalar(
                xqf[:, dch * 516:dch * 516 + 1], ppD[:, 0:1],
                mfn[:, 2 * ob:2 * ob + 1], None, OP.mult), waits=[("st", t0)])
            hvm = p.op("vector", lambda h, dch=dch, ob=ob: h.tensor_scalar(
                xqf[:, dch * 516 + 513:dch * 516 + 514], ppD[:, 1:2],
                mfn[:, 2 * ob + 1:2 * ob + 2], None, OP.mult))

        ldm = p.dma("scalar", lambda h, ob=ob: h.dma_start(
            out=xmidt[:], in_=spillM[ob]),
            waits=[("sv", p.cnt["sv"]), ("sp", p.cnt["sp"])])
        p.wait("vector", [("sp", ldm)])
        for lt in range(4):
            src = xmidt[:, lt * D:(lt + 1) * D]
            mu3 = stat[:, 0:1]
            r3 = stat[:, 1:2]
            ln_stats(src, 128, mu3, r3)
            p.op("vector", lambda h, src=src, mu3=mu3: h.tensor_scalar(
                xoslot[:, 0:D], src, mu3, None, OP.subtract))
            p.op("vector", lambda h: h.tensor_reduce(
                stat[:, 10:11], xoslot[:, 0:D], AX.X, OP.max,
                apply_absolute_value=True))
            p.op("vector", lambda h, r3=r3: h.tensor_mul(stat[:, 10:11], stat[:, 10:11], r3))
            p.op("vector", lambda h: h.tensor_scalar_max(stat[:, 10:11], stat[:, 10:11], 1e-5))
            p.op("vector", lambda h: h.reciprocal(stat[:, 11:12], stat[:, 10:11]))
            p.op("vector", lambda h: h.tensor_scalar_mul(stat[:, 11:12], stat[:, 11:12], 127.0))
            p.op("vector", lambda h: h.reciprocal(stat[:, 12:13], stat[:, 11:12]))
            p.op("vector", lambda h, r3=r3: h.tensor_mul(stat[:, 14:15], r3, stat[:, 11:12]))
            p.op("vector", lambda h: h.tensor_scalar(
                xoslot[:, 0:D], xoslot[:, 0:D], stat[:, 14:15], magic[:, 0:1],
                OP.mult, OP.add))
            cq2 = p.op("vector", lambda h: h.tensor_scalar(
                xqtk[:], xoslot[:, 0:D], magic[:, 0:1], stat[:, 12:13],
                OP.subtract, OP.mult))
            sc = p.cnt["ss"]
            for dch in range(KD):
                t1 = p.op("tensor", lambda h, dch=dch, lt=lt: h.matmul(
                    ppD[:, 512:640], xqtk[:, dch * 128:(dch + 1) * 128],
                    idf[:], is_transpose=True, start=True, stop=True),
                    waits=[("sv", cq2), ("ss", sc)])
                sc = p.op("scalar", lambda h, dch=dch, lt=lt: h.copy(
                    xqf[:, dch * 516 + 1 + lt * 128:dch * 516 + 1 + (lt + 1) * 128],
                    ppD[:, 512:640]), waits=[("st", t1)])
        ss_xq = p.cnt["ss"]
        yield

        mzero = p.op("vector", lambda h: h.memset(mqt[:], 0.0))
        p.wait("tensor", [("ss", ss_xq)])
        convdone = 0
        for mh in range(2):
            w1lds = []
            for k in range(KD):
                if k == 0:
                    ww = [("st", p.cnt["st"])]
                elif k % 2 == 0:
                    ww = [("sq", w1lds[k - 1])]
                else:
                    ww = []
                w1lds.append(p.dma("sync", lambda h, k=k, mh=mh: h.dma_start(
                    out=wstr[k], in_=wb_w1[k * 128:(k + 1) * 128,
                                          mh * 2048:(mh + 1) * 2048]),
                    waits=ww))
            for mi in range(KE):
                m = mh * KE + mi
                for k in range(KD):
                    w_ = []
                    if mi == 0:
                        w_.append(("sq", w1lds[k | 1]))
                    if k == 0:
                        w_.append(("sv", convdone))
                    p.op("tensor", lambda h, mi=mi, k=k: h.matmul(
                        ppD[:, 0:257], wstr[k][:, mi * 128:(mi + 1) * 128],
                        xqf3[:, k, 0:257], start=(k == 0), stop=(k == KD - 1)),
                        waits=w_)
                for k in range(KD):
                    p.op("tensor", lambda h, mi=mi, k=k: h.matmul(
                        ppD[:, 512:769], wstr[k][:, mi * 128:(mi + 1) * 128],
                        xqf3[:, k, 257:514], start=(k == 0), stop=(k == KD - 1)))
                stw1 = p.cnt["st"]
                t = hslot[:, (m % 2) * LB:(m % 2 + 1) * LB]
                p.op("vector", lambda h, m=m, t=t: h.tensor_scalar(
                    t[:, 0:257], ppD[:, 0:257], fcw[:, m * 3:m * 3 + 1],
                    fcb[:, m:m + 1], OP.mult, OP.add), waits=[("st", stw1)])
                p.op("vector", lambda h, m=m, t=t: h.tensor_scalar(
                    t[:, 257:512], ppD[:, 512:767], fcw[:, m * 3:m * 3 + 1],
                    fcb[:, m:m + 1], OP.mult, OP.add))
                p.op("vector", lambda h, m=m, t=t: h.scalar_tensor_tensor(
                    t[:, 0:256], ppD[:, 1:257], fcw[:, m * 3 + 1:m * 3 + 2],
                    t[:, 0:256], OP.mult, OP.add))
                p.op("vector", lambda h, m=m, t=t: h.scalar_tensor_tensor(
                    t[:, 256:512], ppD[:, 512:768], fcw[:, m * 3 + 1:m * 3 + 2],
                    t[:, 256:512], OP.mult, OP.add))
                p.op("vector", lambda h, m=m, t=t: h.scalar_tensor_tensor(
                    t[:, 0:255], ppD[:, 2:257], fcw[:, m * 3 + 2:m * 3 + 3],
                    t[:, 0:255], OP.mult, OP.add))
                convdone = p.op("vector", lambda h, m=m, t=t: h.scalar_tensor_tensor(
                    t[:, 255:512], ppD[:, 512:769], fcw[:, m * 3 + 2:m * 3 + 3],
                    t[:, 255:512], OP.mult, OP.add))
                hqs = hq[:, (m % 4) * LB:((m % 4) + 1) * LB]
                scs = p.op("scalar", lambda h, m=m, t=t, hqs=hqs: h.activation(
                    hqs, t, AF.Silu),
                    waits=[("sv", convdone), ("sp", p.cnt["sp"])])
                p.dma("scalar", lambda h, ob=ob, m=m, hqs=hqs: h.dma_start(
                    out=spillH[ob, :, m * LB:(m + 1) * LB],
                    in_=hqs), waits=[("ss", scs)])
                sab = p.op("scalar", lambda h, hqs=hqs: h.activation(
                    hslot[:, LB:2 * LB], hqs, AF.Abs))
                p.op("vector", lambda h, m=m: h.tensor_max(
                    mqt[:], mqt[:], hslot[:, LB:2 * LB]),
                    waits=[("ss", sab), ("sv", mzero)] if m == 0 else [("ss", sab)])
        svmx = p.cnt["sv"]
        yield

        mcols = stat[:, 24:28]
        prevv = svmx
        for lt in range(4):
            pdst = ppD if lt % 2 == 0 else ppE
            tt = p.op("tensor", lambda h, lt=lt, pdst=pdst: h.matmul(
                pdst[:, 0:128], mqt[:, lt * 128:(lt + 1) * 128], idf[:],
                is_transpose=True, start=True, stop=True),
                waits=[("sv", prevv), ("st", p.cnt["st"])] if lt == 0
                else [("sv", prevv)])
            p.op("vector", lambda h, pdst=pdst: h.tensor_reduce(
                stat[:, 12:13], pdst[:, 0:128], AX.X, OP.max), waits=[("st", tt)])
            p.op("vector", lambda h, lt=lt: h.tensor_scalar_max(
                mcols[:, lt:lt + 1], stat[:, 12:13], 1e-5))
            p.op("vector", lambda h, lt=lt: h.reciprocal(
                stat[:, 13:14], mcols[:, lt:lt + 1]))
            cshc = p.op("vector", lambda h: h.tensor_scalar_mul(
                stat[:, 13:14], stat[:, 13:14], 127.0))
            prevv = cshc
            p.op("tensor", lambda h, lt=lt: h.matmul(
                ppP1[0:1, 512 + lt * 128:512 + (lt + 1) * 128], stat[:, 13:14],
                idf[:], is_transpose=True, start=True, stop=True),
                waits=[("sv", cshc)])
        crow = p.op("vector", lambda h: h.tensor_copy(
            k2st[0:1, 0:LB], ppP1[0:1, 512:512 + LB]), waits=[("st", p.cnt["st"])])
        tb = bcast_row(ppP2[:, 512:1024], LB, 0, crow)
        csh = p.op("vector", lambda h: h.tensor_copy(shb[:], ppP2[:, 512:1024]),
                   waits=[("st", tb)])

        w2lds = {}
        mmk2 = {}
        for k in range(KH):
            if k % 4 == 0:
                for kk in range(k, k + 4):
                    w_ = []
                    if kk % 4 == 0:
                        w_.append(("st", p.cnt["st"]) if kk == 0
                                  else ("sq", w2lds[kk - 1]))
                        if kk >= 16:
                            w_.append(("st", mmk2[kk - 13]))
                    w2lds[kk] = p.dma("sync", lambda h, kk=kk: h.dma_start(
                        out=ar2b[:, (kk % 16) * D:((kk % 16) + 1) * D],
                        in_=wb_w2[kk * 128:(kk + 1) * 128, :]), waits=w_)
            spar = k % 4
            hqs = hq[:, spar * LB:(spar + 1) * LB]
            lh = p.dma("scalar", lambda h, ob=ob, k=k, hqs=hqs: h.dma_start(
                out=hqs, in_=spillH[ob, :, k * LB:(k + 1) * LB]),
                waits=[("st", mmk2.get(k - 4, 0)), ("sp", p.cnt["sp"])] if k >= 4
                else [("sp", p.cnt["sp"])])
            p.op("vector", lambda h, hqs=hqs: h.tensor_mul(
                hslot[:, 0:LB], hqs, shb[:]),
                waits=[("sp", lh)] + ([("sv", csh)] if k == 0 else []))
            svr = p.op("vector", lambda h, hqs=hqs: h.tensor_scalar(
                hqs, hslot[:, 0:LB], magic[:, 0:1], magic[:, 0:1],
                OP.add, OP.subtract))
            p.dma("scalar", lambda h, ob=ob, k=k, hqs=hqs: h.dma_start(
                out=spillH2[ob, :, k * LB:(k + 1) * LB], in_=hqs),
                waits=[("sv", svr)])
            for m in range(4):
                w_ = []
                if m == 0:
                    w_.append(("sq", w2lds[(k // 4) * 4 + 3]))
                    w_.append(("sv", svr))
                    if k == 0:
                        w_.append(("ss", ss_xq))
                p.op("tensor", lambda h, k=k, m=m, hqs=hqs: h.matmul(
                    pbank4[m], ar2b[:, (k % 16) * D + m * 128:(k % 16) * D + (m + 1) * 128],
                    hqs, start=(k == 0), stop=(k == KH - 1)), waits=w_)
            mmk2[k] = p.cnt["st"]
        stmm = p.cnt["st"]
        evw1 = 0
        for m in range(4):
            evw1 = p.op("vector", lambda h, m=m: h.tensor_copy(
                ohi[:, m * LB:(m + 1) * LB], pbank4[m]),
                waits=[("st", stmm)] if m == 0 else ())
        yield
        w2lds2 = {}
        mmk2b = {}
        for k in range(KH):
            if k % 4 == 0:
                for kk in range(k, k + 4):
                    w_ = []
                    if kk % 4 == 0:
                        w_.append(("sq", w2lds[KH - 1] if kk == 0
                                   else w2lds2[kk - 1]))
                        w_.append(("st", mmk2[19 + kk] if kk < 16
                                   else mmk2b[kk - 13]))
                    w2lds2[kk] = p.dma("sync", lambda h, kk=kk: h.dma_start(
                        out=ar2b[:, (kk % 16) * D:((kk % 16) + 1) * D],
                        in_=wb_w2[kk * 128:(kk + 1) * 128, :]), waits=w_)
            spar = k % 4
            hqs = hq[:, spar * LB:(spar + 1) * LB]
            lh = p.dma("scalar", lambda h, ob=ob, k=k, hqs=hqs: h.dma_start(
                out=hqs, in_=spillH2[ob, :, k * LB:(k + 1) * LB]),
                waits=[("st", mmk2b[k - 4] if k >= 4 else mmk2[28 + k]),
                       ("sp", p.cnt["sp"])])
            for m in range(4):
                w_ = []
                if m == 0:
                    w_.append(("sq", w2lds2[(k // 4) * 4 + 3]))
                    w_.append(("sp", lh))
                    if k == 0:
                        w_.append(("sv", evw1))
                p.op("tensor", lambda h, k=k, m=m, hqs=hqs: h.matmul(
                    pbank4[m],
                    ar2b[:, (k % 16) * D + (4 + m) * 128:(k % 16) * D + (5 + m) * 128],
                    hqs, start=(k == 0), stop=(k == KH - 1)), waits=w_)
            mmk2b[k] = p.cnt["st"]
        stmm = p.cnt["st"]
        ev2 = 0
        for m in range(4):
            ev2 = p.op("vector", lambda h, m=m: h.tensor_copy(
                ohi[:, (4 + m) * LB:(5 + m) * LB], pbank4[m]),
                waits=[("st", stmm)] if m == 0 else ())
        svev2 = ev2

        ldm2 = p.dma("scalar", lambda h, ob=ob: h.dma_start(
            out=xmidt[:], in_=spillM[ob]),
            waits=[("sv", p.cnt["sv"]), ("sp", p.cnt["sp"])])
        fin = 0
        cf = 0
        lastout = 0
        for lt in range(4):
            pdst = ppD if lt % 2 == 0 else ppE
            p.op("vector", lambda h, lt=lt: h.tensor_scalar_mul(
                stat[:, 17:18], mcols[:, lt:lt + 1], 1.0 / 127.0))
            cf = p.op("vector", lambda h: h.tensor_mul(stat[:, 17:18], stat[:, 17:18], rw2))
            for m in range(KD):
                w_ = [("sv", svev2 if lt < 2 else fin)] if m == 0 else []
                p.op("tensor", lambda h, lt=lt, m=m, pdst=pdst: h.matmul(
                    pdst[:, m * 128:(m + 1) * 128],
                    ohi[:, m * LB + lt * 128:m * LB + (lt + 1) * 128],
                    idf[:], is_transpose=True, start=True, stop=True), waits=w_)
            sttr = p.cnt["st"]
            fin = p.op("vector", lambda h, lt=lt, pdst=pdst: h.scalar_tensor_tensor(
                xoslot[:, 0:D], pdst[:, 0:D], stat[:, 17:18],
                xmidt[:, lt * D:(lt + 1) * D], OP.mult, OP.add),
                waits=[("st", sttr), ("sp", ldm2), ("sv", cf), ("sp", lastout)])
            lastout = p.dma("scalar", lambda h, ob=ob, lt=lt: h.dma_start(
                out=y_out[ob, lt * 128:(lt + 1) * 128, :], in_=xoslot[:, 0:D]),
                waits=[("sv", fin)])
        pbE[r] = {"sv": p.cnt["sv"], "ss": p.cnt["ss"]}

    def step(g):
        if g is not None:
            next(g, None)

    for r in range(NR + 2):
        sg_ = scan_gen(r) if r < NR else None
        pa_ = post_a_gen(r - 1) if 0 <= r - 1 < NR else None
        pb_ = post_b_gen(r - 2) if 0 <= r - 2 < NR else None
        step(sg_)
        step(pa_)
        step(sg_)
        step(pa_)
        step(sg_)
        step(pb_)
        step(pb_)
        step(sg_)
        step(pb_)
        step(pb_)

    p.wait("vector", [("sp", p.cnt["sp"]), ("sq", p.cnt["sq"]),
                      ("sg", p.cnt["sg"]), ("syq", p.cnt["syq"]),
                      ("sxq", p.cnt["sxq"])])

    with nc.Block() as block:
        p.finalize(block)
    es.close()
    return nc


def _mk_sel32():
    sel = np.zeros((32, 32, 128), np.float32)
    for s in range(32):
        sel[s, s, :] = 1.0
    return np.ascontiguousarray(sel.transpose(1, 0, 2).reshape(32, 32 * 128)
                                ).astype(ml_dtypes.bfloat16)


def host_prep(inputs, NB):
    NR = NB // 4
    OB = NB // 4
    L = NB * LB
    x = np.ascontiguousarray(inputs["L0_attn_seq"], dtype=np.float32)
    f32 = lambda v: np.ascontiguousarray(np.asarray(v), dtype=np.float32)

    shared = {
        "in_projT": f32(np.asarray(inputs["L0_attn_in_proj_w"]).T),
        "out_projT": f32(np.asarray(inputs["L0_attn_out_proj_w"]).T),
        "w1T": f32(np.asarray(inputs["L0_ffn_w1"]).T),
        "w2T": f32(np.asarray(inputs["L0_ffn_w2"]).T),
        "x_projTd": f32(np.asarray(inputs["L0_attn_x_proj_w"]).T),
        "dt_projTd": f32(np.asarray(inputs["L0_attn_dt_proj_w"]).T),
        "ln_g8": f32(np.asarray(inputs["L0_attn_ln_g"]).reshape(KD, 128).T),
        "ln_b8": f32(np.asarray(inputs["L0_attn_ln_b"]).reshape(KD, 128).T),
        "conv_w64": f32(np.asarray(inputs["L0_attn_conv_w"]).reshape(KE, 128, 4)
                        .transpose(1, 0, 2).reshape(128, KE * 4)),
        "conv_b16": f32(np.asarray(inputs["L0_attn_conv_b"]).reshape(KE, 128).T),
        "dtb16": f32(np.asarray(inputs["L0_attn_dt_proj_b"]).reshape(KE, 128).T),
        "D16": f32(np.asarray(inputs["L0_attn_D"]).reshape(KE, 128).T),
        "fcw96": f32(np.asarray(inputs["L0_ffn_conv_w"])[:, 0, :].reshape(KH, 128, 3)
                     .transpose(1, 0, 2).reshape(128, KH * 3)),
        "fcb32": f32(np.asarray(inputs["L0_ffn_conv_b"]).reshape(KH, 128).T),
        "rs_col": f32(np.broadcast_to(np.asarray(inputs["L0_attn_res_scale"],
                                                 np.float32).reshape(1, 1), (128, 1))),
        "identc": np.eye(128, dtype=ml_dtypes.bfloat16),
        "identf": np.eye(128, dtype=np.float32),
        "sel32": _mk_sel32(),
        "onesk2": np.concatenate([np.ones((1, 128)), np.zeros((1, 128))]
                                 ).astype(ml_dtypes.bfloat16),
    }

    in_maps = []
    for c in range(8):
        b, m = c // 4, c % 4
        xb = x[b, :L].reshape(NB, LB, D)
        own = []
        for ob in range(OB):
            g = 4 * ob + m
            blk = np.zeros((LB + 3, D), np.float32)
            blk[3:] = xb[g]
            if g > 0:
                blk[0:3] = xb[g - 1][LB - 3:]
            own.append(blk)
        x_own = np.ascontiguousarray(np.stack(own))

        Al = np.asarray(inputs["L0_attn_A_log"])[512 * m:512 * (m + 1)]
        A_my = f32(Al.reshape(4, 128, S).transpose(1, 0, 2).reshape(128, 4 * S))

        idx = np.zeros((128, NIDX), np.int32)
        pcol = np.arange(128, dtype=np.int32)
        for j in range(4):
            for q in range(4):
                idx[:, IDXC[("dt", j, q)]] = j * 4128 + (m * 4 + q) * 128 + pcol
                idx[:, IDXC[("dx", j, q)]] = j * 4128 + E + (m * 4 + q) * 128 + pcol
            idx[0:32, IDXC[("bcs", j)]] = j * 4128 + 2 * E + np.arange(32)
        for ch in range(16):
            j, qd = ch // 4, ch % 4
            idx[:, IDXC[("y", ch)]] = j * 2048 + m * 512 + qd * 128 + pcol
        for ob in range(min(OB, 8)):
            g = 4 * ob + m
            lrow = 0 if g == 0 else ((g - 1) // 4) * 8 + ((g - 1) % 4) * 2 + 1
            rrow = 0 if g == NB - 1 else ((g + 1) // 4) * 8 + ((g + 1) % 4) * 2 + 0
            idx[0, IDXC[("xh", ob)]] = lrow
            idx[1, IDXC[("xh", ob)]] = rrow

        mcv_ = np.ones((128, 24), np.float32)
        mfn_ = np.ones((128, 16), np.float32)
        for ob in range(min(OB, 8)):
            g = 4 * ob + m
            if g == 0:
                mcv_[:, 3 * ob:3 * ob + 3] = 0.0
                mfn_[:, 2 * ob] = 0.0
            if g == NB - 1:
                mfn_[:, 2 * ob + 1] = 0.0

        im = dict(shared)
        im.update(x_own=x_own, A_log_my=A_my, idxs=idx, maskcv=mcv_, maskfn=mfn_)
        in_maps.append(im)
    return in_maps


def kernel(**inputs):
    NB = np.asarray(inputs["L0_attn_seq"]).shape[1] // LB
    in_maps = host_prep(inputs, NB)
    nc = build_nc(NB)
    res = run_bass_kernel_spmd(nc, in_maps, core_ids=list(range(8)))
    L = NB * LB
    OB = NB // 4
    out = np.empty((2, L, D), np.float32)
    for c in range(8):
        b, m = c // 4, c % 4
        yo = np.asarray(res.results[c]["y_out"])
        for ob in range(OB):
            g = 4 * ob + m
            out[b, g * LB:(g + 1) * LB] = yo[ob]
    return out


# version 07
# speedup vs baseline: 1.0729x; 1.0444x over previous
import numpy as np
from contextlib import ExitStack
import ml_dtypes

import concourse.bass as bass
import concourse.mybir as mybir
from concourse.bass import IndirectOffsetOnAxis
import bass_rust
ROP = bass_rust.ReduceOp
from concourse.bass_utils import run_bass_kernel_spmd

F32 = mybir.dt.float32
BF16 = mybir.dt.bfloat16
I32 = mybir.dt.int32
AF = mybir.ActivationFunctionType
OP = mybir.AluOpType
AX = mybir.AxisListType

D = 1024
E = 2048
S = 16
RK = 64
H = 4096
LB = 512
KD = D // 128
KE = E // 128
KH = H // 128
MAGIC = 12582912.0
EPS = 1e-5

ENGS = ("sync", "vector", "scalar", "tensor", "gpsimd")
SEMS = ("sq", "sp", "sg", "sv", "ss", "st", "sgp", "scc", "syq", "sxq", "sgy0", "sgy1", "sly0", "sly1", "shx")
_QSEM = {"sync": "sq", "scalar": "sp", "gpsimd": "sg"}
_ESEM = {"vector": "sv", "scalar": "ss", "tensor": "st", "gpsimd": "sgp"}


class Prog:
    def __init__(self, nc):
        self.nc = nc
        self.q = {e: [] for e in ENGS}
        self.cnt = {s: 0 for s in SEMS}
        self.sem = {}

    def _push(self, eng, fn, inc, waits):
        sems = self.sem

        def run(h):
            for sname, val in waits:
                if val > 0:
                    h.wait_ge(sems[sname], val)
            ins = fn(h)
            if ins is not None and inc is not None:
                ins.then_inc(sems[inc[0]], inc[1])
        self.q[eng].append(run)

    def op(self, eng, fn, waits=()):
        s = _ESEM[eng]
        selfw = ((s, self.cnt[s]),)
        self.cnt[s] += 1
        self._push(eng, fn, (s, 1), tuple(waits) + selfw)
        return self.cnt[s]

    def dma(self, eng, fn, waits=(), sem=None):
        s = sem or _QSEM[eng]
        self.cnt[s] += 16
        self._push(eng, fn, (s, 16), tuple(waits))
        return self.cnt[s]

    def cc(self, fn, waits=()):
        self.cnt["scc"] += 1
        self._push("gpsimd", fn, ("scc", 1), tuple(waits))
        return self.cnt["scc"]

    def wait(self, eng, waits):
        self._push(eng, lambda h: None, None, tuple(waits))

    def barrier(self):
        snap = [(s, self.cnt[s]) for s in SEMS if self.cnt[s] > 0]
        for eng in ENGS:
            self.wait(eng, snap)

    def finalize(self, block):
        secs = {"sync": block.sync, "vector": block.vector, "scalar": block.scalar,
                "tensor": block.tensor, "gpsimd": block.gpsimd}
        for eng in ENGS:
            fns = self.q[eng]
            if not fns:
                continue
            def body(h, fns=tuple(fns)):
                for f in fns:
                    f(h)
            secs[eng](body)


IDXC = {}
_col = 0
for _j in range(4):
    for _q in range(4):
        IDXC[("dt", _j, _q)] = _col; _col += 1
        IDXC[("dx", _j, _q)] = _col; _col += 1
for _j in range(4):
    IDXC[("bcs", _j)] = _col; _col += 1
for _ch in range(16):
    IDXC[("y", _ch)] = _col; _col += 1
for _ob in range(8):
    IDXC[("xh", _ob)] = _col; _col += 1
NIDX = 96


def build_nc(NB, debug=False):
    NR = NB // 4
    OB = NB // 4
    nc = bass.Bass("TRN2", target_bir_lowering=False, detect_race_conditions=False)

    dr = lambda n, shp, dt_: nc.dram_tensor(n, shp, dt_, kind="ExternalInput").ap()
    x_own = dr("x_own", [OB, LB + 3, D], F32)
    in_projT = dr("in_projT", [D, 2 * E], F32)
    out_projT = dr("out_projT", [E, D], F32)
    w1T = dr("w1T", [D, H], F32)
    w2T = dr("w2T", [H, D], F32)
    x_projTd = dr("x_projTd", [E, 96], F32)
    dt_projTd = dr("dt_projTd", [RK, E], F32)
    ln_g8 = dr("ln_g8", [128, KD], F32)
    ln_b8 = dr("ln_b8", [128, KD], F32)
    A_log_my = dr("A_log_my", [128, 4 * S], F32)
    conv_w64 = dr("conv_w64", [128, KE * 4], F32)
    conv_b16 = dr("conv_b16", [128, KE], F32)
    dtb16 = dr("dtb16", [128, KE], F32)
    D16 = dr("D16", [128, KE], F32)
    fcw96 = dr("fcw96", [128, KH * 3], F32)
    fcb32 = dr("fcb32", [128, KH], F32)
    rs_col = dr("rs_col", [128, 1], F32)
    idxs = dr("idxs", [128, NIDX], I32)
    maskcv = dr("maskcv", [128, 3 * 8], F32)
    maskfn = dr("maskfn", [128, 2 * 8], F32)
    identc = dr("identc", [128, 128], BF16)
    identf = dr("identf", [128, 128], F32)
    sel32 = dr("sel32", [32, 32 * 128], BF16)
    onesk2 = dr("onesk2", [2, 128], BF16)

    y_out = nc.dram_tensor("y_out", [OB, LB, D], F32, kind="ExternalOutput").ap()
    dbgt = nc.dram_tensor("dbgt", [8, 128, 4160], F32, kind="ExternalOutput").ap() if debug else None

    di = lambda n, shp, dt_: nc.dram_tensor(n, shp, dt_, kind="Internal").ap()
    wb_ip = di("wb_ip", [D, 2 * E], BF16)
    wb_op = di("wb_op", [E, D], BF16)
    wb_w1 = di("wb_w1", [D, H], BF16)
    wb_w2 = di("wb_w2", [H, D], BF16)
    ccF_in = di("ccF_in", [NR, 4128, LB], F32)
    ccF_out = [di(f"ccF_out{r}", [4 * 4128, LB], F32) for r in range(NR)]
    ccR_in = di("ccR_in", [NR, 4 * 512, LB], F32)
    ccR_out = [di(f"ccR_out{r}", [4 * 4 * 512, LB], F32) for r in range(NR)]
    ccX_in = di("ccX_in", [NR, 2, D], F32)
    ccX_out = di("ccX_out", [NR * 8, D], F32)
    spillX = di("spillX", [OB, 128, KE * LB], BF16)
    spillZ = di("spillZ", [OB, 128, KE * LB], BF16)
    spillH = di("spillH", [OB, 128, KH * LB], BF16)
    spillH2 = di("spillH2", [OB, 128, KH * LB], BF16)
    spillM = di("spillM", [OB, 128, 4 * D], F32)
    bias0d = di("bias0d", [2 * E], F32)

    es = ExitStack()
    sb = lambda n, shp, dt_: es.enter_context(nc.sbuf_tensor(n, shp, dt_))
    pt = lambda n, shp: es.enter_context(nc.psum_tensor(n, shp, F32))
    p = Prog(nc)
    for sname in SEMS:
        p.sem[sname] = es.enter_context(nc.semaphore(sname))

    ident = sb("ident", [128, 128], BF16)
    idf = sb("idf", [128, 128], F32)
    selt = sb("selt", [32, 32 * 128], BF16)
    ok2 = sb("ok2", [2, 128], BF16)
    k2st = sb("k2st", [2, 512], F32)
    k2hi = sb("k2hi", [2, 512], BF16)
    k2lo = sb("k2lo", [2, 512], BF16)
    lnG = sb("lnG", [128, KD], F32)
    lnB = sb("lnB", [128, KD], F32)
    lnBb = sb("lnBb", [128, KD], BF16)
    A64 = sb("A64", [128, 4 * S], F32)
    cw64 = sb("cw64", [128, KE * 4], F32)
    cb16 = sb("cb16", [128, KE], F32)
    dtb = sb("dtb", [128, KE], F32)
    Dc16 = sb("Dc16", [128, KE], F32)
    fcw = sb("fcw", [128, KH * 3], F32)
    fcb = sb("fcb", [128, KH], F32)
    rsc = sb("rsc", [128, 1], F32)
    magic = sb("magic", [128, 1], F32)
    idxt = sb("idxt", [128, NIDX], I32)
    mcv = sb("mcv", [128, 3 * 8], F32)
    mfn = sb("mfn", [128, 2 * 8], F32)
    bias0 = sb("bias0", [128, 2 * KE], F32)
    xpT = sb("xpT", [128, KE * 96], BF16)
    dtpT = sb("dtpT", [128, E], BF16)
    statc = sb("statc", [128, 8 * OB], F32)
    colw = sb("colw", [128, 16], F32)

    ar1 = sb("ar1", [128, KE * LB + 2 * D], F32)
    ar2 = sb("ar2", [128, KE * LB], F32)
    ar3 = sb("ar3", [128, 9520], F32)
    ar3b = ar3[:].bitcast(BF16)
    xslb = ar3b[:, 0:KE * LB]
    xnf = ar3b[:, KE * LB:KE * LB + KD * 520]
    xcvs = ar3[:, 6176:6176 + 2 * 520]
    tcv = ar3[:, 7216:7216 + 2 * LB]
    zbs = ar3b[:, 2 * 8240:2 * 8240 + 2 * LB]
    xdb = ar3[:, 8752:8752 + LB]
    xdbb = ar3b[:, 2 * 9264:2 * 9264 + LB]
    stat = sb("stat", [128, 40], F32)

    ppS1 = pt("ppS1", [128, 1024])
    ppS2 = pt("ppS2", [128, 1024])
    ppP1 = pt("ppP1", [128, 1024])
    ppP2 = pt("ppP2", [128, 1024])
    ppA = ppS1[:, 0:512]
    ppB = ppS1[:, 512:1024]
    ppF = ppS2[:, 0:512]
    ppG = ppS2[:, 512:1024]
    ppD = ppP1
    ppE = ppP2
    psms8 = [ppA, ppB, ppF, ppG, ppD[:, 0:512], ppD[:, 512:1024],
             ppE[:, 0:512], ppE[:, 512:1024]]
    pbank4 = [ppP1[:, 0:512], ppP1[:, 512:1024],
              ppP2[:, 0:512], ppP2[:, 512:1024]]

    xres = ar1[:, 0:4 * D]
    xhal = ar1[:, 4 * D:5 * D]
    ar1b = ar1[:].bitcast(BF16)
    xnbB = ar1[:, 5 * D:9 * D]
    xnhB = ar1[:, 9 * D:10 * D]
    xsl = ar1[:, 0:KE * LB]
    ar2b = ar2[:].bitcast(BF16)
    wstr = [ar2b[:, k * 2048:(k + 1) * 2048] for k in range(KD)]
    dtt = ar2[:, 0:KE * LB]
    wfull = ar1[:, 0:2 * E]
    wcastB = ar1b[:, 2 * (2 * E):2 * (2 * E) + 2 * E]
    stat0row = ar2[:, 0:2 * E]

    def bcast_row(psdst, w, src_row_off, waitv):
        c1 = p.op("vector", lambda h: h.tensor_copy(
            k2hi[0:1, src_row_off:src_row_off + w],
            k2st[0:1, src_row_off:src_row_off + w]), waits=[("sv", waitv)])
        c2 = p.op("vector", lambda h: h.tensor_tensor(
            k2st[1:2, src_row_off:src_row_off + w] if False else
            k2lo[0:1, src_row_off:src_row_off + w],
            k2st[0:1, src_row_off:src_row_off + w],
            k2hi[0:1, src_row_off:src_row_off + w], OP.subtract))
        t1 = p.op("tensor", lambda h: h.matmul(
            psdst, ok2[:, 0:128], k2hi[0:2, src_row_off:src_row_off + w],
            start=True, stop=False), waits=[("sv", c2)])
        t2 = p.op("tensor", lambda h: h.matmul(
            psdst, ok2[:, 0:128], k2lo[0:2, src_row_off:src_row_off + w],
            start=False, stop=True))
        return t2



    for dst, src in ((ident[:], identc), (idf[:], identf), (lnG[:], ln_g8),
                     (lnB[:], ln_b8), (A64[:], A_log_my), (cw64[:], conv_w64),
                     (cb16[:], conv_b16), (dtb[:], dtb16), (Dc16[:], D16),
                     (fcw[:], fcw96), (fcb[:], fcb32), (rsc[:], rs_col),
                     (idxt[:], idxs), (mcv[:], maskcv), (mfn[:], maskfn),
                     (selt[:], sel32), (ok2[:], onesk2)):
        p.dma("sync", lambda h, d=dst, s=src: h.dma_start(out=d, in_=s))
    ld0 = p.cnt["sq"]
    p.dma("gpsimd", lambda h: h.dma_start(
        out=xpT[:].rearrange("p (k n) -> p k n", n=96),
        in_=x_projTd.rearrange("(k p) n -> p k n", p=128)))
    p.dma("gpsimd", lambda h: h.dma_start(out=dtpT[0:RK, :], in_=dt_projTd))
    lg0 = p.cnt["sg"]
    sA = p.op("scalar", lambda h: h.activation(A64[:], A64[:], AF.Exp),
              waits=[("sq", ld0)])
    p.op("vector", lambda h: h.tensor_scalar_mul(A64[:], A64[:], -1.0),
         waits=[("ss", sA)])
    p.op("vector", lambda h: h.tensor_copy(lnBb[:], lnB[:]), waits=[("sq", ld0)])
    p.op("vector", lambda h: h.memset(magic[:], MAGIC))
    p.op("vector", lambda h: h.memset(k2st[:], 0.0))
    p.op("vector", lambda h: h.memset(k2hi[:], 0.0))
    p.op("vector", lambda h: h.memset(k2lo[:], 0.0))

    for k in range(KD):
        ld = p.dma("sync", lambda h, k=k: h.dma_start(
            out=wfull, in_=in_projT[k * 128:(k + 1) * 128, :]),
            waits=[("sv", p.cnt["sv"])])
        c1 = p.op("vector", lambda h: h.tensor_copy(wcastB, wfull),
                  waits=[("sq", p.cnt["sq"])])
        for nt in range(8):
            p.op("tensor", lambda h, k=k, nt=nt: h.matmul(
                psms8[nt][0:1, :], lnBb[:, k:k + 1],
                wcastB[:, nt * 512:(nt + 1) * 512],
                start=(k == 0), stop=(k == KD - 1)),
                waits=[("sv", c1)] if nt == 0 else ())
        c2 = p.op("vector", lambda h, k=k: h.tensor_scalar(
            wcastB, wfull, lnG[:, k:k + 1], None, OP.mult),
            waits=[("st", p.cnt["st"])])
        p.dma("sync", lambda h, k=k: h.dma_start(
            out=wb_ip[k * 128:(k + 1) * 128, :], in_=wcastB), waits=[("sv", c2)])
    cb0 = 0
    for nt in range(8):
        cb0 = p.op("vector", lambda h, nt=nt: h.tensor_copy(
            stat0row[0:1, nt * 512:(nt + 1) * 512], psms8[nt][0:1, :]),
            waits=[("st", p.cnt["st"])] if nt == 0 else ())
    db0 = p.dma("sync", lambda h: h.dma_start(out=bias0d, in_=stat0row[0:1, :]),
                waits=[("sv", cb0)])
    p.dma("sync", lambda h: h.dma_start(
        out=bias0[:], in_=bias0d.rearrange("(c p) -> p c", p=128),
        allow_slow_non_contiguous=True), waits=[("sq", db0)])

    for k in range(KE):
        ld = p.dma("sync", lambda h, k=k: h.dma_start(
            out=wfull[:, 0:D], in_=out_projT[k * 128:(k + 1) * 128, :]),
            waits=[("sv", p.cnt["sv"])])
        c = p.op("vector", lambda h: h.tensor_copy(wcastB[:, 0:D], wfull[:, 0:D]),
                 waits=[("sq", p.cnt["sq"])])
        p.dma("sync", lambda h, k=k: h.dma_start(
            out=wb_op[k * 128:(k + 1) * 128, :], in_=wcastB[:, 0:D]),
            waits=[("sv", c)])

    def quant_weight(wsrc, wdst, rows, cols, sdst):
        nch = rows // 128
        acc = colw[:, 0:1]
        last = p.op("vector", lambda h: h.memset(acc, 0.0))
        for k in range(nch):
            ld = p.dma("sync", lambda h, k=k: h.dma_start(
                out=wfull[:, 0:cols], in_=wsrc[k * 128:(k + 1) * 128, :]),
                waits=[("sv", p.cnt["sv"])])
            p.op("vector", lambda h: h.tensor_reduce(
                colw[:, 1:2], wfull[:, 0:cols], AX.X, OP.add,
                apply_absolute_value=True), waits=[("sq", ld)])
            last = p.op("vector", lambda h: h.tensor_add(acc, acc, colw[:, 1:2]))
        tt = p.op("tensor", lambda h: h.matmul(
            ppB[0:1, 0:128], acc, idf[:], is_transpose=True, start=True, stop=True),
            waits=[("sv", last)])
        rr = p.op("vector", lambda h: h.tensor_reduce(
            k2st[0:1, 0:1], ppB[0:1, 0:128], AX.X, OP.add), waits=[("st", tt)])
        p.op("vector", lambda h: h.tensor_scalar_mul(
            k2st[0:1, 0:1], k2st[0:1, 0:1], 1.0 / (rows * cols)))
        p.op("vector", lambda h: h.tensor_scalar_max(k2st[0:1, 0:1], k2st[0:1, 0:1], 1e-5))
        rr = p.op("vector", lambda h: h.reciprocal(k2st[0:1, 1:2], k2st[0:1, 0:1]))
        t2 = bcast_row(ppB[:, 0:1], 1, 1, rr)
        cs = p.op("vector", lambda h, sdst=sdst: h.tensor_copy(sdst, ppB[:, 0:1]),
                  waits=[("st", t2)])
        for k in range(nch):
            ld = p.dma("sync", lambda h, k=k: h.dma_start(
                out=wfull[:, 0:cols], in_=wsrc[k * 128:(k + 1) * 128, :]),
                waits=[("sv", p.cnt["sv"]), ("sq", p.cnt["sq"])])
            p.op("vector", lambda h, sdst=sdst: h.tensor_scalar(
                wfull[:, 0:cols], wfull[:, 0:cols], sdst, magic[:, 0:1],
                OP.mult, OP.add), waits=[("sq", ld)])
            p.op("vector", lambda h: h.tensor_scalar(
                wfull[:, 0:cols], wfull[:, 0:cols], magic[:, 0:1], 1.0,
                OP.subtract, OP.min))
            c1 = p.op("vector", lambda h: h.tensor_scalar(
                wcastB[:, 0:cols], wfull[:, 0:cols], -1.0, None, OP.max))
            p.dma("sync", lambda h, k=k: h.dma_start(
                out=wdst[k * 128:(k + 1) * 128, :], in_=wcastB[:, 0:cols]),
                waits=[("sv", c1)])

    sw1 = colw[:, 8:9]
    sw2 = colw[:, 9:10]
    quant_weight(w1T, wb_w1, D, H, sw1)
    quant_weight(w2T, wb_w2, H, D, sw2)
    rw1 = colw[:, 10:11]
    rw2 = colw[:, 11:12]
    p.op("vector", lambda h: h.reciprocal(rw1, sw1))
    p.op("vector", lambda h: h.reciprocal(rw2, sw2))
    p.op("vector", lambda h: h.tensor_scalar(fcw[:], fcw[:], rw1, None, OP.mult))

    zv = p.op("vector", lambda h: h.memset(wfull[:, 0:D], 0.0))
    nzr = (NR * 8 + 127) // 128
    for zi in range(nzr):
        r0, r1 = zi * 128, min((zi + 1) * 128, NR * 8)
        p.dma("sync", lambda h, r0=r0, r1=r1: h.dma_start(
            out=ccX_out[r0:r1, :], in_=wfull[0:r1 - r0, 0:D]), waits=[("sv", zv)])

    p.barrier()

    def rsqrt_newton(w, dst, var_col, s0, s1):
        cv = p.op("vector", lambda h: h.tensor_scalar_add(s0[0:w, :], var_col[0:w, :], EPS))
        c = p.op("scalar", lambda h: h.activation(s1[0:w, :], s0[0:w, :], AF.Sqrt),
                 waits=[("sv", cv)])
        p.op("vector", lambda h: h.reciprocal(dst[0:w, :], s1[0:w, :]), waits=[("ss", c)])
        p.op("vector", lambda h: h.tensor_mul(s1[0:w, :], dst[0:w, :], dst[0:w, :]))
        p.op("vector", lambda h: h.tensor_mul(s1[0:w, :], s1[0:w, :], s0[0:w, :]))
        p.op("vector", lambda h: h.tensor_scalar(s1[0:w, :], s1[0:w, :], -0.5, 1.5,
                                                 OP.mult, OP.add))
        return p.op("vector", lambda h: h.tensor_mul(dst[0:w, :], dst[0:w, :], s1[0:w, :]))

    def ln_stats(src, w, mu_dst, r1_dst, rtot_dst=None):
        bno = stat[:, 20:32]
        p.op("vector", lambda h: h.bn_stats(bno[0:w, 0:6], src[0:w, 0:D // 2]))
        p.op("vector", lambda h: h.bn_stats(bno[0:w, 6:12], src[0:w, D // 2:D]))
        p.op("vector", lambda h: h.bn_aggr(stat[0:w, 32:34], bno[0:w, 0:12]))
        p.op("vector", lambda h: h.tensor_copy(mu_dst[0:w, :], stat[0:w, 32:33]))
        rsqrt_newton(w, r1_dst, stat[:, 33:34], stat[:, 35:36], stat[:, 36:37])
        if rtot_dst is not None:
            p.op("vector", lambda h: h.tensor_mul(
                stat[0:w, 37:38], r1_dst[0:w, :], r1_dst[0:w, :]))
            p.op("vector", lambda h: h.tensor_mul(
                stat[0:w, 37:38], stat[0:w, 37:38], stat[0:w, 33:34]))
            rsqrt_newton(w, stat[:, 38:39], stat[:, 37:38], stat[:, 35:36],
                         stat[:, 36:37])
            return p.op("vector", lambda h: h.tensor_mul(
                rtot_dst[0:w, :], r1_dst[0:w, :], stat[0:w, 38:39]))
        return p.cnt["sv"]


    def bcast_row(psdst, w, src_row_off, waitv):
        c1 = p.op("vector", lambda h: h.tensor_copy(
            k2hi[0:1, src_row_off:src_row_off + w],
            k2st[0:1, src_row_off:src_row_off + w]), waits=[("sv", waitv)])
        c2 = p.op("vector", lambda h: h.tensor_tensor(
            k2st[1:2, src_row_off:src_row_off + w] if False else
            k2lo[0:1, src_row_off:src_row_off + w],
            k2st[0:1, src_row_off:src_row_off + w],
            k2hi[0:1, src_row_off:src_row_off + w], OP.subtract))
        t1 = p.op("tensor", lambda h: h.matmul(
            psdst, ok2[:, 0:128], k2hi[0:2, src_row_off:src_row_off + w],
            start=True, stop=False), waits=[("sv", c2)])
        t2 = p.op("tensor", lambda h: h.matmul(
            psdst, ok2[:, 0:128], k2lo[0:2, src_row_off:src_row_off + w],
            start=False, stop=True))
        return t2

    ccF_done = {}
    ccR_done = {}
    paE = {}
    ccX_done = {}
    sends_end = {}
    ysend_tok = {0: 0}
    last_scan_par = {0: 0, 1: 0, 2: 0, 3: 0}
    scan_last = {"ss": 0, "sv": 0}

    prev_sends = {"d1": 0, "d2": 0, "d3": 0, "spx": 0}
    for ob in range(OB):
        ldx = p.dma("sync", lambda h, ob=ob: h.dma_start(
            out=xres.rearrange("p (t n) -> p t n", n=D),
            in_=x_own[ob, 3:3 + LB, :].rearrange("(t p) n -> p t n", p=128)),
            waits=[("sxq", prev_sends["d3"]), ("sv", p.cnt["sv"]),
                   ("st", p.cnt["st"])], sem="sxq")
        ldh = p.dma("sync", lambda h, ob=ob: h.dma_start(
            out=xhal[0:3, :], in_=x_own[ob, 0:3, :]), sem="sxq")

        p.wait("vector", [("sxq", ldh), ("st", p.cnt["st"])])
        for lt in range(4):
            src = xres[:, lt * D:(lt + 1) * D]
            mu = stat[:, lt:lt + 1]
            rt = stat[:, 4 + lt:5 + lt]
            r1 = stat[:, 34:35]
            ln_stats(src, 128, mu, r1, rt)
            p.op("vector", lambda h, ob=ob, lt=lt, r1=r1: h.tensor_scalar_add(
                statc[:, ob * 8 + lt:ob * 8 + lt + 1], r1, 1.0))
            p.op("vector", lambda h, mu=mu, r1=r1: h.tensor_mul(
                stat[:, 12:13], mu, r1))
            p.op("vector", lambda h, ob=ob, lt=lt: h.tensor_scalar_mul(
                statc[:, ob * 8 + 4 + lt:ob * 8 + 5 + lt], stat[:, 12:13], -1.0))
            p.op("vector", lambda h, src=src, lt=lt, mu=mu, rt=rt: h.tensor_scalar(
                xnbB[:, lt * D:(lt + 1) * D], src, mu, rt, OP.subtract, OP.mult))
        p.wait("vector", [("sxq", ldh)])
        muh = stat[:, 8:9]
        rth = stat[:, 9:10]
        ln_stats(xhal, 3, muh, stat[:, 34:35], rth)
        svh = p.op("vector", lambda h: h.tensor_scalar(
            xnhB[0:3, :], xhal[0:3, :], muh[0:3, :], rth[0:3, :],
            OP.subtract, OP.mult))

        sc = p.cnt["ss"]
        for dch in range(KD):
            t0 = p.op("tensor", lambda h, dch=dch: h.matmul(
                ppD[:, 0:3], xnhB[0:3, dch * 128:(dch + 1) * 128],
                idf[0:3, 0:3], is_transpose=True, start=True, stop=True),
                waits=[("sv", svh), ("ss", sc)])
            sc = p.op("scalar", lambda h, dch=dch: h.copy(
                xnf[:, dch * 520:dch * 520 + 3], ppD[:, 0:3]), waits=[("st", t0)])
            for lt in range(4):
                t1 = p.op("tensor", lambda h, dch=dch, lt=lt: h.matmul(
                    ppD[:, 512:640], xnbB[:, lt * D + dch * 128:lt * D + (dch + 1) * 128],
                    idf[:], is_transpose=True, start=True, stop=True),
                    waits=[("ss", sc)])
                sc = p.op("scalar", lambda h, dch=dch, lt=lt: h.copy(
                    xnf[:, dch * 520 + 3 + lt * 128:dch * 520 + 3 + (lt + 1) * 128],
                    ppD[:, 512:640]), waits=[("st", t1)])
        ss_xnf = p.cnt["ss"]
        xnf3 = xnf[:].rearrange("p (c n) -> p c n", n=520)

        zlds = []
        for k in range(KD):
            if k == 0:
                zw = [("st", p.cnt["st"]), ("sxq", prev_sends["d3"]),
                      ("sv", p.cnt["sv"])]
            elif k % 2 == 0:
                zw = [("sq", zlds[k - 1])]
            else:
                zw = []
            zlds.append(p.dma("sync", lambda h, k=k: h.dma_start(
                out=wstr[k], in_=wb_ip[k * 128:(k + 1) * 128, E:2 * E]),
                waits=zw))
        p.wait("tensor", [("ss", ss_xnf)])
        z_evac = {0: 0, 1: 0}
        for mz in range(KE):
            psm = ppA if mz % 2 == 0 else ppB
            for k in range(KD):
                w_ = []
                if mz == 0:
                    w_.append(("sq", zlds[k | 1]))
                if k == 0:
                    w_.append(("ss", z_evac[mz % 2]))
                p.op("tensor", lambda h, mz=mz, k=k, psm=psm: h.matmul(
                    psm[:], wstr[k][:, mz * 128:(mz + 1) * 128],
                    xnf3[:, k, 3:515], start=(k == 0), stop=(k == KD - 1)),
                    waits=w_)
            stz = p.cnt["st"]
            zslot = zbs[:, (mz % 2) * LB:(mz % 2 + 1) * LB]
            scz = p.op("scalar", lambda h, mz=mz, psm=psm, zslot=zslot: h.activation(
                zslot, psm[:], AF.Silu, bias=bias0[:, KE + mz:KE + mz + 1]),
                waits=[("st", stz), ("sp", p.cnt["sp"])])
            z_evac[mz % 2] = scz
            prev_sends[("z", mz % 2)] = p.dma(
                "scalar", lambda h, ob=ob, mz=mz, zslot=zslot: h.dma_start(
                    out=spillZ[ob, :, mz * LB:(mz + 1) * LB], in_=zslot),
                waits=[("ss", scz)])

        xlds = []
        for k in range(KD):
            if k == 0:
                xw = [("st", p.cnt["st"])]
            elif k % 2 == 0:
                xw = [("sq", xlds[k - 1])]
            else:
                xw = []
            xlds.append(p.dma("sync", lambda h, k=k: h.dma_start(
                out=wstr[k], in_=wb_ip[k * 128:(k + 1) * 128, 0:E]),
                waits=xw))
        x_evac = 0
        for mx in range(KE):
            for k in range(KD):
                w_ = []
                if mx == 0:
                    w_.append(("sq", xlds[k | 1]))
                if k == 0:
                    w_.append(("sv", p.cnt["sv"]))
                p.op("tensor", lambda h, mx=mx, k=k: h.matmul(
                    ppD[:, 0:257], wstr[k][:, mx * 128:(mx + 1) * 128],
                    xnf3[:, k, 0:257], start=(k == 0), stop=(k == KD - 1)),
                    waits=w_)
            for k in range(KD):
                p.op("tensor", lambda h, mx=mx, k=k: h.matmul(
                    ppD[:, 512:770], wstr[k][:, mx * 128:(mx + 1) * 128],
                    xnf3[:, k, 257:515], start=(k == 0), stop=(k == KD - 1)))
            stx = p.cnt["st"]
            xc = xcvs[:, (mx % 2) * 520:(mx % 2) * 520 + 515]
            p.op("vector", lambda h, mx=mx, xc=xc: h.tensor_scalar(
                xc[:, 0:257], ppD[:, 0:257], bias0[:, mx:mx + 1], None, OP.add),
                waits=[("st", stx)])
            sc2 = p.op("vector", lambda h, mx=mx, xc=xc: h.tensor_scalar(
                xc[:, 257:515], ppD[:, 512:770], bias0[:, mx:mx + 1], None, OP.add))
            x_evac = sc2
            p.op("vector", lambda h, ob=ob, xc=xc: h.tensor_scalar(
                xc[:, 0:3], xc[:, 0:3], mcv[:, 3 * ob:3 * ob + 1], None, OP.mult))
            t = tcv[:, (mx % 2) * LB:(mx % 2 + 1) * LB]
            p.op("vector", lambda h, mx=mx, xc=xc, t=t: h.tensor_scalar(
                t, xc[:, 0:LB], cw64[:, mx * 4:mx * 4 + 1],
                cb16[:, mx:mx + 1], OP.mult, OP.add))
            for kk in range(1, 4):
                p.op("vector", lambda h, mx=mx, xc=xc, t=t, kk=kk: h.scalar_tensor_tensor(
                    t, xc[:, kk:kk + LB], cw64[:, mx * 4 + kk:mx * 4 + kk + 1], t,
                    OP.mult, OP.add))
            svt = p.cnt["sv"]
            scs = p.op("scalar", lambda h, mx=mx, t=t: h.activation(
                xsl[:, mx * LB:(mx + 1) * LB], t, AF.Silu), waits=[("sv", svt)])
            p.op("vector", lambda h, mx=mx: h.tensor_copy(
                xslb[:, mx * LB:(mx + 1) * LB], xsl[:, mx * LB:(mx + 1) * LB]),
                waits=[("ss", scs)])
        sv_x = p.cnt["sv"]
        prev_sends["spx"] = p.dma("scalar", lambda h, ob=ob: h.dma_start(
            out=spillX[ob], in_=xslb[:]), waits=[("sv", sv_x)])

        for k in range(KE):
            p.op("tensor", lambda h, k=k: h.matmul(
                ppF[0:96, :], xpT[:, k * 96:(k + 1) * 96],
                xslb[:, k * LB:(k + 1) * LB], start=(k == 0), stop=(k == KE - 1)),
                waits=[("sv", sv_x)] if k == 0 else ())
        st_xd = p.cnt["st"]
        sv_xd = p.op("vector", lambda h: h.tensor_copy(xdb[0:96, :], ppF[0:96, :]),
                     waits=[("st", st_xd), ("sxq", prev_sends["d3"])])
        sv_xdb = p.op("vector", lambda h: h.tensor_copy(xdbb[0:RK, :], xdb[0:RK, :]))

        dt_evac = {0: 0, 1: 0}
        for md in range(KE):
            psm = ppF if md % 2 == 0 else ppG
            w_ = [("ss", dt_evac[md % 2])]
            if md == 0:
                w_.append(("sv", sv_xdb))
            tm = p.op("tensor", lambda h, md=md, psm=psm: h.matmul(
                psm[:], dtpT[0:RK, md * 128:(md + 1) * 128], xdbb[0:RK, :],
                start=True, stop=True), waits=w_)
            p.op("scalar", lambda h, md=md, psm=psm: h.activation(
                dtt[:, md * LB:(md + 1) * LB], psm[:], AF.Exp,
                bias=dtb[:, md:md + 1]), waits=[("st", tm)])
            dt_evac[md % 2] = p.op("scalar", lambda h, md=md: h.activation(
                dtt[:, md * LB:(md + 1) * LB], dtt[:, md * LB:(md + 1) * LB],
                AF.Ln, bias=1.0))
        ss_dt = p.cnt["ss"]

        p.wait("vector", [("ss", ss_dt), ("sp", prev_sends["spx"])])
        for mx in range(KE):
            p.op("vector", lambda h, mx=mx: h.tensor_mul(
                xsl[:, mx * LB:(mx + 1) * LB], xsl[:, mx * LB:(mx + 1) * LB],
                dtt[:, mx * LB:(mx + 1) * LB]))
        sv_dtx = p.cnt["sv"]

        prev_sends["d1"] = p.dma("sync", lambda h, ob=ob: h.dma_start(
            out=ccF_in[ob, 0:E, :].rearrange("(q pp) n -> pp q n", pp=128),
            in_=dtt[:].rearrange("p (q n) -> p q n", n=LB)),
            waits=[("ss", ss_dt)], sem="sxq")
        prev_sends["d2"] = p.dma("sync", lambda h, ob=ob: h.dma_start(
            out=ccF_in[ob, E:2 * E, :].rearrange("(q pp) n -> pp q n", pp=128),
            in_=xsl[:].rearrange("p (q n) -> p q n", n=LB)),
            waits=[("sv", sv_dtx)], sem="sxq")
        prev_sends["d3"] = p.dma("sync", lambda h, ob=ob: h.dma_start(
            out=ccF_in[ob, 2 * E:2 * E + 32, :], in_=xdb[RK:96, :]),
            waits=[("sv", sv_xdb)], sem="sxq")
        ccF_done[ob] = p.cc(lambda h, ob=ob: h.collective_compute(
            "AllGather", OP.bypass,
            replica_groups=[[0, 1, 2, 3], [4, 5, 6, 7]],
            ins=[ccF_in[ob]], outs=[ccF_out[ob]]),
            waits=[("sxq", prev_sends["d3"])])

        if debug and ob == 0:
            w0 = p.cnt["sv"]
            s0 = p.cnt["ss"]
            p.dma("sync", lambda h: h.dma_start(out=dbgt[0, :, 0:32], in_=bias0[:]),
                  waits=[("sv", w0), ("ss", s0)])
            p.dma("sync", lambda h: h.dma_start(out=dbgt[0, :, 32:96], in_=A64[:]))
            p.dma("sync", lambda h: h.dma_start(out=dbgt[0, :, 96:112], in_=colw[:]))
            p.dma("sync", lambda h: h.dma_start(out=dbgt[0, :, 112:120],
                                                in_=statc[:, 0:8]))
            p.dma("gpsimd", lambda h: h.dma_start(out=dbgt[1], in_=xnf[:]),
                  waits=[("sv", w0), ("ss", s0)])
            p.dma("sync", lambda h: h.dma_start(out=dbgt[2, :, 0:4096],
                                                in_=xsl[:, 0:4096]))
            p.dma("sync", lambda h: h.dma_start(out=dbgt[3, :, 0:4096],
                                                in_=dtt[:, 0:4096]))
            p.dma("sync", lambda h: h.dma_start(out=dbgt[4, :, 0:512], in_=xdb[:]))
            p.dma("gpsimd", lambda h: h.dma_start(out=dbgt[5, :, 0:4096],
                                                  in_=xslb[:, 0:4096]))
            p.dma("gpsimd", lambda h: h.dma_start(out=dbgt[6, :, 0:KE * 96],
                                                  in_=xpT[:]),
                  waits=[("sv", w0)])
            p.dma("gpsimd", lambda h: h.dma_start(out=dbgt[7, :, 0:2048],
                                                  in_=dtpT[:]))

    pbE = {"pre": {"sv": 0, "ss": 0}}
    p.barrier()

    bcf = ar1[:, 0:LB]
    bchi = ar1b[:, 2 * LB:3 * LB]
    bclo = ar1b[:, 3 * LB:4 * LB]
    rdt = ar1[:, 2 * LB:6 * LB]
    rdx = ar1[:, 6 * LB:10 * LB]
    hbuf = ar1[:, 10 * LB:14 * LB]
    yq = ar1[:, 14 * LB:18 * LB]
    carry = ar1[:, 18 * LB:18 * LB + 64]
    dAs = sb("dAs", [128, 4 * LB], F32)
    usl = sb("usl", [128, 2 * LB], F32)
    yin = sb("yin", [128, 4 * LB], BF16)
    ysl2 = sb("ysl2", [128, 2 * LB], F32)
    xsz = sb("xsz", [128, 4 * LB], BF16)
    xmidt = ar3[:, 0:4 * D]
    ohi = ar3[:, 4 * D:4 * D + KD * LB]
    xoslot = ar3[:, 4 * D + KD * LB:4 * D + KD * LB + D]
    xqf = sb("xqf", [128, KD * 516], BF16)

    hq = sb("hq", [128, 4 * LB], BF16)
    mqt = sb("mqt", [128, LB], F32)
    shb = sb("shb", [128, LB], F32)
    hmini = sb("hmini", [128, D], F32)
    xqtk = hmini
    hslot = sb("hslot", [128, 2 * LB], F32)
    xqf3 = xqf[:].rearrange("p (c n) -> p c n", n=516)

    p.op("vector", lambda h: h.memset(carry[:], 0.0))

    def scan_gen(r):
        for blkslot in range(4):
            gw = [("scc", ccF_done[r]), ("ss", scan_last["ss"]),
                  ("sv", scan_last["sv"])]
            gd = []
            for qq in range(4):
                gd.append(p.dma("gpsimd", lambda h, r=r, blkslot=blkslot, qq=qq:
                    h.indirect_dma_start(
                        out=rdt[:, qq * LB:(qq + 1) * LB], out_offset=None,
                        in_=ccF_out[r],
                        in_offset=IndirectOffsetOnAxis(
                            ap=idxt[:, IDXC[("dt", blkslot, qq)]:IDXC[("dt", blkslot, qq)] + 1],
                            axis=0)), waits=gw if qq == 0 else ()))
                gd.append(p.dma("gpsimd", lambda h, r=r, blkslot=blkslot, qq=qq:
                    h.indirect_dma_start(
                        out=rdx[:, qq * LB:(qq + 1) * LB], out_offset=None,
                        in_=ccF_out[r],
                        in_offset=IndirectOffsetOnAxis(
                            ap=idxt[:, IDXC[("dx", blkslot, qq)]:IDXC[("dx", blkslot, qq)] + 1],
                            axis=0))))
            p.dma("gpsimd", lambda h, r=r, blkslot=blkslot: h.indirect_dma_start(
                out=bcf[0:32, :], out_offset=None, in_=ccF_out[r],
                in_offset=IndirectOffsetOnAxis(
                    ap=idxt[0:32, IDXC[("bcs", blkslot)]:IDXC[("bcs", blkslot)] + 1],
                    axis=0)))
            gall = p.cnt["sg"]
            ch = p.op("vector", lambda h: h.tensor_copy(bchi[0:32, :], bcf[0:32, :]),
                      waits=[("sg", gall), ("st", p.cnt["st"])])
            cl = p.op("vector", lambda h: h.tensor_tensor(
                bclo[0:32, :], bcf[0:32, :], bchi[0:32, :], OP.subtract))

            for qt in range(4):
                for jp in range(2):
                    selw = [("sv", p.cnt["sv"])]
                    for j2 in range(2):
                        s = 4 * qt + 2 * jp + j2
                        dB = ppS1[:, j2 * 512:(j2 + 1) * 512]
                        dC = ppS2[:, j2 * 512:(j2 + 1) * 512]
                        p.op("tensor", lambda h, s=s, dB=dB: h.matmul(
                            dB, selt[0:32, s * 128:(s + 1) * 128],
                            bchi[0:32, :], start=True, stop=False),
                            waits=selw if j2 == 0 else ())
                        p.op("tensor", lambda h, s=s, dB=dB: h.matmul(
                            dB, selt[0:32, s * 128:(s + 1) * 128],
                            bclo[0:32, :], start=False, stop=True))
                        p.op("tensor", lambda h, s=s, dC=dC: h.matmul(
                            dC, selt[0:32, (16 + s) * 128:(17 + s) * 128],
                            bchi[0:32, :], start=True, stop=False))
                        p.op("tensor", lambda h, s=s, dC=dC: h.matmul(
                            dC, selt[0:32, (16 + s) * 128:(17 + s) * 128],
                            bclo[0:32, :], start=False, stop=True))
                    stsel = p.cnt["st"]
                    for qq in range(4):
                        for j2 in range(2):
                            s = 4 * qt + 2 * jp + j2
                            par = (qq * 2 + j2) % 4
                            da = dAs[:, par * LB:(par + 1) * LB]
                            uu = usl[:, j2 * LB:(j2 + 1) * LB]
                            sw_ = ([("sg", gall)]
                                   if (qt == 0 and jp == 0 and qq == 0 and j2 == 0)
                                   else [])
                            sw_.append(("sv", last_scan_par[par]))
                            se = p.op("scalar", lambda h, qq=qq, s=s, da=da: h.activation(
                                da, rdt[:, qq * LB:(qq + 1) * LB], AF.Exp,
                                scale=A64[:, qq * S + s:qq * S + s + 1]), waits=sw_)
                            uw = [("ss", se), ("st", stsel)]
                            if qt == 0 and jp == 0 and qq == 0 and j2 == 0:
                                uw.append(("syq", p.cnt["syq"]))
                            p.op("vector", lambda h, qq=qq, j2=j2, uu=uu: h.tensor_mul(
                                uu, rdx[:, qq * LB:(qq + 1) * LB],
                                ppS1[:, j2 * 512:(j2 + 1) * 512]), waits=uw)
                            last_scan_par[par] = p.op("vector",
                                lambda h, qq=qq, s=s, j2=j2, da=da, uu=uu: h.tensor_tensor_scan(
                                    hbuf[:, j2 * LB:(j2 + 1) * LB], da, uu,
                                    carry[:, qq * S + s:qq * S + s + 1],
                                    OP.mult, OP.add))
                        p.op("vector", lambda h, qq=qq, qt=qt, jp=jp: h.tensor_copy(
                            carry[:, qq * S + qt * 4 + 2 * jp:qq * S + qt * 4 + 2 * jp + 2],
                            hbuf[:, 0:2 * LB].rearrange("p (s n) -> p s n", n=LB)[:, :, LB - 1:LB]))
                        p.op("vector", lambda h: h.tensor_mul(
                            hbuf[:, 0:2 * LB], hbuf[:, 0:2 * LB], ppS2[:, 0:1024]))
                        h3 = hbuf[:, 0:2 * LB].rearrange("p (s n) -> p n s", n=LB)
                        if qt == 0 and jp == 0:
                            p.op("vector", lambda h, qq=qq, h3=h3: h.tensor_reduce(
                                yq[:, qq * LB:(qq + 1) * LB], h3, AX.X, OP.add))
                        else:
                            p.op("vector", lambda h, h3=h3: h.tensor_reduce(
                                hbuf[:, 2 * LB:3 * LB], h3, AX.X, OP.add))
                            p.op("vector", lambda h, qq=qq: h.tensor_add(
                                yq[:, qq * LB:(qq + 1) * LB],
                                yq[:, qq * LB:(qq + 1) * LB], hbuf[:, 2 * LB:3 * LB]))
                    yield
            svy = p.cnt["sv"]
            scan_last["ss"] = p.cnt["ss"]
            scan_last["sv"] = svy
            ysend_tok[0] = p.dma("sync", lambda h, r=r, blkslot=blkslot: h.dma_start(
                out=ccR_in[r, blkslot * 512:(blkslot + 1) * 512, :]
                    .rearrange("(q pp) n -> pp q n", pp=128),
                in_=yq[:].rearrange("p (q n) -> p q n", n=LB)),
                waits=[("sv", svy)], sem="syq")
        sy = p.cnt["syq"]
        sends_end[r] = sy
        ccR_done[r] = p.cc(lambda h, r=r: h.collective_compute(
            "AllGather", OP.bypass,
            replica_groups=[[0, 1, 2, 3], [4, 5, 6, 7]],
            ins=[ccR_in[r]], outs=[ccR_out[r]]), waits=[("syq", sy)])

    def post_a_gen(r):
        ob = r
        oplds = []
        for k in range(KE):
            if k == 0:
                ow = [("st", p.cnt["st"])]
            elif k % 4 == 0:
                ow = [("sq", oplds[k - 1])]
            else:
                ow = []
            oplds.append(p.dma("sync", lambda h, k=k: h.dma_start(
                out=ar2b[:, k * D:(k + 1) * D], in_=wb_op[k * 128:(k + 1) * 128, :]),
                waits=ow))
        pbe = pbE.get(r - 2, pbE.get("pre", {"sv": 0, "ss": 0}))
        p.wait("tensor", [("sv", pbe["sv"]), ("ss", pbe["ss"])])
        mul_tok = {-2: paE.get(r - 1, 0), -1: paE.get(r - 1, 0)}

        def build_yin(k, slot_war, pno):
            par = k % 2
            spar = k % 4
            gsem = "sgy%d" % par
            ys = ysl2[:, par * LB:(par + 1) * LB]
            gy = p.dma("gpsimd", lambda h, r=r, k=k, ys=ys: h.indirect_dma_start(
                out=ys, out_offset=None, in_=ccR_out[r],
                in_offset=IndirectOffsetOnAxis(
                    ap=idxt[:, IDXC[("y", k)]:IDXC[("y", k)] + 1], axis=0)),
                waits=[("scc", ccR_done[r]), ("sv", mul_tok[pno * KE + k - 2])],
                sem=gsem)
            xs = xsz[:, par * LB:(par + 1) * LB]
            zs = xsz[:, (2 + par) * LB:(3 + par) * LB]
            lsem = "sly%d" % par
            lx = p.dma("scalar", lambda h, ob=ob, k=k, xs=xs: h.dma_start(
                out=xs, in_=spillX[ob, :, k * LB:(k + 1) * LB]),
                waits=[("sv", mul_tok[pno * KE + k - 2])], sem=lsem)
            lz = p.dma("scalar", lambda h, ob=ob, k=k, zs=zs: h.dma_start(
                out=zs, in_=spillZ[ob, :, k * LB:(k + 1) * LB]), sem=lsem)
            p.op("vector", lambda h, k=k, ys=ys, xs=xs: h.scalar_tensor_tensor(
                ys, xs, Dc16[:, k:k + 1], ys, OP.mult, OP.add),
                waits=[(gsem, gy), (lsem, lz)])
            w_ = [(lsem, lz)]
            if slot_war:
                w_.append(("st", slot_war))
            yslot = yin[:, spar * LB:(spar + 1) * LB]
            sv_y = p.op("vector", lambda h, ys=ys, zs=zs, yslot=yslot: h.tensor_mul(
                yslot, ys, zs), waits=w_)
            mul_tok[pno * KE + k] = sv_y
            return yslot, sv_y

        mmk = {}
        for k in range(KE):
            yslot, sv_y = build_yin(k, mmk.get(k - 4), 0)
            for m in range(4):
                p.op("tensor", lambda h, k=k, m=m, yslot=yslot: h.matmul(
                    pbank4[m], ar2b[:, k * D + m * 128:k * D + (m + 1) * 128],
                    yslot, start=(k == 0), stop=(k == KE - 1)),
                    waits=([("sq", oplds[(k // 4) * 4 + 3]), ("sv", sv_y)]
                           if m == 0 else ()))
            mmk[k] = p.cnt["st"]
            yield
        stmm = p.cnt["st"]
        ev1 = 0
        for m in range(4):
            ev1 = p.op("vector", lambda h, m=m: h.tensor_copy(
                ohi[:, m * LB:(m + 1) * LB], pbank4[m]),
                waits=[("st", stmm)] if m == 0 else ())
        yield
        mmk2nd = {}
        for k in range(KE):
            yslot, sv_y = build_yin(k, mmk2nd.get(k - 4, mmk.get(12 + k)), 1)
            for m in range(4):
                w_ = []
                if m == 0:
                    w_ = [("sv", sv_y)] + ([("sv", ev1)] if k == 0 else [])
                p.op("tensor", lambda h, k=k, m=m, yslot=yslot: h.matmul(
                    pbank4[m], ar2b[:, k * D + (4 + m) * 128:k * D + (5 + m) * 128],
                    yslot, start=(k == 0), stop=(k == KE - 1)), waits=w_)
            mmk2nd[k] = p.cnt["st"]
            yield
        stmm = p.cnt["st"]
        ev = 0
        for m in range(4):
            ev = p.op("vector", lambda h, m=m: h.tensor_copy(
                ohi[:, (4 + m) * LB:(5 + m) * LB], pbank4[m]),
                waits=[("st", stmm)] if m == 0 else ())
        svev = ev

        ldxr = p.dma("sync", lambda h, ob=ob: h.dma_start(
            out=xmidt.rearrange("p (t n) -> p t n", n=D),
            in_=x_own[ob, 3:3 + LB, :].rearrange("(t p) n -> p t n", p=128)),
            waits=[("sv", p.cnt["sv"]), ("sp", p.cnt["sp"]),
                   ("sxq", p.cnt["sxq"])], sem="sxq")
        xm_done = 0
        for lt in range(4):
            pdst = ppD if lt % 2 == 0 else ppE
            for m in range(KD):
                w_ = []
                if m == 0:
                    w_.append(("sv", svev if lt < 2 else xm_done))
                p.op("tensor", lambda h, lt=lt, m=m, pdst=pdst: h.matmul(
                    pdst[:, m * 128:(m + 1) * 128],
                    ohi[:, m * LB + lt * 128:m * LB + (lt + 1) * 128],
                    idf[:], is_transpose=True, start=True, stop=True), waits=w_)
            sttr = p.cnt["st"]
            p.op("vector", lambda h, ob=ob, lt=lt: h.tensor_scalar(
                xoslot[:, 0:D], xmidt[:, lt * D:(lt + 1) * D],
                statc[:, ob * 8 + lt:ob * 8 + lt + 1],
                statc[:, ob * 8 + 4 + lt:ob * 8 + 5 + lt], OP.mult, OP.add),
                waits=[("sxq", ldxr)] if lt == 0 else ())
            xm_done = p.op("vector", lambda h, lt=lt, pdst=pdst: h.scalar_tensor_tensor(
                xmidt[:, lt * D:(lt + 1) * D], pdst[:, 0:D], rsc[:, 0:1],
                xoslot[:, 0:D], OP.mult, OP.add), waits=[("st", sttr)])
            yield
        svxm = xm_done
        spm = p.dma("scalar", lambda h, ob=ob: h.dma_start(
            out=spillM[ob], in_=xmidt[:]), waits=[("sv", svxm)])
        d1 = p.dma("sync", lambda h, r=r: h.dma_start(
            out=ccX_in[r, 0:1, :], in_=xmidt[0:1, 0:D]), waits=[("sv", svxm)],
            sem="sxq")
        d2 = p.dma("sync", lambda h, r=r: h.dma_start(
            out=ccX_in[r, 1:2, :], in_=xmidt[127:128, 3 * D:4 * D]), sem="sxq")
        ccX_done[r] = p.cc(lambda h, r=r: h.collective_compute(
            "AllGather", OP.bypass,
            replica_groups=[[0, 1, 2, 3], [4, 5, 6, 7]],
            ins=[ccX_in[r]], outs=[ccX_out[r * 8:(r + 1) * 8, :]]),
            waits=[("sxq", d2)])
        paE[r] = mul_tok[2 * KE - 1]

    def post_b_gen(r):
        ob = r
        need_right = ccX_done[min(r + 1, NR - 1)] if (4 * r + 3) + 1 <= NB - 1 else ccX_done[r]
        gh = p.dma("gpsimd", lambda h, ob=ob: h.indirect_dma_start(
            out=hmini[0:2, :], out_offset=None, in_=ccX_out,
            in_offset=IndirectOffsetOnAxis(
                ap=idxt[0:2, IDXC[("xh", ob)]:IDXC[("xh", ob)] + 1], axis=0)),
            waits=[("scc", need_right), ("sv", p.cnt["sv"])], sem="shx")
        p.wait("vector", [("shx", gh)])
        muh = stat[:, 8:9]
        r3h = stat[:, 9:10]
        ln_stats(hmini, 2, muh, r3h)
        p.op("vector", lambda h: h.tensor_scalar(
            hmini[0:2, :], hmini[0:2, :], muh[0:2, :], None, OP.subtract))
        p.op("vector", lambda h: h.tensor_reduce(
            stat[0:2, 10:11], hmini[0:2, :], AX.X, OP.max, apply_absolute_value=True))
        p.op("vector", lambda h: h.tensor_mul(stat[0:2, 10:11], stat[0:2, 10:11], r3h[0:2, :]))
        p.op("vector", lambda h: h.tensor_scalar_max(stat[0:2, 10:11], stat[0:2, 10:11], 1e-5))
        p.op("vector", lambda h: h.reciprocal(stat[0:2, 11:12], stat[0:2, 10:11]))
        p.op("vector", lambda h: h.tensor_scalar_mul(stat[0:2, 11:12], stat[0:2, 11:12], 127.0))
        p.op("vector", lambda h: h.reciprocal(stat[0:2, 12:13], stat[0:2, 11:12]))
        p.op("vector", lambda h: h.tensor_mul(stat[0:2, 14:15], r3h[0:2, :], stat[0:2, 11:12]))
        p.op("vector", lambda h: h.tensor_scalar(
            hmini[0:2, :], hmini[0:2, :], stat[0:2, 14:15], magic[0:2, 0:1],
            OP.mult, OP.add))
        cq = p.op("vector", lambda h: h.tensor_scalar(
            hmini[0:2, :], hmini[0:2, :], magic[0:2, 0:1], stat[0:2, 12:13],
            OP.subtract, OP.mult))
        hvm = cq
        for dch in range(KD):
            t0 = p.op("tensor", lambda h, dch=dch: h.matmul(
                ppD[:, 0:2], hmini[0:2, dch * 128:(dch + 1) * 128],
                idf[0:2, 0:2], is_transpose=True, start=True, stop=True),
                waits=[("sv", hvm)])
            p.op("vector", lambda h, dch=dch, ob=ob: h.tensor_scalar(
                xqf[:, dch * 516:dch * 516 + 1], ppD[:, 0:1],
                mfn[:, 2 * ob:2 * ob + 1], None, OP.mult), waits=[("st", t0)])
            hvm = p.op("vector", lambda h, dch=dch, ob=ob: h.tensor_scalar(
                xqf[:, dch * 516 + 513:dch * 516 + 514], ppD[:, 1:2],
                mfn[:, 2 * ob + 1:2 * ob + 2], None, OP.mult))

        ldm = p.dma("scalar", lambda h, ob=ob: h.dma_start(
            out=xmidt[:], in_=spillM[ob]),
            waits=[("sv", p.cnt["sv"]), ("sp", p.cnt["sp"])])
        p.wait("vector", [("sp", ldm)])
        for lt in range(4):
            src = xmidt[:, lt * D:(lt + 1) * D]
            mu3 = stat[:, 0:1]
            r3 = stat[:, 1:2]
            ln_stats(src, 128, mu3, r3)
            p.op("vector", lambda h, src=src, mu3=mu3: h.tensor_scalar(
                xoslot[:, 0:D], src, mu3, None, OP.subtract))
            p.op("vector", lambda h: h.tensor_reduce(
                stat[:, 10:11], xoslot[:, 0:D], AX.X, OP.max,
                apply_absolute_value=True))
            p.op("vector", lambda h, r3=r3: h.tensor_mul(stat[:, 10:11], stat[:, 10:11], r3))
            p.op("vector", lambda h: h.tensor_scalar_max(stat[:, 10:11], stat[:, 10:11], 1e-5))
            p.op("vector", lambda h: h.reciprocal(stat[:, 11:12], stat[:, 10:11]))
            p.op("vector", lambda h: h.tensor_scalar_mul(stat[:, 11:12], stat[:, 11:12], 127.0))
            p.op("vector", lambda h: h.reciprocal(stat[:, 12:13], stat[:, 11:12]))
            p.op("vector", lambda h, r3=r3: h.tensor_mul(stat[:, 14:15], r3, stat[:, 11:12]))
            p.op("vector", lambda h: h.tensor_scalar(
                xoslot[:, 0:D], xoslot[:, 0:D], stat[:, 14:15], magic[:, 0:1],
                OP.mult, OP.add))
            cq2 = p.op("vector", lambda h: h.tensor_scalar(
                xqtk[:], xoslot[:, 0:D], magic[:, 0:1], stat[:, 12:13],
                OP.subtract, OP.mult))
            sc = p.cnt["ss"]
            for dch in range(KD):
                t1 = p.op("tensor", lambda h, dch=dch, lt=lt: h.matmul(
                    ppD[:, 512:640], xqtk[:, dch * 128:(dch + 1) * 128],
                    idf[:], is_transpose=True, start=True, stop=True),
                    waits=[("sv", cq2), ("ss", sc)])
                sc = p.op("scalar", lambda h, dch=dch, lt=lt: h.copy(
                    xqf[:, dch * 516 + 1 + lt * 128:dch * 516 + 1 + (lt + 1) * 128],
                    ppD[:, 512:640]), waits=[("st", t1)])
            yield
        ss_xq = p.cnt["ss"]
        yield

        mzero = p.op("vector", lambda h: h.memset(mqt[:], 0.0))
        p.wait("tensor", [("ss", ss_xq)])
        convdone = 0
        for mh in range(2):
            w1lds = []
            for k in range(KD):
                if k == 0:
                    ww = [("st", p.cnt["st"])]
                elif k % 2 == 0:
                    ww = [("sq", w1lds[k - 1])]
                else:
                    ww = []
                w1lds.append(p.dma("sync", lambda h, k=k, mh=mh: h.dma_start(
                    out=wstr[k], in_=wb_w1[k * 128:(k + 1) * 128,
                                          mh * 2048:(mh + 1) * 2048]),
                    waits=ww))
            for mi in range(KE):
                m = mh * KE + mi
                for k in range(KD):
                    w_ = []
                    if mi == 0:
                        w_.append(("sq", w1lds[k | 1]))
                    if k == 0:
                        w_.append(("sv", convdone))
                    p.op("tensor", lambda h, mi=mi, k=k: h.matmul(
                        ppD[:, 0:257], wstr[k][:, mi * 128:(mi + 1) * 128],
                        xqf3[:, k, 0:257], start=(k == 0), stop=(k == KD - 1)),
                        waits=w_)
                for k in range(KD):
                    p.op("tensor", lambda h, mi=mi, k=k: h.matmul(
                        ppD[:, 512:769], wstr[k][:, mi * 128:(mi + 1) * 128],
                        xqf3[:, k, 257:514], start=(k == 0), stop=(k == KD - 1)))
                stw1 = p.cnt["st"]
                t = hslot[:, (m % 2) * LB:(m % 2 + 1) * LB]
                p.op("vector", lambda h, m=m, t=t: h.tensor_scalar(
                    t[:, 0:257], ppD[:, 0:257], fcw[:, m * 3:m * 3 + 1],
                    fcb[:, m:m + 1], OP.mult, OP.add), waits=[("st", stw1)])
                p.op("vector", lambda h, m=m, t=t: h.tensor_scalar(
                    t[:, 257:512], ppD[:, 512:767], fcw[:, m * 3:m * 3 + 1],
                    fcb[:, m:m + 1], OP.mult, OP.add))
                p.op("vector", lambda h, m=m, t=t: h.scalar_tensor_tensor(
                    t[:, 0:256], ppD[:, 1:257], fcw[:, m * 3 + 1:m * 3 + 2],
                    t[:, 0:256], OP.mult, OP.add))
                p.op("vector", lambda h, m=m, t=t: h.scalar_tensor_tensor(
                    t[:, 256:512], ppD[:, 512:768], fcw[:, m * 3 + 1:m * 3 + 2],
                    t[:, 256:512], OP.mult, OP.add))
                p.op("vector", lambda h, m=m, t=t: h.scalar_tensor_tensor(
                    t[:, 0:255], ppD[:, 2:257], fcw[:, m * 3 + 2:m * 3 + 3],
                    t[:, 0:255], OP.mult, OP.add))
                convdone = p.op("vector", lambda h, m=m, t=t: h.scalar_tensor_tensor(
                    t[:, 255:512], ppD[:, 512:769], fcw[:, m * 3 + 2:m * 3 + 3],
                    t[:, 255:512], OP.mult, OP.add))
                hqs = hq[:, (m % 4) * LB:((m % 4) + 1) * LB]
                scs = p.op("scalar", lambda h, m=m, t=t, hqs=hqs: h.activation(
                    hqs, t, AF.Silu),
                    waits=[("sv", convdone), ("sp", p.cnt["sp"])])
                p.dma("scalar", lambda h, ob=ob, m=m, hqs=hqs: h.dma_start(
                    out=spillH[ob, :, m * LB:(m + 1) * LB],
                    in_=hqs), waits=[("ss", scs)])
                sab = p.op("scalar", lambda h, hqs=hqs: h.activation(
                    hslot[:, LB:2 * LB], hqs, AF.Abs))
                p.op("vector", lambda h, m=m: h.tensor_max(
                    mqt[:], mqt[:], hslot[:, LB:2 * LB]),
                    waits=[("ss", sab), ("sv", mzero)] if m == 0 else [("ss", sab)])
                yield
        svmx = p.cnt["sv"]
        yield

        mcols = stat[:, 24:28]
        prevv = svmx
        for lt in range(4):
            pdst = ppD if lt % 2 == 0 else ppE
            tt = p.op("tensor", lambda h, lt=lt, pdst=pdst: h.matmul(
                pdst[:, 0:128], mqt[:, lt * 128:(lt + 1) * 128], idf[:],
                is_transpose=True, start=True, stop=True),
                waits=[("sv", prevv), ("st", p.cnt["st"])] if lt == 0
                else [("sv", prevv)])
            p.op("vector", lambda h, pdst=pdst: h.tensor_reduce(
                stat[:, 12:13], pdst[:, 0:128], AX.X, OP.max), waits=[("st", tt)])
            p.op("vector", lambda h, lt=lt: h.tensor_scalar_max(
                mcols[:, lt:lt + 1], stat[:, 12:13], 1e-5))
            p.op("vector", lambda h, lt=lt: h.reciprocal(
                stat[:, 13:14], mcols[:, lt:lt + 1]))
            cshc = p.op("vector", lambda h: h.tensor_scalar_mul(
                stat[:, 13:14], stat[:, 13:14], 127.0))
            prevv = cshc
            p.op("tensor", lambda h, lt=lt: h.matmul(
                ppP1[0:1, 512 + lt * 128:512 + (lt + 1) * 128], stat[:, 13:14],
                idf[:], is_transpose=True, start=True, stop=True),
                waits=[("sv", cshc)])
        crow = p.op("vector", lambda h: h.tensor_copy(
            k2st[0:1, 0:LB], ppP1[0:1, 512:512 + LB]), waits=[("st", p.cnt["st"])])
        tb = bcast_row(ppP2[:, 512:1024], LB, 0, crow)
        csh = p.op("vector", lambda h: h.tensor_copy(shb[:], ppP2[:, 512:1024]),
                   waits=[("st", tb)])

        w2lds = {}
        mmk2 = {}
        for k in range(KH):
            if k % 4 == 0:
                for kk in range(k, k + 4):
                    w_ = []
                    if kk % 4 == 0:
                        w_.append(("st", p.cnt["st"]) if kk == 0
                                  else ("sq", w2lds[kk - 1]))
                        if kk >= 16:
                            w_.append(("st", mmk2[kk - 13]))
                    w2lds[kk] = p.dma("sync", lambda h, kk=kk: h.dma_start(
                        out=ar2b[:, (kk % 16) * D:((kk % 16) + 1) * D],
                        in_=wb_w2[kk * 128:(kk + 1) * 128, :]), waits=w_)
            spar = k % 4
            hqs = hq[:, spar * LB:(spar + 1) * LB]
            lh = p.dma("scalar", lambda h, ob=ob, k=k, hqs=hqs: h.dma_start(
                out=hqs, in_=spillH[ob, :, k * LB:(k + 1) * LB]),
                waits=[("st", mmk2.get(k - 4, 0)), ("sp", p.cnt["sp"])] if k >= 4
                else [("sp", p.cnt["sp"])])
            p.op("vector", lambda h, hqs=hqs: h.tensor_mul(
                hslot[:, 0:LB], hqs, shb[:]),
                waits=[("sp", lh)] + ([("sv", csh)] if k == 0 else []))
            svr = p.op("vector", lambda h, hqs=hqs: h.tensor_scalar(
                hqs, hslot[:, 0:LB], magic[:, 0:1], magic[:, 0:1],
                OP.add, OP.subtract))
            p.dma("scalar", lambda h, ob=ob, k=k, hqs=hqs: h.dma_start(
                out=spillH2[ob, :, k * LB:(k + 1) * LB], in_=hqs),
                waits=[("sv", svr)])
            for m in range(4):
                w_ = []
                if m == 0:
                    w_.append(("sq", w2lds[(k // 4) * 4 + 3]))
                    w_.append(("sv", svr))
                    if k == 0:
                        w_.append(("ss", ss_xq))
                p.op("tensor", lambda h, k=k, m=m, hqs=hqs: h.matmul(
                    pbank4[m], ar2b[:, (k % 16) * D + m * 128:(k % 16) * D + (m + 1) * 128],
                    hqs, start=(k == 0), stop=(k == KH - 1)), waits=w_)
            mmk2[k] = p.cnt["st"]
            yield
        stmm = p.cnt["st"]
        evw1 = 0
        for m in range(4):
            evw1 = p.op("vector", lambda h, m=m: h.tensor_copy(
                ohi[:, m * LB:(m + 1) * LB], pbank4[m]),
                waits=[("st", stmm)] if m == 0 else ())
        yield
        w2lds2 = {}
        mmk2b = {}
        for k in range(KH):
            if k % 4 == 0:
                for kk in range(k, k + 4):
                    w_ = []
                    if kk % 4 == 0:
                        w_.append(("sq", w2lds[KH - 1] if kk == 0
                                   else w2lds2[kk - 1]))
                        w_.append(("st", mmk2[19 + kk] if kk < 16
                                   else mmk2b[kk - 13]))
                    w2lds2[kk] = p.dma("sync", lambda h, kk=kk: h.dma_start(
                        out=ar2b[:, (kk % 16) * D:((kk % 16) + 1) * D],
                        in_=wb_w2[kk * 128:(kk + 1) * 128, :]), waits=w_)
            spar = k % 4
            hqs = hq[:, spar * LB:(spar + 1) * LB]
            lh = p.dma("scalar", lambda h, ob=ob, k=k, hqs=hqs: h.dma_start(
                out=hqs, in_=spillH2[ob, :, k * LB:(k + 1) * LB]),
                waits=[("st", mmk2b[k - 4] if k >= 4 else mmk2[28 + k]),
                       ("sp", p.cnt["sp"])])
            for m in range(4):
                w_ = []
                if m == 0:
                    w_.append(("sq", w2lds2[(k // 4) * 4 + 3]))
                    w_.append(("sp", lh))
                    if k == 0:
                        w_.append(("sv", evw1))
                p.op("tensor", lambda h, k=k, m=m, hqs=hqs: h.matmul(
                    pbank4[m],
                    ar2b[:, (k % 16) * D + (4 + m) * 128:(k % 16) * D + (5 + m) * 128],
                    hqs, start=(k == 0), stop=(k == KH - 1)), waits=w_)
            mmk2b[k] = p.cnt["st"]
            yield
        stmm = p.cnt["st"]
        ev2 = 0
        for m in range(4):
            ev2 = p.op("vector", lambda h, m=m: h.tensor_copy(
                ohi[:, (4 + m) * LB:(5 + m) * LB], pbank4[m]),
                waits=[("st", stmm)] if m == 0 else ())
        svev2 = ev2

        ldm2 = p.dma("scalar", lambda h, ob=ob: h.dma_start(
            out=xmidt[:], in_=spillM[ob]),
            waits=[("sv", p.cnt["sv"]), ("sp", p.cnt["sp"])])
        fin = 0
        cf = 0
        lastout = 0
        for lt in range(4):
            pdst = ppD if lt % 2 == 0 else ppE
            p.op("vector", lambda h, lt=lt: h.tensor_scalar_mul(
                stat[:, 17:18], mcols[:, lt:lt + 1], 1.0 / 127.0))
            cf = p.op("vector", lambda h: h.tensor_mul(stat[:, 17:18], stat[:, 17:18], rw2))
            for m in range(KD):
                w_ = [("sv", svev2 if lt < 2 else fin)] if m == 0 else []
                p.op("tensor", lambda h, lt=lt, m=m, pdst=pdst: h.matmul(
                    pdst[:, m * 128:(m + 1) * 128],
                    ohi[:, m * LB + lt * 128:m * LB + (lt + 1) * 128],
                    idf[:], is_transpose=True, start=True, stop=True), waits=w_)
            sttr = p.cnt["st"]
            fin = p.op("vector", lambda h, lt=lt, pdst=pdst: h.scalar_tensor_tensor(
                xoslot[:, 0:D], pdst[:, 0:D], stat[:, 17:18],
                xmidt[:, lt * D:(lt + 1) * D], OP.mult, OP.add),
                waits=[("st", sttr), ("sp", ldm2), ("sv", cf), ("sp", lastout)])
            lastout = p.dma("scalar", lambda h, ob=ob, lt=lt: h.dma_start(
                out=y_out[ob, lt * 128:(lt + 1) * 128, :], in_=xoslot[:, 0:D]),
                waits=[("sv", fin)])
            yield
        pbE[r] = {"sv": p.cnt["sv"], "ss": p.cnt["ss"]}

    def mkstep(g):
        done = [g is None]

        def s():
            if done[0]:
                return False
            try:
                next(g)
                return True
            except StopIteration:
                done[0] = True
                return False
        return s

    for r in range(NR + 2):
        sg_ = mkstep(scan_gen(r) if r < NR else None)
        pa_ = mkstep(post_a_gen(r - 1) if 0 <= r - 1 < NR else None)
        pb_ = mkstep(post_b_gen(r - 2) if 0 <= r - 2 < NR else None)
        progress = True
        while progress:
            progress = sg_()
            for _ in range(5):
                if pa_():
                    progress = True
                elif pb_():
                    progress = True

    p.wait("vector", [("sp", p.cnt["sp"]), ("sq", p.cnt["sq"]),
                      ("sg", p.cnt["sg"]), ("syq", p.cnt["syq"]),
                      ("sxq", p.cnt["sxq"])])

    with nc.Block() as block:
        p.finalize(block)
    es.close()
    return nc


def _mk_sel32():
    sel = np.zeros((32, 32, 128), np.float32)
    for s in range(32):
        sel[s, s, :] = 1.0
    return np.ascontiguousarray(sel.transpose(1, 0, 2).reshape(32, 32 * 128)
                                ).astype(ml_dtypes.bfloat16)


def host_prep(inputs, NB):
    NR = NB // 4
    OB = NB // 4
    L = NB * LB
    x = np.ascontiguousarray(inputs["L0_attn_seq"], dtype=np.float32)
    f32 = lambda v: np.ascontiguousarray(np.asarray(v), dtype=np.float32)

    shared = {
        "in_projT": f32(np.asarray(inputs["L0_attn_in_proj_w"]).T),
        "out_projT": f32(np.asarray(inputs["L0_attn_out_proj_w"]).T),
        "w1T": f32(np.asarray(inputs["L0_ffn_w1"]).T),
        "w2T": f32(np.asarray(inputs["L0_ffn_w2"]).T),
        "x_projTd": f32(np.asarray(inputs["L0_attn_x_proj_w"]).T),
        "dt_projTd": f32(np.asarray(inputs["L0_attn_dt_proj_w"]).T),
        "ln_g8": f32(np.asarray(inputs["L0_attn_ln_g"]).reshape(KD, 128).T),
        "ln_b8": f32(np.asarray(inputs["L0_attn_ln_b"]).reshape(KD, 128).T),
        "conv_w64": f32(np.asarray(inputs["L0_attn_conv_w"]).reshape(KE, 128, 4)
                        .transpose(1, 0, 2).reshape(128, KE * 4)),
        "conv_b16": f32(np.asarray(inputs["L0_attn_conv_b"]).reshape(KE, 128).T),
        "dtb16": f32(np.asarray(inputs["L0_attn_dt_proj_b"]).reshape(KE, 128).T),
        "D16": f32(np.asarray(inputs["L0_attn_D"]).reshape(KE, 128).T),
        "fcw96": f32(np.asarray(inputs["L0_ffn_conv_w"])[:, 0, :].reshape(KH, 128, 3)
                     .transpose(1, 0, 2).reshape(128, KH * 3)),
        "fcb32": f32(np.asarray(inputs["L0_ffn_conv_b"]).reshape(KH, 128).T),
        "rs_col": f32(np.broadcast_to(np.asarray(inputs["L0_attn_res_scale"],
                                                 np.float32).reshape(1, 1), (128, 1))),
        "identc": np.eye(128, dtype=ml_dtypes.bfloat16),
        "identf": np.eye(128, dtype=np.float32),
        "sel32": _mk_sel32(),
        "onesk2": np.concatenate([np.ones((1, 128)), np.zeros((1, 128))]
                                 ).astype(ml_dtypes.bfloat16),
    }

    in_maps = []
    for c in range(8):
        b, m = c // 4, c % 4
        xb = x[b, :L].reshape(NB, LB, D)
        own = []
        for ob in range(OB):
            g = 4 * ob + m
            blk = np.zeros((LB + 3, D), np.float32)
            blk[3:] = xb[g]
            if g > 0:
                blk[0:3] = xb[g - 1][LB - 3:]
            own.append(blk)
        x_own = np.ascontiguousarray(np.stack(own))

        Al = np.asarray(inputs["L0_attn_A_log"])[512 * m:512 * (m + 1)]
        A_my = f32(Al.reshape(4, 128, S).transpose(1, 0, 2).reshape(128, 4 * S))

        idx = np.zeros((128, NIDX), np.int32)
        pcol = np.arange(128, dtype=np.int32)
        for j in range(4):
            for q in range(4):
                idx[:, IDXC[("dt", j, q)]] = j * 4128 + (m * 4 + q) * 128 + pcol
                idx[:, IDXC[("dx", j, q)]] = j * 4128 + E + (m * 4 + q) * 128 + pcol
            idx[0:32, IDXC[("bcs", j)]] = j * 4128 + 2 * E + np.arange(32)
        for ch in range(16):
            j, qd = ch // 4, ch % 4
            idx[:, IDXC[("y", ch)]] = j * 2048 + m * 512 + qd * 128 + pcol
        for ob in range(min(OB, 8)):
            g = 4 * ob + m
            lrow = 0 if g == 0 else ((g - 1) // 4) * 8 + ((g - 1) % 4) * 2 + 1
            rrow = 0 if g == NB - 1 else ((g + 1) // 4) * 8 + ((g + 1) % 4) * 2 + 0
            idx[0, IDXC[("xh", ob)]] = lrow
            idx[1, IDXC[("xh", ob)]] = rrow

        mcv_ = np.ones((128, 24), np.float32)
        mfn_ = np.ones((128, 16), np.float32)
        for ob in range(min(OB, 8)):
            g = 4 * ob + m
            if g == 0:
                mcv_[:, 3 * ob:3 * ob + 3] = 0.0
                mfn_[:, 2 * ob] = 0.0
            if g == NB - 1:
                mfn_[:, 2 * ob + 1] = 0.0

        im = dict(shared)
        im.update(x_own=x_own, A_log_my=A_my, idxs=idx, maskcv=mcv_, maskfn=mfn_)
        in_maps.append(im)
    return in_maps


def kernel(**inputs):
    NB = np.asarray(inputs["L0_attn_seq"]).shape[1] // LB
    in_maps = host_prep(inputs, NB)
    nc = build_nc(NB)
    res = run_bass_kernel_spmd(nc, in_maps, core_ids=list(range(8)))
    L = NB * LB
    OB = NB // 4
    out = np.empty((2, L, D), np.float32)
    for c in range(8):
        b, m = c // 4, c % 4
        yo = np.asarray(res.results[c]["y_out"])
        for ob in range(OB):
            g = 4 * ob + m
            out[b, g * LB:(g + 1) * LB] = yo[ob]
    return out


# version 08
# speedup vs baseline: 1.1017x; 1.0269x over previous
import numpy as np
from contextlib import ExitStack
import ml_dtypes

import concourse.bass as bass
import concourse.mybir as mybir
from concourse.bass import IndirectOffsetOnAxis
import bass_rust
ROP = bass_rust.ReduceOp
from concourse.bass_utils import run_bass_kernel_spmd

F32 = mybir.dt.float32
BF16 = mybir.dt.bfloat16
I32 = mybir.dt.int32
AF = mybir.ActivationFunctionType
OP = mybir.AluOpType
AX = mybir.AxisListType

D = 1024
E = 2048
S = 16
RK = 64
H = 4096
LB = 512
KD = D // 128
KE = E // 128
KH = H // 128
MAGIC = 12582912.0
EPS = 1e-5

ENGS = ("sync", "vector", "scalar", "tensor", "gpsimd")
SEMS = ("sq", "sp", "sg", "sv", "ss", "st", "sgp", "scc", "syq", "sxq", "sgy0", "sgy1", "sly0", "sly1", "shx", "sow", "shs", "slh", "slh2")
_QSEM = {"sync": "sq", "scalar": "sp", "gpsimd": "sg"}
_ESEM = {"vector": "sv", "scalar": "ss", "tensor": "st", "gpsimd": "sgp"}


class Prog:
    def __init__(self, nc):
        self.nc = nc
        self.q = {e: [] for e in ENGS}
        self.cnt = {s: 0 for s in SEMS}
        self.sem = {}

    def _push(self, eng, fn, inc, waits):
        sems = self.sem

        def run(h):
            for sname, val in waits:
                if val > 0:
                    h.wait_ge(sems[sname], val)
            ins = fn(h)
            if ins is not None and inc is not None:
                ins.then_inc(sems[inc[0]], inc[1])
        self.q[eng].append(run)

    def op(self, eng, fn, waits=()):
        s = _ESEM[eng]
        selfw = ((s, self.cnt[s]),)
        self.cnt[s] += 1
        self._push(eng, fn, (s, 1), tuple(waits) + selfw)
        return self.cnt[s]

    def dma(self, eng, fn, waits=(), sem=None):
        s = sem or _QSEM[eng]
        self.cnt[s] += 16
        self._push(eng, fn, (s, 16), tuple(waits))
        return self.cnt[s]

    def cc(self, fn, waits=()):
        self.cnt["scc"] += 1
        self._push("gpsimd", fn, ("scc", 1), tuple(waits))
        return self.cnt["scc"]

    def wait(self, eng, waits):
        self._push(eng, lambda h: None, None, tuple(waits))

    def barrier(self):
        snap = [(s, self.cnt[s]) for s in SEMS if self.cnt[s] > 0]
        for eng in ENGS:
            self.wait(eng, snap)

    def finalize(self, block):
        secs = {"sync": block.sync, "vector": block.vector, "scalar": block.scalar,
                "tensor": block.tensor, "gpsimd": block.gpsimd}
        for eng in ENGS:
            fns = self.q[eng]
            if not fns:
                continue
            def body(h, fns=tuple(fns)):
                for f in fns:
                    f(h)
            secs[eng](body)


IDXC = {}
_col = 0
for _j in range(4):
    for _q in range(4):
        IDXC[("dt", _j, _q)] = _col; _col += 1
        IDXC[("dx", _j, _q)] = _col; _col += 1
for _j in range(4):
    IDXC[("bcs", _j)] = _col; _col += 1
for _ch in range(16):
    IDXC[("y", _ch)] = _col; _col += 1
for _ob in range(8):
    IDXC[("xh", _ob)] = _col; _col += 1
NIDX = 96


def build_nc(NB, debug=False):
    NR = NB // 4
    OB = NB // 4
    nc = bass.Bass("TRN2", target_bir_lowering=False, detect_race_conditions=False)

    dr = lambda n, shp, dt_: nc.dram_tensor(n, shp, dt_, kind="ExternalInput").ap()
    x_own = dr("x_own", [OB, LB + 3, D], F32)
    in_projT = dr("in_projT", [D, 2 * E], F32)
    out_projT = dr("out_projT", [E, D], F32)
    w1T = dr("w1T", [D, H], F32)
    w2T = dr("w2T", [H, D], F32)
    x_projTd = dr("x_projTd", [E, 96], F32)
    dt_projTd = dr("dt_projTd", [RK, E], F32)
    ln_g8 = dr("ln_g8", [128, KD], F32)
    ln_b8 = dr("ln_b8", [128, KD], F32)
    A_log_my = dr("A_log_my", [128, 4 * S], F32)
    conv_w64 = dr("conv_w64", [128, KE * 4], F32)
    conv_b16 = dr("conv_b16", [128, KE], F32)
    dtb16 = dr("dtb16", [128, KE], F32)
    D16 = dr("D16", [128, KE], F32)
    fcw96 = dr("fcw96", [128, KH * 3], F32)
    fcb32 = dr("fcb32", [128, KH], F32)
    rs_col = dr("rs_col", [128, 1], F32)
    idxs = dr("idxs", [128, NIDX], I32)
    maskcv = dr("maskcv", [128, 3 * 8], F32)
    maskfn = dr("maskfn", [128, 2 * 8], F32)
    identc = dr("identc", [128, 128], BF16)
    identf = dr("identf", [128, 128], F32)
    sel32 = dr("sel32", [32, 32 * 128], BF16)
    onesk2 = dr("onesk2", [2, 128], BF16)

    y_out = nc.dram_tensor("y_out", [OB, LB, D], F32, kind="ExternalOutput").ap()
    dbgt = nc.dram_tensor("dbgt", [8, 128, 4160], F32, kind="ExternalOutput").ap() if debug else None

    di = lambda n, shp, dt_: nc.dram_tensor(n, shp, dt_, kind="Internal").ap()
    wb_ipz = di("wb_ipz", [128, KD * E], BF16)
    wb_ipx = di("wb_ipx", [128, KD * E], BF16)
    wb_opp = di("wb_opp", [128, KE * D], BF16)
    wb_w1p = di("wb_w1p", [128, 2 * KD * E], BF16)
    wb_w2p = di("wb_w2p", [128, KH * D], BF16)
    ccF_in = di("ccF_in", [NR, 4128, LB], F32)
    ccF_out = [di(f"ccF_out{r}", [4 * 4128, LB], F32) for r in range(NR)]
    ccR_in = di("ccR_in", [NR, 4 * 512, LB], F32)
    ccR_out = [di(f"ccR_out{r}", [4 * 4 * 512, LB], F32) for r in range(NR)]
    ccX_in = di("ccX_in", [NR, 2, D], F32)
    ccX_out = di("ccX_out", [NR * 8, D], F32)
    spillX = di("spillX", [OB, 128, KE * LB], BF16)
    spillZ = di("spillZ", [OB, 128, KE * LB], BF16)
    spillH = di("spillH", [OB, 128, KH * LB], BF16)
    spillH2 = di("spillH2", [OB, 128, KH * LB], BF16)
    spillM = di("spillM", [OB, 128, 4 * D], F32)
    bias0d = di("bias0d", [2 * E], F32)

    es = ExitStack()
    sb = lambda n, shp, dt_: es.enter_context(nc.sbuf_tensor(n, shp, dt_))
    pt = lambda n, shp: es.enter_context(nc.psum_tensor(n, shp, F32))
    p = Prog(nc)
    for sname in SEMS:
        p.sem[sname] = es.enter_context(nc.semaphore(sname))

    ident = sb("ident", [128, 128], BF16)
    idf = sb("idf", [128, 128], F32)
    selt = sb("selt", [32, 32 * 128], BF16)
    ok2 = sb("ok2", [2, 128], BF16)
    k2st = sb("k2st", [2, 512], F32)
    k2hi = sb("k2hi", [2, 512], BF16)
    k2lo = sb("k2lo", [2, 512], BF16)
    lnG = sb("lnG", [128, KD], F32)
    lnB = sb("lnB", [128, KD], F32)
    lnBb = sb("lnBb", [128, KD], BF16)
    A64 = sb("A64", [128, 4 * S], F32)
    cw64 = sb("cw64", [128, KE * 4], F32)
    cb16 = sb("cb16", [128, KE], F32)
    dtb = sb("dtb", [128, KE], F32)
    Dc16 = sb("Dc16", [128, KE], F32)
    fcw = sb("fcw", [128, KH * 3], F32)
    fcb = sb("fcb", [128, KH], F32)
    rsc = sb("rsc", [128, 1], F32)
    magic = sb("magic", [128, 1], F32)
    idxt = sb("idxt", [128, NIDX], I32)
    mcv = sb("mcv", [128, 3 * 8], F32)
    mfn = sb("mfn", [128, 2 * 8], F32)
    bias0 = sb("bias0", [128, 2 * KE], F32)
    xpT = sb("xpT", [128, KE * 96], BF16)
    dtpT = sb("dtpT", [128, E], BF16)
    statc = sb("statc", [128, 8 * OB], F32)
    colw = sb("colw", [128, 16], F32)

    ar1 = sb("ar1", [128, KE * LB + 2 * D], F32)
    ar2 = sb("ar2", [128, KE * LB], F32)
    ar3 = sb("ar3", [128, 9520], F32)
    ar3b = ar3[:].bitcast(BF16)
    xslb = ar3b[:, 0:KE * LB]
    xnf = ar3b[:, KE * LB:KE * LB + KD * 520]
    xcvs = ar3[:, 6176:6176 + 2 * 520]
    tcv = ar3[:, 7216:7216 + 2 * LB]
    zbs = ar3b[:, 2 * 8240:2 * 8240 + 2 * LB]
    xdb = ar3[:, 8752:8752 + LB]
    xdbb = ar3b[:, 2 * 9264:2 * 9264 + LB]
    stat = sb("stat", [128, 40], F32)

    ppS1 = pt("ppS1", [128, 1024])
    ppS2 = pt("ppS2", [128, 1024])
    ppP1 = pt("ppP1", [128, 1024])
    ppP2 = pt("ppP2", [128, 1024])
    ppA = ppS1[:, 0:512]
    ppB = ppS1[:, 512:1024]
    ppF = ppS2[:, 0:512]
    ppG = ppS2[:, 512:1024]
    ppD = ppP1
    ppE = ppP2
    psms8 = [ppA, ppB, ppF, ppG, ppD[:, 0:512], ppD[:, 512:1024],
             ppE[:, 0:512], ppE[:, 512:1024]]
    pbank4 = [ppP1[:, 0:512], ppP1[:, 512:1024],
              ppP2[:, 0:512], ppP2[:, 512:1024]]

    xres = ar1[:, 0:4 * D]
    xhal = ar1[:, 4 * D:5 * D]
    ar1b = ar1[:].bitcast(BF16)
    xnbB = ar1[:, 5 * D:9 * D]
    xnhB = ar1[:, 9 * D:10 * D]
    xsl = ar1[:, 0:KE * LB]
    ar2b = ar2[:].bitcast(BF16)
    wstr = [ar2b[:, k * 2048:(k + 1) * 2048] for k in range(KD)]
    dtt = ar2[:, 0:KE * LB]
    wfull = ar1[:, 0:2 * E]
    wcastB = ar1b[:, 2 * (2 * E):2 * (2 * E) + 2 * E]
    stat0row = ar2[:, 0:2 * E]

    def bcast_row(psdst, w, src_row_off, waitv):
        c1 = p.op("vector", lambda h: h.tensor_copy(
            k2hi[0:1, src_row_off:src_row_off + w],
            k2st[0:1, src_row_off:src_row_off + w]), waits=[("sv", waitv)])
        c2 = p.op("vector", lambda h: h.tensor_tensor(
            k2st[1:2, src_row_off:src_row_off + w] if False else
            k2lo[0:1, src_row_off:src_row_off + w],
            k2st[0:1, src_row_off:src_row_off + w],
            k2hi[0:1, src_row_off:src_row_off + w], OP.subtract))
        t1 = p.op("tensor", lambda h: h.matmul(
            psdst, ok2[:, 0:128], k2hi[0:2, src_row_off:src_row_off + w],
            start=True, stop=False), waits=[("sv", c2)])
        t2 = p.op("tensor", lambda h: h.matmul(
            psdst, ok2[:, 0:128], k2lo[0:2, src_row_off:src_row_off + w],
            start=False, stop=True))
        return t2



    for dst, src in ((ident[:], identc), (idf[:], identf), (lnG[:], ln_g8),
                     (lnB[:], ln_b8), (A64[:], A_log_my), (cw64[:], conv_w64),
                     (cb16[:], conv_b16), (dtb[:], dtb16), (Dc16[:], D16),
                     (fcw[:], fcw96), (fcb[:], fcb32), (rsc[:], rs_col),
                     (idxt[:], idxs), (mcv[:], maskcv), (mfn[:], maskfn),
                     (selt[:], sel32), (ok2[:], onesk2)):
        p.dma("sync", lambda h, d=dst, s=src: h.dma_start(out=d, in_=s))
    ld0 = p.cnt["sq"]
    p.dma("gpsimd", lambda h: h.dma_start(
        out=xpT[:].rearrange("p (k n) -> p k n", n=96),
        in_=x_projTd.rearrange("(k p) n -> p k n", p=128)))
    p.dma("gpsimd", lambda h: h.dma_start(out=dtpT[0:RK, :], in_=dt_projTd))
    lg0 = p.cnt["sg"]
    sA = p.op("scalar", lambda h: h.activation(A64[:], A64[:], AF.Exp),
              waits=[("sq", ld0)])
    p.op("vector", lambda h: h.tensor_scalar_mul(A64[:], A64[:], -1.0),
         waits=[("ss", sA)])
    p.op("vector", lambda h: h.tensor_copy(lnBb[:], lnB[:]), waits=[("sq", ld0)])
    p.op("vector", lambda h: h.memset(magic[:], MAGIC))
    p.op("vector", lambda h: h.memset(k2st[:], 0.0))
    p.op("vector", lambda h: h.memset(k2hi[:], 0.0))
    p.op("vector", lambda h: h.memset(k2lo[:], 0.0))

    for k in range(KD):
        ld = p.dma("sync", lambda h, k=k: h.dma_start(
            out=wfull, in_=in_projT[k * 128:(k + 1) * 128, :]),
            waits=[("sv", p.cnt["sv"])])
        c1 = p.op("vector", lambda h: h.tensor_copy(wcastB, wfull),
                  waits=[("sq", p.cnt["sq"])])
        for nt in range(8):
            p.op("tensor", lambda h, k=k, nt=nt: h.matmul(
                psms8[nt][0:1, :], lnBb[:, k:k + 1],
                wcastB[:, nt * 512:(nt + 1) * 512],
                start=(k == 0), stop=(k == KD - 1)),
                waits=[("sv", c1)] if nt == 0 else ())
        c2 = p.op("vector", lambda h, k=k: h.tensor_scalar(
            wcastB, wfull, lnG[:, k:k + 1], None, OP.mult),
            waits=[("st", p.cnt["st"])])
        p.dma("sync", lambda h, k=k: h.dma_start(
            out=wb_ipx[:, k * E:(k + 1) * E], in_=wcastB[:, 0:E]),
            waits=[("sv", c2)])
        p.dma("sync", lambda h, k=k: h.dma_start(
            out=wb_ipz[:, k * E:(k + 1) * E], in_=wcastB[:, E:2 * E]))
    cb0 = 0
    for nt in range(8):
        cb0 = p.op("vector", lambda h, nt=nt: h.tensor_copy(
            stat0row[0:1, nt * 512:(nt + 1) * 512], psms8[nt][0:1, :]),
            waits=[("st", p.cnt["st"])] if nt == 0 else ())
    db0 = p.dma("sync", lambda h: h.dma_start(out=bias0d, in_=stat0row[0:1, :]),
                waits=[("sv", cb0)])
    p.dma("sync", lambda h: h.dma_start(
        out=bias0[:], in_=bias0d.rearrange("(c p) -> p c", p=128),
        allow_slow_non_contiguous=True), waits=[("sq", db0)])

    for k in range(KE):
        ld = p.dma("sync", lambda h, k=k: h.dma_start(
            out=wfull[:, 0:D], in_=out_projT[k * 128:(k + 1) * 128, :]),
            waits=[("sv", p.cnt["sv"])])
        c = p.op("vector", lambda h: h.tensor_copy(wcastB[:, 0:D], wfull[:, 0:D]),
                 waits=[("sq", p.cnt["sq"])])
        p.dma("sync", lambda h, k=k: h.dma_start(
            out=wb_opp[:, k * D:(k + 1) * D], in_=wcastB[:, 0:D]),
            waits=[("sv", c)])

    def quant_weight(wsrc, rows, cols, sdst, stores):
        nch = rows // 128
        acc = colw[:, 0:1]
        last = p.op("vector", lambda h: h.memset(acc, 0.0))
        for k in range(nch):
            ld = p.dma("sync", lambda h, k=k: h.dma_start(
                out=wfull[:, 0:cols], in_=wsrc[k * 128:(k + 1) * 128, :]),
                waits=[("sv", p.cnt["sv"])])
            p.op("vector", lambda h: h.tensor_reduce(
                colw[:, 1:2], wfull[:, 0:cols], AX.X, OP.add,
                apply_absolute_value=True), waits=[("sq", ld)])
            last = p.op("vector", lambda h: h.tensor_add(acc, acc, colw[:, 1:2]))
        tt = p.op("tensor", lambda h: h.matmul(
            ppB[0:1, 0:128], acc, idf[:], is_transpose=True, start=True, stop=True),
            waits=[("sv", last)])
        rr = p.op("vector", lambda h: h.tensor_reduce(
            k2st[0:1, 0:1], ppB[0:1, 0:128], AX.X, OP.add), waits=[("st", tt)])
        p.op("vector", lambda h: h.tensor_scalar_mul(
            k2st[0:1, 0:1], k2st[0:1, 0:1], 1.0 / (rows * cols)))
        p.op("vector", lambda h: h.tensor_scalar_max(k2st[0:1, 0:1], k2st[0:1, 0:1], 1e-5))
        rr = p.op("vector", lambda h: h.reciprocal(k2st[0:1, 1:2], k2st[0:1, 0:1]))
        t2 = bcast_row(ppB[:, 0:1], 1, 1, rr)
        cs = p.op("vector", lambda h, sdst=sdst: h.tensor_copy(sdst, ppB[:, 0:1]),
                  waits=[("st", t2)])
        for k in range(nch):
            ld = p.dma("sync", lambda h, k=k: h.dma_start(
                out=wfull[:, 0:cols], in_=wsrc[k * 128:(k + 1) * 128, :]),
                waits=[("sv", p.cnt["sv"]), ("sq", p.cnt["sq"])])
            p.op("vector", lambda h, sdst=sdst: h.tensor_scalar(
                wfull[:, 0:cols], wfull[:, 0:cols], sdst, magic[:, 0:1],
                OP.mult, OP.add), waits=[("sq", ld)])
            p.op("vector", lambda h: h.tensor_scalar(
                wfull[:, 0:cols], wfull[:, 0:cols], magic[:, 0:1], 1.0,
                OP.subtract, OP.min))
            c1 = p.op("vector", lambda h: h.tensor_scalar(
                wcastB[:, 0:cols], wfull[:, 0:cols], -1.0, None, OP.max))
            first = True
            for dst_ap, s0, s1 in stores(k):
                p.dma("sync", lambda h, dst_ap=dst_ap, s0=s0, s1=s1: h.dma_start(
                    out=dst_ap, in_=wcastB[:, s0:s1]),
                    waits=[("sv", c1)] if first else ())
                first = False

    sw1 = colw[:, 8:9]
    sw2 = colw[:, 9:10]
    quant_weight(w1T, D, H, sw1, lambda k: [
        (wb_w1p[:, k * E:(k + 1) * E], 0, E),
        (wb_w1p[:, (KD + k) * E:(KD + k + 1) * E], E, 2 * E)])
    quant_weight(w2T, H, D, sw2, lambda k: [
        (wb_w2p[:, k * D:(k + 1) * D], 0, D)])
    rw1 = colw[:, 10:11]
    rw2 = colw[:, 11:12]
    p.op("vector", lambda h: h.reciprocal(rw1, sw1))
    p.op("vector", lambda h: h.reciprocal(rw2, sw2))
    p.op("vector", lambda h: h.tensor_scalar(fcw[:], fcw[:], rw1, None, OP.mult))

    zv = p.op("vector", lambda h: h.memset(wfull[:, 0:D], 0.0))
    nzr = (NR * 8 + 127) // 128
    for zi in range(nzr):
        r0, r1 = zi * 128, min((zi + 1) * 128, NR * 8)
        p.dma("sync", lambda h, r0=r0, r1=r1: h.dma_start(
            out=ccX_out[r0:r1, :], in_=wfull[0:r1 - r0, 0:D]), waits=[("sv", zv)])

    p.barrier()

    def rsqrt_newton(w, dst, var_col, s0, s1):
        cv = p.op("vector", lambda h: h.tensor_scalar_add(s0[0:w, :], var_col[0:w, :], EPS))
        c = p.op("scalar", lambda h: h.activation(s1[0:w, :], s0[0:w, :], AF.Sqrt),
                 waits=[("sv", cv)])
        p.op("vector", lambda h: h.reciprocal(dst[0:w, :], s1[0:w, :]), waits=[("ss", c)])
        p.op("vector", lambda h: h.tensor_mul(s1[0:w, :], dst[0:w, :], dst[0:w, :]))
        p.op("vector", lambda h: h.tensor_mul(s1[0:w, :], s1[0:w, :], s0[0:w, :]))
        p.op("vector", lambda h: h.tensor_scalar(s1[0:w, :], s1[0:w, :], -0.5, 1.5,
                                                 OP.mult, OP.add))
        return p.op("vector", lambda h: h.tensor_mul(dst[0:w, :], dst[0:w, :], s1[0:w, :]))

    def ln_stats(src, w, mu_dst, r1_dst, rtot_dst=None):
        bno = stat[:, 20:32]
        p.op("vector", lambda h: h.bn_stats(bno[0:w, 0:6], src[0:w, 0:D // 2]))
        p.op("vector", lambda h: h.bn_stats(bno[0:w, 6:12], src[0:w, D // 2:D]))
        p.op("vector", lambda h: h.bn_aggr(stat[0:w, 32:34], bno[0:w, 0:12]))
        p.op("vector", lambda h: h.tensor_copy(mu_dst[0:w, :], stat[0:w, 32:33]))
        rsqrt_newton(w, r1_dst, stat[:, 33:34], stat[:, 35:36], stat[:, 36:37])
        if rtot_dst is not None:
            p.op("vector", lambda h: h.tensor_mul(
                stat[0:w, 37:38], r1_dst[0:w, :], r1_dst[0:w, :]))
            p.op("vector", lambda h: h.tensor_mul(
                stat[0:w, 37:38], stat[0:w, 37:38], stat[0:w, 33:34]))
            rsqrt_newton(w, stat[:, 38:39], stat[:, 37:38], stat[:, 35:36],
                         stat[:, 36:37])
            return p.op("vector", lambda h: h.tensor_mul(
                rtot_dst[0:w, :], r1_dst[0:w, :], stat[0:w, 38:39]))
        return p.cnt["sv"]


    def bcast_row(psdst, w, src_row_off, waitv):
        c1 = p.op("vector", lambda h: h.tensor_copy(
            k2hi[0:1, src_row_off:src_row_off + w],
            k2st[0:1, src_row_off:src_row_off + w]), waits=[("sv", waitv)])
        c2 = p.op("vector", lambda h: h.tensor_tensor(
            k2st[1:2, src_row_off:src_row_off + w] if False else
            k2lo[0:1, src_row_off:src_row_off + w],
            k2st[0:1, src_row_off:src_row_off + w],
            k2hi[0:1, src_row_off:src_row_off + w], OP.subtract))
        t1 = p.op("tensor", lambda h: h.matmul(
            psdst, ok2[:, 0:128], k2hi[0:2, src_row_off:src_row_off + w],
            start=True, stop=False), waits=[("sv", c2)])
        t2 = p.op("tensor", lambda h: h.matmul(
            psdst, ok2[:, 0:128], k2lo[0:2, src_row_off:src_row_off + w],
            start=False, stop=True))
        return t2

    ccF_done = {}
    ccR_done = {}
    paE = {}
    ccX_done = {}
    sends_end = {}
    ysend_tok = {0: 0}
    last_scan_par = {0: 0, 1: 0, 2: 0, 3: 0}
    scan_last = {"ss": 0, "sv": 0}

    prev_sends = {"d1": 0, "d2": 0, "d3": 0, "spx": 0}
    for ob in range(OB):
        ldx = p.dma("sync", lambda h, ob=ob: h.dma_start(
            out=xres.rearrange("p (t n) -> p t n", n=D),
            in_=x_own[ob, 3:3 + LB, :].rearrange("(t p) n -> p t n", p=128)),
            waits=[("sxq", prev_sends["d3"]), ("sv", p.cnt["sv"]),
                   ("st", p.cnt["st"])], sem="sxq")
        ldh = p.dma("sync", lambda h, ob=ob: h.dma_start(
            out=xhal[0:3, :], in_=x_own[ob, 0:3, :]), sem="sxq")

        p.wait("vector", [("sxq", ldh), ("st", p.cnt["st"])])
        for lt in range(4):
            src = xres[:, lt * D:(lt + 1) * D]
            mu = stat[:, lt:lt + 1]
            rt = stat[:, 4 + lt:5 + lt]
            r1 = stat[:, 34:35]
            ln_stats(src, 128, mu, r1, rt)
            p.op("vector", lambda h, ob=ob, lt=lt, r1=r1: h.tensor_scalar_add(
                statc[:, ob * 8 + lt:ob * 8 + lt + 1], r1, 1.0))
            p.op("vector", lambda h, mu=mu, r1=r1: h.tensor_mul(
                stat[:, 12:13], mu, r1))
            p.op("vector", lambda h, ob=ob, lt=lt: h.tensor_scalar_mul(
                statc[:, ob * 8 + 4 + lt:ob * 8 + 5 + lt], stat[:, 12:13], -1.0))
            p.op("vector", lambda h, src=src, lt=lt, mu=mu, rt=rt: h.tensor_scalar(
                xnbB[:, lt * D:(lt + 1) * D], src, mu, rt, OP.subtract, OP.mult))
        p.wait("vector", [("sxq", ldh)])
        muh = stat[:, 8:9]
        rth = stat[:, 9:10]
        ln_stats(xhal, 3, muh, stat[:, 34:35], rth)
        svh = p.op("vector", lambda h: h.tensor_scalar(
            xnhB[0:3, :], xhal[0:3, :], muh[0:3, :], rth[0:3, :],
            OP.subtract, OP.mult))

        sc = p.cnt["ss"]
        for dch in range(KD):
            t0 = p.op("tensor", lambda h, dch=dch: h.matmul(
                ppD[:, 0:3], xnhB[0:3, dch * 128:(dch + 1) * 128],
                idf[0:3, 0:3], is_transpose=True, start=True, stop=True),
                waits=[("sv", svh), ("ss", sc)])
            sc = p.op("scalar", lambda h, dch=dch: h.copy(
                xnf[:, dch * 520:dch * 520 + 3], ppD[:, 0:3]), waits=[("st", t0)])
            for lt in range(4):
                t1 = p.op("tensor", lambda h, dch=dch, lt=lt: h.matmul(
                    ppD[:, 512:640], xnbB[:, lt * D + dch * 128:lt * D + (dch + 1) * 128],
                    idf[:], is_transpose=True, start=True, stop=True),
                    waits=[("ss", sc)])
                sc = p.op("scalar", lambda h, dch=dch, lt=lt: h.copy(
                    xnf[:, dch * 520 + 3 + lt * 128:dch * 520 + 3 + (lt + 1) * 128],
                    ppD[:, 512:640]), waits=[("st", t1)])
        ss_xnf = p.cnt["ss"]
        xnf3 = xnf[:].rearrange("p (c n) -> p c n", n=520)

        zld = p.dma("sync", lambda h: h.dma_start(
            out=ar2b[:, 0:KD * E], in_=wb_ipz[:, :]),
            waits=[("st", p.cnt["st"]), ("sxq", prev_sends["d3"]),
                   ("sv", p.cnt["sv"])])
        p.wait("tensor", [("ss", ss_xnf)])
        z_evac = {0: 0, 1: 0}
        for mz in range(KE):
            psm = ppA if mz % 2 == 0 else ppB
            for k in range(KD):
                w_ = []
                if mz == 0 and k == 0:
                    w_.append(("sq", zld))
                if k == 0:
                    w_.append(("ss", z_evac[mz % 2]))
                p.op("tensor", lambda h, mz=mz, k=k, psm=psm: h.matmul(
                    psm[:], wstr[k][:, mz * 128:(mz + 1) * 128],
                    xnf3[:, k, 3:515], start=(k == 0), stop=(k == KD - 1)),
                    waits=w_)
            stz = p.cnt["st"]
            zslot = zbs[:, (mz % 2) * LB:(mz % 2 + 1) * LB]
            scz = p.op("scalar", lambda h, mz=mz, psm=psm, zslot=zslot: h.activation(
                zslot, psm[:], AF.Silu, bias=bias0[:, KE + mz:KE + mz + 1]),
                waits=[("st", stz), ("sp", p.cnt["sp"])])
            z_evac[mz % 2] = scz
            prev_sends[("z", mz % 2)] = p.dma(
                "scalar", lambda h, ob=ob, mz=mz, zslot=zslot: h.dma_start(
                    out=spillZ[ob, :, mz * LB:(mz + 1) * LB], in_=zslot),
                waits=[("ss", scz)])

        xld = p.dma("sync", lambda h: h.dma_start(
            out=ar2b[:, 0:KD * E], in_=wb_ipx[:, :]),
            waits=[("st", p.cnt["st"])])
        x_evac = 0
        for mx in range(KE):
            for k in range(KD):
                w_ = []
                if mx == 0 and k == 0:
                    w_.append(("sq", xld))
                if k == 0:
                    w_.append(("sv", p.cnt["sv"]))
                p.op("tensor", lambda h, mx=mx, k=k: h.matmul(
                    ppD[:, 0:257], wstr[k][:, mx * 128:(mx + 1) * 128],
                    xnf3[:, k, 0:257], start=(k == 0), stop=(k == KD - 1)),
                    waits=w_)
            for k in range(KD):
                p.op("tensor", lambda h, mx=mx, k=k: h.matmul(
                    ppD[:, 512:770], wstr[k][:, mx * 128:(mx + 1) * 128],
                    xnf3[:, k, 257:515], start=(k == 0), stop=(k == KD - 1)))
            stx = p.cnt["st"]
            xc = xcvs[:, (mx % 2) * 520:(mx % 2) * 520 + 515]
            p.op("vector", lambda h, mx=mx, xc=xc: h.tensor_scalar(
                xc[:, 0:257], ppD[:, 0:257], bias0[:, mx:mx + 1], None, OP.add),
                waits=[("st", stx)])
            sc2 = p.op("vector", lambda h, mx=mx, xc=xc: h.tensor_scalar(
                xc[:, 257:515], ppD[:, 512:770], bias0[:, mx:mx + 1], None, OP.add))
            x_evac = sc2
            p.op("vector", lambda h, ob=ob, xc=xc: h.tensor_scalar(
                xc[:, 0:3], xc[:, 0:3], mcv[:, 3 * ob:3 * ob + 1], None, OP.mult))
            t = tcv[:, (mx % 2) * LB:(mx % 2 + 1) * LB]
            p.op("vector", lambda h, mx=mx, xc=xc, t=t: h.tensor_scalar(
                t, xc[:, 0:LB], cw64[:, mx * 4:mx * 4 + 1],
                cb16[:, mx:mx + 1], OP.mult, OP.add))
            for kk in range(1, 4):
                p.op("vector", lambda h, mx=mx, xc=xc, t=t, kk=kk: h.scalar_tensor_tensor(
                    t, xc[:, kk:kk + LB], cw64[:, mx * 4 + kk:mx * 4 + kk + 1], t,
                    OP.mult, OP.add))
            svt = p.cnt["sv"]
            scs = p.op("scalar", lambda h, mx=mx, t=t: h.activation(
                xsl[:, mx * LB:(mx + 1) * LB], t, AF.Silu), waits=[("sv", svt)])
            p.op("vector", lambda h, mx=mx: h.tensor_copy(
                xslb[:, mx * LB:(mx + 1) * LB], xsl[:, mx * LB:(mx + 1) * LB]),
                waits=[("ss", scs)])
        sv_x = p.cnt["sv"]
        prev_sends["spx"] = p.dma("scalar", lambda h, ob=ob: h.dma_start(
            out=spillX[ob], in_=xslb[:]), waits=[("sv", sv_x)])

        for k in range(KE):
            p.op("tensor", lambda h, k=k: h.matmul(
                ppF[0:96, :], xpT[:, k * 96:(k + 1) * 96],
                xslb[:, k * LB:(k + 1) * LB], start=(k == 0), stop=(k == KE - 1)),
                waits=[("sv", sv_x)] if k == 0 else ())
        st_xd = p.cnt["st"]
        sv_xd = p.op("vector", lambda h: h.tensor_copy(xdb[0:96, :], ppF[0:96, :]),
                     waits=[("st", st_xd), ("sxq", prev_sends["d3"])])
        sv_xdb = p.op("vector", lambda h: h.tensor_copy(xdbb[0:RK, :], xdb[0:RK, :]))

        dt_evac = {0: 0, 1: 0}
        for md in range(KE):
            psm = ppF if md % 2 == 0 else ppG
            w_ = [("ss", dt_evac[md % 2])]
            if md == 0:
                w_.append(("sv", sv_xdb))
            tm = p.op("tensor", lambda h, md=md, psm=psm: h.matmul(
                psm[:], dtpT[0:RK, md * 128:(md + 1) * 128], xdbb[0:RK, :],
                start=True, stop=True), waits=w_)
            p.op("scalar", lambda h, md=md, psm=psm: h.activation(
                dtt[:, md * LB:(md + 1) * LB], psm[:], AF.Exp,
                bias=dtb[:, md:md + 1]), waits=[("st", tm)])
            dt_evac[md % 2] = p.op("scalar", lambda h, md=md: h.activation(
                dtt[:, md * LB:(md + 1) * LB], dtt[:, md * LB:(md + 1) * LB],
                AF.Ln, bias=1.0))
        ss_dt = p.cnt["ss"]

        p.wait("vector", [("ss", ss_dt), ("sp", prev_sends["spx"])])
        for mx in range(KE):
            p.op("vector", lambda h, mx=mx: h.tensor_mul(
                xsl[:, mx * LB:(mx + 1) * LB], xsl[:, mx * LB:(mx + 1) * LB],
                dtt[:, mx * LB:(mx + 1) * LB]))
        sv_dtx = p.cnt["sv"]

        prev_sends["d1"] = p.dma("sync", lambda h, ob=ob: h.dma_start(
            out=ccF_in[ob, 0:E, :].rearrange("(pp q) n -> pp q n", pp=128),
            in_=dtt[:].rearrange("p (q n) -> p q n", n=LB)),
            waits=[("ss", ss_dt)], sem="sxq")
        prev_sends["d2"] = p.dma("sync", lambda h, ob=ob: h.dma_start(
            out=ccF_in[ob, E:2 * E, :].rearrange("(pp q) n -> pp q n", pp=128),
            in_=xsl[:].rearrange("p (q n) -> p q n", n=LB)),
            waits=[("sv", sv_dtx)], sem="sxq")
        prev_sends["d3"] = p.dma("sync", lambda h, ob=ob: h.dma_start(
            out=ccF_in[ob, 2 * E:2 * E + 32, :], in_=xdb[RK:96, :]),
            waits=[("sv", sv_xdb)], sem="sxq")
        ccF_done[ob] = p.cc(lambda h, ob=ob: h.collective_compute(
            "AllGather", OP.bypass,
            replica_groups=[[0, 1, 2, 3], [4, 5, 6, 7]],
            ins=[ccF_in[ob]], outs=[ccF_out[ob]]),
            waits=[("sxq", prev_sends["d3"])])

        if debug and ob == 0:
            w0 = p.cnt["sv"]
            s0 = p.cnt["ss"]
            p.dma("sync", lambda h: h.dma_start(out=dbgt[0, :, 0:32], in_=bias0[:]),
                  waits=[("sv", w0), ("ss", s0)])
            p.dma("sync", lambda h: h.dma_start(out=dbgt[0, :, 32:96], in_=A64[:]))
            p.dma("sync", lambda h: h.dma_start(out=dbgt[0, :, 96:112], in_=colw[:]))
            p.dma("sync", lambda h: h.dma_start(out=dbgt[0, :, 112:120],
                                                in_=statc[:, 0:8]))
            p.dma("gpsimd", lambda h: h.dma_start(out=dbgt[1], in_=xnf[:]),
                  waits=[("sv", w0), ("ss", s0)])
            p.dma("sync", lambda h: h.dma_start(out=dbgt[2, :, 0:4096],
                                                in_=xsl[:, 0:4096]))
            p.dma("sync", lambda h: h.dma_start(out=dbgt[3, :, 0:4096],
                                                in_=dtt[:, 0:4096]))
            p.dma("sync", lambda h: h.dma_start(out=dbgt[4, :, 0:512], in_=xdb[:]))
            p.dma("gpsimd", lambda h: h.dma_start(out=dbgt[5, :, 0:4096],
                                                  in_=xslb[:, 0:4096]))
            p.dma("gpsimd", lambda h: h.dma_start(out=dbgt[6, :, 0:KE * 96],
                                                  in_=xpT[:]),
                  waits=[("sv", w0)])
            p.dma("gpsimd", lambda h: h.dma_start(out=dbgt[7, :, 0:2048],
                                                  in_=dtpT[:]))

    pbE = {"pre": {"sv": 0, "ss": 0}}
    p.barrier()

    bcf = ar1[:, 0:LB]
    bchi = ar1b[:, 2 * LB:3 * LB]
    bclo = ar1b[:, 3 * LB:4 * LB]
    rdt = ar1[:, 2 * LB:6 * LB]
    rdx = ar1[:, 6 * LB:10 * LB]
    hbuf = ar1[:, 10 * LB:14 * LB]
    yq = ar1[:, 14 * LB:18 * LB]
    carry = ar1[:, 18 * LB:18 * LB + 64]
    dAs = sb("dAs", [128, 4 * LB], F32)
    usl = sb("usl", [128, 2 * LB], F32)
    yin = sb("yin", [128, 4 * LB], BF16)
    ysl2 = sb("ysl2", [128, 2 * LB], F32)
    xsz = sb("xsz", [128, 8 * LB], BF16)
    xmidt = ar3[:, 0:4 * D]
    ohi = ar3[:, 4 * D:4 * D + KD * LB]
    xoslot = ar3[:, 4 * D + KD * LB:4 * D + KD * LB + D]
    xqf = sb("xqf", [128, KD * 516], BF16)

    hq = sb("hq", [128, 4 * LB], BF16)
    mqt = sb("mqt", [128, LB], F32)
    shb = sb("shb", [128, LB], F32)
    hmini = sb("hmini", [128, D], F32)
    xqtk = hmini
    hslot = sb("hslot", [128, 2 * LB], F32)
    xqf3 = xqf[:].rearrange("p (c n) -> p c n", n=516)

    p.op("vector", lambda h: h.memset(carry[:], 0.0))

    def scan_gen(r):
        for blkslot in range(4):
            gw = [("scc", ccF_done[r]), ("ss", scan_last["ss"]),
                  ("sv", scan_last["sv"])]
            gd = []
            for qq in range(4):
                gd.append(p.dma("gpsimd", lambda h, r=r, blkslot=blkslot, qq=qq:
                    h.indirect_dma_start(
                        out=rdt[:, qq * LB:(qq + 1) * LB], out_offset=None,
                        in_=ccF_out[r],
                        in_offset=IndirectOffsetOnAxis(
                            ap=idxt[:, IDXC[("dt", blkslot, qq)]:IDXC[("dt", blkslot, qq)] + 1],
                            axis=0)), waits=gw if qq == 0 else ()))
                gd.append(p.dma("gpsimd", lambda h, r=r, blkslot=blkslot, qq=qq:
                    h.indirect_dma_start(
                        out=rdx[:, qq * LB:(qq + 1) * LB], out_offset=None,
                        in_=ccF_out[r],
                        in_offset=IndirectOffsetOnAxis(
                            ap=idxt[:, IDXC[("dx", blkslot, qq)]:IDXC[("dx", blkslot, qq)] + 1],
                            axis=0))))
            p.dma("gpsimd", lambda h, r=r, blkslot=blkslot: h.indirect_dma_start(
                out=bcf[0:32, :], out_offset=None, in_=ccF_out[r],
                in_offset=IndirectOffsetOnAxis(
                    ap=idxt[0:32, IDXC[("bcs", blkslot)]:IDXC[("bcs", blkslot)] + 1],
                    axis=0)))
            gall = p.cnt["sg"]
            ch = p.op("vector", lambda h: h.tensor_copy(bchi[0:32, :], bcf[0:32, :]),
                      waits=[("sg", gall), ("st", p.cnt["st"])])
            cl = p.op("vector", lambda h: h.tensor_tensor(
                bclo[0:32, :], bcf[0:32, :], bchi[0:32, :], OP.subtract))

            for qt in range(4):
                for jp in range(2):
                    selw = [("sv", p.cnt["sv"])]
                    for j2 in range(2):
                        s = 4 * qt + 2 * jp + j2
                        dB = ppS1[:, j2 * 512:(j2 + 1) * 512]
                        dC = ppS2[:, j2 * 512:(j2 + 1) * 512]
                        p.op("tensor", lambda h, s=s, dB=dB: h.matmul(
                            dB, selt[0:32, s * 128:(s + 1) * 128],
                            bchi[0:32, :], start=True, stop=False),
                            waits=selw if j2 == 0 else ())
                        p.op("tensor", lambda h, s=s, dB=dB: h.matmul(
                            dB, selt[0:32, s * 128:(s + 1) * 128],
                            bclo[0:32, :], start=False, stop=True))
                        p.op("tensor", lambda h, s=s, dC=dC: h.matmul(
                            dC, selt[0:32, (16 + s) * 128:(17 + s) * 128],
                            bchi[0:32, :], start=True, stop=False))
                        p.op("tensor", lambda h, s=s, dC=dC: h.matmul(
                            dC, selt[0:32, (16 + s) * 128:(17 + s) * 128],
                            bclo[0:32, :], start=False, stop=True))
                    stsel = p.cnt["st"]
                    for qq in range(4):
                        for j2 in range(2):
                            s = 4 * qt + 2 * jp + j2
                            par = (qq * 2 + j2) % 4
                            da = dAs[:, par * LB:(par + 1) * LB]
                            uu = usl[:, j2 * LB:(j2 + 1) * LB]
                            sw_ = ([("sg", gall)]
                                   if (qt == 0 and jp == 0 and qq == 0 and j2 == 0)
                                   else [])
                            sw_.append(("sv", last_scan_par[par]))
                            se = p.op("scalar", lambda h, qq=qq, s=s, da=da: h.activation(
                                da, rdt[:, qq * LB:(qq + 1) * LB], AF.Exp,
                                scale=A64[:, qq * S + s:qq * S + s + 1]), waits=sw_)
                            uw = [("ss", se), ("st", stsel)]
                            if qt == 0 and jp == 0 and qq == 0 and j2 == 0:
                                uw.append(("syq", p.cnt["syq"]))
                            p.op("vector", lambda h, qq=qq, j2=j2, uu=uu: h.tensor_mul(
                                uu, rdx[:, qq * LB:(qq + 1) * LB],
                                ppS1[:, j2 * 512:(j2 + 1) * 512]), waits=uw)
                            last_scan_par[par] = p.op("vector",
                                lambda h, qq=qq, s=s, j2=j2, da=da, uu=uu: h.tensor_tensor_scan(
                                    hbuf[:, j2 * LB:(j2 + 1) * LB], da, uu,
                                    carry[:, qq * S + s:qq * S + s + 1],
                                    OP.mult, OP.add))
                        p.op("vector", lambda h, qq=qq, qt=qt, jp=jp: h.tensor_copy(
                            carry[:, qq * S + qt * 4 + 2 * jp:qq * S + qt * 4 + 2 * jp + 2],
                            hbuf[:, 0:2 * LB].rearrange("p (s n) -> p s n", n=LB)[:, :, LB - 1:LB]))
                        p.op("vector", lambda h: h.tensor_mul(
                            hbuf[:, 0:2 * LB], hbuf[:, 0:2 * LB], ppS2[:, 0:1024]))
                        h3 = hbuf[:, 0:2 * LB].rearrange("p (s n) -> p n s", n=LB)
                        if qt == 0 and jp == 0:
                            p.op("vector", lambda h, qq=qq, h3=h3: h.tensor_reduce(
                                yq[:, qq * LB:(qq + 1) * LB], h3, AX.X, OP.add))
                        else:
                            p.op("vector", lambda h, h3=h3: h.tensor_reduce(
                                hbuf[:, 2 * LB:3 * LB], h3, AX.X, OP.add))
                            p.op("vector", lambda h, qq=qq: h.tensor_add(
                                yq[:, qq * LB:(qq + 1) * LB],
                                yq[:, qq * LB:(qq + 1) * LB], hbuf[:, 2 * LB:3 * LB]))
                    yield
            svy = p.cnt["sv"]
            scan_last["ss"] = p.cnt["ss"]
            scan_last["sv"] = svy
            ysend_tok[0] = p.dma("sync", lambda h, r=r, blkslot=blkslot: h.dma_start(
                out=ccR_in[r, blkslot * 512:(blkslot + 1) * 512, :]
                    .rearrange("(pp q) n -> pp q n", pp=128),
                in_=yq[:].rearrange("p (q n) -> p q n", n=LB)),
                waits=[("sv", svy)], sem="syq")
        sy = p.cnt["syq"]
        sends_end[r] = sy
        ccR_done[r] = p.cc(lambda h, r=r: h.collective_compute(
            "AllGather", OP.bypass,
            replica_groups=[[0, 1, 2, 3], [4, 5, 6, 7]],
            ins=[ccR_in[r]], outs=[ccR_out[r]]), waits=[("syq", sy)])

    def post_a_gen(r):
        ob = r
        opld1 = p.dma("sync", lambda h: h.dma_start(
            out=ar2b[:, 0:8 * D], in_=wb_opp[:, 0:8 * D]),
            waits=[("st", p.cnt["st"])], sem="sow")
        opld2 = p.dma("sync", lambda h: h.dma_start(
            out=ar2b[:, 8 * D:KE * D], in_=wb_opp[:, 8 * D:KE * D]),
            waits=[("sow", opld1)], sem="sow")
        pbe = pbE.get(r - 2, pbE.get("pre", {"sv": 0, "ss": 0}))
        p.wait("tensor", [("sv", pbe["sv"]), ("ss", pbe["ss"])])
        pe = paE.get(r - 1, 0)
        mul_tok = {-3: pe, -2: pe, -1: pe}

        lz_tok = {}

        def build_yin(k, slot_war, pno):
            par = k % 2
            spar = k % 4
            gsem = "sgy%d" % par
            ys = ysl2[:, par * LB:(par + 1) * LB]
            gy = p.dma("gpsimd", lambda h, r=r, k=k, ys=ys: h.indirect_dma_start(
                out=ys, out_offset=None, in_=ccR_out[r],
                in_offset=IndirectOffsetOnAxis(
                    ap=idxt[:, IDXC[("y", k)]:IDXC[("y", k)] + 1], axis=0)),
                waits=[("scc", ccR_done[r]), ("sv", mul_tok[pno * KE + k - 2])],
                sem=gsem)
            xs = xsz[:, spar * LB:(spar + 1) * LB]
            zs = xsz[:, (4 + spar) * LB:(5 + spar) * LB]
            ki = pno * KE + k
            lsem = "sly%d" % ((k // 2) % 2)
            if k % 2 == 0:
                s0 = k % 4
                p.dma("scalar", lambda h, ob=ob, k=k, s0=s0: h.dma_start(
                    out=xsz[:, s0 * LB:(s0 + 2) * LB],
                    in_=spillX[ob, :, k * LB:(k + 2) * LB]),
                    waits=[("sv", mul_tok[ki - 3])], sem=lsem)
                lz_tok[ki] = lz_tok[ki + 1] = p.dma(
                    "scalar", lambda h, ob=ob, k=k, s0=s0: h.dma_start(
                        out=xsz[:, (4 + s0) * LB:(6 + s0) * LB],
                        in_=spillZ[ob, :, k * LB:(k + 2) * LB]), sem=lsem)
            lz = lz_tok[ki]
            p.op("vector", lambda h, k=k, ys=ys, xs=xs: h.scalar_tensor_tensor(
                ys, xs, Dc16[:, k:k + 1], ys, OP.mult, OP.add),
                waits=[(gsem, gy), (lsem, lz)])
            w_ = [(lsem, lz)]
            if slot_war:
                w_.append(("st", slot_war))
            yslot = yin[:, spar * LB:(spar + 1) * LB]
            sv_y = p.op("vector", lambda h, ys=ys, zs=zs, yslot=yslot: h.tensor_mul(
                yslot, ys, zs), waits=w_)
            mul_tok[pno * KE + k] = sv_y
            return yslot, sv_y

        mmk = {}
        for k in range(KE):
            yslot, sv_y = build_yin(k, mmk.get(k - 4), 0)
            for m in range(4):
                p.op("tensor", lambda h, k=k, m=m, yslot=yslot: h.matmul(
                    pbank4[m], ar2b[:, k * D + m * 128:k * D + (m + 1) * 128],
                    yslot, start=(k == 0), stop=(k == KE - 1)),
                    waits=([("sow", opld1 if k < 8 else opld2), ("sv", sv_y)]
                           if m == 0 and k % 8 == 0 else
                           ([("sv", sv_y)] if m == 0 else ())))
            mmk[k] = p.cnt["st"]
            yield
        stmm = p.cnt["st"]
        ev1 = 0
        for m in range(4):
            ev1 = p.op("vector", lambda h, m=m: h.tensor_copy(
                ohi[:, m * LB:(m + 1) * LB], pbank4[m]),
                waits=[("st", stmm)] if m == 0 else ())
        yield
        mmk2nd = {}
        for k in range(KE):
            yslot, sv_y = build_yin(k, mmk2nd.get(k - 4, mmk.get(12 + k)), 1)
            for m in range(4):
                w_ = []
                if m == 0:
                    w_ = [("sv", sv_y)] + ([("sv", ev1)] if k == 0 else [])
                p.op("tensor", lambda h, k=k, m=m, yslot=yslot: h.matmul(
                    pbank4[m], ar2b[:, k * D + (4 + m) * 128:k * D + (5 + m) * 128],
                    yslot, start=(k == 0), stop=(k == KE - 1)), waits=w_)
            mmk2nd[k] = p.cnt["st"]
            yield
        stmm = p.cnt["st"]
        ev = 0
        for m in range(4):
            ev = p.op("vector", lambda h, m=m: h.tensor_copy(
                ohi[:, (4 + m) * LB:(5 + m) * LB], pbank4[m]),
                waits=[("st", stmm)] if m == 0 else ())
        svev = ev

        ldxr = p.dma("sync", lambda h, ob=ob: h.dma_start(
            out=xmidt.rearrange("p (t n) -> p t n", n=D),
            in_=x_own[ob, 3:3 + LB, :].rearrange("(t p) n -> p t n", p=128)),
            waits=[("sv", p.cnt["sv"]), ("sp", p.cnt["sp"]),
                   ("sxq", p.cnt["sxq"])], sem="sxq")
        xm_done = 0
        for lt in range(4):
            pdst = ppD if lt % 2 == 0 else ppE
            for m in range(KD):
                w_ = []
                if m == 0:
                    w_.append(("sv", svev if lt < 2 else xm_done))
                p.op("tensor", lambda h, lt=lt, m=m, pdst=pdst: h.matmul(
                    pdst[:, m * 128:(m + 1) * 128],
                    ohi[:, m * LB + lt * 128:m * LB + (lt + 1) * 128],
                    idf[:], is_transpose=True, start=True, stop=True), waits=w_)
            sttr = p.cnt["st"]
            p.op("vector", lambda h, ob=ob, lt=lt: h.tensor_scalar(
                xoslot[:, 0:D], xmidt[:, lt * D:(lt + 1) * D],
                statc[:, ob * 8 + lt:ob * 8 + lt + 1],
                statc[:, ob * 8 + 4 + lt:ob * 8 + 5 + lt], OP.mult, OP.add),
                waits=[("sxq", ldxr)] if lt == 0 else ())
            xm_done = p.op("vector", lambda h, lt=lt, pdst=pdst: h.scalar_tensor_tensor(
                xmidt[:, lt * D:(lt + 1) * D], pdst[:, 0:D], rsc[:, 0:1],
                xoslot[:, 0:D], OP.mult, OP.add), waits=[("st", sttr)])
            yield
        svxm = xm_done
        spm = p.dma("scalar", lambda h, ob=ob: h.dma_start(
            out=spillM[ob], in_=xmidt[:]), waits=[("sv", svxm)])
        d1 = p.dma("sync", lambda h, r=r: h.dma_start(
            out=ccX_in[r, 0:1, :], in_=xmidt[0:1, 0:D]), waits=[("sv", svxm)],
            sem="sxq")
        d2 = p.dma("sync", lambda h, r=r: h.dma_start(
            out=ccX_in[r, 1:2, :], in_=xmidt[127:128, 3 * D:4 * D]), sem="sxq")
        ccX_done[r] = p.cc(lambda h, r=r: h.collective_compute(
            "AllGather", OP.bypass,
            replica_groups=[[0, 1, 2, 3], [4, 5, 6, 7]],
            ins=[ccX_in[r]], outs=[ccX_out[r * 8:(r + 1) * 8, :]]),
            waits=[("sxq", d2)])
        paE[r] = mul_tok[2 * KE - 1]

    def post_b_gen(r):
        ob = r
        need_right = ccX_done[min(r + 1, NR - 1)] if (4 * r + 3) + 1 <= NB - 1 else ccX_done[r]
        gh = p.dma("gpsimd", lambda h, ob=ob: h.indirect_dma_start(
            out=hmini[0:2, :], out_offset=None, in_=ccX_out,
            in_offset=IndirectOffsetOnAxis(
                ap=idxt[0:2, IDXC[("xh", ob)]:IDXC[("xh", ob)] + 1], axis=0)),
            waits=[("scc", need_right), ("sv", p.cnt["sv"])], sem="shx")
        p.wait("vector", [("shx", gh)])
        muh = stat[:, 8:9]
        r3h = stat[:, 9:10]
        ln_stats(hmini, 2, muh, r3h)
        p.op("vector", lambda h: h.tensor_scalar(
            hmini[0:2, :], hmini[0:2, :], muh[0:2, :], None, OP.subtract))
        p.op("vector", lambda h: h.tensor_reduce(
            stat[0:2, 10:11], hmini[0:2, :], AX.X, OP.max, apply_absolute_value=True))
        p.op("vector", lambda h: h.tensor_mul(stat[0:2, 10:11], stat[0:2, 10:11], r3h[0:2, :]))
        p.op("vector", lambda h: h.tensor_scalar_max(stat[0:2, 10:11], stat[0:2, 10:11], 1e-5))
        p.op("vector", lambda h: h.reciprocal(stat[0:2, 11:12], stat[0:2, 10:11]))
        p.op("vector", lambda h: h.tensor_scalar_mul(stat[0:2, 11:12], stat[0:2, 11:12], 127.0))
        p.op("vector", lambda h: h.reciprocal(stat[0:2, 12:13], stat[0:2, 11:12]))
        p.op("vector", lambda h: h.tensor_mul(stat[0:2, 14:15], r3h[0:2, :], stat[0:2, 11:12]))
        p.op("vector", lambda h: h.tensor_scalar(
            hmini[0:2, :], hmini[0:2, :], stat[0:2, 14:15], magic[0:2, 0:1],
            OP.mult, OP.add))
        cq = p.op("vector", lambda h: h.tensor_scalar(
            hmini[0:2, :], hmini[0:2, :], magic[0:2, 0:1], stat[0:2, 12:13],
            OP.subtract, OP.mult))
        hvm = cq
        for dch in range(KD):
            t0 = p.op("tensor", lambda h, dch=dch: h.matmul(
                ppD[:, 0:2], hmini[0:2, dch * 128:(dch + 1) * 128],
                idf[0:2, 0:2], is_transpose=True, start=True, stop=True),
                waits=[("sv", hvm)])
            p.op("vector", lambda h, dch=dch, ob=ob: h.tensor_scalar(
                xqf[:, dch * 516:dch * 516 + 1], ppD[:, 0:1],
                mfn[:, 2 * ob:2 * ob + 1], None, OP.mult), waits=[("st", t0)])
            hvm = p.op("vector", lambda h, dch=dch, ob=ob: h.tensor_scalar(
                xqf[:, dch * 516 + 513:dch * 516 + 514], ppD[:, 1:2],
                mfn[:, 2 * ob + 1:2 * ob + 2], None, OP.mult))

        ldm = p.dma("scalar", lambda h, ob=ob: h.dma_start(
            out=xmidt[:], in_=spillM[ob]),
            waits=[("sv", p.cnt["sv"]), ("sp", p.cnt["sp"])])
        p.wait("vector", [("sp", ldm)])
        for lt in range(4):
            src = xmidt[:, lt * D:(lt + 1) * D]
            mu3 = stat[:, 0:1]
            r3 = stat[:, 1:2]
            ln_stats(src, 128, mu3, r3)
            p.op("vector", lambda h, src=src, mu3=mu3: h.tensor_scalar(
                xoslot[:, 0:D], src, mu3, None, OP.subtract))
            p.op("vector", lambda h: h.tensor_reduce(
                stat[:, 10:11], xoslot[:, 0:D], AX.X, OP.max,
                apply_absolute_value=True))
            p.op("vector", lambda h, r3=r3: h.tensor_mul(stat[:, 10:11], stat[:, 10:11], r3))
            p.op("vector", lambda h: h.tensor_scalar_max(stat[:, 10:11], stat[:, 10:11], 1e-5))
            p.op("vector", lambda h: h.reciprocal(stat[:, 11:12], stat[:, 10:11]))
            p.op("vector", lambda h: h.tensor_scalar_mul(stat[:, 11:12], stat[:, 11:12], 127.0))
            p.op("vector", lambda h: h.reciprocal(stat[:, 12:13], stat[:, 11:12]))
            p.op("vector", lambda h, r3=r3: h.tensor_mul(stat[:, 14:15], r3, stat[:, 11:12]))
            p.op("vector", lambda h: h.tensor_scalar(
                xoslot[:, 0:D], xoslot[:, 0:D], stat[:, 14:15], magic[:, 0:1],
                OP.mult, OP.add))
            cq2 = p.op("vector", lambda h: h.tensor_scalar(
                xqtk[:], xoslot[:, 0:D], magic[:, 0:1], stat[:, 12:13],
                OP.subtract, OP.mult))
            sc = p.cnt["ss"]
            for dch in range(KD):
                t1 = p.op("tensor", lambda h, dch=dch, lt=lt: h.matmul(
                    ppD[:, 512:640], xqtk[:, dch * 128:(dch + 1) * 128],
                    idf[:], is_transpose=True, start=True, stop=True),
                    waits=[("sv", cq2), ("ss", sc)])
                sc = p.op("scalar", lambda h, dch=dch, lt=lt: h.copy(
                    xqf[:, dch * 516 + 1 + lt * 128:dch * 516 + 1 + (lt + 1) * 128],
                    ppD[:, 512:640]), waits=[("st", t1)])
            yield
        ss_xq = p.cnt["ss"]
        yield

        mzero = p.op("vector", lambda h: h.memset(mqt[:], 0.0))
        p.wait("tensor", [("ss", ss_xq)])
        convdone = 0
        for mh in range(2):
            w1ld = p.dma("sync", lambda h, mh=mh: h.dma_start(
                out=ar2b[:, 0:KD * E],
                in_=wb_w1p[:, mh * KD * E:(mh + 1) * KD * E]),
                waits=[("st", p.cnt["st"])])
            for mi in range(KE):
                m = mh * KE + mi
                for k in range(KD):
                    w_ = []
                    if mi == 0 and k == 0:
                        w_.append(("sq", w1ld))
                    if k == 0:
                        w_.append(("sv", convdone))
                    p.op("tensor", lambda h, mi=mi, k=k: h.matmul(
                        ppD[:, 0:257], wstr[k][:, mi * 128:(mi + 1) * 128],
                        xqf3[:, k, 0:257], start=(k == 0), stop=(k == KD - 1)),
                        waits=w_)
                for k in range(KD):
                    p.op("tensor", lambda h, mi=mi, k=k: h.matmul(
                        ppD[:, 512:769], wstr[k][:, mi * 128:(mi + 1) * 128],
                        xqf3[:, k, 257:514], start=(k == 0), stop=(k == KD - 1)))
                stw1 = p.cnt["st"]
                t = hslot[:, (m % 2) * LB:(m % 2 + 1) * LB]
                p.op("vector", lambda h, m=m, t=t: h.tensor_scalar(
                    t[:, 0:257], ppD[:, 0:257], fcw[:, m * 3:m * 3 + 1],
                    fcb[:, m:m + 1], OP.mult, OP.add), waits=[("st", stw1)])
                p.op("vector", lambda h, m=m, t=t: h.tensor_scalar(
                    t[:, 257:512], ppD[:, 512:767], fcw[:, m * 3:m * 3 + 1],
                    fcb[:, m:m + 1], OP.mult, OP.add))
                p.op("vector", lambda h, m=m, t=t: h.scalar_tensor_tensor(
                    t[:, 0:256], ppD[:, 1:257], fcw[:, m * 3 + 1:m * 3 + 2],
                    t[:, 0:256], OP.mult, OP.add))
                p.op("vector", lambda h, m=m, t=t: h.scalar_tensor_tensor(
                    t[:, 256:512], ppD[:, 512:768], fcw[:, m * 3 + 1:m * 3 + 2],
                    t[:, 256:512], OP.mult, OP.add))
                p.op("vector", lambda h, m=m, t=t: h.scalar_tensor_tensor(
                    t[:, 0:255], ppD[:, 2:257], fcw[:, m * 3 + 2:m * 3 + 3],
                    t[:, 0:255], OP.mult, OP.add))
                convdone = p.op("vector", lambda h, m=m, t=t: h.scalar_tensor_tensor(
                    t[:, 255:512], ppD[:, 512:769], fcw[:, m * 3 + 2:m * 3 + 3],
                    t[:, 255:512], OP.mult, OP.add))
                hqs = hq[:, (m % 4) * LB:((m % 4) + 1) * LB]
                scs = p.op("scalar", lambda h, m=m, t=t, hqs=hqs: h.activation(
                    hqs, t, AF.Silu),
                    waits=[("sv", convdone), ("sp", p.cnt["sp"])])
                p.dma("scalar", lambda h, ob=ob, m=m, hqs=hqs: h.dma_start(
                    out=spillH[ob, :, m * LB:(m + 1) * LB],
                    in_=hqs), waits=[("ss", scs)])
                sab = p.op("scalar", lambda h, hqs=hqs: h.activation(
                    hslot[:, LB:2 * LB], hqs, AF.Abs))
                p.op("vector", lambda h, m=m: h.tensor_max(
                    mqt[:], mqt[:], hslot[:, LB:2 * LB]),
                    waits=[("ss", sab), ("sv", mzero)] if m == 0 else [("ss", sab)])
                yield
        svmx = p.cnt["sv"]
        yield

        mcols = stat[:, 24:28]
        prevv = svmx
        for lt in range(4):
            pdst = ppD if lt % 2 == 0 else ppE
            tt = p.op("tensor", lambda h, lt=lt, pdst=pdst: h.matmul(
                pdst[:, 0:128], mqt[:, lt * 128:(lt + 1) * 128], idf[:],
                is_transpose=True, start=True, stop=True),
                waits=[("sv", prevv), ("st", p.cnt["st"])] if lt == 0
                else [("sv", prevv)])
            p.op("vector", lambda h, pdst=pdst: h.tensor_reduce(
                stat[:, 12:13], pdst[:, 0:128], AX.X, OP.max), waits=[("st", tt)])
            p.op("vector", lambda h, lt=lt: h.tensor_scalar_max(
                mcols[:, lt:lt + 1], stat[:, 12:13], 1e-5))
            p.op("vector", lambda h, lt=lt: h.reciprocal(
                stat[:, 13:14], mcols[:, lt:lt + 1]))
            cshc = p.op("vector", lambda h: h.tensor_scalar_mul(
                stat[:, 13:14], stat[:, 13:14], 127.0))
            prevv = cshc
            p.op("tensor", lambda h, lt=lt: h.matmul(
                ppP1[0:1, 512 + lt * 128:512 + (lt + 1) * 128], stat[:, 13:14],
                idf[:], is_transpose=True, start=True, stop=True),
                waits=[("sv", cshc)])
        crow = p.op("vector", lambda h: h.tensor_copy(
            k2st[0:1, 0:LB], ppP1[0:1, 512:512 + LB]), waits=[("st", p.cnt["st"])])
        tb = bcast_row(ppP2[:, 512:1024], LB, 0, crow)
        csh = p.op("vector", lambda h: h.tensor_copy(shb[:], ppP2[:, 512:1024]),
                   waits=[("st", tb)])

        sp_w1end = p.cnt["sp"]
        lh_tok = 0
        lh2_tok = 0
        shs_tok = {}
        mmB = {15: 0}
        svev2 = 0
        for hh in range(2):
            w2ld = p.dma("sync", lambda h, hh=hh: h.dma_start(
                out=ar2b[:, 0:16 * D],
                in_=wb_w2p[:, hh * 16 * D:(hh + 1) * 16 * D]),
                waits=[("st", p.cnt["st"])])
            mmA = {}
            for k in range(16):
                kk = hh * 16 + k
                spar = k % 4
                hqs = hq[:, spar * LB:(spar + 1) * LB]
                if k % 4 == 0:
                    g = kk // 4
                    lw = [("slh", lh_tok)]
                    if kk == 0:
                        lw.append(("sp", sp_w1end))
                        lw.append(("st", p.cnt["st"]))
                    elif k == 0:
                        lw.append(("st", mmB[15]))
                    else:
                        lw.append(("st", mmA[k - 1]))
                    if g - 1 in shs_tok:
                        lw.append(("shs", shs_tok[g - 1]))
                    lh_tok = p.dma("scalar", lambda h, ob=ob, kk=kk: h.dma_start(
                        out=hq[:, 0:4 * LB],
                        in_=spillH[ob, :, kk * LB:(kk + 4) * LB]),
                        waits=lw, sem="slh")
                p.op("vector", lambda h, hqs=hqs: h.tensor_mul(
                    hslot[:, 0:LB], hqs, shb[:]),
                    waits=[("slh", lh_tok)]
                    + ([("sv", csh)] if kk == 0 else []))
                svr = p.op("vector", lambda h, hqs=hqs: h.tensor_scalar(
                    hqs, hslot[:, 0:LB], magic[:, 0:1], magic[:, 0:1],
                    OP.add, OP.subtract))
                if k % 4 == 3:
                    shs_tok[kk // 4] = p.dma(
                        "scalar", lambda h, ob=ob, kk=kk: h.dma_start(
                            out=spillH2[ob, :, (kk - 3) * LB:(kk + 1) * LB],
                            in_=hq[:, 0:4 * LB]),
                        waits=[("sv", svr)], sem="shs")
                for m in range(4):
                    w_ = []
                    if m == 0:
                        w_.append(("sv", svr))
                        if k == 0:
                            w_.append(("sq", w2ld))
                            if hh == 0:
                                w_.append(("ss", ss_xq))
                            else:
                                w_.append(("sv", svev2))
                    p.op("tensor", lambda h, k=k, m=m, hqs=hqs: h.matmul(
                        pbank4[m], ar2b[:, k * D + m * 128:k * D + (m + 1) * 128],
                        hqs, start=(k == 0), stop=(k == 15)), waits=w_)
                mmA[k] = p.cnt["st"]
                yield
            stmm = p.cnt["st"]
            evA = 0
            for m in range(4):
                if hh == 0:
                    evA = p.op("vector", lambda h, m=m: h.tensor_copy(
                        ohi[:, m * LB:(m + 1) * LB], pbank4[m]),
                        waits=[("st", stmm)] if m == 0 else ())
                else:
                    evA = p.op("vector", lambda h, m=m: h.tensor_add(
                        ohi[:, m * LB:(m + 1) * LB],
                        ohi[:, m * LB:(m + 1) * LB], pbank4[m]),
                        waits=[("st", stmm)] if m == 0 else ())
            yield
            mmB = {}
            for k in range(16):
                kk = hh * 16 + k
                spar = k % 4
                hqs = hq[:, spar * LB:(spar + 1) * LB]
                if k % 4 == 0:
                    g = kk // 4
                    lw = [("slh2", lh2_tok), ("shs", shs_tok[hh * 4 + 3]),
                          ("st", mmA[15] if k == 0 else mmB[k - 1])]
                    lh2_tok = p.dma("scalar", lambda h, ob=ob, kk=kk: h.dma_start(
                        out=hq[:, 0:4 * LB],
                        in_=spillH2[ob, :, kk * LB:(kk + 4) * LB]),
                        waits=lw, sem="slh2")
                for m in range(4):
                    w_ = []
                    if m == 0:
                        w_.append(("slh2", lh2_tok))
                        if k == 0:
                            w_.append(("sv", evA))
                    p.op("tensor", lambda h, k=k, m=m, hqs=hqs: h.matmul(
                        pbank4[m],
                        ar2b[:, k * D + (4 + m) * 128:k * D + (5 + m) * 128],
                        hqs, start=(k == 0), stop=(k == 15)), waits=w_)
                mmB[k] = p.cnt["st"]
                yield
            stmm = p.cnt["st"]
            for m in range(4):
                if hh == 0:
                    svev2 = p.op("vector", lambda h, m=m: h.tensor_copy(
                        ohi[:, (4 + m) * LB:(5 + m) * LB], pbank4[m]),
                        waits=[("st", stmm)] if m == 0 else ())
                else:
                    svev2 = p.op("vector", lambda h, m=m: h.tensor_add(
                        ohi[:, (4 + m) * LB:(5 + m) * LB],
                        ohi[:, (4 + m) * LB:(5 + m) * LB], pbank4[m]),
                        waits=[("st", stmm)] if m == 0 else ())
            yield

        ldm2 = p.dma("scalar", lambda h, ob=ob: h.dma_start(
            out=xmidt[:], in_=spillM[ob]),
            waits=[("sv", p.cnt["sv"]), ("sp", p.cnt["sp"])])
        fin = 0
        cf = 0
        lastout = 0
        for lt in range(4):
            pdst = ppD if lt % 2 == 0 else ppE
            p.op("vector", lambda h, lt=lt: h.tensor_scalar_mul(
                stat[:, 17:18], mcols[:, lt:lt + 1], 1.0 / 127.0))
            cf = p.op("vector", lambda h: h.tensor_mul(stat[:, 17:18], stat[:, 17:18], rw2))
            for m in range(KD):
                w_ = [("sv", svev2 if lt < 2 else fin)] if m == 0 else []
                p.op("tensor", lambda h, lt=lt, m=m, pdst=pdst: h.matmul(
                    pdst[:, m * 128:(m + 1) * 128],
                    ohi[:, m * LB + lt * 128:m * LB + (lt + 1) * 128],
                    idf[:], is_transpose=True, start=True, stop=True), waits=w_)
            sttr = p.cnt["st"]
            fin = p.op("vector", lambda h, lt=lt, pdst=pdst: h.scalar_tensor_tensor(
                xoslot[:, 0:D], pdst[:, 0:D], stat[:, 17:18],
                xmidt[:, lt * D:(lt + 1) * D], OP.mult, OP.add),
                waits=[("st", sttr), ("sp", ldm2), ("sv", cf), ("sp", lastout)])
            lastout = p.dma("scalar", lambda h, ob=ob, lt=lt: h.dma_start(
                out=y_out[ob, lt * 128:(lt + 1) * 128, :], in_=xoslot[:, 0:D]),
                waits=[("sv", fin)])
            yield
        pbE[r] = {"sv": p.cnt["sv"], "ss": p.cnt["ss"]}

    def mkstep(g):
        done = [g is None]

        def s():
            if done[0]:
                return False
            try:
                next(g)
                return True
            except StopIteration:
                done[0] = True
                return False
        return s

    for r in range(NR + 2):
        sg_ = mkstep(scan_gen(r) if r < NR else None)
        pa_ = mkstep(post_a_gen(r - 1) if 0 <= r - 1 < NR else None)
        pb_ = mkstep(post_b_gen(r - 2) if 0 <= r - 2 < NR else None)
        progress = True
        while progress:
            progress = sg_()
            for _ in range(5):
                if pa_():
                    progress = True
                elif pb_():
                    progress = True

    p.wait("vector", [("sp", p.cnt["sp"]), ("sq", p.cnt["sq"]),
                      ("sg", p.cnt["sg"]), ("syq", p.cnt["syq"]),
                      ("sxq", p.cnt["sxq"])])

    with nc.Block() as block:
        p.finalize(block)
    es.close()
    return nc


def _mk_sel32():
    sel = np.zeros((32, 32, 128), np.float32)
    for s in range(32):
        sel[s, s, :] = 1.0
    return np.ascontiguousarray(sel.transpose(1, 0, 2).reshape(32, 32 * 128)
                                ).astype(ml_dtypes.bfloat16)


def host_prep(inputs, NB):
    NR = NB // 4
    OB = NB // 4
    L = NB * LB
    x = np.ascontiguousarray(inputs["L0_attn_seq"], dtype=np.float32)
    f32 = lambda v: np.ascontiguousarray(np.asarray(v), dtype=np.float32)

    shared = {
        "in_projT": f32(np.asarray(inputs["L0_attn_in_proj_w"]).T),
        "out_projT": f32(np.asarray(inputs["L0_attn_out_proj_w"]).T),
        "w1T": f32(np.asarray(inputs["L0_ffn_w1"]).T),
        "w2T": f32(np.asarray(inputs["L0_ffn_w2"]).T),
        "x_projTd": f32(np.asarray(inputs["L0_attn_x_proj_w"]).T),
        "dt_projTd": f32(np.asarray(inputs["L0_attn_dt_proj_w"]).T),
        "ln_g8": f32(np.asarray(inputs["L0_attn_ln_g"]).reshape(KD, 128).T),
        "ln_b8": f32(np.asarray(inputs["L0_attn_ln_b"]).reshape(KD, 128).T),
        "conv_w64": f32(np.asarray(inputs["L0_attn_conv_w"]).reshape(KE, 128, 4)
                        .transpose(1, 0, 2).reshape(128, KE * 4)),
        "conv_b16": f32(np.asarray(inputs["L0_attn_conv_b"]).reshape(KE, 128).T),
        "dtb16": f32(np.asarray(inputs["L0_attn_dt_proj_b"]).reshape(KE, 128).T),
        "D16": f32(np.asarray(inputs["L0_attn_D"]).reshape(KE, 128).T),
        "fcw96": f32(np.asarray(inputs["L0_ffn_conv_w"])[:, 0, :].reshape(KH, 128, 3)
                     .transpose(1, 0, 2).reshape(128, KH * 3)),
        "fcb32": f32(np.asarray(inputs["L0_ffn_conv_b"]).reshape(KH, 128).T),
        "rs_col": f32(np.broadcast_to(np.asarray(inputs["L0_attn_res_scale"],
                                                 np.float32).reshape(1, 1), (128, 1))),
        "identc": np.eye(128, dtype=ml_dtypes.bfloat16),
        "identf": np.eye(128, dtype=np.float32),
        "sel32": _mk_sel32(),
        "onesk2": np.concatenate([np.ones((1, 128)), np.zeros((1, 128))]
                                 ).astype(ml_dtypes.bfloat16),
    }

    in_maps = []
    for c in range(8):
        b, m = c // 4, c % 4
        xb = x[b, :L].reshape(NB, LB, D)
        own = []
        for ob in range(OB):
            g = 4 * ob + m
            blk = np.zeros((LB + 3, D), np.float32)
            blk[3:] = xb[g]
            if g > 0:
                blk[0:3] = xb[g - 1][LB - 3:]
            own.append(blk)
        x_own = np.ascontiguousarray(np.stack(own))

        Al = np.asarray(inputs["L0_attn_A_log"])[512 * m:512 * (m + 1)]
        A_my = f32(Al.reshape(4, 128, S).transpose(1, 0, 2).reshape(128, 4 * S))

        idx = np.zeros((128, NIDX), np.int32)
        pcol = np.arange(128, dtype=np.int32)
        for j in range(4):
            for q in range(4):
                idx[:, IDXC[("dt", j, q)]] = j * 4128 + pcol * 16 + (m * 4 + q)
                idx[:, IDXC[("dx", j, q)]] = j * 4128 + E + pcol * 16 + (m * 4 + q)
            idx[0:32, IDXC[("bcs", j)]] = j * 4128 + 2 * E + np.arange(32)
        for ch in range(16):
            j, qd = ch // 4, ch % 4
            idx[:, IDXC[("y", ch)]] = j * 2048 + m * 512 + pcol * 4 + qd
        for ob in range(min(OB, 8)):
            g = 4 * ob + m
            lrow = 0 if g == 0 else ((g - 1) // 4) * 8 + ((g - 1) % 4) * 2 + 1
            rrow = 0 if g == NB - 1 else ((g + 1) // 4) * 8 + ((g + 1) % 4) * 2 + 0
            idx[0, IDXC[("xh", ob)]] = lrow
            idx[1, IDXC[("xh", ob)]] = rrow

        mcv_ = np.ones((128, 24), np.float32)
        mfn_ = np.ones((128, 16), np.float32)
        for ob in range(min(OB, 8)):
            g = 4 * ob + m
            if g == 0:
                mcv_[:, 3 * ob:3 * ob + 3] = 0.0
                mfn_[:, 2 * ob] = 0.0
            if g == NB - 1:
                mfn_[:, 2 * ob + 1] = 0.0

        im = dict(shared)
        im.update(x_own=x_own, A_log_my=A_my, idxs=idx, maskcv=mcv_, maskfn=mfn_)
        in_maps.append(im)
    return in_maps


def kernel(**inputs):
    NB = np.asarray(inputs["L0_attn_seq"]).shape[1] // LB
    in_maps = host_prep(inputs, NB)
    nc = build_nc(NB)
    res = run_bass_kernel_spmd(nc, in_maps, core_ids=list(range(8)))
    L = NB * LB
    OB = NB // 4
    out = np.empty((2, L, D), np.float32)
    for c in range(8):
        b, m = c // 4, c % 4
        yo = np.asarray(res.results[c]["y_out"])
        for ob in range(OB):
            g = 4 * ob + m
            out[b, g * LB:(g + 1) * LB] = yo[ob]
    return out


# version 09
# speedup vs baseline: 1.1172x; 1.0141x over previous
import numpy as np
from contextlib import ExitStack
import ml_dtypes

import concourse.bass as bass
import concourse.mybir as mybir
from concourse.bass import IndirectOffsetOnAxis
import bass_rust
ROP = bass_rust.ReduceOp
from concourse.bass_utils import run_bass_kernel_spmd

F32 = mybir.dt.float32
BF16 = mybir.dt.bfloat16
I32 = mybir.dt.int32
AF = mybir.ActivationFunctionType
OP = mybir.AluOpType
AX = mybir.AxisListType

D = 1024
E = 2048
S = 16
RK = 64
H = 4096
LB = 512
KD = D // 128
KE = E // 128
KH = H // 128
MAGIC = 12582912.0
EPS = 1e-5

ENGS = ("sync", "vector", "scalar", "tensor", "gpsimd")
SEMS = ("sq", "sp", "sg", "sv", "ss", "st", "sgp", "scc", "syq", "sxq", "sgy0", "sgy1", "sly0", "sly1", "shx", "sow", "shs", "slh", "slh2")
_QSEM = {"sync": "sq", "scalar": "sp", "gpsimd": "sg"}
_ESEM = {"vector": "sv", "scalar": "ss", "tensor": "st", "gpsimd": "sgp"}


class Prog:
    def __init__(self, nc):
        self.nc = nc
        self.q = {e: [] for e in ENGS}
        self.cnt = {s: 0 for s in SEMS}
        self.sem = {}

    def _push(self, eng, fn, inc, waits):
        sems = self.sem

        def run(h):
            for sname, val in waits:
                if val > 0:
                    h.wait_ge(sems[sname], val)
            ins = fn(h)
            if ins is not None and inc is not None:
                ins.then_inc(sems[inc[0]], inc[1])
        self.q[eng].append(run)

    def op(self, eng, fn, waits=()):
        s = _ESEM[eng]
        selfw = ((s, self.cnt[s]),)
        self.cnt[s] += 1
        self._push(eng, fn, (s, 1), tuple(waits) + selfw)
        return self.cnt[s]

    def dma(self, eng, fn, waits=(), sem=None):
        s = sem or _QSEM[eng]
        self.cnt[s] += 16
        self._push(eng, fn, (s, 16), tuple(waits))
        return self.cnt[s]

    def cc(self, fn, waits=()):
        self.cnt["scc"] += 1
        self._push("gpsimd", fn, ("scc", 1), tuple(waits))
        return self.cnt["scc"]

    def wait(self, eng, waits):
        self._push(eng, lambda h: None, None, tuple(waits))

    def barrier(self):
        snap = [(s, self.cnt[s]) for s in SEMS if self.cnt[s] > 0]
        for eng in ENGS:
            self.wait(eng, snap)

    def finalize(self, block):
        secs = {"sync": block.sync, "vector": block.vector, "scalar": block.scalar,
                "tensor": block.tensor, "gpsimd": block.gpsimd}
        for eng in ENGS:
            fns = self.q[eng]
            if not fns:
                continue
            def body(h, fns=tuple(fns)):
                for f in fns:
                    f(h)
            secs[eng](body)


IDXC = {}
_col = 0
for _j in range(4):
    for _q in range(4):
        IDXC[("dt", _j, _q)] = _col; _col += 1
        IDXC[("dx", _j, _q)] = _col; _col += 1
for _j in range(4):
    IDXC[("bcs", _j)] = _col; _col += 1
for _ch in range(16):
    IDXC[("y", _ch)] = _col; _col += 1
for _ob in range(8):
    IDXC[("xh", _ob)] = _col; _col += 1
NIDX = 96


def build_nc(NB, debug=False):
    NR = NB // 4
    OB = NB // 4
    nc = bass.Bass("TRN2", target_bir_lowering=False, detect_race_conditions=False)

    dr = lambda n, shp, dt_: nc.dram_tensor(n, shp, dt_, kind="ExternalInput").ap()
    x_own = dr("x_own", [OB, 4, LB + 3, D], F32)
    x_mine = dr("x_mine", [OB, LB, D], F32)
    in_projTs = dr("in_projTs", [D, 1024], F32)
    out_projT = dr("out_projT", [E, D], F32)
    w1T = dr("w1T", [D, H], F32)
    w2T = dr("w2T", [H, D], F32)
    x_projTs = dr("x_projTs", [128, 4 * 96], F32)
    dt_projTs = dr("dt_projTs", [RK, 512], F32)
    ln_g8 = dr("ln_g8", [128, KD], F32)
    ln_b8 = dr("ln_b8", [128, KD], F32)
    A_log_my = dr("A_log_my", [128, 4 * S], F32)
    conv_w16 = dr("conv_w16", [128, 16], F32)
    conv_b4 = dr("conv_b4", [128, 4], F32)
    dtb4 = dr("dtb4", [128, 4], F32)
    D4 = dr("D4", [128, 4], F32)
    fcw96 = dr("fcw96", [128, KH * 3], F32)
    fcb32 = dr("fcb32", [128, KH], F32)
    rs_col = dr("rs_col", [128, 1], F32)
    idxs = dr("idxs", [128, NIDX], I32)
    maskcv = dr("maskcv", [128, 3 * NB], F32)
    maskfn = dr("maskfn", [128, 2 * 8], F32)
    identc = dr("identc", [128, 128], BF16)
    identf = dr("identf", [128, 128], F32)
    sel32 = dr("sel32", [32, 32 * 128], BF16)
    onesk2 = dr("onesk2", [2, 128], BF16)

    y_out = nc.dram_tensor("y_out", [OB, LB, D], F32, kind="ExternalOutput").ap()
    dbgt = nc.dram_tensor("dbgt", [8, 128, 4160], F32, kind="ExternalOutput").ap() if debug else None

    di = lambda n, shp, dt_: nc.dram_tensor(n, shp, dt_, kind="Internal").ap()
    wb_ipz = di("wb_ipz", [128, KD * 512], BF16)
    wb_ipx = di("wb_ipx", [128, KD * 512], BF16)
    wb_opp = di("wb_opp", [128, KE * D], BF16)
    wb_w1p = di("wb_w1p", [128, 2 * KD * E], BF16)
    wb_w2p = di("wb_w2p", [128, KH * D], BF16)
    dtS = di("dtS", [OB, 128, 16 * LB], F32)
    dxS = di("dxS", [OB, 128, 16 * LB], F32)
    ccA_in = di("ccA_in", [NR, 4 * 96, LB], F32)
    ccA_out = di("ccA_out", [NR, 4 * 96, LB], F32)
    ccR_in = di("ccR_in", [NR, 4 * 512, LB], BF16)
    ccR_out = [di(f"ccR_out{r}", [4 * 4 * 512, LB], BF16) for r in range(NR)]
    ccX_in = di("ccX_in", [NR, 2, D], F32)
    ccX_out = di("ccX_out", [NR * 8, D], F32)
    spillX = di("spillX", [OB, 128, KE * LB], BF16)
    spillZ = di("spillZ", [OB, 128, KE * LB], BF16)
    spillH = di("spillH", [OB, 128, KH * LB], BF16)
    spillH2 = di("spillH2", [OB, 128, KH * LB], BF16)
    spillM = di("spillM", [OB, 128, 4 * D], F32)
    bias0d = di("bias0d", [1024], F32)

    es = ExitStack()
    sb = lambda n, shp, dt_: es.enter_context(nc.sbuf_tensor(n, shp, dt_))
    pt = lambda n, shp: es.enter_context(nc.psum_tensor(n, shp, F32))
    p = Prog(nc)
    for sname in SEMS:
        p.sem[sname] = es.enter_context(nc.semaphore(sname))

    ident = sb("ident", [128, 128], BF16)
    idf = sb("idf", [128, 128], F32)
    selt = sb("selt", [32, 32 * 128], BF16)
    ok2 = sb("ok2", [2, 128], BF16)
    k2st = sb("k2st", [2, 512], F32)
    k2hi = sb("k2hi", [2, 512], BF16)
    k2lo = sb("k2lo", [2, 512], BF16)
    lnG = sb("lnG", [128, KD], F32)
    lnB = sb("lnB", [128, KD], F32)
    lnBb = sb("lnBb", [128, KD], BF16)
    A64 = sb("A64", [128, 4 * S], F32)
    cw64 = sb("cw64", [128, 16], F32)
    cb16 = sb("cb16", [128, 4], F32)
    dtb = sb("dtb", [128, 4], F32)
    Dc16 = sb("Dc16", [128, 4], F32)
    fcw = sb("fcw", [128, KH * 3], F32)
    fcb = sb("fcb", [128, KH], F32)
    rsc = sb("rsc", [128, 1], F32)
    magic = sb("magic", [128, 1], F32)
    idxt = sb("idxt", [128, NIDX], I32)
    mcv = sb("mcv", [128, 3 * NB], F32)
    mfn = sb("mfn", [128, 2 * 8], F32)
    bias0 = sb("bias0", [128, 8], F32)
    xpT = sb("xpT", [128, 4 * 96], BF16)
    dtpT = sb("dtpT", [128, 512], BF16)
    statc = sb("statc", [128, 8 * OB], F32)
    colw = sb("colw", [128, 16], F32)

    ar1 = sb("ar1", [128, KE * LB + 2 * D], F32)
    ar2 = sb("ar2", [128, KE * LB], F32)
    ar3 = sb("ar3", [128, 9520], F32)
    ar3b = ar3[:].bitcast(BF16)
    xslb = ar3b[:, 0:KE * LB]
    xnf = ar3b[:, KE * LB:KE * LB + KD * 520]
    xcvs = ar3[:, 6176:6176 + 2 * 520]
    tcv = ar3[:, 7216:7216 + 2 * LB]
    zbs = ar3b[:, 2 * 8240:2 * 8240 + 2 * LB]
    xdb = ar3[:, 8752:8752 + LB]
    xdbb = ar3b[:, 2 * 9264:2 * 9264 + LB]
    stat = sb("stat", [128, 40], F32)

    ppS1 = pt("ppS1", [128, 1024])
    ppS2 = pt("ppS2", [128, 1024])
    ppP1 = pt("ppP1", [128, 1024])
    ppP2 = pt("ppP2", [128, 1024])
    ppA = ppS1[:, 0:512]
    ppB = ppS1[:, 512:1024]
    ppF = ppS2[:, 0:512]
    ppG = ppS2[:, 512:1024]
    ppD = ppP1
    ppE = ppP2
    psms8 = [ppA, ppB, ppF, ppG, ppD[:, 0:512], ppD[:, 512:1024],
             ppE[:, 0:512], ppE[:, 512:1024]]
    pbank4 = [ppP1[:, 0:512], ppP1[:, 512:1024],
              ppP2[:, 0:512], ppP2[:, 512:1024]]

    xres = ar1[:, 0:4 * D]
    xhal = ar1[:, 4 * D:5 * D]
    ar1b = ar1[:].bitcast(BF16)
    xnbB = ar1[:, 5 * D:9 * D]
    xnhB = ar1[:, 9 * D:10 * D]
    xsl = ar1[:, 0:KE * LB]
    ar2b = ar2[:].bitcast(BF16)
    wstr = [ar2b[:, k * 2048:(k + 1) * 2048] for k in range(KD)]
    dtt = ar2[:, 0:KE * LB]
    wfull = ar1[:, 0:2 * E]
    wcastB = ar1b[:, 2 * (2 * E):2 * (2 * E) + 2 * E]
    stat0row = ar2[:, 0:2 * E]

    def bcast_row(psdst, w, src_row_off, waitv):
        c1 = p.op("vector", lambda h: h.tensor_copy(
            k2hi[0:1, src_row_off:src_row_off + w],
            k2st[0:1, src_row_off:src_row_off + w]), waits=[("sv", waitv)])
        c2 = p.op("vector", lambda h: h.tensor_tensor(
            k2st[1:2, src_row_off:src_row_off + w] if False else
            k2lo[0:1, src_row_off:src_row_off + w],
            k2st[0:1, src_row_off:src_row_off + w],
            k2hi[0:1, src_row_off:src_row_off + w], OP.subtract))
        t1 = p.op("tensor", lambda h: h.matmul(
            psdst, ok2[:, 0:128], k2hi[0:2, src_row_off:src_row_off + w],
            start=True, stop=False), waits=[("sv", c2)])
        t2 = p.op("tensor", lambda h: h.matmul(
            psdst, ok2[:, 0:128], k2lo[0:2, src_row_off:src_row_off + w],
            start=False, stop=True))
        return t2



    for dst, src in ((ident[:], identc), (idf[:], identf), (lnG[:], ln_g8),
                     (lnB[:], ln_b8), (A64[:], A_log_my), (cw64[:], conv_w16),
                     (cb16[:], conv_b4), (dtb[:], dtb4), (Dc16[:], D4),
                     (fcw[:], fcw96), (fcb[:], fcb32), (rsc[:], rs_col),
                     (idxt[:], idxs), (mcv[:], maskcv), (mfn[:], maskfn),
                     (selt[:], sel32), (ok2[:], onesk2)):
        p.dma("sync", lambda h, d=dst, s=src: h.dma_start(out=d, in_=s))
    ld0 = p.cnt["sq"]
    p.dma("gpsimd", lambda h: h.dma_start(out=xpT[:], in_=x_projTs))
    p.dma("gpsimd", lambda h: h.dma_start(out=dtpT[0:RK, :], in_=dt_projTs))
    lg0 = p.cnt["sg"]
    sA = p.op("scalar", lambda h: h.activation(A64[:], A64[:], AF.Exp),
              waits=[("sq", ld0)])
    p.op("vector", lambda h: h.tensor_scalar_mul(A64[:], A64[:], -1.0),
         waits=[("ss", sA)])
    p.op("vector", lambda h: h.tensor_copy(lnBb[:], lnB[:]), waits=[("sq", ld0)])
    p.op("vector", lambda h: h.memset(magic[:], MAGIC))
    p.op("vector", lambda h: h.memset(k2st[:], 0.0))
    p.op("vector", lambda h: h.memset(k2hi[:], 0.0))
    p.op("vector", lambda h: h.memset(k2lo[:], 0.0))

    for k in range(KD):
        ld = p.dma("sync", lambda h, k=k: h.dma_start(
            out=wfull[:, 0:1024], in_=in_projTs[k * 128:(k + 1) * 128, :]),
            waits=[("sv", p.cnt["sv"])])
        c1 = p.op("vector", lambda h: h.tensor_copy(
            wcastB[:, 0:1024], wfull[:, 0:1024]), waits=[("sq", p.cnt["sq"])])
        for nt in range(2):
            p.op("tensor", lambda h, k=k, nt=nt: h.matmul(
                psms8[nt][0:1, :], lnBb[:, k:k + 1],
                wcastB[:, nt * 512:(nt + 1) * 512],
                start=(k == 0), stop=(k == KD - 1)),
                waits=[("sv", c1)] if nt == 0 else ())
        c2 = p.op("vector", lambda h, k=k: h.tensor_scalar(
            wcastB[:, 0:1024], wfull[:, 0:1024], lnG[:, k:k + 1], None, OP.mult),
            waits=[("st", p.cnt["st"])])
        p.dma("sync", lambda h, k=k: h.dma_start(
            out=wb_ipx[:, k * 512:(k + 1) * 512], in_=wcastB[:, 0:512]),
            waits=[("sv", c2)])
        p.dma("sync", lambda h, k=k: h.dma_start(
            out=wb_ipz[:, k * 512:(k + 1) * 512], in_=wcastB[:, 512:1024]))
    cb0 = 0
    for nt in range(2):
        cb0 = p.op("vector", lambda h, nt=nt: h.tensor_copy(
            stat0row[0:1, nt * 512:(nt + 1) * 512], psms8[nt][0:1, :]),
            waits=[("st", p.cnt["st"])] if nt == 0 else ())
    db0 = p.dma("sync", lambda h: h.dma_start(out=bias0d,
                in_=stat0row[0:1, 0:1024]), waits=[("sv", cb0)])
    p.dma("sync", lambda h: h.dma_start(
        out=bias0[:], in_=bias0d.rearrange("(c p) -> p c", p=128),
        allow_slow_non_contiguous=True), waits=[("sq", db0)])

    for k in range(KE):
        ld = p.dma("sync", lambda h, k=k: h.dma_start(
            out=wfull[:, 0:D], in_=out_projT[k * 128:(k + 1) * 128, :]),
            waits=[("sv", p.cnt["sv"])])
        c = p.op("vector", lambda h: h.tensor_copy(wcastB[:, 0:D], wfull[:, 0:D]),
                 waits=[("sq", p.cnt["sq"])])
        p.dma("sync", lambda h, k=k: h.dma_start(
            out=wb_opp[:, k * D:(k + 1) * D], in_=wcastB[:, 0:D]),
            waits=[("sv", c)])

    def quant_weight(wsrc, rows, cols, sdst, stores):
        nch = rows // 128
        acc = colw[:, 0:1]
        last = p.op("vector", lambda h: h.memset(acc, 0.0))
        for k in range(nch):
            ld = p.dma("sync", lambda h, k=k: h.dma_start(
                out=wfull[:, 0:cols], in_=wsrc[k * 128:(k + 1) * 128, :]),
                waits=[("sv", p.cnt["sv"])])
            p.op("vector", lambda h: h.tensor_reduce(
                colw[:, 1:2], wfull[:, 0:cols], AX.X, OP.add,
                apply_absolute_value=True), waits=[("sq", ld)])
            last = p.op("vector", lambda h: h.tensor_add(acc, acc, colw[:, 1:2]))
        tt = p.op("tensor", lambda h: h.matmul(
            ppB[0:1, 0:128], acc, idf[:], is_transpose=True, start=True, stop=True),
            waits=[("sv", last)])
        rr = p.op("vector", lambda h: h.tensor_reduce(
            k2st[0:1, 0:1], ppB[0:1, 0:128], AX.X, OP.add), waits=[("st", tt)])
        p.op("vector", lambda h: h.tensor_scalar_mul(
            k2st[0:1, 0:1], k2st[0:1, 0:1], 1.0 / (rows * cols)))
        p.op("vector", lambda h: h.tensor_scalar_max(k2st[0:1, 0:1], k2st[0:1, 0:1], 1e-5))
        rr = p.op("vector", lambda h: h.reciprocal(k2st[0:1, 1:2], k2st[0:1, 0:1]))
        t2 = bcast_row(ppB[:, 0:1], 1, 1, rr)
        cs = p.op("vector", lambda h, sdst=sdst: h.tensor_copy(sdst, ppB[:, 0:1]),
                  waits=[("st", t2)])
        for k in range(nch):
            ld = p.dma("sync", lambda h, k=k: h.dma_start(
                out=wfull[:, 0:cols], in_=wsrc[k * 128:(k + 1) * 128, :]),
                waits=[("sv", p.cnt["sv"]), ("sq", p.cnt["sq"])])
            p.op("vector", lambda h, sdst=sdst: h.tensor_scalar(
                wfull[:, 0:cols], wfull[:, 0:cols], sdst, magic[:, 0:1],
                OP.mult, OP.add), waits=[("sq", ld)])
            p.op("vector", lambda h: h.tensor_scalar(
                wfull[:, 0:cols], wfull[:, 0:cols], magic[:, 0:1], 1.0,
                OP.subtract, OP.min))
            c1 = p.op("vector", lambda h: h.tensor_scalar(
                wcastB[:, 0:cols], wfull[:, 0:cols], -1.0, None, OP.max))
            first = True
            for dst_ap, s0, s1 in stores(k):
                p.dma("sync", lambda h, dst_ap=dst_ap, s0=s0, s1=s1: h.dma_start(
                    out=dst_ap, in_=wcastB[:, s0:s1]),
                    waits=[("sv", c1)] if first else ())
                first = False

    sw1 = colw[:, 8:9]
    sw2 = colw[:, 9:10]
    quant_weight(w1T, D, H, sw1, lambda k: [
        (wb_w1p[:, k * E:(k + 1) * E], 0, E),
        (wb_w1p[:, (KD + k) * E:(KD + k + 1) * E], E, 2 * E)])
    quant_weight(w2T, H, D, sw2, lambda k: [
        (wb_w2p[:, k * D:(k + 1) * D], 0, D)])
    rw1 = colw[:, 10:11]
    rw2 = colw[:, 11:12]
    p.op("vector", lambda h: h.reciprocal(rw1, sw1))
    p.op("vector", lambda h: h.reciprocal(rw2, sw2))
    p.op("vector", lambda h: h.tensor_scalar(fcw[:], fcw[:], rw1, None, OP.mult))

    zv = p.op("vector", lambda h: h.memset(wfull[:, 0:D], 0.0))
    nzr = (NR * 8 + 127) // 128
    for zi in range(nzr):
        r0, r1 = zi * 128, min((zi + 1) * 128, NR * 8)
        p.dma("sync", lambda h, r0=r0, r1=r1: h.dma_start(
            out=ccX_out[r0:r1, :], in_=wfull[0:r1 - r0, 0:D]), waits=[("sv", zv)])

    p.barrier()

    def rsqrt_newton(w, dst, var_col, s0, s1):
        cv = p.op("vector", lambda h: h.tensor_scalar_add(s0[0:w, :], var_col[0:w, :], EPS))
        c = p.op("scalar", lambda h: h.activation(s1[0:w, :], s0[0:w, :], AF.Sqrt),
                 waits=[("sv", cv)])
        p.op("vector", lambda h: h.reciprocal(dst[0:w, :], s1[0:w, :]), waits=[("ss", c)])
        p.op("vector", lambda h: h.tensor_mul(s1[0:w, :], dst[0:w, :], dst[0:w, :]))
        p.op("vector", lambda h: h.tensor_mul(s1[0:w, :], s1[0:w, :], s0[0:w, :]))
        p.op("vector", lambda h: h.tensor_scalar(s1[0:w, :], s1[0:w, :], -0.5, 1.5,
                                                 OP.mult, OP.add))
        return p.op("vector", lambda h: h.tensor_mul(dst[0:w, :], dst[0:w, :], s1[0:w, :]))

    def ln_stats(src, w, mu_dst, r1_dst, rtot_dst=None):
        bno = stat[:, 20:32]
        p.op("vector", lambda h: h.bn_stats(bno[0:w, 0:6], src[0:w, 0:D // 2]))
        p.op("vector", lambda h: h.bn_stats(bno[0:w, 6:12], src[0:w, D // 2:D]))
        p.op("vector", lambda h: h.bn_aggr(stat[0:w, 32:34], bno[0:w, 0:12]))
        p.op("vector", lambda h: h.tensor_copy(mu_dst[0:w, :], stat[0:w, 32:33]))
        rsqrt_newton(w, r1_dst, stat[:, 33:34], stat[:, 35:36], stat[:, 36:37])
        if rtot_dst is not None:
            p.op("vector", lambda h: h.tensor_mul(
                stat[0:w, 37:38], r1_dst[0:w, :], r1_dst[0:w, :]))
            p.op("vector", lambda h: h.tensor_mul(
                stat[0:w, 37:38], stat[0:w, 37:38], stat[0:w, 33:34]))
            rsqrt_newton(w, stat[:, 38:39], stat[:, 37:38], stat[:, 35:36],
                         stat[:, 36:37])
            return p.op("vector", lambda h: h.tensor_mul(
                rtot_dst[0:w, :], r1_dst[0:w, :], stat[0:w, 38:39]))
        return p.cnt["sv"]


    def bcast_row(psdst, w, src_row_off, waitv):
        c1 = p.op("vector", lambda h: h.tensor_copy(
            k2hi[0:1, src_row_off:src_row_off + w],
            k2st[0:1, src_row_off:src_row_off + w]), waits=[("sv", waitv)])
        c2 = p.op("vector", lambda h: h.tensor_tensor(
            k2st[1:2, src_row_off:src_row_off + w] if False else
            k2lo[0:1, src_row_off:src_row_off + w],
            k2st[0:1, src_row_off:src_row_off + w],
            k2hi[0:1, src_row_off:src_row_off + w], OP.subtract))
        t1 = p.op("tensor", lambda h: h.matmul(
            psdst, ok2[:, 0:128], k2hi[0:2, src_row_off:src_row_off + w],
            start=True, stop=False), waits=[("sv", c2)])
        t2 = p.op("tensor", lambda h: h.matmul(
            psdst, ok2[:, 0:128], k2lo[0:2, src_row_off:src_row_off + w],
            start=False, stop=True))
        return t2

    ccF_done = {}
    ccR_done = {}
    paE = {}
    ccX_done = {}
    sends_end = {}
    ysend_tok = {0: 0}
    last_scan_par = {0: 0, 1: 0, 2: 0, 3: 0}
    scan_last = {"ss": 0, "sv": 0}

    bcf = ar1[:, 0:LB]
    bchi = ar1b[:, 2 * LB:3 * LB]
    bclo = ar1b[:, 3 * LB:4 * LB]
    rdt = ar1[:, 2 * LB:6 * LB]
    rdx = ar1[:, 6 * LB:10 * LB]
    hbuf = ar1[:, 10 * LB:14 * LB]
    yq = ar1[:, 14 * LB:18 * LB]
    carry = ar1[:, 18 * LB:18 * LB + 64]
    dAs = sb("dAs", [128, 4 * LB], F32)
    usl = sb("usl", [128, 2 * LB], F32)
    yin = sb("yin", [128, 4 * LB], BF16)
    ysl2 = sb("ysl2", [128, 2 * LB], F32)
    xsz = sb("xsz", [128, 8 * LB], BF16)
    xmidt = ar3[:, 0:4 * D]
    ohi = ar3[:, 4 * D:4 * D + KD * LB]
    xoslot = ar3[:, 4 * D + KD * LB:4 * D + KD * LB + D]
    xqf = sb("xqf", [128, KD * 516], BF16)

    hq = sb("hq", [128, 4 * LB], BF16)
    mqt = sb("mqt", [128, LB], F32)
    shb = sb("shb", [128, LB], F32)
    hmini = sb("hmini", [128, D], F32)
    xqtk = hmini
    hslot = sb("hslot", [128, 2 * LB], F32)
    xqf3 = xqf[:].rearrange("p (c n) -> p c n", n=516)

    prev_sends = {"d3": 0}
    ccA_done = {}
    ldzx = p.dma("sync", lambda h: h.dma_start(
        out=ar2b[:, 0:KD * 512], in_=wb_ipz[:, :]),
        waits=[("sq", p.cnt["sq"])])
    ldzx2 = p.dma("sync", lambda h: h.dma_start(
        out=ar2b[:, KD * 512:2 * KD * 512], in_=wb_ipx[:, :]),
        waits=[("sq", ldzx)])
    xszf = xsz[:].bitcast(F32)
    xslP = {0: dAs[:, 0:4 * LB], 1: xszf[:, 0:4 * LB]}
    spdt = {"slast": 0, "ssdt": 0, "xdbe": 0}

    def dt_phase(ob, j):
        lxd = p.dma("sync", lambda h, ob=ob, j=j: h.dma_start(
            out=xdb[0:RK, :], in_=ccA_out[ob, j * 96:j * 96 + RK, :]),
            waits=[("scc", ccA_done[(ob, j)]), ("sxq", p.cnt["sxq"])],
            sem="sxq")
        cbb = p.op("vector", lambda h: h.tensor_copy(
            xdbb[0:RK, :], xdb[0:RK, :]), waits=[("sxq", lxd)])
        for pair in range(2):
            ssdt = 0
            for u in range(2):
                i = pair * 2 + u
                psm = ppF if u == 0 else ppG
                w_ = [("ss", spdt["ssdt"])]
                if pair == 0 and u == 0:
                    w_.append(("sv", cbb))
                    w_.append(("sv", spdt["xdbe"]))
                tm = p.op("tensor", lambda h, i=i, psm=psm: h.matmul(
                    psm, dtpT[0:RK, i * 128:(i + 1) * 128], xdbb[0:RK, :],
                    start=True, stop=True), waits=w_)
                p.op("scalar", lambda h, i=i, u=u, psm=psm: h.activation(
                    usl[:, u * LB:(u + 1) * LB], psm, AF.Exp,
                    bias=dtb[:, i:i + 1]),
                    waits=[("st", tm), ("sg", spdt["slast"])])
                ssdt = p.op("scalar", lambda h, u=u: h.activation(
                    usl[:, u * LB:(u + 1) * LB], usl[:, u * LB:(u + 1) * LB],
                    AF.Ln, bias=1.0))
            spdt["ssdt"] = ssdt
            mv = 0
            for u in range(2):
                i = pair * 2 + u
                mv = p.op("vector", lambda h, ob=ob, j=j, i=i, u=u: h.tensor_mul(
                    ysl2[:, u * LB:(u + 1) * LB],
                    xslP[j % 2][:, i * LB:(i + 1) * LB],
                    usl[:, u * LB:(u + 1) * LB]),
                    waits=[("ss", ssdt), ("sg", spdt["slast"])] if u == 0
                    else ())
            spdt["slast"] = p.dma("gpsimd", lambda h, ob=ob, j=j, pair=pair:
                h.dma_start(
                    out=dtS[ob, :, (j * 4 + pair * 2) * LB:(j * 4 + pair * 2 + 2) * LB],
                    in_=usl[:, 0:2 * LB]), waits=[("ss", ssdt)])
            spdt["slast"] = p.dma("gpsimd", lambda h, ob=ob, j=j, pair=pair:
                h.dma_start(
                    out=dxS[ob, :, (j * 4 + pair * 2) * LB:(j * 4 + pair * 2 + 2) * LB],
                    in_=ysl2[:, 0:2 * LB]), waits=[("sv", mv)])

    dt_pend = None
    for ob in range(OB):
        for j in range(4):
            blkid = ob * 4 + j
            ldx = p.dma("sync", lambda h, ob=ob, j=j: h.dma_start(
                out=xres.rearrange("p (t n) -> p t n", n=D),
                in_=x_own[ob, j, 3:3 + LB, :].rearrange("(t p) n -> p t n", p=128)),
                waits=[("sxq", prev_sends["d3"]), ("sv", p.cnt["sv"]),
                       ("st", p.cnt["st"])], sem="sxq")
            ldh = p.dma("sync", lambda h, ob=ob, j=j: h.dma_start(
                out=xhal[0:3, :], in_=x_own[ob, j, 0:3, :]), sem="sxq")

            p.wait("vector", [("sxq", ldh), ("st", p.cnt["st"])])
            for lt in range(4):
                src = xres[:, lt * D:(lt + 1) * D]
                mu = stat[:, lt:lt + 1]
                rt = stat[:, 4 + lt:5 + lt]
                r1 = stat[:, 34:35]
                ln_stats(src, 128, mu, r1, rt)
                p.op("vector", lambda h, src=src, lt=lt, mu=mu, rt=rt: h.tensor_scalar(
                    xnbB[:, lt * D:(lt + 1) * D], src, mu, rt, OP.subtract, OP.mult))
            p.wait("vector", [("sxq", ldh)])
            muh = stat[:, 8:9]
            rth = stat[:, 9:10]
            ln_stats(xhal, 3, muh, stat[:, 34:35], rth)
            svh = p.op("vector", lambda h: h.tensor_scalar(
                xnhB[0:3, :], xhal[0:3, :], muh[0:3, :], rth[0:3, :],
                OP.subtract, OP.mult))

            sc = p.cnt["ss"]
            for dch in range(KD):
                t0 = p.op("tensor", lambda h, dch=dch: h.matmul(
                    ppD[:, 0:3], xnhB[0:3, dch * 128:(dch + 1) * 128],
                    idf[0:3, 0:3], is_transpose=True, start=True, stop=True),
                    waits=[("sv", svh), ("ss", sc)])
                sc = p.op("scalar", lambda h, dch=dch: h.copy(
                    xnf[:, dch * 520:dch * 520 + 3], ppD[:, 0:3]), waits=[("st", t0)])
                for lt in range(4):
                    t1 = p.op("tensor", lambda h, dch=dch, lt=lt: h.matmul(
                        ppD[:, 512:640], xnbB[:, lt * D + dch * 128:lt * D + (dch + 1) * 128],
                        idf[:], is_transpose=True, start=True, stop=True),
                        waits=[("ss", sc)])
                    sc = p.op("scalar", lambda h, dch=dch, lt=lt: h.copy(
                        xnf[:, dch * 520 + 3 + lt * 128:dch * 520 + 3 + (lt + 1) * 128],
                        ppD[:, 512:640]), waits=[("st", t1)])
            ss_xnf = p.cnt["ss"]
            xnf3 = xnf[:].rearrange("p (c n) -> p c n", n=520)

            if dt_pend is not None:
                dt_phase(*dt_pend)
                dt_pend = None

            p.wait("tensor", [("ss", ss_xnf)])
            z_evac = {0: 0, 1: 0}
            for i in range(4):
                psm = ppA if i % 2 == 0 else ppB
                for k in range(KD):
                    w_ = []
                    if ob == 0 and j == 0 and i == 0 and k == 0:
                        w_.append(("sq", ldzx))
                    if k == 0:
                        w_.append(("ss", z_evac[i % 2]))
                    p.op("tensor", lambda h, i=i, k=k, psm=psm: h.matmul(
                        psm, ar2b[:, k * 512 + i * 128:k * 512 + (i + 1) * 128],
                        xnf3[:, k, 3:515], start=(k == 0), stop=(k == KD - 1)),
                        waits=w_)
                stz = p.cnt["st"]
                zslot = zbs[:, (i % 2) * LB:(i % 2 + 1) * LB]
                scz = p.op("scalar", lambda h, i=i, psm=psm, zslot=zslot: h.activation(
                    zslot, psm, AF.Silu, bias=bias0[:, 4 + i:5 + i]),
                    waits=[("st", stz), ("sp", p.cnt["sp"])])
                z_evac[i % 2] = scz
                p.dma("scalar", lambda h, ob=ob, j=j, i=i, zslot=zslot: h.dma_start(
                    out=spillZ[ob, :, (j * 4 + i) * LB:(j * 4 + i + 1) * LB],
                    in_=zslot), waits=[("ss", scz)])

            xsl_j = xslP[j % 2]
            xslb_j = xslb[:, 0:4 * LB]
            for i in range(4):
                for k in range(KD):
                    w_ = []
                    if ob == 0 and j == 0 and i == 0 and k == 0:
                        w_.append(("sq", ldzx2))
                    if k == 0:
                        w_.append(("sv", p.cnt["sv"]))
                    p.op("tensor", lambda h, i=i, k=k: h.matmul(
                        ppD[:, 0:257],
                        ar2b[:, 4096 + k * 512 + i * 128:4096 + k * 512 + (i + 1) * 128],
                        xnf3[:, k, 0:257], start=(k == 0), stop=(k == KD - 1)),
                        waits=w_)
                for k in range(KD):
                    p.op("tensor", lambda h, i=i, k=k: h.matmul(
                        ppD[:, 512:770],
                        ar2b[:, 4096 + k * 512 + i * 128:4096 + k * 512 + (i + 1) * 128],
                        xnf3[:, k, 257:515], start=(k == 0), stop=(k == KD - 1)))
                stx = p.cnt["st"]
                xc = xcvs[:, (i % 2) * 520:(i % 2) * 520 + 515]
                p.op("vector", lambda h, xc=xc, i=i: h.tensor_scalar(
                    xc[:, 0:257], ppD[:, 0:257], bias0[:, i:i + 1], None, OP.add),
                    waits=[("st", stx)])
                p.op("vector", lambda h, xc=xc, i=i: h.tensor_scalar(
                    xc[:, 257:515], ppD[:, 512:770], bias0[:, i:i + 1], None, OP.add))
                p.op("vector", lambda h, xc=xc, blkid=blkid: h.tensor_scalar(
                    xc[:, 0:3], xc[:, 0:3], mcv[:, 3 * blkid:3 * blkid + 1],
                    None, OP.mult))
                t = tcv[:, (i % 2) * LB:(i % 2 + 1) * LB]
                p.op("vector", lambda h, i=i, xc=xc, t=t: h.tensor_scalar(
                    t, xc[:, 0:LB], cw64[:, i * 4:i * 4 + 1],
                    cb16[:, i:i + 1], OP.mult, OP.add))
                for kk in range(1, 4):
                    p.op("vector", lambda h, i=i, xc=xc, t=t, kk=kk: h.scalar_tensor_tensor(
                        t, xc[:, kk:kk + LB], cw64[:, i * 4 + kk:i * 4 + kk + 1], t,
                        OP.mult, OP.add))
                svt = p.cnt["sv"]
                scs = p.op("scalar", lambda h, i=i, t=t, xsl_j=xsl_j: h.activation(
                    xsl_j[:, i * LB:(i + 1) * LB], t, AF.Silu), waits=[("sv", svt)])
                p.op("vector", lambda h, i=i, xsl_j=xsl_j, xslb_j=xslb_j: h.tensor_copy(
                    xslb_j[:, i * LB:(i + 1) * LB], xsl_j[:, i * LB:(i + 1) * LB]),
                    waits=[("ss", scs)])
            sv_x = p.cnt["sv"]
            p.dma("scalar", lambda h, ob=ob, j=j, xslb_j=xslb_j: h.dma_start(
                out=spillX[ob, :, j * 4 * LB:(j * 4 + 4) * LB], in_=xslb_j[:]),
                waits=[("sv", sv_x)])

            for i in range(4):
                p.op("tensor", lambda h, i=i, xslb_j=xslb_j: h.matmul(
                    ppF[0:96, :], xpT[:, i * 96:(i + 1) * 96],
                    xslb_j[:, i * LB:(i + 1) * LB], start=(i == 0), stop=(i == 3)),
                    waits=[("sv", sv_x), ("ss", spdt["ssdt"])] if i == 0 else ())
            st_xd = p.cnt["st"]
            xdbe = p.op("vector", lambda h: h.tensor_copy(
                xdb[0:96, :], ppF[0:96, :]),
                waits=[("st", st_xd), ("sxq", prev_sends["d3"])])
            spdt["xdbe"] = xdbe
            prev_sends["d3"] = p.dma("sync", lambda h, ob=ob, j=j: h.dma_start(
                out=ccA_in[ob, j * 96:(j + 1) * 96, :], in_=xdb[0:96, :]),
                waits=[("sv", xdbe)], sem="sxq")
            ccA_done[(ob, j)] = p.cc(lambda h, ob=ob, j=j: h.collective_compute(
                "AllReduce", OP.add,
                replica_groups=[[0, 1, 2, 3], [4, 5, 6, 7]],
                ins=[ccA_in[ob, j * 96:(j + 1) * 96, :]],
                outs=[ccA_out[ob, j * 96:(j + 1) * 96, :]]),
                waits=[("sxq", prev_sends["d3"])])
            dt_pend = (ob, j)
    dt_phase(*dt_pend)
    dt_pend = None

    pbE = {"pre": {"sv": 0, "ss": 0}}
    p.barrier()

    p.op("vector", lambda h: h.memset(carry[:], 0.0))

    def scan_gen(r):
        for blkslot in range(4):
            gw = [("ss", scan_last["ss"]), ("sv", scan_last["sv"])]
            p.dma("gpsimd", lambda h, r=r, blkslot=blkslot: h.dma_start(
                out=rdt[:], in_=dtS[r, :, blkslot * 4 * LB:(blkslot + 1) * 4 * LB]),
                waits=gw)
            p.dma("gpsimd", lambda h, r=r, blkslot=blkslot: h.dma_start(
                out=rdx[:], in_=dxS[r, :, blkslot * 4 * LB:(blkslot + 1) * 4 * LB]))
            p.dma("gpsimd", lambda h, r=r, blkslot=blkslot: h.dma_start(
                out=bcf[0:32, :],
                in_=ccA_out[r, blkslot * 96 + RK:blkslot * 96 + 96, :]))
            gall = p.cnt["sg"]
            lsem = "sly%d" % (blkslot % 2)
            p.dma("scalar", lambda h, r=r, blkslot=blkslot: h.dma_start(
                out=xsz[:, 0:4 * LB],
                in_=spillX[r, :, blkslot * 4 * LB:(blkslot + 1) * 4 * LB]),
                waits=[("sv", scan_last["sv"]), ("sp", p.cnt["sp"])], sem=lsem)
            lzs = p.dma("scalar", lambda h, r=r, blkslot=blkslot: h.dma_start(
                out=xsz[:, 4 * LB:8 * LB],
                in_=spillZ[r, :, blkslot * 4 * LB:(blkslot + 1) * 4 * LB]),
                sem=lsem)
            ch = p.op("vector", lambda h: h.tensor_copy(bchi[0:32, :], bcf[0:32, :]),
                      waits=[("sg", gall), ("st", p.cnt["st"])])
            cl = p.op("vector", lambda h: h.tensor_tensor(
                bclo[0:32, :], bcf[0:32, :], bchi[0:32, :], OP.subtract))

            for qt in range(4):
                for jp in range(2):
                    selw = [("sv", p.cnt["sv"])]
                    for j2 in range(2):
                        s = 4 * qt + 2 * jp + j2
                        dB = ppS1[:, j2 * 512:(j2 + 1) * 512]
                        dC = ppS2[:, j2 * 512:(j2 + 1) * 512]
                        p.op("tensor", lambda h, s=s, dB=dB: h.matmul(
                            dB, selt[0:32, s * 128:(s + 1) * 128],
                            bchi[0:32, :], start=True, stop=False),
                            waits=selw if j2 == 0 else ())
                        p.op("tensor", lambda h, s=s, dB=dB: h.matmul(
                            dB, selt[0:32, s * 128:(s + 1) * 128],
                            bclo[0:32, :], start=False, stop=True))
                        p.op("tensor", lambda h, s=s, dC=dC: h.matmul(
                            dC, selt[0:32, (16 + s) * 128:(17 + s) * 128],
                            bchi[0:32, :], start=True, stop=False))
                        p.op("tensor", lambda h, s=s, dC=dC: h.matmul(
                            dC, selt[0:32, (16 + s) * 128:(17 + s) * 128],
                            bclo[0:32, :], start=False, stop=True))
                    stsel = p.cnt["st"]
                    for qq in range(4):
                        for j2 in range(2):
                            s = 4 * qt + 2 * jp + j2
                            par = (qq * 2 + j2) % 4
                            da = dAs[:, par * LB:(par + 1) * LB]
                            uu = usl[:, j2 * LB:(j2 + 1) * LB]
                            sw_ = ([("sg", gall)]
                                   if (qt == 0 and jp == 0 and qq == 0 and j2 == 0)
                                   else [])
                            sw_.append(("sv", last_scan_par[par]))
                            se = p.op("scalar", lambda h, qq=qq, s=s, da=da: h.activation(
                                da, rdt[:, qq * LB:(qq + 1) * LB], AF.Exp,
                                scale=A64[:, qq * S + s:qq * S + s + 1]), waits=sw_)
                            uw = [("ss", se), ("st", stsel)]
                            if qt == 0 and jp == 0 and qq == 0 and j2 == 0:
                                uw.append(("syq", p.cnt["syq"]))
                            p.op("vector", lambda h, qq=qq, j2=j2, uu=uu: h.tensor_mul(
                                uu, rdx[:, qq * LB:(qq + 1) * LB],
                                ppS1[:, j2 * 512:(j2 + 1) * 512]), waits=uw)
                            last_scan_par[par] = p.op("vector",
                                lambda h, qq=qq, s=s, j2=j2, da=da, uu=uu: h.tensor_tensor_scan(
                                    hbuf[:, j2 * LB:(j2 + 1) * LB], da, uu,
                                    carry[:, qq * S + s:qq * S + s + 1],
                                    OP.mult, OP.add))
                        p.op("vector", lambda h, qq=qq, qt=qt, jp=jp: h.tensor_copy(
                            carry[:, qq * S + qt * 4 + 2 * jp:qq * S + qt * 4 + 2 * jp + 2],
                            hbuf[:, 0:2 * LB].rearrange("p (s n) -> p s n", n=LB)[:, :, LB - 1:LB]))
                        p.op("vector", lambda h: h.tensor_mul(
                            hbuf[:, 0:2 * LB], hbuf[:, 0:2 * LB], ppS2[:, 0:1024]))
                        h3 = hbuf[:, 0:2 * LB].rearrange("p (s n) -> p n s", n=LB)
                        if qt == 0 and jp == 0:
                            p.op("vector", lambda h, qq=qq, h3=h3: h.tensor_reduce(
                                yq[:, qq * LB:(qq + 1) * LB], h3, AX.X, OP.add))
                        else:
                            p.op("vector", lambda h, h3=h3: h.tensor_reduce(
                                hbuf[:, 2 * LB:3 * LB], h3, AX.X, OP.add))
                            p.op("vector", lambda h, qq=qq: h.tensor_add(
                                yq[:, qq * LB:(qq + 1) * LB],
                                yq[:, qq * LB:(qq + 1) * LB], hbuf[:, 2 * LB:3 * LB]))
                    yield
            ybf = ar1b[:, 20 * LB:24 * LB]
            for i in range(4):
                p.op("vector", lambda h, i=i: h.scalar_tensor_tensor(
                    yq[:, i * LB:(i + 1) * LB], xsz[:, i * LB:(i + 1) * LB],
                    Dc16[:, i:i + 1], yq[:, i * LB:(i + 1) * LB],
                    OP.mult, OP.add),
                    waits=[(lsem, lzs)] if i == 0 else ())
                p.op("vector", lambda h, i=i, ybf=ybf: h.tensor_mul(
                    ybf[:, i * LB:(i + 1) * LB], yq[:, i * LB:(i + 1) * LB],
                    xsz[:, (4 + i) * LB:(5 + i) * LB]))
            svy = p.cnt["sv"]
            scan_last["ss"] = p.cnt["ss"]
            scan_last["sv"] = svy
            ysend_tok[0] = p.dma("sync", lambda h, r=r, blkslot=blkslot, ybf=ybf:
                h.dma_start(
                    out=ccR_in[r, blkslot * 512:(blkslot + 1) * 512, :]
                        .rearrange("(pp q) n -> pp q n", pp=128),
                    in_=ybf.rearrange("p (q n) -> p q n", n=LB)),
                waits=[("sv", svy)], sem="syq")
        sy = p.cnt["syq"]
        sends_end[r] = sy
        ccR_done[r] = p.cc(lambda h, r=r: h.collective_compute(
            "AllGather", OP.bypass,
            replica_groups=[[0, 1, 2, 3], [4, 5, 6, 7]],
            ins=[ccR_in[r]], outs=[ccR_out[r]]), waits=[("syq", sy)])

    def post_a_gen(r):
        ob = r
        opld1 = p.dma("sync", lambda h: h.dma_start(
            out=ar2b[:, 0:8 * D], in_=wb_opp[:, 0:8 * D]),
            waits=[("st", p.cnt["st"])], sem="sow")
        opld2 = p.dma("sync", lambda h: h.dma_start(
            out=ar2b[:, 8 * D:KE * D], in_=wb_opp[:, 8 * D:KE * D]),
            waits=[("sow", opld1)], sem="sow")
        pbe = pbE.get(r - 2, pbE.get("pre", {"sv": 0, "ss": 0}))
        p.wait("tensor", [("sv", pbe["sv"]), ("ss", pbe["ss"])])

        def gather_y(k, slot_war):
            spar = k % 4
            gsem = "sgy%d" % (k % 2)
            yslot = yin[:, spar * LB:(spar + 1) * LB]
            gw_ = [("scc", ccR_done[r])]
            if slot_war:
                gw_.append(("st", slot_war))
            gy = p.dma("gpsimd", lambda h, r=r, k=k, yslot=yslot:
                h.indirect_dma_start(
                    out=yslot, out_offset=None, in_=ccR_out[r],
                    in_offset=IndirectOffsetOnAxis(
                        ap=idxt[:, IDXC[("y", k)]:IDXC[("y", k)] + 1], axis=0)),
                waits=gw_, sem=gsem)
            return yslot, (gsem, gy)

        mmk = {}
        for k in range(KE):
            yslot, gtok = gather_y(k, mmk.get(k - 2))
            for m in range(4):
                p.op("tensor", lambda h, k=k, m=m, yslot=yslot: h.matmul(
                    pbank4[m], ar2b[:, k * D + m * 128:k * D + (m + 1) * 128],
                    yslot, start=(k == 0), stop=(k == KE - 1)),
                    waits=([("sow", opld1 if k < 8 else opld2), gtok]
                           if m == 0 and k % 8 == 0 else
                           ([gtok] if m == 0 else ())))
            mmk[k] = p.cnt["st"]
            yield
        stmm = p.cnt["st"]
        ev1 = 0
        for m in range(4):
            ev1 = p.op("vector", lambda h, m=m: h.tensor_copy(
                ohi[:, m * LB:(m + 1) * LB], pbank4[m]),
                waits=[("st", stmm)] if m == 0 else ())
        yield
        mmk2nd = {}
        for k in range(KE):
            yslot, gtok = gather_y(k, mmk2nd.get(k - 2, mmk.get(14 + k)))
            for m in range(4):
                w_ = []
                if m == 0:
                    w_ = [gtok] + ([("sv", ev1)] if k == 0 else [])
                p.op("tensor", lambda h, k=k, m=m, yslot=yslot: h.matmul(
                    pbank4[m], ar2b[:, k * D + (4 + m) * 128:k * D + (5 + m) * 128],
                    yslot, start=(k == 0), stop=(k == KE - 1)), waits=w_)
            mmk2nd[k] = p.cnt["st"]
            yield
        stmm = p.cnt["st"]
        ev = 0
        for m in range(4):
            ev = p.op("vector", lambda h, m=m: h.tensor_copy(
                ohi[:, (4 + m) * LB:(5 + m) * LB], pbank4[m]),
                waits=[("st", stmm)] if m == 0 else ())
        svev = ev

        ldxr = p.dma("sync", lambda h, ob=ob: h.dma_start(
            out=xmidt.rearrange("p (t n) -> p t n", n=D),
            in_=x_mine[ob].rearrange("(t p) n -> p t n", p=128)),
            waits=[("sv", p.cnt["sv"]), ("sp", p.cnt["sp"]),
                   ("sxq", p.cnt["sxq"])], sem="sxq")
        p.wait("vector", [("sxq", ldxr)])
        for lt in range(4):
            src_ = xmidt[:, lt * D:(lt + 1) * D]
            mu = stat[:, lt:lt + 1]
            r1 = stat[:, 8:9]
            ln_stats(src_, 128, mu, r1)
            p.op("vector", lambda h, lt=lt, r1=r1: h.tensor_scalar_add(
                stat[:, 4 + lt:5 + lt], r1, 1.0))
            p.op("vector", lambda h, lt=lt, mu=mu, r1=r1: h.tensor_mul(
                stat[:, lt:lt + 1], mu, r1))
        xm_done = 0
        for lt in range(4):
            pdst = ppD if lt % 2 == 0 else ppE
            for m in range(KD):
                w_ = []
                if m == 0:
                    w_.append(("sv", svev if lt < 2 else xm_done))
                p.op("tensor", lambda h, lt=lt, m=m, pdst=pdst: h.matmul(
                    pdst[:, m * 128:(m + 1) * 128],
                    ohi[:, m * LB + lt * 128:m * LB + (lt + 1) * 128],
                    idf[:], is_transpose=True, start=True, stop=True), waits=w_)
            sttr = p.cnt["st"]
            p.op("vector", lambda h, lt=lt: h.tensor_scalar(
                xoslot[:, 0:D], xmidt[:, lt * D:(lt + 1) * D],
                stat[:, 4 + lt:5 + lt], stat[:, lt:lt + 1],
                OP.mult, OP.subtract))
            xm_done = p.op("vector", lambda h, lt=lt, pdst=pdst: h.scalar_tensor_tensor(
                xmidt[:, lt * D:(lt + 1) * D], pdst[:, 0:D], rsc[:, 0:1],
                xoslot[:, 0:D], OP.mult, OP.add), waits=[("st", sttr)])
            yield
        svxm = xm_done
        spm = p.dma("scalar", lambda h, ob=ob: h.dma_start(
            out=spillM[ob], in_=xmidt[:]), waits=[("sv", svxm)])
        d1 = p.dma("sync", lambda h, r=r: h.dma_start(
            out=ccX_in[r, 0:1, :], in_=xmidt[0:1, 0:D]), waits=[("sv", svxm)],
            sem="sxq")
        d2 = p.dma("sync", lambda h, r=r: h.dma_start(
            out=ccX_in[r, 1:2, :], in_=xmidt[127:128, 3 * D:4 * D]), sem="sxq")
        ccX_done[r] = p.cc(lambda h, r=r: h.collective_compute(
            "AllGather", OP.bypass,
            replica_groups=[[0, 1, 2, 3], [4, 5, 6, 7]],
            ins=[ccX_in[r]], outs=[ccX_out[r * 8:(r + 1) * 8, :]]),
            waits=[("sxq", d2)])

    def post_b_gen(r):
        ob = r
        need_right = ccX_done[min(r + 1, NR - 1)] if (4 * r + 3) + 1 <= NB - 1 else ccX_done[r]
        gh = p.dma("gpsimd", lambda h, ob=ob: h.indirect_dma_start(
            out=hmini[0:2, :], out_offset=None, in_=ccX_out,
            in_offset=IndirectOffsetOnAxis(
                ap=idxt[0:2, IDXC[("xh", ob)]:IDXC[("xh", ob)] + 1], axis=0)),
            waits=[("scc", need_right), ("sv", p.cnt["sv"])], sem="shx")
        p.wait("vector", [("shx", gh)])
        muh = stat[:, 8:9]
        r3h = stat[:, 9:10]
        ln_stats(hmini, 2, muh, r3h)
        p.op("vector", lambda h: h.tensor_scalar(
            hmini[0:2, :], hmini[0:2, :], muh[0:2, :], None, OP.subtract))
        p.op("vector", lambda h: h.tensor_reduce(
            stat[0:2, 10:11], hmini[0:2, :], AX.X, OP.max, apply_absolute_value=True))
        p.op("vector", lambda h: h.tensor_mul(stat[0:2, 10:11], stat[0:2, 10:11], r3h[0:2, :]))
        p.op("vector", lambda h: h.tensor_scalar_max(stat[0:2, 10:11], stat[0:2, 10:11], 1e-5))
        p.op("vector", lambda h: h.reciprocal(stat[0:2, 11:12], stat[0:2, 10:11]))
        p.op("vector", lambda h: h.tensor_scalar_mul(stat[0:2, 11:12], stat[0:2, 11:12], 127.0))
        p.op("vector", lambda h: h.reciprocal(stat[0:2, 12:13], stat[0:2, 11:12]))
        p.op("vector", lambda h: h.tensor_mul(stat[0:2, 14:15], r3h[0:2, :], stat[0:2, 11:12]))
        p.op("vector", lambda h: h.tensor_scalar(
            hmini[0:2, :], hmini[0:2, :], stat[0:2, 14:15], magic[0:2, 0:1],
            OP.mult, OP.add))
        cq = p.op("vector", lambda h: h.tensor_scalar(
            hmini[0:2, :], hmini[0:2, :], magic[0:2, 0:1], stat[0:2, 12:13],
            OP.subtract, OP.mult))
        hvm = cq
        for dch in range(KD):
            t0 = p.op("tensor", lambda h, dch=dch: h.matmul(
                ppD[:, 0:2], hmini[0:2, dch * 128:(dch + 1) * 128],
                idf[0:2, 0:2], is_transpose=True, start=True, stop=True),
                waits=[("sv", hvm)])
            p.op("vector", lambda h, dch=dch, ob=ob: h.tensor_scalar(
                xqf[:, dch * 516:dch * 516 + 1], ppD[:, 0:1],
                mfn[:, 2 * ob:2 * ob + 1], None, OP.mult), waits=[("st", t0)])
            hvm = p.op("vector", lambda h, dch=dch, ob=ob: h.tensor_scalar(
                xqf[:, dch * 516 + 513:dch * 516 + 514], ppD[:, 1:2],
                mfn[:, 2 * ob + 1:2 * ob + 2], None, OP.mult))

        ldm = p.dma("scalar", lambda h, ob=ob: h.dma_start(
            out=xmidt[:], in_=spillM[ob]),
            waits=[("sv", p.cnt["sv"]), ("sp", p.cnt["sp"])])
        p.wait("vector", [("sp", ldm)])
        for lt in range(4):
            src = xmidt[:, lt * D:(lt + 1) * D]
            mu3 = stat[:, 0:1]
            r3 = stat[:, 1:2]
            ln_stats(src, 128, mu3, r3)
            p.op("vector", lambda h, src=src, mu3=mu3: h.tensor_scalar(
                xoslot[:, 0:D], src, mu3, None, OP.subtract))
            p.op("vector", lambda h: h.tensor_reduce(
                stat[:, 10:11], xoslot[:, 0:D], AX.X, OP.max,
                apply_absolute_value=True))
            p.op("vector", lambda h, r3=r3: h.tensor_mul(stat[:, 10:11], stat[:, 10:11], r3))
            p.op("vector", lambda h: h.tensor_scalar_max(stat[:, 10:11], stat[:, 10:11], 1e-5))
            p.op("vector", lambda h: h.reciprocal(stat[:, 11:12], stat[:, 10:11]))
            p.op("vector", lambda h: h.tensor_scalar_mul(stat[:, 11:12], stat[:, 11:12], 127.0))
            p.op("vector", lambda h: h.reciprocal(stat[:, 12:13], stat[:, 11:12]))
            p.op("vector", lambda h, r3=r3: h.tensor_mul(stat[:, 14:15], r3, stat[:, 11:12]))
            p.op("vector", lambda h: h.tensor_scalar(
                xoslot[:, 0:D], xoslot[:, 0:D], stat[:, 14:15], magic[:, 0:1],
                OP.mult, OP.add))
            cq2 = p.op("vector", lambda h: h.tensor_scalar(
                xqtk[:], xoslot[:, 0:D], magic[:, 0:1], stat[:, 12:13],
                OP.subtract, OP.mult))
            sc = p.cnt["ss"]
            for dch in range(KD):
                t1 = p.op("tensor", lambda h, dch=dch, lt=lt: h.matmul(
                    ppD[:, 512:640], xqtk[:, dch * 128:(dch + 1) * 128],
                    idf[:], is_transpose=True, start=True, stop=True),
                    waits=[("sv", cq2), ("ss", sc)])
                sc = p.op("scalar", lambda h, dch=dch, lt=lt: h.copy(
                    xqf[:, dch * 516 + 1 + lt * 128:dch * 516 + 1 + (lt + 1) * 128],
                    ppD[:, 512:640]), waits=[("st", t1)])
            yield
        ss_xq = p.cnt["ss"]
        yield

        mzero = p.op("vector", lambda h: h.memset(mqt[:], 0.0))
        p.wait("tensor", [("ss", ss_xq)])
        convdone = 0
        for mh in range(2):
            w1ld = p.dma("sync", lambda h, mh=mh: h.dma_start(
                out=ar2b[:, 0:KD * E],
                in_=wb_w1p[:, mh * KD * E:(mh + 1) * KD * E]),
                waits=[("st", p.cnt["st"])])
            for mi in range(KE):
                m = mh * KE + mi
                for k in range(KD):
                    w_ = []
                    if mi == 0 and k == 0:
                        w_.append(("sq", w1ld))
                    if k == 0:
                        w_.append(("sv", convdone))
                    p.op("tensor", lambda h, mi=mi, k=k: h.matmul(
                        ppD[:, 0:257], wstr[k][:, mi * 128:(mi + 1) * 128],
                        xqf3[:, k, 0:257], start=(k == 0), stop=(k == KD - 1)),
                        waits=w_)
                for k in range(KD):
                    p.op("tensor", lambda h, mi=mi, k=k: h.matmul(
                        ppD[:, 512:769], wstr[k][:, mi * 128:(mi + 1) * 128],
                        xqf3[:, k, 257:514], start=(k == 0), stop=(k == KD - 1)))
                stw1 = p.cnt["st"]
                t = hslot[:, (m % 2) * LB:(m % 2 + 1) * LB]
                p.op("vector", lambda h, m=m, t=t: h.tensor_scalar(
                    t[:, 0:257], ppD[:, 0:257], fcw[:, m * 3:m * 3 + 1],
                    fcb[:, m:m + 1], OP.mult, OP.add), waits=[("st", stw1)])
                p.op("vector", lambda h, m=m, t=t: h.tensor_scalar(
                    t[:, 257:512], ppD[:, 512:767], fcw[:, m * 3:m * 3 + 1],
                    fcb[:, m:m + 1], OP.mult, OP.add))
                p.op("vector", lambda h, m=m, t=t: h.scalar_tensor_tensor(
                    t[:, 0:256], ppD[:, 1:257], fcw[:, m * 3 + 1:m * 3 + 2],
                    t[:, 0:256], OP.mult, OP.add))
                p.op("vector", lambda h, m=m, t=t: h.scalar_tensor_tensor(
                    t[:, 256:512], ppD[:, 512:768], fcw[:, m * 3 + 1:m * 3 + 2],
                    t[:, 256:512], OP.mult, OP.add))
                p.op("vector", lambda h, m=m, t=t: h.scalar_tensor_tensor(
                    t[:, 0:255], ppD[:, 2:257], fcw[:, m * 3 + 2:m * 3 + 3],
                    t[:, 0:255], OP.mult, OP.add))
                convdone = p.op("vector", lambda h, m=m, t=t: h.scalar_tensor_tensor(
                    t[:, 255:512], ppD[:, 512:769], fcw[:, m * 3 + 2:m * 3 + 3],
                    t[:, 255:512], OP.mult, OP.add))
                hqs = hq[:, (m % 4) * LB:((m % 4) + 1) * LB]
                scs = p.op("scalar", lambda h, m=m, t=t, hqs=hqs: h.activation(
                    hqs, t, AF.Silu),
                    waits=[("sv", convdone), ("sp", p.cnt["sp"])])
                p.dma("scalar", lambda h, ob=ob, m=m, hqs=hqs: h.dma_start(
                    out=spillH[ob, :, m * LB:(m + 1) * LB],
                    in_=hqs), waits=[("ss", scs)])
                sab = p.op("scalar", lambda h, hqs=hqs: h.activation(
                    hslot[:, LB:2 * LB], hqs, AF.Abs))
                p.op("vector", lambda h, m=m: h.tensor_max(
                    mqt[:], mqt[:], hslot[:, LB:2 * LB]),
                    waits=[("ss", sab), ("sv", mzero)] if m == 0 else [("ss", sab)])
                yield
        svmx = p.cnt["sv"]
        yield

        mcols = stat[:, 24:28]
        prevv = svmx
        for lt in range(4):
            pdst = ppD if lt % 2 == 0 else ppE
            tt = p.op("tensor", lambda h, lt=lt, pdst=pdst: h.matmul(
                pdst[:, 0:128], mqt[:, lt * 128:(lt + 1) * 128], idf[:],
                is_transpose=True, start=True, stop=True),
                waits=[("sv", prevv), ("st", p.cnt["st"])] if lt == 0
                else [("sv", prevv)])
            p.op("vector", lambda h, pdst=pdst: h.tensor_reduce(
                stat[:, 12:13], pdst[:, 0:128], AX.X, OP.max), waits=[("st", tt)])
            p.op("vector", lambda h, lt=lt: h.tensor_scalar_max(
                mcols[:, lt:lt + 1], stat[:, 12:13], 1e-5))
            p.op("vector", lambda h, lt=lt: h.reciprocal(
                stat[:, 13:14], mcols[:, lt:lt + 1]))
            cshc = p.op("vector", lambda h: h.tensor_scalar_mul(
                stat[:, 13:14], stat[:, 13:14], 127.0))
            prevv = cshc
            p.op("tensor", lambda h, lt=lt: h.matmul(
                ppP1[0:1, 512 + lt * 128:512 + (lt + 1) * 128], stat[:, 13:14],
                idf[:], is_transpose=True, start=True, stop=True),
                waits=[("sv", cshc)])
        crow = p.op("vector", lambda h: h.tensor_copy(
            k2st[0:1, 0:LB], ppP1[0:1, 512:512 + LB]), waits=[("st", p.cnt["st"])])
        tb = bcast_row(ppP2[:, 512:1024], LB, 0, crow)
        csh = p.op("vector", lambda h: h.tensor_copy(shb[:], ppP2[:, 512:1024]),
                   waits=[("st", tb)])

        sp_w1end = p.cnt["sp"]
        lh_tok = 0
        lh2_tok = 0
        shs_tok = {}
        mmB = {15: 0}
        svev2 = 0
        for hh in range(2):
            w2ld = p.dma("sync", lambda h, hh=hh: h.dma_start(
                out=ar2b[:, 0:16 * D],
                in_=wb_w2p[:, hh * 16 * D:(hh + 1) * 16 * D]),
                waits=[("st", p.cnt["st"])])
            mmA = {}
            for k in range(16):
                kk = hh * 16 + k
                spar = k % 4
                hqs = hq[:, spar * LB:(spar + 1) * LB]
                if k % 4 == 0:
                    g = kk // 4
                    lw = [("slh", lh_tok)]
                    if kk == 0:
                        lw.append(("sp", sp_w1end))
                        lw.append(("st", p.cnt["st"]))
                    elif k == 0:
                        lw.append(("st", mmB[15]))
                    else:
                        lw.append(("st", mmA[k - 1]))
                    if g - 1 in shs_tok:
                        lw.append(("shs", shs_tok[g - 1]))
                    lh_tok = p.dma("scalar", lambda h, ob=ob, kk=kk: h.dma_start(
                        out=hq[:, 0:4 * LB],
                        in_=spillH[ob, :, kk * LB:(kk + 4) * LB]),
                        waits=lw, sem="slh")
                p.op("vector", lambda h, hqs=hqs: h.tensor_mul(
                    hslot[:, 0:LB], hqs, shb[:]),
                    waits=[("slh", lh_tok)]
                    + ([("sv", csh)] if kk == 0 else []))
                svr = p.op("vector", lambda h, hqs=hqs: h.tensor_scalar(
                    hqs, hslot[:, 0:LB], magic[:, 0:1], magic[:, 0:1],
                    OP.add, OP.subtract))
                if k % 4 == 3:
                    shs_tok[kk // 4] = p.dma(
                        "scalar", lambda h, ob=ob, kk=kk: h.dma_start(
                            out=spillH2[ob, :, (kk - 3) * LB:(kk + 1) * LB],
                            in_=hq[:, 0:4 * LB]),
                        waits=[("sv", svr)], sem="shs")
                for m in range(4):
                    w_ = []
                    if m == 0:
                        w_.append(("sv", svr))
                        if k == 0:
                            w_.append(("sq", w2ld))
                            if hh == 0:
                                w_.append(("ss", ss_xq))
                            else:
                                w_.append(("sv", svev2))
                    p.op("tensor", lambda h, k=k, m=m, hqs=hqs: h.matmul(
                        pbank4[m], ar2b[:, k * D + m * 128:k * D + (m + 1) * 128],
                        hqs, start=(k == 0), stop=(k == 15)), waits=w_)
                mmA[k] = p.cnt["st"]
                yield
            stmm = p.cnt["st"]
            evA = 0
            for m in range(4):
                if hh == 0:
                    evA = p.op("vector", lambda h, m=m: h.tensor_copy(
                        ohi[:, m * LB:(m + 1) * LB], pbank4[m]),
                        waits=[("st", stmm)] if m == 0 else ())
                else:
                    evA = p.op("vector", lambda h, m=m: h.tensor_add(
                        ohi[:, m * LB:(m + 1) * LB],
                        ohi[:, m * LB:(m + 1) * LB], pbank4[m]),
                        waits=[("st", stmm)] if m == 0 else ())
            yield
            mmB = {}
            for k in range(16):
                kk = hh * 16 + k
                spar = k % 4
                hqs = hq[:, spar * LB:(spar + 1) * LB]
                if k % 4 == 0:
                    g = kk // 4
                    lw = [("slh2", lh2_tok), ("shs", shs_tok[hh * 4 + 3]),
                          ("st", mmA[15] if k == 0 else mmB[k - 1])]
                    lh2_tok = p.dma("scalar", lambda h, ob=ob, kk=kk: h.dma_start(
                        out=hq[:, 0:4 * LB],
                        in_=spillH2[ob, :, kk * LB:(kk + 4) * LB]),
                        waits=lw, sem="slh2")
                for m in range(4):
                    w_ = []
                    if m == 0:
                        w_.append(("slh2", lh2_tok))
                        if k == 0:
                            w_.append(("sv", evA))
                    p.op("tensor", lambda h, k=k, m=m, hqs=hqs: h.matmul(
                        pbank4[m],
                        ar2b[:, k * D + (4 + m) * 128:k * D + (5 + m) * 128],
                        hqs, start=(k == 0), stop=(k == 15)), waits=w_)
                mmB[k] = p.cnt["st"]
                yield
            stmm = p.cnt["st"]
            for m in range(4):
                if hh == 0:
                    svev2 = p.op("vector", lambda h, m=m: h.tensor_copy(
                        ohi[:, (4 + m) * LB:(5 + m) * LB], pbank4[m]),
                        waits=[("st", stmm)] if m == 0 else ())
                else:
                    svev2 = p.op("vector", lambda h, m=m: h.tensor_add(
                        ohi[:, (4 + m) * LB:(5 + m) * LB],
                        ohi[:, (4 + m) * LB:(5 + m) * LB], pbank4[m]),
                        waits=[("st", stmm)] if m == 0 else ())
            yield

        ldm2 = p.dma("scalar", lambda h, ob=ob: h.dma_start(
            out=xmidt[:], in_=spillM[ob]),
            waits=[("sv", p.cnt["sv"]), ("sp", p.cnt["sp"])])
        fin = 0
        cf = 0
        lastout = 0
        for lt in range(4):
            pdst = ppD if lt % 2 == 0 else ppE
            p.op("vector", lambda h, lt=lt: h.tensor_scalar_mul(
                stat[:, 17:18], mcols[:, lt:lt + 1], 1.0 / 127.0))
            cf = p.op("vector", lambda h: h.tensor_mul(stat[:, 17:18], stat[:, 17:18], rw2))
            for m in range(KD):
                w_ = [("sv", svev2 if lt < 2 else fin)] if m == 0 else []
                p.op("tensor", lambda h, lt=lt, m=m, pdst=pdst: h.matmul(
                    pdst[:, m * 128:(m + 1) * 128],
                    ohi[:, m * LB + lt * 128:m * LB + (lt + 1) * 128],
                    idf[:], is_transpose=True, start=True, stop=True), waits=w_)
            sttr = p.cnt["st"]
            fin = p.op("vector", lambda h, lt=lt, pdst=pdst: h.scalar_tensor_tensor(
                xoslot[:, 0:D], pdst[:, 0:D], stat[:, 17:18],
                xmidt[:, lt * D:(lt + 1) * D], OP.mult, OP.add),
                waits=[("st", sttr), ("sp", ldm2), ("sv", cf), ("sp", lastout)])
            lastout = p.dma("scalar", lambda h, ob=ob, lt=lt: h.dma_start(
                out=y_out[ob, lt * 128:(lt + 1) * 128, :], in_=xoslot[:, 0:D]),
                waits=[("sv", fin)])
            yield
        pbE[r] = {"sv": p.cnt["sv"], "ss": p.cnt["ss"]}

    def mkstep(g):
        done = [g is None]

        def s():
            if done[0]:
                return False
            try:
                next(g)
                return True
            except StopIteration:
                done[0] = True
                return False
        return s

    for r in range(NR + 2):
        sg_ = mkstep(scan_gen(r) if r < NR else None)
        pa_ = mkstep(post_a_gen(r - 1) if 0 <= r - 1 < NR else None)
        pb_ = mkstep(post_b_gen(r - 2) if 0 <= r - 2 < NR else None)
        progress = True
        while progress:
            progress = sg_()
            for _ in range(5):
                if pa_():
                    progress = True
                elif pb_():
                    progress = True

    p.wait("vector", [("sp", p.cnt["sp"]), ("sq", p.cnt["sq"]),
                      ("sg", p.cnt["sg"]), ("syq", p.cnt["syq"]),
                      ("sxq", p.cnt["sxq"])])

    with nc.Block() as block:
        p.finalize(block)
    es.close()
    return nc


def _mk_sel32():
    sel = np.zeros((32, 32, 128), np.float32)
    for s in range(32):
        sel[s, s, :] = 1.0
    return np.ascontiguousarray(sel.transpose(1, 0, 2).reshape(32, 32 * 128)
                                ).astype(ml_dtypes.bfloat16)


def host_prep(inputs, NB):
    NR = NB // 4
    OB = NB // 4
    L = NB * LB
    x = np.ascontiguousarray(inputs["L0_attn_seq"], dtype=np.float32)
    f32 = lambda v: np.ascontiguousarray(np.asarray(v), dtype=np.float32)

    ipT = f32(np.asarray(inputs["L0_attn_in_proj_w"]).T)
    xprT = f32(np.asarray(inputs["L0_attn_x_proj_w"]).T)
    dtpT_full = f32(np.asarray(inputs["L0_attn_dt_proj_w"]).T)
    cw_full = f32(np.asarray(inputs["L0_attn_conv_w"]))
    cb_full = f32(np.asarray(inputs["L0_attn_conv_b"]))
    dtb_full = f32(np.asarray(inputs["L0_attn_dt_proj_b"]))
    D_full = f32(np.asarray(inputs["L0_attn_D"]))
    shared = {
        "out_projT": f32(np.asarray(inputs["L0_attn_out_proj_w"]).T),
        "w1T": f32(np.asarray(inputs["L0_ffn_w1"]).T),
        "w2T": f32(np.asarray(inputs["L0_ffn_w2"]).T),
        "ln_g8": f32(np.asarray(inputs["L0_attn_ln_g"]).reshape(KD, 128).T),
        "ln_b8": f32(np.asarray(inputs["L0_attn_ln_b"]).reshape(KD, 128).T),
        "fcw96": f32(np.asarray(inputs["L0_ffn_conv_w"])[:, 0, :].reshape(KH, 128, 3)
                     .transpose(1, 0, 2).reshape(128, KH * 3)),
        "fcb32": f32(np.asarray(inputs["L0_ffn_conv_b"]).reshape(KH, 128).T),
        "rs_col": f32(np.broadcast_to(np.asarray(inputs["L0_attn_res_scale"],
                                                 np.float32).reshape(1, 1), (128, 1))),
        "identc": np.eye(128, dtype=ml_dtypes.bfloat16),
        "identf": np.eye(128, dtype=np.float32),
        "sel32": _mk_sel32(),
        "onesk2": np.concatenate([np.ones((1, 128)), np.zeros((1, 128))]
                                 ).astype(ml_dtypes.bfloat16),
    }

    in_maps = []
    for c in range(8):
        b, m = c // 4, c % 4
        sl = slice(512 * m, 512 * (m + 1))
        xb = x[b, :L].reshape(NB, LB, D)
        own = []
        for ob in range(OB):
            blks = []
            for j in range(4):
                g = 4 * ob + j
                blk = np.zeros((LB + 3, D), np.float32)
                blk[3:] = xb[g]
                if g > 0:
                    blk[0:3] = xb[g - 1][LB - 3:]
                blks.append(blk)
            own.append(np.stack(blks))
        x_own = np.ascontiguousarray(np.stack(own))
        x_mine = np.ascontiguousarray(
            np.stack([xb[4 * ob + m] for ob in range(OB)]))
        ip_s = np.concatenate([ipT[:, sl], ipT[:, E + 512 * m:E + 512 * (m + 1)]],
                              axis=1)
        xpT_s = np.ascontiguousarray(
            xprT[sl].reshape(4, 128, 96).transpose(1, 0, 2).reshape(128, 4 * 96))
        dtpT_s = np.ascontiguousarray(dtpT_full[:, sl])
        cw_s = f32(cw_full[sl].reshape(4, 128, 4).transpose(1, 0, 2)
                   .reshape(128, 16))
        cb_s = f32(cb_full[sl].reshape(4, 128).T)
        dtb_s = f32(dtb_full[sl].reshape(4, 128).T)
        D_s = f32(D_full[sl].reshape(4, 128).T)

        Al = np.asarray(inputs["L0_attn_A_log"])[512 * m:512 * (m + 1)]
        A_my = f32(Al.reshape(4, 128, S).transpose(1, 0, 2).reshape(128, 4 * S))

        idx = np.zeros((128, NIDX), np.int32)
        pcol = np.arange(128, dtype=np.int32)
        for j in range(4):
            for q in range(4):
                idx[:, IDXC[("dt", j, q)]] = j * 4128 + pcol * 16 + (m * 4 + q)
                idx[:, IDXC[("dx", j, q)]] = j * 4128 + E + pcol * 16 + (m * 4 + q)
            idx[0:32, IDXC[("bcs", j)]] = j * 4128 + 2 * E + np.arange(32)
        for ch in range(16):
            j, qd = ch // 4, ch % 4
            idx[:, IDXC[("y", ch)]] = j * 2048 + m * 512 + pcol * 4 + qd
        for ob in range(min(OB, 8)):
            g = 4 * ob + m
            lrow = 0 if g == 0 else ((g - 1) // 4) * 8 + ((g - 1) % 4) * 2 + 1
            rrow = 0 if g == NB - 1 else ((g + 1) // 4) * 8 + ((g + 1) % 4) * 2 + 0
            idx[0, IDXC[("xh", ob)]] = lrow
            idx[1, IDXC[("xh", ob)]] = rrow

        mcv_ = np.ones((128, 3 * NB), np.float32)
        mfn_ = np.ones((128, 16), np.float32)
        for g in range(NB):
            if g == 0:
                mcv_[:, 3 * g:3 * g + 3] = 0.0
        for ob in range(min(OB, 8)):
            g = 4 * ob + m
            if g == 0:
                mfn_[:, 2 * ob] = 0.0
            if g == NB - 1:
                mfn_[:, 2 * ob + 1] = 0.0

        im = dict(shared)
        im.update(x_own=x_own, x_mine=x_mine, in_projTs=np.ascontiguousarray(ip_s),
                  x_projTs=xpT_s, dt_projTs=dtpT_s, conv_w16=cw_s, conv_b4=cb_s,
                  dtb4=dtb_s, D4=D_s, A_log_my=A_my, idxs=idx, maskcv=mcv_,
                  maskfn=mfn_)
        in_maps.append(im)
    return in_maps


def kernel(**inputs):
    NB = np.asarray(inputs["L0_attn_seq"]).shape[1] // LB
    in_maps = host_prep(inputs, NB)
    nc = build_nc(NB)
    res = run_bass_kernel_spmd(nc, in_maps, core_ids=list(range(8)))
    L = NB * LB
    OB = NB // 4
    out = np.empty((2, L, D), np.float32)
    for c in range(8):
        b, m = c // 4, c % 4
        yo = np.asarray(res.results[c]["y_out"])
        for ob in range(OB):
            g = 4 * ob + m
            out[b, g * LB:(g + 1) * LB] = yo[ob]
    return out


# version 10 (final)
# speedup vs baseline: 1.1575x; 1.0361x over previous
import numpy as np
from contextlib import ExitStack
import ml_dtypes

import concourse.bass as bass
import concourse.mybir as mybir
from concourse.bass import IndirectOffsetOnAxis
import bass_rust
ROP = bass_rust.ReduceOp
from concourse.bass_utils import run_bass_kernel_spmd

F32 = mybir.dt.float32
BF16 = mybir.dt.bfloat16
I32 = mybir.dt.int32
AF = mybir.ActivationFunctionType
OP = mybir.AluOpType
AX = mybir.AxisListType

D = 1024
E = 2048
S = 16
RK = 64
H = 4096
LB = 512
KD = D // 128
KE = E // 128
KH = H // 128
MAGIC = 12582912.0
EPS = 1e-5

ENGS = ("sync", "vector", "scalar", "tensor", "gpsimd")
SEMS = ("sq", "sp", "sg", "sv", "ss", "st", "sgp", "scc", "syq", "sxq", "sgy0", "sgy1", "sly0", "sly1", "shx", "sow", "shs", "slh", "slh2")
_QSEM = {"sync": "sq", "scalar": "sp", "gpsimd": "sg"}
_ESEM = {"vector": "sv", "scalar": "ss", "tensor": "st", "gpsimd": "sgp"}


class Prog:
    def __init__(self, nc):
        self.nc = nc
        self.q = {e: [] for e in ENGS}
        self.cnt = {s: 0 for s in SEMS}
        self.sem = {}

    def _push(self, eng, fn, inc, waits):
        sems = self.sem

        def run(h):
            for sname, val in waits:
                if val > 0:
                    h.wait_ge(sems[sname], val)
            ins = fn(h)
            if ins is not None and inc is not None:
                ins.then_inc(sems[inc[0]], inc[1])
        self.q[eng].append(run)

    def op(self, eng, fn, waits=()):
        s = _ESEM[eng]
        selfw = ((s, self.cnt[s]),)
        self.cnt[s] += 1
        self._push(eng, fn, (s, 1), tuple(waits) + selfw)
        return self.cnt[s]

    def dma(self, eng, fn, waits=(), sem=None):
        s = sem or _QSEM[eng]
        self.cnt[s] += 16
        self._push(eng, fn, (s, 16), tuple(waits))
        return self.cnt[s]

    def cc(self, fn, waits=()):
        self.cnt["scc"] += 1
        self._push("gpsimd", fn, ("scc", 1), tuple(waits))
        return self.cnt["scc"]

    def wait(self, eng, waits):
        self._push(eng, lambda h: None, None, tuple(waits))

    def barrier(self):
        snap = [(s, self.cnt[s]) for s in SEMS if self.cnt[s] > 0]
        for eng in ENGS:
            self.wait(eng, snap)

    def finalize(self, block):
        secs = {"sync": block.sync, "vector": block.vector, "scalar": block.scalar,
                "tensor": block.tensor, "gpsimd": block.gpsimd}
        for eng in ENGS:
            fns = self.q[eng]
            if not fns:
                continue
            def body(h, fns=tuple(fns)):
                for f in fns:
                    f(h)
            secs[eng](body)


IDXC = {}
_col = 0
for _j in range(4):
    for _q in range(4):
        IDXC[("dt", _j, _q)] = _col; _col += 1
        IDXC[("dx", _j, _q)] = _col; _col += 1
for _j in range(4):
    IDXC[("bcs", _j)] = _col; _col += 1
for _ch in range(16):
    IDXC[("y", _ch)] = _col; _col += 1
for _ob in range(8):
    IDXC[("xh", _ob)] = _col; _col += 1
NIDX = 96


def build_nc(NB, debug=False):
    NR = NB // 4
    OB = NB // 4
    nc = bass.Bass("TRN2", target_bir_lowering=False, detect_race_conditions=False)

    dr = lambda n, shp, dt_: nc.dram_tensor(n, shp, dt_, kind="ExternalInput").ap()
    x_own = dr("x_own", [OB, 4, LB + 3, D], F32)
    x_mine = dr("x_mine", [OB, LB, D], F32)
    in_projTs = dr("in_projTs", [D, 1024], F32)
    out_projT = dr("out_projT", [E, D], F32)
    w1T = dr("w1T", [D, H], F32)
    w2T = dr("w2T", [H, D], F32)
    x_projTs = dr("x_projTs", [128, 4 * 96], F32)
    dt_projTs = dr("dt_projTs", [RK, 512], F32)
    ln_g8 = dr("ln_g8", [128, KD], F32)
    ln_b8 = dr("ln_b8", [128, KD], F32)
    A_log_my = dr("A_log_my", [128, 4 * S], F32)
    conv_w16 = dr("conv_w16", [128, 16], F32)
    conv_b4 = dr("conv_b4", [128, 4], F32)
    dtb4 = dr("dtb4", [128, 4], F32)
    D4 = dr("D4", [128, 4], F32)
    fcw96 = dr("fcw96", [128, KH * 3], F32)
    fcb32 = dr("fcb32", [128, KH], F32)
    rs_col = dr("rs_col", [128, 1], F32)
    idxs = dr("idxs", [128, NIDX], I32)
    maskcv = dr("maskcv", [128, 3 * NB], F32)
    maskfn = dr("maskfn", [128, 2 * 8], F32)
    identc = dr("identc", [128, 128], BF16)
    identf = dr("identf", [128, 128], F32)
    sel32 = dr("sel32", [32, 32 * 128], BF16)
    onesk2 = dr("onesk2", [2, 128], BF16)

    y_out = nc.dram_tensor("y_out", [OB, LB, D], F32, kind="ExternalOutput").ap()
    dbgt = nc.dram_tensor("dbgt", [8, 128, 4160], F32, kind="ExternalOutput").ap() if debug else None

    di = lambda n, shp, dt_: nc.dram_tensor(n, shp, dt_, kind="Internal").ap()
    wb_ipz = di("wb_ipz", [128, KD * 512], BF16)
    wb_ipx = di("wb_ipx", [128, KD * 512], BF16)
    wb_opp = di("wb_opp", [128, KE * D], BF16)
    wb_w1p = di("wb_w1p", [128, 2 * KD * E], BF16)
    wb_w2p = di("wb_w2p", [128, KH * D], BF16)
    dtS = di("dtS", [OB, 128, 16 * LB], F32)
    dxS = di("dxS", [OB, 128, 16 * LB], F32)
    ccA_in = di("ccA_in", [NR, 4 * 96, LB], F32)
    ccA_out = di("ccA_out", [NR, 4 * 96, LB], F32)
    ccR_in = di("ccR_in", [NR, 4 * 512, LB], BF16)
    ccR_out = [di(f"ccR_out{r}", [4 * 4 * 512, LB], BF16) for r in range(NR)]
    ccX_in = di("ccX_in", [NR, 2, D], F32)
    ccX_out = di("ccX_out", [NR * 8, D], F32)
    spillX = di("spillX", [OB, 128, KE * LB], BF16)
    spillZ = di("spillZ", [OB, 128, KE * LB], BF16)
    spillH = di("spillH", [OB, 128, KH * LB], BF16)
    spillH2 = di("spillH2", [OB, 128, KH * LB], BF16)
    spillM = di("spillM", [OB, 128, 4 * D], F32)
    bias0d = di("bias0d", [1024], F32)

    es = ExitStack()
    sb = lambda n, shp, dt_: es.enter_context(nc.sbuf_tensor(n, shp, dt_))
    pt = lambda n, shp: es.enter_context(nc.psum_tensor(n, shp, F32))
    p = Prog(nc)
    for sname in SEMS:
        p.sem[sname] = es.enter_context(nc.semaphore(sname))

    ident = sb("ident", [128, 128], BF16)
    idf = sb("idf", [128, 128], F32)
    selt = sb("selt", [32, 32 * 128], BF16)
    ok2 = sb("ok2", [2, 128], BF16)
    k2st = sb("k2st", [2, 512], F32)
    k2hi = sb("k2hi", [2, 512], BF16)
    k2lo = sb("k2lo", [2, 512], BF16)
    lnG = sb("lnG", [128, KD], F32)
    lnB = sb("lnB", [128, KD], F32)
    lnBb = sb("lnBb", [128, KD], BF16)
    A64 = sb("A64", [128, 4 * S], F32)
    cw64 = sb("cw64", [128, 16], F32)
    cb16 = sb("cb16", [128, 4], F32)
    dtb = sb("dtb", [128, 4], F32)
    Dc16 = sb("Dc16", [128, 4], F32)
    fcw = sb("fcw", [128, KH * 3], F32)
    fcb = sb("fcb", [128, KH], F32)
    rsc = sb("rsc", [128, 1], F32)
    magic = sb("magic", [128, 1], F32)
    idxt = sb("idxt", [128, NIDX], I32)
    mcv = sb("mcv", [128, 3 * NB], F32)
    mfn = sb("mfn", [128, 2 * 8], F32)
    bias0 = sb("bias0", [128, 8], F32)
    xpT = sb("xpT", [128, 4 * 96], BF16)
    dtpT = sb("dtpT", [128, 512], BF16)
    statc = sb("statc", [128, 8 * OB], F32)
    colw = sb("colw", [128, 16], F32)

    ar1 = sb("ar1", [128, KE * LB + 2 * D], F32)
    ar2 = sb("ar2", [128, KE * LB], F32)
    ar3 = sb("ar3", [128, 9520], F32)
    ar3b = ar3[:].bitcast(BF16)
    xslb = ar3b[:, 0:KE * LB]
    xnf = ar3b[:, KE * LB:KE * LB + KD * 520]
    xcvs = ar3[:, 6176:6176 + 2 * 520]
    tcv = ar3[:, 7216:7216 + 2 * LB]
    zbs = ar3b[:, 2 * 8240:2 * 8240 + 2 * LB]
    xdb = ar3[:, 8752:8752 + LB]
    xdbb = ar3b[:, 2 * 9264:2 * 9264 + LB]
    stat = sb("stat", [128, 40], F32)

    ppS1 = pt("ppS1", [128, 1024])
    ppS2 = pt("ppS2", [128, 1024])
    ppP1 = pt("ppP1", [128, 1024])
    ppP2 = pt("ppP2", [128, 1024])
    ppA = ppS1[:, 0:512]
    ppB = ppS1[:, 512:1024]
    ppF = ppS2[:, 0:512]
    ppG = ppS2[:, 512:1024]
    ppD = ppP1
    ppE = ppP2
    psms8 = [ppA, ppB, ppF, ppG, ppD[:, 0:512], ppD[:, 512:1024],
             ppE[:, 0:512], ppE[:, 512:1024]]
    pbank4 = [ppP1[:, 0:512], ppP1[:, 512:1024],
              ppP2[:, 0:512], ppP2[:, 512:1024]]

    xres = ar1[:, 0:4 * D]
    xhal = ar1[:, 4 * D:5 * D]
    ar1b = ar1[:].bitcast(BF16)
    xnbB = ar1[:, 5 * D:9 * D]
    xnhB = ar1[:, 9 * D:10 * D]
    xsl = ar1[:, 0:KE * LB]
    ar2b = ar2[:].bitcast(BF16)
    wstr = [ar2b[:, k * 2048:(k + 1) * 2048] for k in range(KD)]
    dtt = ar2[:, 0:KE * LB]
    wfull = ar1[:, 0:2 * E]
    wcastB = ar1b[:, 2 * (2 * E):2 * (2 * E) + 2 * E]
    stat0row = ar2[:, 0:2 * E]

    def bcast_row(psdst, w, src_row_off, waitv):
        c1 = p.op("vector", lambda h: h.tensor_copy(
            k2hi[0:1, src_row_off:src_row_off + w],
            k2st[0:1, src_row_off:src_row_off + w]), waits=[("sv", waitv)])
        c2 = p.op("vector", lambda h: h.tensor_tensor(
            k2st[1:2, src_row_off:src_row_off + w] if False else
            k2lo[0:1, src_row_off:src_row_off + w],
            k2st[0:1, src_row_off:src_row_off + w],
            k2hi[0:1, src_row_off:src_row_off + w], OP.subtract))
        t1 = p.op("tensor", lambda h: h.matmul(
            psdst, ok2[:, 0:128], k2hi[0:2, src_row_off:src_row_off + w],
            start=True, stop=False), waits=[("sv", c2)])
        t2 = p.op("tensor", lambda h: h.matmul(
            psdst, ok2[:, 0:128], k2lo[0:2, src_row_off:src_row_off + w],
            start=False, stop=True))
        return t2



    for dst, src in ((ident[:], identc), (idf[:], identf), (lnG[:], ln_g8),
                     (lnB[:], ln_b8), (A64[:], A_log_my), (cw64[:], conv_w16),
                     (cb16[:], conv_b4), (dtb[:], dtb4), (Dc16[:], D4),
                     (fcw[:], fcw96), (fcb[:], fcb32), (rsc[:], rs_col),
                     (idxt[:], idxs), (mcv[:], maskcv), (mfn[:], maskfn),
                     (selt[:], sel32), (ok2[:], onesk2)):
        p.dma("sync", lambda h, d=dst, s=src: h.dma_start(out=d, in_=s))
    ld0 = p.cnt["sq"]
    p.dma("gpsimd", lambda h: h.dma_start(out=xpT[:], in_=x_projTs))
    p.dma("gpsimd", lambda h: h.dma_start(out=dtpT[0:RK, :], in_=dt_projTs))
    lg0 = p.cnt["sg"]
    sA = p.op("scalar", lambda h: h.activation(A64[:], A64[:], AF.Exp),
              waits=[("sq", ld0)])
    p.op("vector", lambda h: h.tensor_scalar_mul(A64[:], A64[:], -1.0),
         waits=[("ss", sA)])
    p.op("vector", lambda h: h.tensor_copy(lnBb[:], lnB[:]), waits=[("sq", ld0)])
    p.op("vector", lambda h: h.memset(magic[:], MAGIC))
    p.op("vector", lambda h: h.memset(k2st[:], 0.0))
    p.op("vector", lambda h: h.memset(k2hi[:], 0.0))
    p.op("vector", lambda h: h.memset(k2lo[:], 0.0))

    for k in range(KD):
        ld = p.dma("sync", lambda h, k=k: h.dma_start(
            out=wfull[:, 0:1024], in_=in_projTs[k * 128:(k + 1) * 128, :]),
            waits=[("sv", p.cnt["sv"])])
        c1 = p.op("vector", lambda h: h.tensor_copy(
            wcastB[:, 0:1024], wfull[:, 0:1024]), waits=[("sq", p.cnt["sq"])])
        for nt in range(2):
            p.op("tensor", lambda h, k=k, nt=nt: h.matmul(
                psms8[nt][0:1, :], lnBb[:, k:k + 1],
                wcastB[:, nt * 512:(nt + 1) * 512],
                start=(k == 0), stop=(k == KD - 1)),
                waits=[("sv", c1)] if nt == 0 else ())
        c2 = p.op("vector", lambda h, k=k: h.tensor_scalar(
            wcastB[:, 0:1024], wfull[:, 0:1024], lnG[:, k:k + 1], None, OP.mult),
            waits=[("st", p.cnt["st"])])
        p.dma("sync", lambda h, k=k: h.dma_start(
            out=wb_ipx[:, k * 512:(k + 1) * 512], in_=wcastB[:, 0:512]),
            waits=[("sv", c2)])
        p.dma("sync", lambda h, k=k: h.dma_start(
            out=wb_ipz[:, k * 512:(k + 1) * 512], in_=wcastB[:, 512:1024]))
    cb0 = 0
    for nt in range(2):
        cb0 = p.op("vector", lambda h, nt=nt: h.tensor_copy(
            stat0row[0:1, nt * 512:(nt + 1) * 512], psms8[nt][0:1, :]),
            waits=[("st", p.cnt["st"])] if nt == 0 else ())
    db0 = p.dma("sync", lambda h: h.dma_start(out=bias0d,
                in_=stat0row[0:1, 0:1024]), waits=[("sv", cb0)])
    p.dma("sync", lambda h: h.dma_start(
        out=bias0[:], in_=bias0d.rearrange("(c p) -> p c", p=128),
        allow_slow_non_contiguous=True), waits=[("sq", db0)])

    for k in range(KE):
        ld = p.dma("sync", lambda h, k=k: h.dma_start(
            out=wfull[:, 0:D], in_=out_projT[k * 128:(k + 1) * 128, :]),
            waits=[("sv", p.cnt["sv"])])
        c = p.op("vector", lambda h: h.tensor_copy(wcastB[:, 0:D], wfull[:, 0:D]),
                 waits=[("sq", p.cnt["sq"])])
        p.dma("sync", lambda h, k=k: h.dma_start(
            out=wb_opp[:, k * D:(k + 1) * D], in_=wcastB[:, 0:D]),
            waits=[("sv", c)])

    def quant_weight(wsrc, rows, cols, sdst, stores):
        nch = rows // 128
        acc = colw[:, 0:1]
        last = p.op("vector", lambda h: h.memset(acc, 0.0))
        for k in range(nch):
            ld = p.dma("sync", lambda h, k=k: h.dma_start(
                out=wfull[:, 0:cols], in_=wsrc[k * 128:(k + 1) * 128, :]),
                waits=[("sv", p.cnt["sv"])])
            p.op("vector", lambda h: h.tensor_reduce(
                colw[:, 1:2], wfull[:, 0:cols], AX.X, OP.add,
                apply_absolute_value=True), waits=[("sq", ld)])
            last = p.op("vector", lambda h: h.tensor_add(acc, acc, colw[:, 1:2]))
        tt = p.op("tensor", lambda h: h.matmul(
            ppB[0:1, 0:128], acc, idf[:], is_transpose=True, start=True, stop=True),
            waits=[("sv", last)])
        rr = p.op("vector", lambda h: h.tensor_reduce(
            k2st[0:1, 0:1], ppB[0:1, 0:128], AX.X, OP.add), waits=[("st", tt)])
        p.op("vector", lambda h: h.tensor_scalar_mul(
            k2st[0:1, 0:1], k2st[0:1, 0:1], 1.0 / (rows * cols)))
        p.op("vector", lambda h: h.tensor_scalar_max(k2st[0:1, 0:1], k2st[0:1, 0:1], 1e-5))
        rr = p.op("vector", lambda h: h.reciprocal(k2st[0:1, 1:2], k2st[0:1, 0:1]))
        t2 = bcast_row(ppB[:, 0:1], 1, 1, rr)
        cs = p.op("vector", lambda h, sdst=sdst: h.tensor_copy(sdst, ppB[:, 0:1]),
                  waits=[("st", t2)])
        for k in range(nch):
            ld = p.dma("sync", lambda h, k=k: h.dma_start(
                out=wfull[:, 0:cols], in_=wsrc[k * 128:(k + 1) * 128, :]),
                waits=[("sv", p.cnt["sv"]), ("sq", p.cnt["sq"])])
            p.op("vector", lambda h, sdst=sdst: h.tensor_scalar(
                wfull[:, 0:cols], wfull[:, 0:cols], sdst, magic[:, 0:1],
                OP.mult, OP.add), waits=[("sq", ld)])
            p.op("vector", lambda h: h.tensor_scalar(
                wfull[:, 0:cols], wfull[:, 0:cols], magic[:, 0:1], 1.0,
                OP.subtract, OP.min))
            c1 = p.op("vector", lambda h: h.tensor_scalar(
                wcastB[:, 0:cols], wfull[:, 0:cols], -1.0, None, OP.max))
            first = True
            for dst_ap, s0, s1 in stores(k):
                p.dma("sync", lambda h, dst_ap=dst_ap, s0=s0, s1=s1: h.dma_start(
                    out=dst_ap, in_=wcastB[:, s0:s1]),
                    waits=[("sv", c1)] if first else ())
                first = False

    sw1 = colw[:, 8:9]
    sw2 = colw[:, 9:10]
    quant_weight(w1T, D, H, sw1, lambda k: [
        (wb_w1p[:, k * E:(k + 1) * E], 0, E),
        (wb_w1p[:, (KD + k) * E:(KD + k + 1) * E], E, 2 * E)])
    quant_weight(w2T, H, D, sw2, lambda k: [
        (wb_w2p[:, k * D:(k + 1) * D], 0, D)])
    rw1 = colw[:, 10:11]
    rw2 = colw[:, 11:12]
    p.op("vector", lambda h: h.reciprocal(rw1, sw1))
    p.op("vector", lambda h: h.reciprocal(rw2, sw2))
    p.op("vector", lambda h: h.tensor_scalar(fcw[:], fcw[:], rw1, None, OP.mult))

    zv = p.op("vector", lambda h: h.memset(wfull[:, 0:D], 0.0))
    nzr = (NR * 8 + 127) // 128
    for zi in range(nzr):
        r0, r1 = zi * 128, min((zi + 1) * 128, NR * 8)
        p.dma("sync", lambda h, r0=r0, r1=r1: h.dma_start(
            out=ccX_out[r0:r1, :], in_=wfull[0:r1 - r0, 0:D]), waits=[("sv", zv)])

    p.barrier()

    def rsqrt_newton(w, dst, var_col, s0, s1):
        cv = p.op("vector", lambda h: h.tensor_scalar_add(s0[0:w, :], var_col[0:w, :], EPS))
        c = p.op("scalar", lambda h: h.activation(s1[0:w, :], s0[0:w, :], AF.Sqrt),
                 waits=[("sv", cv)])
        p.op("vector", lambda h: h.reciprocal(dst[0:w, :], s1[0:w, :]), waits=[("ss", c)])
        p.op("vector", lambda h: h.tensor_mul(s1[0:w, :], dst[0:w, :], dst[0:w, :]))
        p.op("vector", lambda h: h.tensor_mul(s1[0:w, :], s1[0:w, :], s0[0:w, :]))
        p.op("vector", lambda h: h.tensor_scalar(s1[0:w, :], s1[0:w, :], -0.5, 1.5,
                                                 OP.mult, OP.add))
        return p.op("vector", lambda h: h.tensor_mul(dst[0:w, :], dst[0:w, :], s1[0:w, :]))

    def ln_stats(src, w, mu_dst, r1_dst, rtot_dst=None):
        bno = stat[:, 20:32]
        p.op("vector", lambda h: h.bn_stats(bno[0:w, 0:6], src[0:w, 0:D // 2]))
        p.op("vector", lambda h: h.bn_stats(bno[0:w, 6:12], src[0:w, D // 2:D]))
        p.op("vector", lambda h: h.bn_aggr(stat[0:w, 32:34], bno[0:w, 0:12]))
        p.op("vector", lambda h: h.tensor_copy(mu_dst[0:w, :], stat[0:w, 32:33]))
        rsqrt_newton(w, r1_dst, stat[:, 33:34], stat[:, 35:36], stat[:, 36:37])
        if rtot_dst is not None:
            p.op("vector", lambda h: h.tensor_mul(
                stat[0:w, 37:38], r1_dst[0:w, :], r1_dst[0:w, :]))
            p.op("vector", lambda h: h.tensor_mul(
                stat[0:w, 37:38], stat[0:w, 37:38], stat[0:w, 33:34]))
            rsqrt_newton(w, stat[:, 38:39], stat[:, 37:38], stat[:, 35:36],
                         stat[:, 36:37])
            return p.op("vector", lambda h: h.tensor_mul(
                rtot_dst[0:w, :], r1_dst[0:w, :], stat[0:w, 38:39]))
        return p.cnt["sv"]


    def bcast_row(psdst, w, src_row_off, waitv):
        c1 = p.op("vector", lambda h: h.tensor_copy(
            k2hi[0:1, src_row_off:src_row_off + w],
            k2st[0:1, src_row_off:src_row_off + w]), waits=[("sv", waitv)])
        c2 = p.op("vector", lambda h: h.tensor_tensor(
            k2st[1:2, src_row_off:src_row_off + w] if False else
            k2lo[0:1, src_row_off:src_row_off + w],
            k2st[0:1, src_row_off:src_row_off + w],
            k2hi[0:1, src_row_off:src_row_off + w], OP.subtract))
        t1 = p.op("tensor", lambda h: h.matmul(
            psdst, ok2[:, 0:128], k2hi[0:2, src_row_off:src_row_off + w],
            start=True, stop=False), waits=[("sv", c2)])
        t2 = p.op("tensor", lambda h: h.matmul(
            psdst, ok2[:, 0:128], k2lo[0:2, src_row_off:src_row_off + w],
            start=False, stop=True))
        return t2

    ccF_done = {}
    ccR_done = {}
    paE = {}
    ccX_done = {}
    sends_end = {}
    ysend_tok = {0: 0}
    last_scan_par = {0: 0, 1: 0, 2: 0, 3: 0}
    scan_last = {"ss": 0, "sv": 0}

    bcf = ar1[:, 0:LB]
    bchi = ar1b[:, 2 * LB:3 * LB]
    bclo = ar1b[:, 3 * LB:4 * LB]
    rdt = ar1[:, 2 * LB:6 * LB]
    rdx = ar1[:, 6 * LB:10 * LB]
    hbuf = ar1[:, 10 * LB:14 * LB]
    yq = ar1[:, 14 * LB:18 * LB]
    carry = ar1[:, 18 * LB:18 * LB + 64]
    dAs = sb("dAs", [128, 4 * LB], F32)
    usl = sb("usl", [128, 2 * LB], F32)
    yin = sb("yin", [128, 4 * LB], BF16)
    ysl2 = sb("ysl2", [128, 2 * LB], F32)
    xsz = sb("xsz", [128, 8 * LB], BF16)
    xmidt = ar3[:, 0:4 * D]
    ohi = ar3[:, 4 * D:4 * D + KD * LB]
    xoslot = ar3[:, 4 * D + KD * LB:4 * D + KD * LB + D]
    xqf = sb("xqf", [128, KD * 516], BF16)

    hq = sb("hq", [128, 4 * LB], BF16)
    mqt = sb("mqt", [128, LB], F32)
    shb = sb("shb", [128, LB], F32)
    hmini = sb("hmini", [128, D], F32)
    xqtk = hmini
    hslot = sb("hslot", [128, 2 * LB], F32)
    xqf3 = xqf[:].rearrange("p (c n) -> p c n", n=516)

    prev_sends = {"d3": 0}
    ccA_done = {}
    ldzx = p.dma("sync", lambda h: h.dma_start(
        out=ar2b[:, 0:KD * 512], in_=wb_ipz[:, :]),
        waits=[("sq", p.cnt["sq"])])
    ldzx2 = p.dma("sync", lambda h: h.dma_start(
        out=ar2b[:, KD * 512:2 * KD * 512], in_=wb_ipx[:, :]),
        waits=[("sq", ldzx)])
    xszf = xsz[:].bitcast(F32)
    xslP = {0: dAs[:, 0:4 * LB], 1: xszf[:, 0:4 * LB]}
    spdt = {"slast": 0, "ssdt": 0, "xdbe": 0}

    def dt_phase(ob, j):
        lxd = p.dma("sync", lambda h, ob=ob, j=j: h.dma_start(
            out=xdb[0:RK, :], in_=ccA_out[ob, j * 96:j * 96 + RK, :]),
            waits=[("scc", ccA_done[(ob, j)]), ("sxq", p.cnt["sxq"])],
            sem="sxq")
        cbb = p.op("vector", lambda h: h.tensor_copy(
            xdbb[0:RK, :], xdb[0:RK, :]), waits=[("sxq", lxd)])
        for pair in range(2):
            ssdt = 0
            for u in range(2):
                i = pair * 2 + u
                psm = ppF if u == 0 else ppG
                w_ = [("ss", spdt["ssdt"])]
                if pair == 0 and u == 0:
                    w_.append(("sv", cbb))
                    w_.append(("sv", spdt["xdbe"]))
                tm = p.op("tensor", lambda h, i=i, psm=psm: h.matmul(
                    psm, dtpT[0:RK, i * 128:(i + 1) * 128], xdbb[0:RK, :],
                    start=True, stop=True), waits=w_)
                p.op("scalar", lambda h, i=i, u=u, psm=psm: h.activation(
                    usl[:, u * LB:(u + 1) * LB], psm, AF.Exp,
                    bias=dtb[:, i:i + 1]),
                    waits=[("st", tm), ("sg", spdt["slast"])])
                ssdt = p.op("scalar", lambda h, u=u: h.activation(
                    usl[:, u * LB:(u + 1) * LB], usl[:, u * LB:(u + 1) * LB],
                    AF.Ln, bias=1.0))
            spdt["ssdt"] = ssdt
            mv = 0
            for u in range(2):
                i = pair * 2 + u
                mv = p.op("vector", lambda h, ob=ob, j=j, i=i, u=u: h.tensor_mul(
                    ysl2[:, u * LB:(u + 1) * LB],
                    xslP[j % 2][:, i * LB:(i + 1) * LB],
                    usl[:, u * LB:(u + 1) * LB]),
                    waits=[("ss", ssdt), ("sg", spdt["slast"])] if u == 0
                    else ())
            spdt["slast"] = p.dma("gpsimd", lambda h, ob=ob, j=j, pair=pair:
                h.dma_start(
                    out=dtS[ob, :, (j * 4 + pair * 2) * LB:(j * 4 + pair * 2 + 2) * LB],
                    in_=usl[:, 0:2 * LB]), waits=[("ss", ssdt)])
            spdt["slast"] = p.dma("gpsimd", lambda h, ob=ob, j=j, pair=pair:
                h.dma_start(
                    out=dxS[ob, :, (j * 4 + pair * 2) * LB:(j * 4 + pair * 2 + 2) * LB],
                    in_=ysl2[:, 0:2 * LB]), waits=[("sv", mv)])

    dt_pend = None
    for ob in range(OB):
        for j in range(4):
            blkid = ob * 4 + j
            ldx = p.dma("sync", lambda h, ob=ob, j=j: h.dma_start(
                out=xres.rearrange("p (t n) -> p t n", n=D),
                in_=x_own[ob, j, 3:3 + LB, :].rearrange("(t p) n -> p t n", p=128)),
                waits=[("sxq", prev_sends["d3"]), ("sv", p.cnt["sv"]),
                       ("st", p.cnt["st"])], sem="sxq")
            ldh = p.dma("sync", lambda h, ob=ob, j=j: h.dma_start(
                out=xhal[0:3, :], in_=x_own[ob, j, 0:3, :]), sem="sxq")

            p.wait("vector", [("sxq", ldh), ("st", p.cnt["st"])])
            for lt in range(4):
                src = xres[:, lt * D:(lt + 1) * D]
                mu = stat[:, lt:lt + 1]
                rt = stat[:, 4 + lt:5 + lt]
                r1 = stat[:, 34:35]
                ln_stats(src, 128, mu, r1, rt)
                p.op("vector", lambda h, src=src, lt=lt, mu=mu, rt=rt: h.tensor_scalar(
                    xnbB[:, lt * D:(lt + 1) * D], src, mu, rt, OP.subtract, OP.mult))
            p.wait("vector", [("sxq", ldh)])
            muh = stat[:, 8:9]
            rth = stat[:, 9:10]
            ln_stats(xhal, 3, muh, stat[:, 34:35], rth)
            svh = p.op("vector", lambda h: h.tensor_scalar(
                xnhB[0:3, :], xhal[0:3, :], muh[0:3, :], rth[0:3, :],
                OP.subtract, OP.mult))

            sc = {0: 0, 1: 0}
            for dch in range(KD):
                pdst = ppD if dch % 2 == 0 else ppE
                t0 = p.op("tensor", lambda h, dch=dch, pdst=pdst: h.matmul(
                    pdst[:, 509:512], xnhB[0:3, dch * 128:(dch + 1) * 128],
                    idf[0:3, 0:3], is_transpose=True, start=True, stop=True),
                    waits=[("sv", svh), ("ss", sc[dch % 2])] if dch < 2
                    else [("ss", sc[dch % 2])])
                for lt in range(4):
                    p.op("tensor", lambda h, dch=dch, lt=lt, pdst=pdst: h.matmul(
                        pdst[:, 512 + lt * 128:640 + lt * 128],
                        xnbB[:, lt * D + dch * 128:lt * D + (dch + 1) * 128],
                        idf[:], is_transpose=True, start=True, stop=True))
                tt = p.cnt["st"]
                sc[dch % 2] = p.op("scalar", lambda h, dch=dch, pdst=pdst: h.copy(
                    xnf[:, dch * 520:dch * 520 + 515], pdst[:, 509:1024]),
                    waits=[("st", tt)])
            ss_xnf = p.cnt["ss"]
            xnf3 = xnf[:].rearrange("p (c n) -> p c n", n=520)

            if dt_pend is not None:
                dt_phase(*dt_pend)
                dt_pend = None

            p.wait("tensor", [("ss", ss_xnf)])
            z_evac = {0: 0, 1: 0}
            for i in range(4):
                psm = ppA if i % 2 == 0 else ppB
                for k in range(KD):
                    w_ = []
                    if ob == 0 and j == 0 and i == 0 and k == 0:
                        w_.append(("sq", ldzx))
                    if k == 0:
                        w_.append(("ss", z_evac[i % 2]))
                    p.op("tensor", lambda h, i=i, k=k, psm=psm: h.matmul(
                        psm, ar2b[:, k * 512 + i * 128:k * 512 + (i + 1) * 128],
                        xnf3[:, k, 3:515], start=(k == 0), stop=(k == KD - 1)),
                        waits=w_)
                stz = p.cnt["st"]
                zslot = zbs[:, (i % 2) * LB:(i % 2 + 1) * LB]
                scz = p.op("scalar", lambda h, i=i, psm=psm, zslot=zslot: h.activation(
                    zslot, psm, AF.Silu, bias=bias0[:, 4 + i:5 + i]),
                    waits=[("st", stz), ("sp", p.cnt["sp"])])
                z_evac[i % 2] = scz
                p.dma("scalar", lambda h, ob=ob, j=j, i=i, zslot=zslot: h.dma_start(
                    out=spillZ[ob, :, (j * 4 + i) * LB:(j * 4 + i + 1) * LB],
                    in_=zslot), waits=[("ss", scz)])

            xsl_j = xslP[j % 2]
            xslb_j = xslb[:, 0:4 * LB]
            for i in range(4):
                for k in range(KD):
                    w_ = []
                    if ob == 0 and j == 0 and i == 0 and k == 0:
                        w_.append(("sq", ldzx2))
                    if k == 0:
                        w_.append(("sv", p.cnt["sv"]))
                    p.op("tensor", lambda h, i=i, k=k: h.matmul(
                        ppD[:, 0:257],
                        ar2b[:, 4096 + k * 512 + i * 128:4096 + k * 512 + (i + 1) * 128],
                        xnf3[:, k, 0:257], start=(k == 0), stop=(k == KD - 1)),
                        waits=w_)
                for k in range(KD):
                    p.op("tensor", lambda h, i=i, k=k: h.matmul(
                        ppD[:, 512:770],
                        ar2b[:, 4096 + k * 512 + i * 128:4096 + k * 512 + (i + 1) * 128],
                        xnf3[:, k, 257:515], start=(k == 0), stop=(k == KD - 1)))
                stx = p.cnt["st"]
                xc = xcvs[:, (i % 2) * 520:(i % 2) * 520 + 515]
                p.op("vector", lambda h, xc=xc, i=i: h.tensor_scalar(
                    xc[:, 0:257], ppD[:, 0:257], bias0[:, i:i + 1], None, OP.add),
                    waits=[("st", stx)])
                p.op("vector", lambda h, xc=xc, i=i: h.tensor_scalar(
                    xc[:, 257:515], ppD[:, 512:770], bias0[:, i:i + 1], None, OP.add))
                p.op("vector", lambda h, xc=xc, blkid=blkid: h.tensor_scalar(
                    xc[:, 0:3], xc[:, 0:3], mcv[:, 3 * blkid:3 * blkid + 1],
                    None, OP.mult))
                t = tcv[:, (i % 2) * LB:(i % 2 + 1) * LB]
                p.op("vector", lambda h, i=i, xc=xc, t=t: h.tensor_scalar(
                    t, xc[:, 0:LB], cw64[:, i * 4:i * 4 + 1],
                    cb16[:, i:i + 1], OP.mult, OP.add))
                for kk in range(1, 4):
                    p.op("vector", lambda h, i=i, xc=xc, t=t, kk=kk: h.scalar_tensor_tensor(
                        t, xc[:, kk:kk + LB], cw64[:, i * 4 + kk:i * 4 + kk + 1], t,
                        OP.mult, OP.add))
                svt = p.cnt["sv"]
                scs = p.op("scalar", lambda h, i=i, t=t, xsl_j=xsl_j: h.activation(
                    xsl_j[:, i * LB:(i + 1) * LB], t, AF.Silu), waits=[("sv", svt)])
                p.op("vector", lambda h, i=i, xsl_j=xsl_j, xslb_j=xslb_j: h.tensor_copy(
                    xslb_j[:, i * LB:(i + 1) * LB], xsl_j[:, i * LB:(i + 1) * LB]),
                    waits=[("ss", scs)])
            sv_x = p.cnt["sv"]
            p.dma("scalar", lambda h, ob=ob, j=j, xslb_j=xslb_j: h.dma_start(
                out=spillX[ob, :, j * 4 * LB:(j * 4 + 4) * LB], in_=xslb_j[:]),
                waits=[("sv", sv_x)])

            for i in range(4):
                p.op("tensor", lambda h, i=i, xslb_j=xslb_j: h.matmul(
                    ppF[0:96, :], xpT[:, i * 96:(i + 1) * 96],
                    xslb_j[:, i * LB:(i + 1) * LB], start=(i == 0), stop=(i == 3)),
                    waits=[("sv", sv_x), ("ss", spdt["ssdt"])] if i == 0 else ())
            st_xd = p.cnt["st"]
            xdbe = p.op("vector", lambda h: h.tensor_copy(
                xdb[0:96, :], ppF[0:96, :]),
                waits=[("st", st_xd), ("sxq", prev_sends["d3"])])
            spdt["xdbe"] = xdbe
            prev_sends["d3"] = p.dma("sync", lambda h, ob=ob, j=j: h.dma_start(
                out=ccA_in[ob, j * 96:(j + 1) * 96, :], in_=xdb[0:96, :]),
                waits=[("sv", xdbe)], sem="sxq")
            ccA_done[(ob, j)] = p.cc(lambda h, ob=ob, j=j: h.collective_compute(
                "AllReduce", OP.add,
                replica_groups=[[0, 1, 2, 3], [4, 5, 6, 7]],
                ins=[ccA_in[ob, j * 96:(j + 1) * 96, :]],
                outs=[ccA_out[ob, j * 96:(j + 1) * 96, :]]),
                waits=[("sxq", prev_sends["d3"])])
            dt_pend = (ob, j)
    dt_phase(*dt_pend)
    dt_pend = None

    pbE = {"pre": {"sv": 0, "ss": 0}}
    p.barrier()

    p.op("vector", lambda h: h.memset(carry[:], 0.0))

    def scan_gen(r):
        for blkslot in range(4):
            gw = [("ss", scan_last["ss"]), ("sv", scan_last["sv"])]
            p.dma("gpsimd", lambda h, r=r, blkslot=blkslot: h.dma_start(
                out=rdt[:], in_=dtS[r, :, blkslot * 4 * LB:(blkslot + 1) * 4 * LB]),
                waits=gw)
            p.dma("gpsimd", lambda h, r=r, blkslot=blkslot: h.dma_start(
                out=rdx[:], in_=dxS[r, :, blkslot * 4 * LB:(blkslot + 1) * 4 * LB]))
            p.dma("gpsimd", lambda h, r=r, blkslot=blkslot: h.dma_start(
                out=bcf[0:32, :],
                in_=ccA_out[r, blkslot * 96 + RK:blkslot * 96 + 96, :]))
            gall = p.cnt["sg"]
            lsem = "sly%d" % (blkslot % 2)
            p.dma("scalar", lambda h, r=r, blkslot=blkslot: h.dma_start(
                out=xsz[:, 0:4 * LB],
                in_=spillX[r, :, blkslot * 4 * LB:(blkslot + 1) * 4 * LB]),
                waits=[("sv", scan_last["sv"]), ("sp", p.cnt["sp"])], sem=lsem)
            lzs = p.dma("scalar", lambda h, r=r, blkslot=blkslot: h.dma_start(
                out=xsz[:, 4 * LB:8 * LB],
                in_=spillZ[r, :, blkslot * 4 * LB:(blkslot + 1) * 4 * LB]),
                sem=lsem)
            ch = p.op("vector", lambda h: h.tensor_copy(bchi[0:32, :], bcf[0:32, :]),
                      waits=[("sg", gall), ("st", p.cnt["st"])])
            cl = p.op("vector", lambda h: h.tensor_tensor(
                bclo[0:32, :], bcf[0:32, :], bchi[0:32, :], OP.subtract))

            for qt in range(4):
                for jp in range(2):
                    selw = [("sv", p.cnt["sv"])]
                    for j2 in range(2):
                        s = 4 * qt + 2 * jp + j2
                        dB = ppS1[:, j2 * 512:(j2 + 1) * 512]
                        dC = ppS2[:, j2 * 512:(j2 + 1) * 512]
                        p.op("tensor", lambda h, s=s, dB=dB: h.matmul(
                            dB, selt[0:32, s * 128:(s + 1) * 128],
                            bchi[0:32, :], start=True, stop=False),
                            waits=selw if j2 == 0 else ())
                        p.op("tensor", lambda h, s=s, dB=dB: h.matmul(
                            dB, selt[0:32, s * 128:(s + 1) * 128],
                            bclo[0:32, :], start=False, stop=True))
                        p.op("tensor", lambda h, s=s, dC=dC: h.matmul(
                            dC, selt[0:32, (16 + s) * 128:(17 + s) * 128],
                            bchi[0:32, :], start=True, stop=False))
                        p.op("tensor", lambda h, s=s, dC=dC: h.matmul(
                            dC, selt[0:32, (16 + s) * 128:(17 + s) * 128],
                            bclo[0:32, :], start=False, stop=True))
                    stsel = p.cnt["st"]
                    for qq in range(4):
                        for j2 in range(2):
                            s = 4 * qt + 2 * jp + j2
                            par = (qq * 2 + j2) % 4
                            da = dAs[:, par * LB:(par + 1) * LB]
                            uu = usl[:, j2 * LB:(j2 + 1) * LB]
                            sw_ = ([("sg", gall)]
                                   if (qt == 0 and jp == 0 and qq == 0 and j2 == 0)
                                   else [])
                            sw_.append(("sv", last_scan_par[par]))
                            se = p.op("scalar", lambda h, qq=qq, s=s, da=da: h.activation(
                                da, rdt[:, qq * LB:(qq + 1) * LB], AF.Exp,
                                scale=A64[:, qq * S + s:qq * S + s + 1]), waits=sw_)
                            uw = [("ss", se), ("st", stsel)]
                            if qt == 0 and jp == 0 and qq == 0 and j2 == 0:
                                uw.append(("syq", p.cnt["syq"]))
                            p.op("vector", lambda h, qq=qq, j2=j2, uu=uu: h.tensor_mul(
                                uu, rdx[:, qq * LB:(qq + 1) * LB],
                                ppS1[:, j2 * 512:(j2 + 1) * 512]), waits=uw)
                            last_scan_par[par] = p.op("vector",
                                lambda h, qq=qq, s=s, j2=j2, da=da, uu=uu: h.tensor_tensor_scan(
                                    hbuf[:, j2 * LB:(j2 + 1) * LB], da, uu,
                                    carry[:, qq * S + s:qq * S + s + 1],
                                    OP.mult, OP.add))
                        p.op("vector", lambda h, qq=qq, qt=qt, jp=jp: h.tensor_copy(
                            carry[:, qq * S + qt * 4 + 2 * jp:qq * S + qt * 4 + 2 * jp + 2],
                            hbuf[:, 0:2 * LB].rearrange("p (s n) -> p s n", n=LB)[:, :, LB - 1:LB]))
                        p.op("vector", lambda h: h.tensor_mul(
                            hbuf[:, 0:2 * LB], hbuf[:, 0:2 * LB], ppS2[:, 0:1024]))
                        h3 = hbuf[:, 0:2 * LB].rearrange("p (s n) -> p n s", n=LB)
                        if qt == 0 and jp == 0:
                            p.op("vector", lambda h, qq=qq, h3=h3: h.tensor_reduce(
                                yq[:, qq * LB:(qq + 1) * LB], h3, AX.X, OP.add))
                        else:
                            p.op("vector", lambda h, h3=h3: h.tensor_reduce(
                                hbuf[:, 2 * LB:3 * LB], h3, AX.X, OP.add))
                            p.op("vector", lambda h, qq=qq: h.tensor_add(
                                yq[:, qq * LB:(qq + 1) * LB],
                                yq[:, qq * LB:(qq + 1) * LB], hbuf[:, 2 * LB:3 * LB]))
                    yield
            ybf = ar1b[:, 20 * LB:24 * LB]
            for i in range(4):
                p.op("vector", lambda h, i=i: h.scalar_tensor_tensor(
                    yq[:, i * LB:(i + 1) * LB], xsz[:, i * LB:(i + 1) * LB],
                    Dc16[:, i:i + 1], yq[:, i * LB:(i + 1) * LB],
                    OP.mult, OP.add),
                    waits=[(lsem, lzs)] if i == 0 else ())
                p.op("vector", lambda h, i=i, ybf=ybf: h.tensor_mul(
                    ybf[:, i * LB:(i + 1) * LB], yq[:, i * LB:(i + 1) * LB],
                    xsz[:, (4 + i) * LB:(5 + i) * LB]))
            svy = p.cnt["sv"]
            scan_last["ss"] = p.cnt["ss"]
            scan_last["sv"] = svy
            ysend_tok[0] = p.dma("sync", lambda h, r=r, blkslot=blkslot, ybf=ybf:
                h.dma_start(
                    out=ccR_in[r, blkslot * 512:(blkslot + 1) * 512, :]
                        .rearrange("(pp q) n -> pp q n", pp=128),
                    in_=ybf.rearrange("p (q n) -> p q n", n=LB)),
                waits=[("sv", svy)], sem="syq")
        sy = p.cnt["syq"]
        sends_end[r] = sy
        ccR_done[r] = p.cc(lambda h, r=r: h.collective_compute(
            "AllGather", OP.bypass,
            replica_groups=[[0, 1, 2, 3], [4, 5, 6, 7]],
            ins=[ccR_in[r]], outs=[ccR_out[r]]), waits=[("syq", sy)])

    def post_a_gen(r):
        ob = r
        opld1 = p.dma("sync", lambda h: h.dma_start(
            out=ar2b[:, 0:8 * D], in_=wb_opp[:, 0:8 * D]),
            waits=[("st", p.cnt["st"])], sem="sow")
        opld2 = p.dma("sync", lambda h: h.dma_start(
            out=ar2b[:, 8 * D:KE * D], in_=wb_opp[:, 8 * D:KE * D]),
            waits=[("sow", opld1)], sem="sow")
        pbe = pbE.get(r - 2, pbE.get("pre", {"sv": 0, "ss": 0}))
        p.wait("tensor", [("sv", pbe["sv"]), ("ss", pbe["ss"])])

        def gather_y(k, slot_war):
            spar = k % 4
            gsem = "sgy%d" % (k % 2)
            yslot = yin[:, spar * LB:(spar + 1) * LB]
            gw_ = [("scc", ccR_done[r])]
            if slot_war:
                gw_.append(("st", slot_war))
            gy = p.dma("gpsimd", lambda h, r=r, k=k, yslot=yslot:
                h.indirect_dma_start(
                    out=yslot, out_offset=None, in_=ccR_out[r],
                    in_offset=IndirectOffsetOnAxis(
                        ap=idxt[:, IDXC[("y", k)]:IDXC[("y", k)] + 1], axis=0)),
                waits=gw_, sem=gsem)
            return yslot, (gsem, gy)

        mmk = {}
        for k in range(KE):
            yslot, gtok = gather_y(k, mmk.get(k - 2))
            for m in range(4):
                p.op("tensor", lambda h, k=k, m=m, yslot=yslot: h.matmul(
                    pbank4[m], ar2b[:, k * D + m * 128:k * D + (m + 1) * 128],
                    yslot, start=(k == 0), stop=(k == KE - 1)),
                    waits=([("sow", opld1 if k < 8 else opld2), gtok]
                           if m == 0 and k % 8 == 0 else
                           ([gtok] if m == 0 else ())))
            mmk[k] = p.cnt["st"]
            yield
        stmm = p.cnt["st"]
        ev1 = 0
        for m in range(4):
            ev1 = p.op("vector", lambda h, m=m: h.tensor_copy(
                ohi[:, m * LB:(m + 1) * LB], pbank4[m]),
                waits=[("st", stmm)] if m == 0 else ())
        yield
        mmk2nd = {}
        for k in range(KE):
            yslot, gtok = gather_y(k, mmk2nd.get(k - 2, mmk.get(14 + k)))
            for m in range(4):
                w_ = []
                if m == 0:
                    w_ = [gtok] + ([("sv", ev1)] if k == 0 else [])
                p.op("tensor", lambda h, k=k, m=m, yslot=yslot: h.matmul(
                    pbank4[m], ar2b[:, k * D + (4 + m) * 128:k * D + (5 + m) * 128],
                    yslot, start=(k == 0), stop=(k == KE - 1)), waits=w_)
            mmk2nd[k] = p.cnt["st"]
            yield
        stmm = p.cnt["st"]
        ev = 0
        for m in range(4):
            ev = p.op("vector", lambda h, m=m: h.tensor_copy(
                ohi[:, (4 + m) * LB:(5 + m) * LB], pbank4[m]),
                waits=[("st", stmm)] if m == 0 else ())
        svev = ev

        ldxr = p.dma("sync", lambda h, ob=ob: h.dma_start(
            out=xmidt.rearrange("p (t n) -> p t n", n=D),
            in_=x_mine[ob].rearrange("(t p) n -> p t n", p=128)),
            waits=[("sv", p.cnt["sv"]), ("sp", p.cnt["sp"]),
                   ("sxq", p.cnt["sxq"])], sem="sxq")
        p.wait("vector", [("sxq", ldxr)])
        for lt in range(4):
            src_ = xmidt[:, lt * D:(lt + 1) * D]
            mu = stat[:, lt:lt + 1]
            r1 = stat[:, 8:9]
            ln_stats(src_, 128, mu, r1)
            p.op("vector", lambda h, lt=lt, r1=r1: h.tensor_scalar_add(
                stat[:, 4 + lt:5 + lt], r1, 1.0))
            p.op("vector", lambda h, lt=lt, mu=mu, r1=r1: h.tensor_mul(
                stat[:, lt:lt + 1], mu, r1))
        xm_done = 0
        for lt in range(4):
            pdst = ppD if lt % 2 == 0 else ppE
            for m in range(KD):
                w_ = []
                if m == 0:
                    w_.append(("sv", svev if lt < 2 else xm_done))
                p.op("tensor", lambda h, lt=lt, m=m, pdst=pdst: h.matmul(
                    pdst[:, m * 128:(m + 1) * 128],
                    ohi[:, m * LB + lt * 128:m * LB + (lt + 1) * 128],
                    idf[:], is_transpose=True, start=True, stop=True), waits=w_)
            sttr = p.cnt["st"]
            p.op("vector", lambda h, lt=lt: h.tensor_scalar(
                xoslot[:, 0:D], xmidt[:, lt * D:(lt + 1) * D],
                stat[:, 4 + lt:5 + lt], stat[:, lt:lt + 1],
                OP.mult, OP.subtract))
            xm_done = p.op("vector", lambda h, lt=lt, pdst=pdst: h.scalar_tensor_tensor(
                xmidt[:, lt * D:(lt + 1) * D], pdst[:, 0:D], rsc[:, 0:1],
                xoslot[:, 0:D], OP.mult, OP.add), waits=[("st", sttr)])
            yield
        svxm = xm_done
        spm = p.dma("scalar", lambda h, ob=ob: h.dma_start(
            out=spillM[ob], in_=xmidt[:]), waits=[("sv", svxm)])
        d1 = p.dma("sync", lambda h, r=r: h.dma_start(
            out=ccX_in[r, 0:1, :], in_=xmidt[0:1, 0:D]), waits=[("sv", svxm)],
            sem="sxq")
        d2 = p.dma("sync", lambda h, r=r: h.dma_start(
            out=ccX_in[r, 1:2, :], in_=xmidt[127:128, 3 * D:4 * D]), sem="sxq")
        ccX_done[r] = p.cc(lambda h, r=r: h.collective_compute(
            "AllGather", OP.bypass,
            replica_groups=[[0, 1, 2, 3], [4, 5, 6, 7]],
            ins=[ccX_in[r]], outs=[ccX_out[r * 8:(r + 1) * 8, :]]),
            waits=[("sxq", d2)])

    def post_b_gen(r):
        ob = r
        need_right = ccX_done[min(r + 1, NR - 1)] if (4 * r + 3) + 1 <= NB - 1 else ccX_done[r]
        gh = p.dma("gpsimd", lambda h, ob=ob: h.indirect_dma_start(
            out=hmini[0:2, :], out_offset=None, in_=ccX_out,
            in_offset=IndirectOffsetOnAxis(
                ap=idxt[0:2, IDXC[("xh", ob)]:IDXC[("xh", ob)] + 1], axis=0)),
            waits=[("scc", need_right), ("sv", p.cnt["sv"])], sem="shx")
        p.wait("vector", [("shx", gh)])
        muh = stat[:, 8:9]
        r3h = stat[:, 9:10]
        ln_stats(hmini, 2, muh, r3h)
        p.op("vector", lambda h: h.tensor_scalar(
            hmini[0:2, :], hmini[0:2, :], muh[0:2, :], None, OP.subtract))
        p.op("vector", lambda h: h.tensor_reduce(
            stat[0:2, 10:11], hmini[0:2, :], AX.X, OP.max, apply_absolute_value=True))
        p.op("vector", lambda h: h.tensor_mul(stat[0:2, 10:11], stat[0:2, 10:11], r3h[0:2, :]))
        p.op("vector", lambda h: h.tensor_scalar_max(stat[0:2, 10:11], stat[0:2, 10:11], 1e-5))
        p.op("vector", lambda h: h.reciprocal(stat[0:2, 11:12], stat[0:2, 10:11]))
        p.op("vector", lambda h: h.tensor_scalar_mul(stat[0:2, 11:12], stat[0:2, 11:12], 127.0))
        p.op("vector", lambda h: h.reciprocal(stat[0:2, 12:13], stat[0:2, 11:12]))
        p.op("vector", lambda h: h.tensor_mul(stat[0:2, 14:15], r3h[0:2, :], stat[0:2, 11:12]))
        p.op("vector", lambda h: h.tensor_scalar(
            hmini[0:2, :], hmini[0:2, :], stat[0:2, 14:15], magic[0:2, 0:1],
            OP.mult, OP.add))
        cq = p.op("vector", lambda h: h.tensor_scalar(
            hmini[0:2, :], hmini[0:2, :], magic[0:2, 0:1], stat[0:2, 12:13],
            OP.subtract, OP.mult))
        for dch in range(KD):
            p.op("tensor", lambda h, dch=dch: h.matmul(
                ppD[:, 2 * dch:2 * dch + 2], hmini[0:2, dch * 128:(dch + 1) * 128],
                idf[0:2, 0:2], is_transpose=True, start=True, stop=True),
                waits=[("sv", cq)] if dch == 0 else ())
        tmini = p.cnt["st"]
        hvm = 0
        for dch in range(KD):
            p.op("vector", lambda h, dch=dch, ob=ob: h.tensor_scalar(
                xqf[:, dch * 516:dch * 516 + 1], ppD[:, 2 * dch:2 * dch + 1],
                mfn[:, 2 * ob:2 * ob + 1], None, OP.mult),
                waits=[("st", tmini)] if dch == 0 else ())
            hvm = p.op("vector", lambda h, dch=dch, ob=ob: h.tensor_scalar(
                xqf[:, dch * 516 + 513:dch * 516 + 514], ppD[:, 2 * dch + 1:2 * dch + 2],
                mfn[:, 2 * ob + 1:2 * ob + 2], None, OP.mult))

        ldm = p.dma("scalar", lambda h, ob=ob: h.dma_start(
            out=xmidt[:], in_=spillM[ob]),
            waits=[("sv", p.cnt["sv"]), ("sp", p.cnt["sp"])])
        p.wait("vector", [("sp", ldm)])
        scq = {0: 0, 1: 0}
        for lt in range(4):
            src = xmidt[:, lt * D:(lt + 1) * D]
            mu3 = stat[:, 0:1]
            r3 = stat[:, 1:2]
            ln_stats(src, 128, mu3, r3)
            p.op("vector", lambda h, src=src, mu3=mu3: h.tensor_scalar(
                xoslot[:, 0:D], src, mu3, None, OP.subtract))
            p.op("vector", lambda h: h.tensor_reduce(
                stat[:, 10:11], xoslot[:, 0:D], AX.X, OP.max,
                apply_absolute_value=True))
            p.op("vector", lambda h, r3=r3: h.tensor_mul(stat[:, 10:11], stat[:, 10:11], r3))
            p.op("vector", lambda h: h.tensor_scalar_max(stat[:, 10:11], stat[:, 10:11], 1e-5))
            p.op("vector", lambda h: h.reciprocal(stat[:, 11:12], stat[:, 10:11]))
            p.op("vector", lambda h: h.tensor_scalar_mul(stat[:, 11:12], stat[:, 11:12], 127.0))
            p.op("vector", lambda h: h.reciprocal(stat[:, 12:13], stat[:, 11:12]))
            p.op("vector", lambda h, r3=r3: h.tensor_mul(stat[:, 14:15], r3, stat[:, 11:12]))
            p.op("vector", lambda h: h.tensor_scalar(
                xoslot[:, 0:D], xoslot[:, 0:D], stat[:, 14:15], magic[:, 0:1],
                OP.mult, OP.add))
            cq2 = p.op("vector", lambda h: h.tensor_scalar(
                xqtk[:], xoslot[:, 0:D], magic[:, 0:1], stat[:, 12:13],
                OP.subtract, OP.mult))
            pq = ppD if lt % 2 == 0 else ppE
            for dch in range(KD):
                p.op("tensor", lambda h, dch=dch, pq=pq: h.matmul(
                    pq[:, dch * 128:(dch + 1) * 128],
                    xqtk[:, dch * 128:(dch + 1) * 128],
                    idf[:], is_transpose=True, start=True, stop=True),
                    waits=[("sv", cq2), ("ss", scq[lt % 2])] if dch == 0 else ())
            tq = p.cnt["st"]
            for dch in range(KD):
                scq[lt % 2] = p.op("scalar", lambda h, dch=dch, lt=lt, pq=pq: h.copy(
                    xqf[:, dch * 516 + 1 + lt * 128:dch * 516 + 1 + (lt + 1) * 128],
                    pq[:, dch * 128:(dch + 1) * 128]),
                    waits=[("st", tq)] if dch == 0 else ())
            yield
        ss_xq = p.cnt["ss"]
        yield

        mzero = p.op("vector", lambda h: h.memset(mqt[:], 0.0))
        p.wait("tensor", [("ss", ss_xq)])
        convdone = 0
        for mh in range(2):
            w1ld = p.dma("sync", lambda h, mh=mh: h.dma_start(
                out=ar2b[:, 0:KD * E],
                in_=wb_w1p[:, mh * KD * E:(mh + 1) * KD * E]),
                waits=[("st", p.cnt["st"])])
            for mi in range(KE):
                m = mh * KE + mi
                for k in range(KD):
                    w_ = []
                    if mi == 0 and k == 0:
                        w_.append(("sq", w1ld))
                    if k == 0:
                        w_.append(("sv", convdone))
                    p.op("tensor", lambda h, mi=mi, k=k: h.matmul(
                        ppD[:, 0:257], wstr[k][:, mi * 128:(mi + 1) * 128],
                        xqf3[:, k, 0:257], start=(k == 0), stop=(k == KD - 1)),
                        waits=w_)
                for k in range(KD):
                    p.op("tensor", lambda h, mi=mi, k=k: h.matmul(
                        ppD[:, 512:769], wstr[k][:, mi * 128:(mi + 1) * 128],
                        xqf3[:, k, 257:514], start=(k == 0), stop=(k == KD - 1)))
                stw1 = p.cnt["st"]
                t = hslot[:, (m % 2) * LB:(m % 2 + 1) * LB]
                p.op("vector", lambda h, m=m, t=t: h.tensor_scalar(
                    t[:, 0:257], ppD[:, 0:257], fcw[:, m * 3:m * 3 + 1],
                    fcb[:, m:m + 1], OP.mult, OP.add), waits=[("st", stw1)])
                p.op("vector", lambda h, m=m, t=t: h.tensor_scalar(
                    t[:, 257:512], ppD[:, 512:767], fcw[:, m * 3:m * 3 + 1],
                    fcb[:, m:m + 1], OP.mult, OP.add))
                p.op("vector", lambda h, m=m, t=t: h.scalar_tensor_tensor(
                    t[:, 0:256], ppD[:, 1:257], fcw[:, m * 3 + 1:m * 3 + 2],
                    t[:, 0:256], OP.mult, OP.add))
                p.op("vector", lambda h, m=m, t=t: h.scalar_tensor_tensor(
                    t[:, 256:512], ppD[:, 512:768], fcw[:, m * 3 + 1:m * 3 + 2],
                    t[:, 256:512], OP.mult, OP.add))
                p.op("vector", lambda h, m=m, t=t: h.scalar_tensor_tensor(
                    t[:, 0:255], ppD[:, 2:257], fcw[:, m * 3 + 2:m * 3 + 3],
                    t[:, 0:255], OP.mult, OP.add))
                convdone = p.op("vector", lambda h, m=m, t=t: h.scalar_tensor_tensor(
                    t[:, 255:512], ppD[:, 512:769], fcw[:, m * 3 + 2:m * 3 + 3],
                    t[:, 255:512], OP.mult, OP.add))
                hqs = hq[:, (m % 4) * LB:((m % 4) + 1) * LB]
                scs = p.op("scalar", lambda h, m=m, t=t, hqs=hqs: h.activation(
                    hqs, t, AF.Silu),
                    waits=[("sv", convdone), ("sp", p.cnt["sp"])])
                p.dma("scalar", lambda h, ob=ob, m=m, hqs=hqs: h.dma_start(
                    out=spillH[ob, :, m * LB:(m + 1) * LB],
                    in_=hqs), waits=[("ss", scs)])
                sab = p.op("scalar", lambda h, hqs=hqs: h.activation(
                    hslot[:, LB:2 * LB], hqs, AF.Abs))
                p.op("vector", lambda h, m=m: h.tensor_max(
                    mqt[:], mqt[:], hslot[:, LB:2 * LB]),
                    waits=[("ss", sab), ("sv", mzero)] if m == 0 else [("ss", sab)])
                yield
        svmx = p.cnt["sv"]
        yield

        mcols = stat[:, 24:28]
        prevv = svmx
        for lt in range(4):
            pdst = ppD if lt % 2 == 0 else ppE
            tt = p.op("tensor", lambda h, lt=lt, pdst=pdst: h.matmul(
                pdst[:, 0:128], mqt[:, lt * 128:(lt + 1) * 128], idf[:],
                is_transpose=True, start=True, stop=True),
                waits=[("sv", prevv), ("st", p.cnt["st"])] if lt == 0
                else [("sv", prevv)])
            p.op("vector", lambda h, pdst=pdst: h.tensor_reduce(
                stat[:, 12:13], pdst[:, 0:128], AX.X, OP.max), waits=[("st", tt)])
            p.op("vector", lambda h, lt=lt: h.tensor_scalar_max(
                mcols[:, lt:lt + 1], stat[:, 12:13], 1e-5))
            p.op("vector", lambda h, lt=lt: h.reciprocal(
                stat[:, 13:14], mcols[:, lt:lt + 1]))
            cshc = p.op("vector", lambda h: h.tensor_scalar_mul(
                stat[:, 13:14], stat[:, 13:14], 127.0))
            prevv = cshc
            p.op("tensor", lambda h, lt=lt: h.matmul(
                ppP1[0:1, 512 + lt * 128:512 + (lt + 1) * 128], stat[:, 13:14],
                idf[:], is_transpose=True, start=True, stop=True),
                waits=[("sv", cshc)])
        crow = p.op("vector", lambda h: h.tensor_copy(
            k2st[0:1, 0:LB], ppP1[0:1, 512:512 + LB]), waits=[("st", p.cnt["st"])])
        tb = bcast_row(ppP2[:, 512:1024], LB, 0, crow)
        csh = p.op("vector", lambda h: h.tensor_copy(shb[:], ppP2[:, 512:1024]),
                   waits=[("st", tb)])

        sp_w1end = p.cnt["sp"]
        lh_tok = 0
        lh2_tok = 0
        shs_tok = {}
        mmB = {15: 0}
        svev2 = 0
        for hh in range(2):
            w2ld = p.dma("sync", lambda h, hh=hh: h.dma_start(
                out=ar2b[:, 0:16 * D],
                in_=wb_w2p[:, hh * 16 * D:(hh + 1) * 16 * D]),
                waits=[("st", p.cnt["st"])])
            mmA = {}
            for k in range(16):
                kk = hh * 16 + k
                spar = k % 4
                hqs = hq[:, spar * LB:(spar + 1) * LB]
                if k % 4 == 0:
                    g = kk // 4
                    lw = [("slh", lh_tok)]
                    if kk == 0:
                        lw.append(("sp", sp_w1end))
                        lw.append(("st", p.cnt["st"]))
                    elif k == 0:
                        lw.append(("st", mmB[15]))
                    else:
                        lw.append(("st", mmA[k - 1]))
                    if g - 1 in shs_tok:
                        lw.append(("shs", shs_tok[g - 1]))
                    lh_tok = p.dma("scalar", lambda h, ob=ob, kk=kk: h.dma_start(
                        out=hq[:, 0:4 * LB],
                        in_=spillH[ob, :, kk * LB:(kk + 4) * LB]),
                        waits=lw, sem="slh")
                p.op("vector", lambda h, hqs=hqs: h.tensor_mul(
                    hslot[:, 0:LB], hqs, shb[:]),
                    waits=[("slh", lh_tok)]
                    + ([("sv", csh)] if kk == 0 else []))
                svr = p.op("vector", lambda h, hqs=hqs: h.tensor_scalar(
                    hqs, hslot[:, 0:LB], magic[:, 0:1], magic[:, 0:1],
                    OP.add, OP.subtract))
                if k % 4 == 3:
                    shs_tok[kk // 4] = p.dma(
                        "scalar", lambda h, ob=ob, kk=kk: h.dma_start(
                            out=spillH2[ob, :, (kk - 3) * LB:(kk + 1) * LB],
                            in_=hq[:, 0:4 * LB]),
                        waits=[("sv", svr)], sem="shs")
                for m in range(4):
                    w_ = []
                    if m == 0:
                        w_.append(("sv", svr))
                        if k == 0:
                            w_.append(("sq", w2ld))
                            if hh == 0:
                                w_.append(("ss", ss_xq))
                            else:
                                w_.append(("sv", svev2))
                    p.op("tensor", lambda h, k=k, m=m, hqs=hqs: h.matmul(
                        pbank4[m], ar2b[:, k * D + m * 128:k * D + (m + 1) * 128],
                        hqs, start=(k == 0), stop=(k == 15)), waits=w_)
                mmA[k] = p.cnt["st"]
                yield
            stmm = p.cnt["st"]
            evA = 0
            for m in range(4):
                if hh == 0:
                    evA = p.op("vector", lambda h, m=m: h.tensor_copy(
                        ohi[:, m * LB:(m + 1) * LB], pbank4[m]),
                        waits=[("st", stmm)] if m == 0 else ())
                else:
                    evA = p.op("vector", lambda h, m=m: h.tensor_add(
                        ohi[:, m * LB:(m + 1) * LB],
                        ohi[:, m * LB:(m + 1) * LB], pbank4[m]),
                        waits=[("st", stmm)] if m == 0 else ())
            yield
            mmB = {}
            for k in range(16):
                kk = hh * 16 + k
                spar = k % 4
                hqs = hq[:, spar * LB:(spar + 1) * LB]
                if k % 4 == 0:
                    g = kk // 4
                    lw = [("slh2", lh2_tok), ("shs", shs_tok[hh * 4 + 3]),
                          ("st", mmA[15] if k == 0 else mmB[k - 1])]
                    lh2_tok = p.dma("scalar", lambda h, ob=ob, kk=kk: h.dma_start(
                        out=hq[:, 0:4 * LB],
                        in_=spillH2[ob, :, kk * LB:(kk + 4) * LB]),
                        waits=lw, sem="slh2")
                for m in range(4):
                    w_ = []
                    if m == 0:
                        w_.append(("slh2", lh2_tok))
                        if k == 0:
                            w_.append(("sv", evA))
                    p.op("tensor", lambda h, k=k, m=m, hqs=hqs: h.matmul(
                        pbank4[m],
                        ar2b[:, k * D + (4 + m) * 128:k * D + (5 + m) * 128],
                        hqs, start=(k == 0), stop=(k == 15)), waits=w_)
                mmB[k] = p.cnt["st"]
                yield
            stmm = p.cnt["st"]
            for m in range(4):
                if hh == 0:
                    svev2 = p.op("vector", lambda h, m=m: h.tensor_copy(
                        ohi[:, (4 + m) * LB:(5 + m) * LB], pbank4[m]),
                        waits=[("st", stmm)] if m == 0 else ())
                else:
                    svev2 = p.op("vector", lambda h, m=m: h.tensor_add(
                        ohi[:, (4 + m) * LB:(5 + m) * LB],
                        ohi[:, (4 + m) * LB:(5 + m) * LB], pbank4[m]),
                        waits=[("st", stmm)] if m == 0 else ())
            yield

        ldm2 = p.dma("scalar", lambda h, ob=ob: h.dma_start(
            out=xmidt[:], in_=spillM[ob]),
            waits=[("sv", p.cnt["sv"]), ("sp", p.cnt["sp"])])
        fin = 0
        cf = 0
        lastout = 0
        for lt in range(4):
            pdst = ppD if lt % 2 == 0 else ppE
            p.op("vector", lambda h, lt=lt: h.tensor_scalar_mul(
                stat[:, 17:18], mcols[:, lt:lt + 1], 1.0 / 127.0))
            cf = p.op("vector", lambda h: h.tensor_mul(stat[:, 17:18], stat[:, 17:18], rw2))
            for m in range(KD):
                w_ = [("sv", svev2 if lt < 2 else fin)] if m == 0 else []
                p.op("tensor", lambda h, lt=lt, m=m, pdst=pdst: h.matmul(
                    pdst[:, m * 128:(m + 1) * 128],
                    ohi[:, m * LB + lt * 128:m * LB + (lt + 1) * 128],
                    idf[:], is_transpose=True, start=True, stop=True), waits=w_)
            sttr = p.cnt["st"]
            fin = p.op("vector", lambda h, lt=lt, pdst=pdst: h.scalar_tensor_tensor(
                xoslot[:, 0:D], pdst[:, 0:D], stat[:, 17:18],
                xmidt[:, lt * D:(lt + 1) * D], OP.mult, OP.add),
                waits=[("st", sttr), ("sp", ldm2), ("sv", cf), ("sp", lastout)])
            lastout = p.dma("scalar", lambda h, ob=ob, lt=lt: h.dma_start(
                out=y_out[ob, lt * 128:(lt + 1) * 128, :], in_=xoslot[:, 0:D]),
                waits=[("sv", fin)])
            yield
        pbE[r] = {"sv": p.cnt["sv"], "ss": p.cnt["ss"]}

    def mkstep(g):
        done = [g is None]

        def s():
            if done[0]:
                return False
            try:
                next(g)
                return True
            except StopIteration:
                done[0] = True
                return False
        return s

    for r in range(NR + 2):
        sg_ = mkstep(scan_gen(r) if r < NR else None)
        pa_ = mkstep(post_a_gen(r - 1) if 0 <= r - 1 < NR else None)
        pb_ = mkstep(post_b_gen(r - 2) if 0 <= r - 2 < NR else None)
        progress = True
        while progress:
            progress = sg_()
            for _ in range(5):
                if pa_():
                    progress = True
                elif pb_():
                    progress = True

    p.wait("vector", [("sp", p.cnt["sp"]), ("sq", p.cnt["sq"]),
                      ("sg", p.cnt["sg"]), ("syq", p.cnt["syq"]),
                      ("sxq", p.cnt["sxq"])])

    with nc.Block() as block:
        p.finalize(block)
    es.close()
    return nc


def _mk_sel32():
    sel = np.zeros((32, 32, 128), np.float32)
    for s in range(32):
        sel[s, s, :] = 1.0
    return np.ascontiguousarray(sel.transpose(1, 0, 2).reshape(32, 32 * 128)
                                ).astype(ml_dtypes.bfloat16)


def host_prep(inputs, NB):
    NR = NB // 4
    OB = NB // 4
    L = NB * LB
    x = np.ascontiguousarray(inputs["L0_attn_seq"], dtype=np.float32)
    f32 = lambda v: np.ascontiguousarray(np.asarray(v), dtype=np.float32)

    ipT = f32(np.asarray(inputs["L0_attn_in_proj_w"]).T)
    xprT = f32(np.asarray(inputs["L0_attn_x_proj_w"]).T)
    dtpT_full = f32(np.asarray(inputs["L0_attn_dt_proj_w"]).T)
    cw_full = f32(np.asarray(inputs["L0_attn_conv_w"]))
    cb_full = f32(np.asarray(inputs["L0_attn_conv_b"]))
    dtb_full = f32(np.asarray(inputs["L0_attn_dt_proj_b"]))
    D_full = f32(np.asarray(inputs["L0_attn_D"]))
    shared = {
        "out_projT": f32(np.asarray(inputs["L0_attn_out_proj_w"]).T),
        "w1T": f32(np.asarray(inputs["L0_ffn_w1"]).T),
        "w2T": f32(np.asarray(inputs["L0_ffn_w2"]).T),
        "ln_g8": f32(np.asarray(inputs["L0_attn_ln_g"]).reshape(KD, 128).T),
        "ln_b8": f32(np.asarray(inputs["L0_attn_ln_b"]).reshape(KD, 128).T),
        "fcw96": f32(np.asarray(inputs["L0_ffn_conv_w"])[:, 0, :].reshape(KH, 128, 3)
                     .transpose(1, 0, 2).reshape(128, KH * 3)),
        "fcb32": f32(np.asarray(inputs["L0_ffn_conv_b"]).reshape(KH, 128).T),
        "rs_col": f32(np.broadcast_to(np.asarray(inputs["L0_attn_res_scale"],
                                                 np.float32).reshape(1, 1), (128, 1))),
        "identc": np.eye(128, dtype=ml_dtypes.bfloat16),
        "identf": np.eye(128, dtype=np.float32),
        "sel32": _mk_sel32(),
        "onesk2": np.concatenate([np.ones((1, 128)), np.zeros((1, 128))]
                                 ).astype(ml_dtypes.bfloat16),
    }

    in_maps = []
    for c in range(8):
        b, m = c // 4, c % 4
        sl = slice(512 * m, 512 * (m + 1))
        xb = x[b, :L].reshape(NB, LB, D)
        own = []
        for ob in range(OB):
            blks = []
            for j in range(4):
                g = 4 * ob + j
                blk = np.zeros((LB + 3, D), np.float32)
                blk[3:] = xb[g]
                if g > 0:
                    blk[0:3] = xb[g - 1][LB - 3:]
                blks.append(blk)
            own.append(np.stack(blks))
        x_own = np.ascontiguousarray(np.stack(own))
        x_mine = np.ascontiguousarray(
            np.stack([xb[4 * ob + m] for ob in range(OB)]))
        ip_s = np.concatenate([ipT[:, sl], ipT[:, E + 512 * m:E + 512 * (m + 1)]],
                              axis=1)
        xpT_s = np.ascontiguousarray(
            xprT[sl].reshape(4, 128, 96).transpose(1, 0, 2).reshape(128, 4 * 96))
        dtpT_s = np.ascontiguousarray(dtpT_full[:, sl])
        cw_s = f32(cw_full[sl].reshape(4, 128, 4).transpose(1, 0, 2)
                   .reshape(128, 16))
        cb_s = f32(cb_full[sl].reshape(4, 128).T)
        dtb_s = f32(dtb_full[sl].reshape(4, 128).T)
        D_s = f32(D_full[sl].reshape(4, 128).T)

        Al = np.asarray(inputs["L0_attn_A_log"])[512 * m:512 * (m + 1)]
        A_my = f32(Al.reshape(4, 128, S).transpose(1, 0, 2).reshape(128, 4 * S))

        idx = np.zeros((128, NIDX), np.int32)
        pcol = np.arange(128, dtype=np.int32)
        for j in range(4):
            for q in range(4):
                idx[:, IDXC[("dt", j, q)]] = j * 4128 + pcol * 16 + (m * 4 + q)
                idx[:, IDXC[("dx", j, q)]] = j * 4128 + E + pcol * 16 + (m * 4 + q)
            idx[0:32, IDXC[("bcs", j)]] = j * 4128 + 2 * E + np.arange(32)
        for ch in range(16):
            j, qd = ch // 4, ch % 4
            idx[:, IDXC[("y", ch)]] = j * 2048 + m * 512 + pcol * 4 + qd
        for ob in range(min(OB, 8)):
            g = 4 * ob + m
            lrow = 0 if g == 0 else ((g - 1) // 4) * 8 + ((g - 1) % 4) * 2 + 1
            rrow = 0 if g == NB - 1 else ((g + 1) // 4) * 8 + ((g + 1) % 4) * 2 + 0
            idx[0, IDXC[("xh", ob)]] = lrow
            idx[1, IDXC[("xh", ob)]] = rrow

        mcv_ = np.ones((128, 3 * NB), np.float32)
        mfn_ = np.ones((128, 16), np.float32)
        for g in range(NB):
            if g == 0:
                mcv_[:, 3 * g:3 * g + 3] = 0.0
        for ob in range(min(OB, 8)):
            g = 4 * ob + m
            if g == 0:
                mfn_[:, 2 * ob] = 0.0
            if g == NB - 1:
                mfn_[:, 2 * ob + 1] = 0.0

        im = dict(shared)
        im.update(x_own=x_own, x_mine=x_mine, in_projTs=np.ascontiguousarray(ip_s),
                  x_projTs=xpT_s, dt_projTs=dtpT_s, conv_w16=cw_s, conv_b4=cb_s,
                  dtb4=dtb_s, D4=D_s, A_log_my=A_my, idxs=idx, maskcv=mcv_,
                  maskfn=mfn_)
        in_maps.append(im)
    return in_maps


def kernel(**inputs):
    NB = np.asarray(inputs["L0_attn_seq"]).shape[1] // LB
    in_maps = host_prep(inputs, NB)
    nc = build_nc(NB)
    res = run_bass_kernel_spmd(nc, in_maps, core_ids=list(range(8)))
    L = NB * LB
    OB = NB // 4
    out = np.empty((2, L, D), np.float32)
    for c in range(8):
        b, m = c // 4, c % 4
        yo = np.asarray(res.results[c]["y_out"])
        for ob in range(OB):
            g = 4 * ob + m
            out[b, g * LB:(g + 1) * LB] = yo[ob]
    return out
